# version 03
# baseline (speedup 1.0000x reference)
import numpy as np
import ml_dtypes
import concourse.bass as bass
import concourse.mybir as mybir
from concourse.bass_utils import run_bass_kernel_spmd

F32 = mybir.dt.float32
BF16 = mybir.dt.bfloat16
AF = mybir.ActivationFunctionType
ALU = mybir.AluOpType
AX = mybir.AxisListType
BF = ml_dtypes.bfloat16

NCORES = 8
B = 2
T = 4096
D = 4096
TC = 1024
NH = 32
NKV = 8
DH = 128
EPS = 1e-5



class Prog:

    ENGS = ("tensor", "vector", "scalar", "gpsimd", "sync")
    SERIAL = ("vector", "scalar", "gpsimd")

    def __init__(self, name):
        self.nc = bass.Bass("TRN2", target_bir_lowering=False, debug=False)
        self.name = name
        self.plan = {e: [] for e in self.ENGS}
        self.sems = {}
        self.cnt = {}
        self.selfc = {e: 0 for e in self.SERIAL}
        for e in self.SERIAL:
            self.sem("@" + e)

    def sem(self, name):
        if name not in self.sems:
            self.sems[name] = self.nc.alloc_semaphore(name=name.replace("@", "s_"))
            self.cnt[name] = 0
        return name

    def val(self, name):
        self.sem(name)
        return self.cnt[name]

    def op(self, eng, fn, waits=(), inc=None, serial=True):
        waits = [t for t in waits if t is not None]
        waits = [(s, v) for (s, v) in waits if v > 0]
        for s, _ in waits:
            self.sem(s)
        incs = []
        tok = None
        if eng in self.selfc and serial:
            assert inc is None, "serialized engines use the self-sem only"
            sname = "@" + eng
            if self.selfc[eng] > 0:
                waits.append((sname, self.selfc[eng]))
            self.selfc[eng] += 1
            self.cnt[sname] = self.selfc[eng]
            incs.append((sname, 1))
            tok = (sname, self.selfc[eng])
        elif inc is not None:
            self.sem(inc)
            self.cnt[inc] += 1
            incs.append((inc, 1))
            tok = (inc, self.cnt[inc])
        self.plan[eng].append((waits, fn, incs))
        return tok

    def dma(self, eng, out, in_, inc, waits=(), **kw):
        self.sem(inc)
        waits = [t for t in waits if t is not None]
        waits = [(s, v) for (s, v) in waits if v > 0]
        for s, _ in waits:
            self.sem(s)
        if self.cnt[inc] > 0:
            waits.append((inc, self.cnt[inc]))
        if eng == "gpsimd" and self.selfc["gpsimd"] > 0:
            waits.append(("@gpsimd", self.selfc["gpsimd"]))
        self.cnt[inc] += 16
        self.plan[eng].append(
            (waits, lambda E, o=out, i=in_, k=kw: E.dma_start(out=o, in_=i, **k),
             [(inc, 16)]))
        return (inc, self.cnt[inc])

    def barrier(self, eng, waits=()):
        waits = [t for t in waits if t is not None]
        waits = [(s, v) for (s, v) in waits if v > 0]
        for s, _ in waits:
            self.sem(s)
        self.plan[eng].append((waits, None, []))

    def engine_done_tok(self, eng):
        return ("@" + eng, self.selfc[eng])

    def build(self):
        nc = self.nc
        with nc.Block() as block:
            for eng in self.ENGS:
                items = self.plan[eng]

                def mk(items_):
                    def body(E):
                        for waits, fn, incs in items_:
                            for s, v in waits:
                                E.wait_ge(self.sems[s], v)
                            if fn is None:
                                continue
                            inst = fn(E)
                            for s, n in incs:
                                inst = inst.then_inc(self.sems[s], n)
                    return body
                getattr(block, eng)(mk(items))
        return nc


def to_bf(x):
    return np.ascontiguousarray(np.asarray(x).astype(BF))


def pack_lhsT(wT):
    K, M = wT.shape
    Kc, Mc = K // 128, M // 128
    a = wT.reshape(Kc, 128, Mc, 128).transpose(2, 1, 0, 3).reshape(Mc, 128, Kc * 128)
    return np.ascontiguousarray(a)


def rope_tables(pos0, n):
    half = 64
    inv = 1.0 / (10000.0 ** (np.arange(half, dtype=np.float64) / half))
    ang = np.arange(pos0, pos0 + n, dtype=np.float64)[None, :] * inv[:, None]
    return (np.cos(ang).astype(np.float32), np.sin(ang).astype(np.float32))



WCH = 4096


def build_wcast(nch):
    p = Prog("wcast")
    nc = p.nc
    R = nch * 128
    src = nc.dram_tensor("wsrc", [R, WCH], F32, kind="ExternalInput").ap()
    dst = nc.dram_tensor("wdst", [R, WCH], BF16, kind="ExternalOutput").ap()
    bufs = [nc.alloc_sbuf_tensor(f"wi{i}", [128, WCH], F32) for i in range(3)]
    obufs = [nc.alloc_sbuf_tensor(f"wo{i}", [128, WCH], BF16) for i in range(3)]
    engs = ["vector", "scalar", "gpsimd"]
    st_toks = []
    for i in range(nch):
        b = i % 3
        w = [st_toks[i - 3]] if i >= 3 else []
        lt = p.dma("sync", bufs[b][:, :], src[128 * i:128 * (i + 1), :], f"ld{b}",
                   waits=w)
        e = engs[i % 3]
        if e == "scalar":
            ct = p.op(e, lambda E, b=b: E.copy(out=obufs[b][:, :], in_=bufs[b][:, :]),
                      waits=[lt])
        else:
            ct = p.op(e, lambda E, b=b: E.tensor_copy(out=obufs[b][:, :],
                                                      in_=bufs[b][:, :]), waits=[lt])
        st_toks.append(p.dma("sync", dst[128 * i:128 * (i + 1), :], obufs[b][:, :],
                             f"st{b}", waits=[ct]))
    p.barrier("sync", waits=[("st0", p.val("st0")), ("st1", p.val("st1")),
                             ("st2", p.val("st2"))])
    return p.build()


WEIGHT_KEYS = [
    "L0_attn_Wq", "L0_attn_Wk", "L0_attn_Wv", "L0_attn_Wo",
    "L0_ffn_W_org", "L0_ffn_lora_down", "L0_ffn_lora_up", "L0_ffn_lora_keys",
    "L0_ffn_Wq",
    "L1_attn_in_proj_w", "L1_attn_conv_weight", "L1_attn_conv_bias",
    "L1_attn_dt_bias", "L1_attn_A_log", "L1_attn_D", "L1_attn_norm_weight",
    "L1_attn_out_proj_w",
    "L1_ffn_W", "L1_ffn_bias", "L1_ffn_A", "L1_ffn_Bf", "L1_ffn_coeffs",
    "L1_ffn_scales",
]


def cast_weights(inputs):
    flats = [np.asarray(inputs[k], dtype=np.float32).reshape(-1) for k in WEIGHT_KEYS]
    sizes = [f.size for f in flats]
    cat = np.concatenate(flats)
    per = -(-cat.size // (NCORES * 128 * WCH)) * 128 * WCH
    cat = np.concatenate([cat, np.zeros(per * NCORES - cat.size, np.float32)])
    shards = cat.reshape(NCORES, per // WCH, WCH)
    nc = build_wcast(per // (WCH * 128))
    res = run_bass_kernel_spmd(nc, [{"wsrc": shards[c]} for c in range(NCORES)],
                               core_ids=list(range(NCORES)))
    out = np.concatenate([np.asarray(res.results[c]["wdst"]).reshape(-1)
                          for c in range(NCORES)])
    w = {}
    off = 0
    for k, s in zip(WEIGHT_KEYS, sizes):
        w[k] = out[off:off + s].reshape(np.asarray(inputs[k]).shape)
        off += s
    return w




def emit_ln(p, x_src_fn, nchunks, bufs, img, ident, ps_tr, tag, lnout_cb=None):
    xbuf, stats, mv, rstd, lntm = bufs
    TT = nchunks * 128
    imgap = img[:, :].rearrange("q (kc tt) -> q kc tt", tt=TT)
    nrm_toks, ev_toks, tr_toks = [], [], []
    for i in range(nchunks):
        b = i % 2
        w = [nrm_toks[i - 2]] if i >= 2 else []
        lt = x_src_fn(i, xbuf[b], w)
        for j in range(8):
            p.op("vector", lambda E, b=b, j=j: E.bn_stats(
                out=stats[:, 6 * j:6 * j + 6], in_=xbuf[b][:, 512 * j:512 * (j + 1)]),
                waits=[lt] if j == 0 else [])
        p.op("vector", lambda E: E.bn_aggr(out=mv[:, :], in_=stats[:, 0:48]))
        vt = p.op("vector", lambda E: E.tensor_scalar_add(rstd[:, :], mv[:, 1:2], EPS))
        p.op("scalar", lambda E: E.activation(out=rstd[:, :], in_=rstd[:, :],
                                              func=AF.Ln), waits=[vt])
        st = p.op("scalar", lambda E: E.activation(out=rstd[:, :], in_=rstd[:, :],
                                                   func=AF.Exp, scale=-0.5))
        nw = [st]
        if i >= 1:
            nw.append(tr_toks[-1])
        nt = p.op("vector", lambda E, b=b: E.tensor_scalar(
            out=lntm[:, :], in0=xbuf[b][:, :], scalar1=mv[:, 0:1], scalar2=rstd[:, :],
            op0=ALU.subtract, op1=ALU.mult), waits=nw)
        nrm_toks.append(nt)
        if lnout_cb is not None:
            nt = lnout_cb(i, xbuf[b], lntm) or nt
            nrm_toks[-1] = nt
        for g in range(4):
            pb = g % 2
            gi = i * 4 + g
            w = [nt] if g == 0 else []
            if gi >= 2:
                w.append(ev_toks[gi - 2])
            tt = None
            for u in range(8):
                kc = g * 8 + u
                tt = p.op("tensor", lambda E, pb=pb, u=u, kc=kc: E.transpose(
                    ps_tr[pb][:, 128 * u:128 * (u + 1)],
                    lntm[:, 128 * kc:128 * (kc + 1)], ident[:, :]),
                    waits=w if u == 0 else [],
                    inc=(tag + "_tr") if u == 7 else None)
            tr_toks.append(tt)
            ev_toks.append(p.op("vector", lambda E, pb=pb, g=g, i=i: E.tensor_copy(
                out=imgap[:, g * 8:(g + 1) * 8, 128 * i:128 * (i + 1)],
                in_=ps_tr[pb][:, :].rearrange("q (u t) -> q u t", u=8)),
                waits=[tt]))
    return ev_toks




def build_a1():
    p = Prog("a1")
    nc = p.nc
    x = nc.dram_tensor("x", [TC, D], F32, kind="ExternalInput").ap()
    wq = nc.dram_tensor("wq", [NH, 128, 4096], BF16, kind="ExternalInput").ap()
    wk = nc.dram_tensor("wk", [NKV, 128, 4096], BF16, kind="ExternalInput").ap()
    wv = nc.dram_tensor("wv", [NKV, 128, 4096], BF16, kind="ExternalInput").ap()
    ctab = nc.dram_tensor("ctab", [64, TC], F32, kind="ExternalInput").ap()
    stab = nc.dram_tensor("stab", [64, TC], F32, kind="ExternalInput").ap()
    idin = nc.dram_tensor("idin", [128, 128], BF16, kind="ExternalInput").ap()
    q_o = nc.dram_tensor("q_o", [NH, 128, TC], BF16, kind="ExternalOutput").ap()
    k_o = nc.dram_tensor("k_o", [NKV, 128, TC], BF16, kind="ExternalOutput").ap()
    v_o = nc.dram_tensor("v_o", [NKV, 128, TC], BF16, kind="ExternalOutput").ap()

    xbuf = [nc.alloc_sbuf_tensor(f"xb{i}", [128, D], F32) for i in range(2)]
    stats = nc.alloc_sbuf_tensor("stats", [128, 48], F32)
    mv = nc.alloc_sbuf_tensor("mv", [128, 2], F32)
    rstd = nc.alloc_sbuf_tensor("rstd", [128, 1], F32)
    lntm = nc.alloc_sbuf_tensor("lntm", [128, D], BF16)
    img = nc.alloc_sbuf_tensor("img", [128, 32 * TC], BF16)
    ident = nc.alloc_sbuf_tensor("ident", [128, 128], BF16)
    ps_tr = [nc.alloc_psum_tensor(f"pstr{i}", [128, 1024], BF16) for i in range(2)]
    ps_mm = [nc.alloc_psum_tensor(f"psmm{i}", [128, 512], F32) for i in range(4)]
    wbuf = [nc.alloc_sbuf_tensor(f"wb{i}", [128, 4096], BF16) for i in range(2)]
    cos = nc.alloc_sbuf_tensor("cos", [64, TC], F32)
    sin = nc.alloc_sbuf_tensor("sin", [64, TC], F32)
    t1 = nc.alloc_sbuf_tensor("t1", [64, TC], F32)
    t2 = nc.alloc_sbuf_tensor("t2", [64, TC], F32)
    ov = [nc.alloc_sbuf_tensor(f"ov{i}", [128, TC], BF16) for i in range(2)]

    p.dma("sync", ident[:, :], idin[:, :], "cld")
    p.dma("sync", cos[:, :], ctab[:, :], "cld")
    cld = p.dma("sync", sin[:, :], stab[:, :], "cld")
    p.barrier("tensor", waits=[cld])
    p.barrier("vector", waits=[cld])

    def xsrc(i, buf, w):
        return p.dma("sync", buf[:, :], x[128 * i:128 * (i + 1), :],
                     f"a_xld{i % 2}", waits=w)

    ev_toks = emit_ln(p, xsrc, 8, (xbuf, stats, mv, rstd, lntm), img, ident,
                      ps_tr, "a")
    imgdone = ev_toks[-1]
    imgap = img[:, :].rearrange("q (kc tt) -> q kc tt", tt=TC)

    jobs = ([("q", wq[h], q_o[h]) for h in range(NH)]
            + [("k", wk[h], k_o[h]) for h in range(NKV)]
            + [("v", wv[h], v_o[h]) for h in range(NKV)])

    mm_toks, ev2_toks, st_toks = [], [], []
    for ji, (kind, wsrc, osl) in enumerate(jobs):
        wb = ji % 2
        wlt = p.dma("sync", wbuf[wb][:, :], wsrc[:, :], f"wl{wb}",
                    waits=[mm_toks[ji - 2]] if ji >= 2 else [])
        mt = None
        for n in range(2):
            pb = 2 * (ji % 2) + n
            for kc in range(32):
                w = []
                if kc == 0 and n == 0:
                    w = [wlt, imgdone]
                    if ji >= 2:
                        w.append(ev2_toks[ji - 2])
                mt = p.op("tensor", lambda E, pb=pb, kc=kc, n=n, wb=wb: E.matmul(
                    ps_mm[pb][:, :], wbuf[wb][:, 128 * kc:128 * (kc + 1)],
                    imgap[:, kc, 512 * n:512 * (n + 1)],
                    start=(kc == 0), stop=(kc == 31)),
                    waits=w, inc="a_mme" if (kc == 31 and n == 1) else None)
        mm_toks.append(mt)
        ob = ov[ji % 2]
        sw = [st_toks[ji - 2]] if ji >= 2 else []
        if kind == "v":
            p.op("vector", lambda E, pb=2 * (ji % 2), ob=ob: E.tensor_copy(
                out=ob[:, 0:512], in_=ps_mm[pb][:, :]), waits=[mt] + sw)
            et = p.op("vector", lambda E, pb=2 * (ji % 2) + 1, ob=ob: E.tensor_copy(
                out=ob[:, 512:1024], in_=ps_mm[pb][:, :]))
        else:
            for n in range(2):
                ps = ps_mm[2 * (ji % 2) + n]
                sl = slice(512 * n, 512 * (n + 1))
                p.op("vector", lambda E, ps=ps, sl=sl: E.tensor_mul(
                    t1[:, sl], ps[0:64, :], cos[:, sl]),
                    waits=[mt] + sw if n == 0 else [])
                p.op("vector", lambda E, ps=ps, sl=sl: E.tensor_mul(
                    t2[:, sl], ps[64:128, :], sin[:, sl]))
                p.op("vector", lambda E, sl=sl, ob=ob: E.tensor_sub(
                    ob[0:64, sl], t1[:, sl], t2[:, sl]))
                p.op("vector", lambda E, ps=ps, sl=sl: E.tensor_mul(
                    t1[:, sl], ps[0:64, :], sin[:, sl]))
                p.op("vector", lambda E, ps=ps, sl=sl: E.tensor_mul(
                    t2[:, sl], ps[64:128, :], cos[:, sl]))
                et = p.op("vector", lambda E, sl=sl, ob=ob: E.tensor_add(
                    ob[64:128, sl], t1[:, sl], t2[:, sl]))
        ev2_toks.append(et)
        st_toks.append(p.dma("sync", osl[:, :], ob[:, :], f"a_sto{ji % 2}",
                             waits=[et]))
    p.barrier("sync", waits=[("a_sto0", p.val("a_sto0")),
                             ("a_sto1", p.val("a_sto1"))])
    return p.build()


def run_a1(x_slices, wbf):
    wq_img = pack_lhsT(to_bf(wbf["L0_attn_Wq"]))
    wk_img = pack_lhsT(to_bf(wbf["L0_attn_Wk"]))
    wv_img = pack_lhsT(to_bf(wbf["L0_attn_Wv"]))
    ident = np.eye(128, dtype=BF)
    nc = build_a1()
    in_maps = []
    for c in range(NCORES):
        cosb, sinb = rope_tables((c % 4) * TC, TC)
        in_maps.append({"x": x_slices[c], "wq": wq_img, "wk": wk_img,
                        "wv": wv_img, "ctab": cosb, "stab": sinb, "idin": ident})
    return run_bass_kernel_spmd(nc, in_maps, core_ids=list(range(NCORES))).results






class LNEmitter:

    def __init__(self, p, nc, ident, pool, tag):
        self.p, self.nc, self.ident, self.pool, self.tag = p, nc, ident, pool, tag
        self.stats = nc.alloc_sbuf_tensor(tag + "_stats", [128, 48], F32)
        self.mv = nc.alloc_sbuf_tensor(tag + "_mv", [128, 2], F32)
        self.rstd = nc.alloc_sbuf_tensor(tag + "_rstd", [128, 1], F32)
        self.lntm = nc.alloc_sbuf_tensor(tag + "_lntm", [128, D], BF16)
        self.last_tr = None

    def chunk(self, buf, ready, img_dst):
        p = self.p
        for j in range(8):
            p.op("vector", lambda E, j=j, buf=buf: E.bn_stats(
                out=self.stats[:, 6 * j:6 * j + 6],
                in_=buf[:, 512 * j:512 * (j + 1)]),
                waits=[ready] if j == 0 else [])
        p.op("vector", lambda E: E.bn_aggr(out=self.mv[:, :], in_=self.stats[:, 0:48]))
        vt = p.op("vector", lambda E: E.tensor_scalar_add(
            self.rstd[:, :], self.mv[:, 1:2], EPS))
        p.op("scalar", lambda E: E.activation(out=self.rstd[:, :], in_=self.rstd[:, :],
                                              func=AF.Ln), waits=[vt])
        st = p.op("scalar", lambda E: E.activation(out=self.rstd[:, :],
                                                   in_=self.rstd[:, :],
                                                   func=AF.Exp, scale=-0.5))
        nw = [st] + ([self.last_tr] if self.last_tr else [])
        rel = p.op("vector", lambda E, buf=buf: E.tensor_scalar(
            out=self.lntm[:, :], in0=buf[:, :], scalar1=self.mv[:, 0:1],
            scalar2=self.rstd[:, :], op0=ALU.subtract, op1=ALU.mult), waits=nw)
        ev = None
        for g in range(4):
            dst, extraw = img_dst(g)
            w = ([rel] if g == 0 else []) + ([self.pool["rel"]] if self.pool["rel"] else [])
            tt = None
            for u in range(8):
                kc = g * 8 + u
                tt = p.op("tensor", lambda E, u=u, kc=kc: E.transpose(
                    self.pool["ap"][:, 128 * u:128 * (u + 1)],
                    self.lntm[:, 128 * kc:128 * (kc + 1)], self.ident[:, :]),
                    waits=w if u == 0 else [],
                    inc=(self.tag + "_tr") if u == 7 else None)
            self.last_tr = tt
            ev = p.op("vector", lambda E, dst=dst: E.tensor_copy(
                out=dst, in_=self.pool["ap"][:, :].rearrange("q (u t) -> q u t", u=8)),
                waits=[tt] + [x for x in extraw if x])
            self.pool["rel"] = ev
        return rel, ev



SCL = 1.0 / float(np.sqrt(np.float32(128.0)))
EBIAS = -20.0


def build_a2(dbg=False):
    p = Prog("a2")
    nc = p.nc
    q_i = nc.dram_tensor("q_i", [NH, 128, TC], BF16, kind="ExternalInput").ap()
    kin = nc.dram_tensor("kin", [2, NKV, 128, T], BF16, kind="ExternalInput").ap()
    vin = nc.dram_tensor("vin", [2, NKV, 128, T], BF16, kind="ExternalInput").ap()
    vcol = nc.dram_tensor("vcol", [128, 64], BF16, kind="ExternalInput").ap()
    tri = nc.dram_tensor("tri", [128, 2048], BF16, kind="ExternalInput").ap()
    wo = nc.dram_tensor("wo", [32, 128, 4096], BF16, kind="ExternalInput").ap()
    x = nc.dram_tensor("x", [TC, D], F32, kind="ExternalInput").ap()
    onesr = nc.dram_tensor("onesr", [1, 128], F32, kind="ExternalInput").ap()
    idin = nc.dram_tensor("idin", [128, 128], BF16, kind="ExternalInput").ap()
    x1_o = nc.dram_tensor("x1_o", [TC, D], F32, kind="ExternalOutput").ap()
    if dbg:
        d_pt = nc.dram_tensor("d_pt", [128, 16384], BF16, kind="ExternalOutput").ap()
        d_l = nc.dram_tensor("d_l", [1, 512], F32, kind="ExternalOutput").ap()
        d_rl = nc.dram_tensor("d_rl", [128, 512], F32, kind="ExternalOutput").ap()
        d_att = nc.dram_tensor("d_att", [128, 512], BF16, kind="ExternalOutput").ap()
    ln_o = nc.dram_tensor("ln_o", [8, 128, 4096], BF16, kind="ExternalOutput").ap()
    pt_o = nc.dram_tensor("pt_o", [128, 32], F32, kind="ExternalOutput").ap()

    kb = [nc.alloc_sbuf_tensor(f"kb{i}", [128, T], BF16) for i in range(2)]
    vb = [nc.alloc_sbuf_tensor(f"vb{i}", [128, T], BF16) for i in range(2)]
    qb_ = [nc.alloc_sbuf_tensor(f"qbuf{i}", [128, 512], BF16) for i in range(2)]
    pt = nc.alloc_sbuf_tensor("pt", [128, 32 * 512], BF16)
    att = nc.alloc_sbuf_tensor("att", [128, 32 * 512], BF16)
    tri_sb = nc.alloc_sbuf_tensor("tri_sb", [128, 2048], BF16)
    vcol_sb = nc.alloc_sbuf_tensor("vcol_sb", [128, 64], BF16)
    lsb = nc.alloc_sbuf_tensor("lsb", [1, 512], F32)
    rlsb = nc.alloc_sbuf_tensor("rlsb", [128, 512], F32)
    ones_sb = nc.alloc_sbuf_tensor("ones_sb", [1, 128], F32)
    wob = [nc.alloc_sbuf_tensor(f"wob{i}", [128, 4096], BF16) for i in range(2)]
    projsb = nc.alloc_sbuf_tensor("projsb", [128, 32 * 512], BF16)
    xres = [nc.alloc_sbuf_tensor(f"xres{i}", [128, D], F32) for i in range(2)]
    ident = nc.alloc_sbuf_tensor("ident", [128, 128], BF16)
    strip = nc.alloc_sbuf_tensor("strip", [128, 4096], BF16)
    pacc = nc.alloc_sbuf_tensor("pacc", [128, 8 * 32], F32)
    ptls = nc.alloc_sbuf_tensor("ptls", [128, 32], F32)
    ebias = nc.alloc_sbuf_tensor("ebias", [128, 1], F32)

    pss = [nc.alloc_psum_tensor(f"pss{i}", [128, 1024], F32) for i in range(2)]
    psl = nc.alloc_psum_tensor("psl", [128, 512], F32)
    psav = nc.alloc_psum_tensor("psav", [128, 512], F32)
    psrl = nc.alloc_psum_tensor("psrl", [128, 512], F32)
    psb = nc.alloc_psum_tensor("psb", [128, 1024], BF16)

    p.dma("sync", tri_sb[:, :], tri[:, :], "cld")
    p.dma("sync", vcol_sb[:, :], vcol[:, :], "cld")
    p.dma("sync", ones_sb[:, :], onesr[:, :], "cld")
    cld = p.dma("sync", ident[:, :], idin[:, :], "cld")
    p.barrier("tensor", waits=[cld])
    p.barrier("vector", waits=[cld])
    p.barrier("scalar", waits=[cld])

    ebt = p.op("vector", lambda E: E.memset(ebias[:, :], EBIAS))
    p.barrier("scalar", waits=[ebt])
    pool = {"ap": psb, "rel": None}
    lne = LNEmitter(p, nc, ident, pool, "a2ln")
    stripap = strip[:, :].rearrange("q (kc t) -> q kc t", t=128)

    kbrel = [None, None]
    vbrel = [None, None]
    qrel = [None, None]
    exp_tok = [None, None]
    prev_av = None
    prev_evl = None
    prev_evrl = None
    prev_nrm = None
    wo_last_mm = None
    evp_last = None
    tr_after_qb = None
    lnrel, x1st = {}, {}
    strip_st = None
    kt_cur = [None, None]
    vt_cur = [None, None]

    for qb in range(2):
        nrm = None
        for h in range(NH):
            hk = h // 4
            hb = hk % 2
            if h % 4 == 0:
                kt_cur[hb] = p.dma("sync", kb[hb][:, :], kin[qb][hk][:, :],
                                   f"kld{hb}",
                                   waits=[kbrel[hb]] if kbrel[hb] else [])
                vt_cur[hb] = p.dma("sync", vb[hb][:, :], vin[qb][hk][:, :],
                                   f"vld{hb}",
                                   waits=[vbrel[hb]] if vbrel[hb] else [])
            qt = p.dma("sync", qb_[h % 2][:, :], q_i[h][:, 512 * qb:512 * (qb + 1)],
                       f"qld{h % 2}", waits=[qrel[h % 2]] if qrel[h % 2] else [])
            smm = None
            for g in range(16):
                pp = g % 2
                for u in range(2):
                    kc = 2 * g + u
                    w = []
                    if u == 0:
                        if exp_tok[pp] is not None:
                            w.append(exp_tok[pp])
                        if g == 0:
                            w += [kt_cur[hb], qt]
                            if evp_last is not None:
                                w.append(evp_last)
                                evp_last = None
                    smm = p.op("tensor", lambda E, pp=pp, u=u, kc=kc, hb=hb, hq=h % 2:
                               E.matmul(pss[pp][:, 512 * u:512 * (u + 1)],
                                        kb[hb][:, 128 * kc:128 * (kc + 1)],
                                        qb_[hq][:, :], start=True, stop=True),
                               waits=w, inc="a2s" if u == 1 else None)
                ew = [smm]
                if g <= 1 and prev_av is not None:
                    ew.append(prev_av)
                exp_tok[pp] = p.op("scalar", lambda E, pp=pp, g=g: E.activation(
                    out=pt[:, 1024 * g:1024 * (g + 1)], in_=pss[pp][:, :],
                    func=AF.Exp, scale=SCL, bias=ebias[:, :]), waits=ew)
            kbrel[hb] = smm
            qrel[h % 2] = smm
            mk = p.op("vector", lambda E: E.tensor_mul(
                pt[:, 28 * 512:32 * 512], pt[:, 28 * 512:32 * 512], tri_sb[:, :]),
                waits=[exp_tok[0], exp_tok[1]])
            lmm = None
            for kc in range(32):
                w = []
                if kc == 0:
                    w = [mk] + ([prev_evl] if prev_evl else [])
                lmm = p.op("tensor", lambda E, kc=kc, qb=qb: E.matmul(
                    psl[0:1, :], vcol_sb[:, 32 * qb + kc:32 * qb + kc + 1],
                    pt[:, 512 * kc:512 * (kc + 1)],
                    start=(kc == 0), stop=(kc == 31)),
                    waits=w, inc="a2l" if kc == 31 else None)
            avm = None
            for kc in range(32):
                w = []
                if kc == 0:
                    w = [vt_cur[hb]] + ([prev_nrm] if prev_nrm else [])
                avm = p.op("tensor", lambda E, kc=kc, hb=hb: E.matmul(
                    psav[:, :], vb[hb][:, 128 * kc:128 * (kc + 1)],
                    pt[:, 512 * kc:512 * (kc + 1)],
                    start=(kc == 0), stop=(kc == 31)),
                    waits=w, inc="a2av" if kc == 31 else None)
            vbrel[hb] = avm
            prev_av = avm
            evl = p.op("vector", lambda E: E.tensor_copy(out=lsb[:, :],
                                                         in_=psl[0:1, :]),
                       waits=[lmm])
            prev_evl = evl
            rcp = p.op("vector", lambda E: E.reciprocal(lsb[:, :], lsb[:, :]))
            bc = p.op("tensor", lambda E: E.matmul(psrl[:, :], ones_sb[:, :],
                                                   lsb[:, :], start=True, stop=True),
                      waits=[rcp] + ([prev_evrl] if prev_evrl else []), inc="a2rl")
            evrl = p.op("vector", lambda E: E.tensor_copy(out=rlsb[:, :],
                                                          in_=psrl[:, :]), waits=[bc])
            prev_evrl = evrl
            nw = [avm]
            if h == 0 and wo_last_mm is not None:
                nw.append(wo_last_mm)
            nrm = p.op("vector", lambda E, h=h: E.tensor_mul(
                att[:, 512 * h:512 * (h + 1)], psav[:, :], rlsb[:, :]), waits=nw)
            prev_nrm = nrm
            if dbg and qb == 0 and h == 0:
                p.dma("sync", d_pt[:, :], pt[:, :], "dbg", waits=[mk])
                p.dma("sync", d_l[:, :], lsb[:, :], "dbg", waits=[rcp])
                p.dma("sync", d_rl[:, :], rlsb[:, :], "dbg", waits=[evrl])
                p.dma("sync", d_att[:, :], att[:, 0:512], "dbg", waits=[nrm])
                prev_nrm = ("dbg", p.val("dbg"))
                prev_evl = prev_nrm
                prev_av = None
                p.barrier("tensor", waits=[("dbg", p.val("dbg"))])
        womm, evp = {}, {}
        for mc in range(32):
            ww = [womm[mc - 2]] if mc >= 2 else (
                [wo_last_mm] if wo_last_mm is not None else [])
            wt = p.dma("sync", wob[mc % 2][:, :], wo[mc][:, :], f"wld{mc % 2}",
                       waits=ww)
            mmt = None
            for kc in range(32):
                w = []
                if kc == 0:
                    w = [wt, nrm]
                    if mc >= 2:
                        w.append(evp[mc - 2])
                    if mc <= 1 and tr_after_qb is not None:
                        w.append(tr_after_qb)
                mmt = p.op("tensor", lambda E, kc=kc, mc=mc: E.matmul(
                    pss[mc % 2][:, 0:512], wob[mc % 2][:, 128 * kc:128 * (kc + 1)],
                    att[:, 512 * kc:512 * (kc + 1)],
                    start=(kc == 0), stop=(kc == 31)),
                    waits=w, inc="a2wo" if kc == 31 else None)
            womm[mc] = mmt
            evp[mc] = p.op("vector", lambda E, mc=mc: E.tensor_copy(
                out=projsb[:, 512 * mc:512 * (mc + 1)], in_=pss[mc % 2][:, 0:512]),
                waits=[mmt])
        wo_last_mm = womm[31]
        evp_last = evp[31]
        for t in range(4):
            i = 4 * qb + t
            xw = []
            if i >= 2:
                xw = [lnrel[i - 2], x1st[i - 2]]
            xt = p.dma("sync", xres[i % 2][:, :], x[128 * i:128 * (i + 1), :],
                       f"xld{i % 2}", waits=xw)
            add = None
            for g2 in range(4):
                w = [evp[31]] if g2 == 0 else []
                if pool["rel"] is not None:
                    w.append(pool["rel"])
                tt = None
                for u in range(8):
                    mcc = 8 * g2 + u
                    tt = p.op("tensor", lambda E, u=u, mcc=mcc, t=t: E.transpose(
                        psb[:, 128 * u:128 * (u + 1)],
                        projsb[:, 512 * mcc + 128 * t:512 * mcc + 128 * (t + 1)],
                        ident[:, :]),
                        waits=w if u == 0 else [], inc="a2tr" if u == 7 else None)
                add = p.op("vector", lambda E, g2=g2, i=i: E.tensor_add(
                    xres[i % 2][:, 1024 * g2:1024 * (g2 + 1)],
                    xres[i % 2][:, 1024 * g2:1024 * (g2 + 1)], psb[:, :]),
                    waits=[tt, xt])
                pool["rel"] = add
            x1st[i] = p.dma("sync", x1_o[128 * i:128 * (i + 1), :], xres[i % 2][:, :],
                            f"x1st{i % 2}", waits=[add])
            rel, ev = lne.chunk(xres[i % 2], add,
                                lambda g, st_=strip_st: (stripap[:, g * 8:(g + 1) * 8, :],
                                                         [st_]))
            lnrel[i] = rel
            prd = p.op("vector", lambda E, i=i: E.tensor_reduce(
                pacc[:, 32 * i:32 * (i + 1)], stripap[:, :, :], AX.X, ALU.add),
                waits=[ev])
            strip_st = p.dma("sync", ln_o[i][:, :], strip[:, :], "lnst", waits=[prd])
        tr_after_qb = ("a2tr", p.val("a2tr"))
    fr = p.op("vector", lambda E: E.tensor_reduce(
        ptls[:, :], pacc[:, :].rearrange("q (i k) -> q k i", i=8), AX.X, ALU.add))
    p.dma("sync", pt_o[:, :], ptls[:, :], "ptst", waits=[fr])
    p.barrier("sync", waits=[("x1st0", p.val("x1st0")), ("x1st1", p.val("x1st1")),
                             ("lnst", p.val("lnst")), ("ptst", p.val("ptst"))])
    return p.build()


def prep_a2_core(c, a1res, wo_img, x_slices, tri_m, onesr, ident):
    j = c % 4
    bc0 = 4 * (c // 4)
    Kf = [np.concatenate([np.asarray(a1res[cc]["k_o"])[hk] for cc in
                          range(bc0, bc0 + 4)], axis=1) for hk in range(NKV)]
    Vf = [np.concatenate([np.asarray(a1res[cc]["v_o"])[hk] for cc in
                          range(bc0, bc0 + 4)], axis=1) for hk in range(NKV)]
    kin = np.zeros((2, NKV, 128, T), BF)
    vin = np.zeros((2, NKV, 128, T), BF)
    vcol = np.zeros((128, 64), BF)
    for b in range(2):
        d0 = 8 * j + 4 * b
        for kc in range(32):
            a = (kc - 28 + d0) % 32
            valid = (kc >= 28) or (a < d0)
            if valid:
                vcol[:, 32 * b + kc] = BF(1.0)
            for hk in range(NKV):
                kin[b, hk, :, 128 * kc:128 * (kc + 1)] = Kf[hk][:, 128 * a:128 * (a + 1)]
                if valid:
                    vin[b, hk, :, 128 * kc:128 * (kc + 1)] = \
                        Kf[hk][:, 0:0] if False else Vf[hk][:, 128 * a:128 * (a + 1)].T
    return {"q_i": np.asarray(a1res[c]["q_o"]), "kin": kin, "vin": vin,
            "vcol": vcol, "tri": tri_m, "wo": wo_img, "x": x_slices[c],
            "onesr": onesr, "idin": ident}


def run_a2(x_slices, a1res, wbf):
    wo_img = pack_lhsT(to_bf(wbf["L0_attn_Wo"]))
    tri_m = np.zeros((128, 2048), BF)
    for i_ in range(4):
        for pp in range(128):
            tri_m[pp, 512 * i_:512 * (i_ + 1)] = (128 * i_ + pp <=
                                                  np.arange(512)).astype(BF)
    onesr = np.ones((1, 128), np.float32)
    ident = np.eye(128, dtype=BF)
    nc = build_a2()
    in_maps = [prep_a2_core(c, a1res, wo_img, x_slices, tri_m, onesr, ident)
               for c in range(NCORES)]
    return run_bass_kernel_spmd(nc, in_maps, core_ids=list(range(NCORES))).results




def build_f2(dbg=False):
    p = Prog("f2")
    nc = p.nc
    lnin = nc.dram_tensor("lnin", [128, 32 * TC], BF16, kind="ExternalInput").ap()
    x1 = nc.dram_tensor("x1", [TC, D], F32, kind="ExternalInput").ap()
    parts = nc.dram_tensor("parts", [4, 128, 32], F32, kind="ExternalInput").ap()
    worg = nc.dram_tensor("worg", [32, 128, 4096], BF16, kind="ExternalInput").ap()
    upw = nc.dram_tensor("upw", [32, 128, 512], BF16, kind="ExternalInput").ap()
    dnw = nc.dram_tensor("dnw", [4, 128, 4096], BF16, kind="ExternalInput").ap()
    wqt = nc.dram_tensor("wqt", [1, 128, 4096], BF16, kind="ExternalInput").ap()
    keyt = nc.dram_tensor("keyt", [128, 8], BF16, kind="ExternalInput").ap()
    sel = nc.dram_tensor("sel", [8, 512], BF16, kind="ExternalInput").ap()
    ones1 = nc.dram_tensor("ones1", [1, 1], BF16, kind="ExternalInput").ap()
    idin = nc.dram_tensor("idin", [128, 128], BF16, kind="ExternalInput").ap()
    x2_o = nc.dram_tensor("x2_o", [TC, D], F32, kind="ExternalOutput").ap()
    ln_o = nc.dram_tensor("ln_o", [8, 128, 4096], BF16, kind="ExternalOutput").ap()
    if dbg:
        d_mean = nc.dram_tensor("d_mean", [128, 32], F32, kind="ExternalOutput").ap()
        d_q = nc.dram_tensor("d_q", [128, 1], BF16, kind="ExternalOutput").ap()
        d_co = nc.dram_tensor("d_co", [1, 8], BF16, kind="ExternalOutput").ap()
        d_cv = nc.dram_tensor("d_cv", [128, 4], F32, kind="ExternalOutput").ap()
        d_xd = nc.dram_tensor("d_xd", [128, 4 * TC], BF16, kind="ExternalOutput").ap()

    img = nc.alloc_sbuf_tensor("img", [128, 32 * TC], BF16)
    wb = [nc.alloc_sbuf_tensor(f"wb{i}", [128, 4096], BF16) for i in range(2)]
    ub = [nc.alloc_sbuf_tensor(f"ub{i}", [128, 512], BF16) for i in range(2)]
    parts_sb = nc.alloc_sbuf_tensor("parts_sb", [128, 4 * 32], F32)
    mean_sb = nc.alloc_sbuf_tensor("mean_sb", [128, 32], F32)
    mean_bf = nc.alloc_sbuf_tensor("mean_bf", [128, 32], BF16)
    qsb = nc.alloc_sbuf_tensor("qsb", [128, 1], BF16)
    keyt_sb = nc.alloc_sbuf_tensor("keyt_sb", [128, 8], BF16)
    sel_sb = nc.alloc_sbuf_tensor("sel_sb", [8, 512], BF16)
    ones1_sb = nc.alloc_sbuf_tensor("ones1_sb", [1, 1], BF16)
    ssb = nc.alloc_sbuf_tensor("ssb", [1, 8], F32)
    nm = nc.alloc_sbuf_tensor("nm", [1, 1], F32)
    esum = nc.alloc_sbuf_tensor("esum", [1, 1], F32)
    co_bf = nc.alloc_sbuf_tensor("co_bf", [1, 8], BF16)
    cocol = nc.alloc_sbuf_tensor("cocol", [8, 1], BF16)
    cvec = nc.alloc_sbuf_tensor("cvec", [128, 4], F32)
    xds = nc.alloc_sbuf_tensor("xds", [128, 4 * TC], BF16)
    projsb = nc.alloc_sbuf_tensor("projsb", [128, 32 * 512], BF16)
    xres = [nc.alloc_sbuf_tensor(f"xres{i}", [128, D], F32) for i in range(2)]
    ident = nc.alloc_sbuf_tensor("ident", [128, 128], BF16)
    strip = nc.alloc_sbuf_tensor("strip", [128, 4096], BF16)

    ps = [nc.alloc_psum_tensor(f"f2ps{i}", [128, 1024], F32) for i in range(2)]
    ps1 = nc.alloc_psum_tensor("f2psq", [128, 512], F32)
    psb = nc.alloc_psum_tensor("f2psb", [128, 1024], BF16)

    imgap = img[:, :].rearrange("q (kc tt) -> q kc tt", tt=TC)
    stripap = strip[:, :].rearrange("q (kc t) -> q kc t", t=128)

    ildt = p.dma("sync", img[:, :], lnin[:, :], "ild")
    p.dma("sync", parts_sb[:, :].rearrange("q (j k) -> q j k", j=4),
          parts[:, :].rearrange("j q k -> q j k"), "cld")
    p.dma("sync", keyt_sb[:, :], keyt[:, :], "cld")
    p.dma("sync", sel_sb[:, :], sel[:, :], "cld")
    p.dma("sync", ones1_sb[:, :], ones1[:, :], "cld")
    cld = p.dma("sync", ident[:, :], idin[:, :], "cld")
    p.barrier("vector", waits=[cld])
    p.barrier("tensor", waits=[cld, ildt])

    p.op("vector", lambda E: E.tensor_add(mean_sb[:, :], parts_sb[:, 0:32],
                                          parts_sb[:, 32:64]))
    p.op("vector", lambda E: E.tensor_add(mean_sb[:, :], mean_sb[:, :],
                                          parts_sb[:, 64:96]))
    p.op("vector", lambda E: E.tensor_add(mean_sb[:, :], mean_sb[:, :],
                                          parts_sb[:, 96:128]))
    mt = p.op("vector", lambda E: E.tensor_scalar_mul(mean_bf[:, :], mean_sb[:, :],
                                                      1.0 / 4096.0))
    wqld = p.dma("sync", wb[0][:, :], wqt[0][:, :], "wld0")
    qmm = None
    for kc in range(32):
        qmm = p.op("tensor", lambda E, kc=kc: E.matmul(
            ps1[:, 0:1], wb[0][:, 128 * kc:128 * (kc + 1)],
            mean_bf[:, kc:kc + 1], start=(kc == 0), stop=(kc == 31)),
            waits=[wqld, mt] if kc == 0 else [], inc="qmm" if kc == 31 else None)
    qe = p.op("vector", lambda E: E.tensor_copy(out=qsb[:, :], in_=ps1[:, 0:1]),
              waits=[qmm])
    smm = p.op("tensor", lambda E: E.matmul(ps1[0:1, 4:12], qsb[:, :],
                                            keyt_sb[:, :], start=True, stop=True),
               waits=[qe], inc="smm")
    se = p.op("vector", lambda E: E.tensor_scalar_mul(ssb[:, :], ps1[0:1, 4:12],
                                                      SCL), waits=[smm])
    p.op("vector", lambda E: E.reduce_max(nm[:, :], ssb[:, :], axis=AX.X,
                                          negate=True))
    et = p.op("scalar", lambda E: E.activation(out=ssb[:, :], in_=ssb[:, :],
                                               func=AF.Exp, bias=nm[:, :],
                                               accum_out=esum[:, :]),
              waits=[p.engine_done_tok("vector")])
    p.op("vector", lambda E: E.reciprocal(esum[:, :], esum[:, :]), waits=[et])
    ct = p.op("vector", lambda E: E.tensor_scalar_mul(co_bf[:, :], ssb[:, :],
                                                      esum[:, :]))
    ctm = p.op("tensor", lambda E: E.matmul(ps1[0:8, 16:17], co_bf[:, :],
                                            ones1_sb[:, :], start=True, stop=True),
               waits=[ct], inc="ctm")
    cce = p.op("vector", lambda E: E.tensor_copy(out=cocol[:, :],
                                                 in_=ps1[0:8, 16:17]), waits=[ctm])
    vmm = None
    for j in range(4):
        vmm = p.op("tensor", lambda E, j=j: E.matmul(
            ps1[:, 20 + j:21 + j], sel_sb[:, 128 * j:128 * (j + 1)], cocol[:, :],
            start=True, stop=True), waits=[cce] if j == 0 else [],
            inc="vmm" if j == 3 else None)
    cvt = p.op("vector", lambda E: E.tensor_copy(out=cvec[:, :], in_=ps1[:, 20:24]),
               waits=[vmm])
    if dbg:
        p.dma("sync", d_mean[:, :], mean_sb[:, :], "dbg", waits=[mt])
        p.dma("sync", d_q[:, :], qsb[:, :], "dbg", waits=[qe])
        p.dma("sync", d_co[:, :], co_bf[:, :], "dbg", waits=[ct])
        p.dma("sync", d_cv[:, :], cvec[:, :], "dbg", waits=[cvt])

    dnt = {}
    xde = {}
    prev_dn_mm = None
    for j in range(4):
        dw = []
        if j == 1:
            dw = [qmm]
        elif j >= 2:
            dw = [prev_dn_mm]
        dnt[j] = p.dma("sync", wb[(j + 1) % 2][:, :], dnw[j][:, :],
                       f"wld{(j + 1) % 2}", waits=dw)
        mmt = None
        for n in range(2):
            w0 = [dnt[j]]
            if j >= 2:
                w0.append(xde[(j - 2, n)])
            for kc in range(32):
                mmt = p.op("tensor", lambda E, kc=kc, n=n, j=j: E.matmul(
                    ps[j % 2][:, 512 * n:512 * (n + 1)],
                    wb[(j + 1) % 2][:, 128 * kc:128 * (kc + 1)],
                    imgap[:, kc, 512 * n:512 * (n + 1)],
                    start=(kc == 0), stop=(kc == 31)),
                    waits=w0 if kc == 0 else [],
                    inc="f2xd" if kc == 31 else None)
            xde[(j, n)] = p.op("vector", lambda E, j=j, n=n: E.tensor_scalar(
                out=xds[:, TC * j + 512 * n:TC * j + 512 * (n + 1)],
                in0=ps[j % 2][:, 512 * n:512 * (n + 1)],
                scalar1=cvec[:, j:j + 1], scalar2=None, op0=ALU.mult),
                waits=[mmt, cvt] if n == 0 else [mmt])
        prev_dn_mm = mmt
    if dbg:
        p.dma("sync", d_xd[:, :], xds[:, :], "dbg", waits=[xde[(3, 1)]])

    pool = {"ap": psb, "rel": None}
    lne = LNEmitter(p, nc, ident, pool, "f2ln")
    lnrel, x2st = {}, {}
    strip_st = None
    tr_after = None
    last_wo_of_half = None
    for half in range(2):
        evp = {}
        womm = {}
        for mc in range(32):
            ww = []
            if mc >= 2:
                ww = [womm[mc - 2]]
            elif half == 0:
                ww = [xde[(3, 1)]]
            else:
                ww = [last_wo_of_half]
            wt = p.dma("sync", wb[mc % 2][:, :], worg[mc][:, :], f"wld{mc % 2}",
                       waits=ww)
            ut = p.dma("sync", ub[mc % 2][:, :], upw[mc][:, :], f"uld{mc % 2}",
                       waits=ww)
            mmt = None
            w0 = [wt, ut]
            if mc >= 2:
                w0.append(evp[mc - 2])
            if mc <= 1 and tr_after is not None:
                w0.append(tr_after)
            for kc in range(32):
                mmt = p.op("tensor", lambda E, kc=kc, mc=mc, half=half: E.matmul(
                    ps[mc % 2][:, 0:512], wb[mc % 2][:, 128 * kc:128 * (kc + 1)],
                    imgap[:, kc, 512 * half:512 * (half + 1)],
                    start=(kc == 0), stop=False),
                    waits=w0 if kc == 0 else [])
            for j in range(4):
                mmt = p.op("tensor", lambda E, j=j, mc=mc, half=half: E.matmul(
                    ps[mc % 2][:, 0:512], ub[mc % 2][:, 128 * j:128 * (j + 1)],
                    xds[:, TC * j + 512 * half:TC * j + 512 * (half + 1)],
                    start=False, stop=(j == 3)),
                    waits=[], inc="f2wo" if j == 3 else None)
            womm[mc] = mmt
            evp[mc] = p.op("vector", lambda E, mc=mc: E.tensor_copy(
                out=projsb[:, 512 * mc:512 * (mc + 1)], in_=ps[mc % 2][:, 0:512]),
                waits=[mmt])
        last_wo_of_half = womm[31]
        for t in range(4):
            i = 4 * half + t
            xw = []
            if i >= 2:
                xw = [lnrel[i - 2], x2st[i - 2]]
            xt = p.dma("sync", xres[i % 2][:, :], x1[128 * i:128 * (i + 1), :],
                       f"xld{i % 2}", waits=xw)
            add = None
            for g2 in range(4):
                w = [evp[31]] if g2 == 0 else []
                if pool["rel"] is not None:
                    w.append(pool["rel"])
                tt = None
                for u in range(8):
                    mcc = 8 * g2 + u
                    tt = p.op("tensor", lambda E, u=u, mcc=mcc, t=t: E.transpose(
                        psb[:, 128 * u:128 * (u + 1)],
                        projsb[:, 512 * mcc + 128 * t:512 * mcc + 128 * (t + 1)],
                        ident[:, :]),
                        waits=w if u == 0 else [], inc="f2tr" if u == 7 else None)
                add = p.op("vector", lambda E, g2=g2, i=i: E.tensor_add(
                    xres[i % 2][:, 1024 * g2:1024 * (g2 + 1)],
                    xres[i % 2][:, 1024 * g2:1024 * (g2 + 1)], psb[:, :]),
                    waits=[tt, xt])
                pool["rel"] = add
            x2st[i] = p.dma("sync", x2_o[128 * i:128 * (i + 1), :], xres[i % 2][:, :],
                            f"x2st{i % 2}", waits=[add])
            rel, ev = lne.chunk(xres[i % 2], add,
                                lambda g, st_=strip_st: (stripap[:, g * 8:(g + 1) * 8, :],
                                                         [st_]))
            lnrel[i] = rel
            strip_st = p.dma("sync", ln_o[i][:, :], strip[:, :], "lnst", waits=[ev])
        tr_after = ("f2tr", p.val("f2tr"))
    p.barrier("sync", waits=[("x2st0", p.val("x2st0")), ("x2st1", p.val("x2st1")),
                             ("lnst", p.val("lnst"))])
    return p.build()


def assemble_lnimg(blocks):
    a = np.asarray(blocks).reshape(8, 128, 32, 128)
    return np.ascontiguousarray(a.transpose(1, 2, 0, 3).reshape(128, 32 * 1024))


def run_f2(a2res, wbf):
    worg_img = pack_lhsT(to_bf(np.asarray(wbf["L0_ffn_W_org"]).T))
    up_img = pack_lhsT(np.ascontiguousarray(
        np.asarray(wbf["L0_ffn_lora_up"]).reshape(512, 4096)))
    dn_img = pack_lhsT(np.ascontiguousarray(
        np.asarray(wbf["L0_ffn_lora_down"]).transpose(1, 0, 2).reshape(4096, 512)))
    wq_img = pack_lhsT(to_bf(np.asarray(wbf["L0_ffn_Wq"]).T))
    keyt = np.ascontiguousarray(np.asarray(wbf["L0_ffn_lora_keys"]).T)
    sel = np.zeros((8, 512), BF)
    for j in range(4):
        for pp in range(128):
            sel[2 * j + pp // 64, 128 * j + pp] = BF(1.0)
    ones1 = np.ones((1, 1), BF)
    ident = np.eye(128, dtype=BF)
    nc = build_f2()
    in_maps = []
    for c in range(NCORES):
        bc0 = 4 * (c // 4)
        parts = np.stack([np.asarray(a2res[cc]["pt_o"]) for cc in
                          range(bc0, bc0 + 4)])
        in_maps.append({"lnin": assemble_lnimg(a2res[c]["ln_o"]),
                        "x1": np.asarray(a2res[c]["x1_o"]),
                        "parts": parts, "worg": worg_img, "upw": up_img,
                        "dnw": dn_img, "wqt": wq_img, "keyt": keyt, "sel": sel,
                        "ones1": ones1, "idin": ident})
    return run_bass_kernel_spmd(nc, in_maps, core_ids=list(range(NCORES))).results



MNEG = -30000.0


def build_m1():
    p = Prog("m1")
    nc = p.nc
    TOV = TC + 3
    lnin = nc.dram_tensor("lnin", [128, 32 * TOV], BF16, kind="ExternalInput").ap()
    wip = nc.dram_tensor("wip", [81, 128, 4096], BF16, kind="ExternalInput").ap()
    convw = nc.dram_tensor("convw", [128, 192], F32, kind="ExternalInput").ap()
    convb = nc.dram_tensor("convb", [128, 48], F32, kind="ExternalInput").ap()
    dtb = nc.dram_tensor("dtb", [64, 1], F32, kind="ExternalInput").ap()
    alog = nc.dram_tensor("alog", [64, 1], F32, kind="ExternalInput").ap()
    dd = nc.dram_tensor("dd", [1, 64], F32, kind="ExternalInput").ap()
    trisu = nc.dram_tensor("trisu", [128, 128], BF16, kind="ExternalInput").ap()
    maskt = nc.dram_tensor("maskt", [128, 128], F32, kind="ExternalInput").ap()
    onesc = nc.dram_tensor("onesc", [128, 1], BF16, kind="ExternalInput").ap()
    onesr = nc.dram_tensor("onesr", [1, 128], F32, kind="ExternalInput").ap()
    idbf = nc.dram_tensor("idbf", [128, 128], BF16, kind="ExternalInput").ap()
    idf = nc.dram_tensor("idf", [128, 128], F32, kind="ExternalInput").ap()
    y_o = nc.dram_tensor("y_o", [8, 128, 4096], F32, kind="ExternalOutput").ap()
    c_o = nc.dram_tensor("c_o", [8, 128, 1024], BF16, kind="ExternalOutput").ap()
    el_o = nc.dram_tensor("el_o", [8, 128, 64], F32, kind="ExternalOutput").ap()
    dc_o = nc.dram_tensor("dc_o", [8, 1, 64], F32, kind="ExternalOutput").ap()
    g_o = nc.dram_tensor("g_o", [32, 128, 1024], BF16, kind="ExternalOutput").ap()
    u_o = nc.dram_tensor("u_o", [128, 4096], F32, kind="ExternalOutput").ap()
    ds_o = nc.dram_tensor("ds_o", [1, 64], F32, kind="ExternalOutput").ap()

    proj_d = nc.dram_tensor("proj_d", [81, 128, TOV], BF16).ap()
    cum_d = nc.dram_tensor("cum_d", [64, 128], F32).ap()

    img = nc.alloc_sbuf_tensor("img", [128, 32 * TOV], BF16)
    wb = [nc.alloc_sbuf_tensor(f"wb{i}", [128, 4096], BF16) for i in range(2)]
    stg = [nc.alloc_sbuf_tensor(f"stg{i}", [128, TOV], BF16) for i in range(2)]
    hbc = nc.alloc_sbuf_tensor("hbc", [128, 48 * 131], BF16)
    convw_sb = nc.alloc_sbuf_tensor("convw_sb", [128, 192], F32)
    convb_sb = nc.alloc_sbuf_tensor("convb_sb", [128, 48], F32)
    cvtmp = nc.alloc_sbuf_tensor("cvtmp", [128, 256], F32)
    xbc = nc.alloc_sbuf_tensor("xbc", [128, 48 * 128], BF16)
    dt_sb = nc.alloc_sbuf_tensor("dt_sb", [64, 128], BF16)
    dtp = nc.alloc_sbuf_tensor("dtp", [64, 128], F32)
    t64a = nc.alloc_sbuf_tensor("t64a", [64, 128], F32)
    t64b = nc.alloc_sbuf_tensor("t64b", [64, 128], F32)
    ones64 = nc.alloc_sbuf_tensor("ones64", [64, 128], F32)
    cum_hm = nc.alloc_sbuf_tensor("cum_hm", [64, 128], F32)
    cum_tm = nc.alloc_sbuf_tensor("cum_tm", [128, 64], F32)
    ncum_tm = nc.alloc_sbuf_tensor("ncum_tm", [128, 64], F32)
    expl = nc.alloc_sbuf_tensor("expl", [128, 64], F32)
    dta_bf = nc.alloc_sbuf_tensor("dta_bf", [128, 64], BF16)
    dtp_tm = nc.alloc_sbuf_tensor("dtp_tm", [128, 64], F32)
    wdec = nc.alloc_sbuf_tensor("wdec", [128, 64], F32)
    drow = nc.alloc_sbuf_tensor("drow", [1, 64], F32)
    dcols = nc.alloc_sbuf_tensor("dcols", [128, 64], F32)
    dsl = nc.alloc_sbuf_tensor("dsl", [1, 64], F32)
    dtb_sb = nc.alloc_sbuf_tensor("dtb_sb", [64, 1], F32)
    acol = nc.alloc_sbuf_tensor("acol", [64, 1], F32)
    dd_sb = nc.alloc_sbuf_tensor("dd_sb", [1, 64], F32)
    ddcols = nc.alloc_sbuf_tensor("ddcols", [128, 64], F32)
    trisu_sb = nc.alloc_sbuf_tensor("trisu_sb", [128, 128], BF16)
    maskt_sb = nc.alloc_sbuf_tensor("maskt_sb", [128, 128], F32)
    onesc_sb = nc.alloc_sbuf_tensor("onesc_sb", [128, 1], BF16)
    onesr_sb = nc.alloc_sbuf_tensor("onesr_sb", [1, 128], F32)
    idbf_sb = nc.alloc_sbuf_tensor("idbf_sb", [128, 128], BF16)
    idf_sb = nc.alloc_sbuf_tensor("idf_sb", [128, 128], F32)
    cumrow = nc.alloc_sbuf_tensor("cumrow", [1, 1024], F32)
    x_tm = nc.alloc_sbuf_tensor("x_tm", [128, 4096], BF16)
    dtx_tm = nc.alloc_sbuf_tensor("dtx_tm", [128, 4096], BF16)
    b_tm = nc.alloc_sbuf_tensor("b_tm", [128, 8 * 128], BF16)
    bw_sb = nc.alloc_sbuf_tensor("bw_sb", [128, 128], BF16)
    dpre = [nc.alloc_sbuf_tensor(f"dpre{i}", [128, 128], F32) for i in range(2)]
    dexp = [nc.alloc_sbuf_tensor(f"dexp{i}", [128, 128], F32) for i in range(2)]
    ptil = nc.alloc_sbuf_tensor("ptil", [128, 128], BF16)
    state = nc.alloc_sbuf_tensor("state", [128, 4096], F32)
    st_bf = nc.alloc_sbuf_tensor("st_bf", [128, 64], BF16)
    y1sb = nc.alloc_sbuf_tensor("y1sb", [128, 64], F32)
    y_tm = nc.alloc_sbuf_tensor("y_tm", [128, 4096], F32)

    psA = nc.alloc_psum_tensor("m1psA", [128, 1024], F32)
    psB = nc.alloc_psum_tensor("m1psB", [128, 1024], F32)
    psC = nc.alloc_psum_tensor("m1psC", [128, 512], F32)
    psD = nc.alloc_psum_tensor("m1psD", [128, 512], F32)
    psE = nc.alloc_psum_tensor("m1psE", [128, 1024], BF16)

    imgap = img[:, :].rearrange("q (kc tt) -> q kc tt", tt=TOV)

    ilt = p.dma("sync", img[:, :], lnin[:, :], "ild")
    p.dma("sync", convw_sb[:, :], convw[:, :], "cld")
    p.dma("sync", convb_sb[:, :], convb[:, :], "cld")
    p.dma("sync", dtb_sb[:, :], dtb[:, :], "cld")
    p.dma("sync", acol[:, :], alog[:, :], "cld")
    p.dma("sync", dd_sb[:, :], dd[:, :], "cld")
    p.dma("sync", trisu_sb[:, :], trisu[:, :], "cld")
    p.dma("sync", maskt_sb[:, :], maskt[:, :], "cld")
    p.dma("sync", onesc_sb[:, :], onesc[:, :], "cld")
    p.dma("sync", onesr_sb[:, :], onesr[:, :], "cld")
    p.dma("sync", idbf_sb[:, :], idbf[:, :], "cld")
    cld = p.dma("sync", idf_sb[:, :], idf[:, :], "cld")
    p.barrier("tensor", waits=[cld, ilt])
    p.barrier("vector", waits=[cld])
    p.barrier("scalar", waits=[cld])

    at = p.op("scalar", lambda E: E.activation(out=acol[:, :], in_=acol[:, :],
                                               func=AF.Exp))
    p.op("vector", lambda E: E.tensor_scalar_mul(acol[:, :], acol[:, :], -1.0),
         waits=[at])
    p.op("vector", lambda E: E.memset(ones64[:, :], 1.0))
    p.op("vector", lambda E: E.memset(state[:, :], 0.0))
    msd = p.op("vector", lambda E: E.memset(dsl[:, :], 1.0))
    ddm = p.op("tensor", lambda E: E.matmul(psD[:, 0:64], onesr_sb[:, :],
                                            dd_sb[:, :], start=True, stop=True),
               waits=[msd], inc="m1dd")
    dde = p.op("vector", lambda E: E.tensor_copy(out=ddcols[:, :], in_=psD[:, 0:64]),
               waits=[ddm])

    ipmm = {}
    stw_t = {}
    for mc in range(81):
        wt = p.dma("sync", wb[mc % 2][:, :], wip[mc][:, :], f"wld{mc % 2}",
                   waits=[("m1ip", mc - 1)] if mc >= 2 else [])
        mmt = None
        for n in range(3):
            cl = [slice(0, 512), slice(512, 1024), slice(1024, 1027)][n]
            dsts = [psA[:, 0:512], psA[:, 512:1024], psB[:, 0:3]]
            w0 = [wt]
            if mc >= 2:
                w0.append(stw_t[mc - 2])
            for kc in range(32):
                mmt = p.op("tensor", lambda E, kc=kc, mc=mc, cl=cl, n=n: E.matmul(
                    dsts[n], wb[mc % 2][:, 128 * kc:128 * (kc + 1)],
                    imgap[:, kc, cl], start=(kc == 0), stop=(kc == 31)),
                    waits=w0 if (kc == 0 and n == 0) else [],
                    inc="m1ip" if (kc == 31 and n == 2) else None)
        ipmm[mc] = mmt
        sb = stg[mc % 2]
        sw = [("pst%d" % (mc % 2), p.val("pst%d" % (mc % 2)))]
        if mc < 34:
            sw.append(("gst%d" % (mc % 2), p.val("gst%d" % (mc % 2))))
        p.op("vector", lambda E, sb=sb: E.tensor_copy(out=sb[:, 0:512],
                                                      in_=psA[:, 0:512]),
             waits=[mmt] + sw)
        p.op("vector", lambda E, sb=sb: E.tensor_copy(out=sb[:, 512:1024],
                                                      in_=psA[:, 512:1024]))
        evt = p.op("vector", lambda E, sb=sb: E.tensor_copy(
            out=sb[:, 1024:1027], in_=psB[:, 0:3]))
        stw_t[mc] = evt
        p.dma("sync", proj_d[mc][:, :], sb[:, :], f"pst{mc % 2}", waits=[evt])
        if mc < 32:
            p.dma("sync", g_o[mc][:, :], sb[:, 3:1027], f"gst{mc % 2}", waits=[evt])
    projdone = [(s, p.val(s)) for s in ("pst0", "pst1", "gst0", "gst1")]

    prev = {}
    for c in range(8):
        hw = projdone if c == 0 else [prev["conv_last"]]
        hl = p.dma("sync", hbc[:, :].rearrange("q (m t) -> q m t", t=131),
                   proj_d[32:80, :, 128 * c:128 * c + 131].rearrange(
                       "m q t -> q m t"), "hld", waits=hw)
        dl = p.dma("sync", dt_sb[:, :],
                   proj_d[80][0:64, 3 + 128 * c:3 + 128 * (c + 1)], "dld",
                   waits=[prev["rlu"]] if c > 0 else projdone)
        silt = None
        sil_t = {}
        xbc_w = ([prev["xbc_rel"], prev.get("cst"), ("m1xt", p.val("m1xt"))]
                 if c > 0 else [])
        for m in range(48):
            sl = slice(128 * (m % 2), 128 * (m % 2 + 1))
            cw = [hl] if m == 0 else []
            if m >= 2:
                cw = cw + [sil_t[m - 2]]
            p.op("vector", lambda E, m=m, sl=sl: E.tensor_scalar(
                out=cvtmp[:, sl], in0=hbc[:, 131 * m + 0:131 * m + 128],
                scalar1=convw_sb[:, 4 * m:4 * m + 1],
                scalar2=convb_sb[:, m:m + 1], op0=ALU.mult, op1=ALU.add), waits=cw)
            for k in range(1, 4):
                p.op("vector", lambda E, m=m, k=k, sl=sl: E.scalar_tensor_tensor(
                    out=cvtmp[:, sl], in0=hbc[:, 131 * m + k:131 * m + k + 128],
                    scalar=convw_sb[:, 4 * m + k:4 * m + k + 1],
                    in1=cvtmp[:, sl], op0=ALU.mult, op1=ALU.add))
            cvl = p.engine_done_tok("vector")
            silt = p.op("scalar", lambda E, m=m, sl=sl: E.activation(
                out=xbc[:, 128 * m:128 * (m + 1)], in_=cvtmp[:, sl], func=AF.Silu),
                waits=[cvl] + (xbc_w if m == 0 else []))
            sil_t[m] = silt
        prev["conv_last"] = p.engine_done_tok("vector")
        p.op("scalar", lambda E: E.activation(out=t64a[:, :], in_=dt_sb[:, :],
                                              func=AF.Abs, bias=dtb_sb[:, :]),
             waits=[dl])
        p.op("scalar", lambda E: E.activation(out=t64a[:, :], in_=t64a[:, :],
                                              func=AF.Exp, scale=-1.0))
        p.op("scalar", lambda E: E.activation(out=t64a[:, :], in_=t64a[:, :],
                                              func=AF.Ln, bias=ones64[:, 0:1]))
        rlu = p.op("scalar", lambda E: E.activation(out=t64b[:, :], in_=dt_sb[:, :],
                                                    func=AF.Relu, bias=dtb_sb[:, :]))
        prev["rlu"] = rlu
        dpt = p.op("vector", lambda E: E.tensor_add(dtp[:, :], t64a[:, :],
                                                    t64b[:, :]), waits=[rlu])
        dat = p.op("vector", lambda E: E.tensor_scalar_mul(t64a[:, :], dtp[:, :],
                                                           acol[:, :]))
        cst = p.op("vector", lambda E: E.tensor_tensor_scan(
            out=cum_hm[:, :], data0=ones64[:, :], data1=t64a[:, :], initial=0.0,
            op0=ALU.mult, op1=ALU.add))
        cdst = p.dma("sync", cum_d[:, :], cum_hm[:, :], "cdst",
                     waits=[cst, ("crl", p.val("crl"))])
        w0 = [cst, ("m1dd", 1)]
        if c > 0:
            w0.append(prev["psD_rel"])
        p.op("tensor", lambda E: E.transpose(psD[:, 64:128], cum_hm[:, :],
                                             idf_sb[0:64, 0:64]), waits=w0)
        p.op("tensor", lambda E: E.transpose(psD[:, 128:192], t64a[:, :],
                                             idf_sb[0:64, 0:64]))
        tdp = p.op("tensor", lambda E: E.transpose(psD[:, 192:256], dtp[:, :],
                                                   idf_sb[0:64, 0:64]), waits=[dpt],
                   inc="m1tr")
        e1 = p.op("vector", lambda E: E.tensor_copy(out=cum_tm[:, :],
                                                    in_=psD[:, 64:128]), waits=[tdp])
        p.op("vector", lambda E: E.tensor_scalar_mul(ncum_tm[:, :], cum_tm[:, :],
                                                     -1.0))
        p.op("vector", lambda E: E.tensor_copy(out=dta_bf[:, :], in_=psD[:, 128:192]))
        e3 = p.op("vector", lambda E: E.tensor_copy(out=dtp_tm[:, :],
                                                    in_=psD[:, 192:256]))
        elt = p.op("scalar", lambda E: E.activation(out=expl[:, :], in_=cum_tm[:, :],
                                                    func=AF.Exp), waits=[e1])
        p.dma("sync", el_o[c][:, :], expl[:, :], "elst", waits=[elt])
        wmw = [e3]
        if c > 0:
            wmw.append(prev["psC_rel"])
        wmt = p.op("tensor", lambda E: E.matmul(psC[:, 0:64], trisu_sb[:, :],
                                                dta_bf[:, :], start=True, stop=True),
                   waits=wmw)
        dmt = p.op("tensor", lambda E: E.matmul(psC[0:1, 64:128], onesc_sb[:, :],
                                                dta_bf[:, :], start=True, stop=True),
                   inc="m1w")
        we = p.op("vector", lambda E: E.tensor_copy(out=wdec[:, :], in_=psC[:, 0:64]),
                  waits=[dmt])
        wex = p.op("scalar", lambda E: E.activation(out=wdec[:, :], in_=wdec[:, :],
                                                    func=AF.Exp), waits=[we])
        de = p.op("vector", lambda E: E.tensor_copy(out=drow[:, :],
                                                    in_=psC[0:1, 64:128]))
        dex = p.op("scalar", lambda E: E.activation(out=drow[:, :], in_=drow[:, :],
                                                    func=AF.Exp), waits=[de])
        p.dma("sync", dc_o[c][:, :], drow[:, :], "dcst", waits=[dex])
        dbm = p.op("tensor", lambda E: E.matmul(psC[:, 128:192], onesr_sb[:, :],
                                                drow[:, :], start=True, stop=True),
                   waits=[dex], inc="m1w")
        dce = p.op("vector", lambda E: E.tensor_copy(out=dcols[:, :],
                                                     in_=psC[:, 128:192]),
                   waits=[dbm])
        dslt = p.op("vector", lambda E: E.tensor_mul(dsl[:, :], dsl[:, :],
                                                     drow[:, :]), waits=[dex])
        prev["psC_rel"] = dce
        pe_rel = prev.get("psE_rel")
        ev_x = None
        for g4 in range(4):
            w = [silt] if g4 == 0 else []
            if pe_rel is not None:
                w.append(pe_rel)
            tt = None
            for u in range(8):
                m = 8 * g4 + u
                tt = p.op("tensor", lambda E, u=u, m=m: E.transpose(
                    psE[:, 128 * u:128 * (u + 1)], xbc[:, 128 * m:128 * (m + 1)],
                    idbf_sb[:, :]), waits=w if u == 0 else [],
                    inc="m1xt" if u == 7 else None)
            ev_x = p.op("vector", lambda E, g4=g4: E.tensor_copy(
                out=x_tm[:, 1024 * g4:1024 * (g4 + 1)], in_=psE[:, :]), waits=[tt])
            pe_rel = ev_x
        tt = None
        for u in range(8):
            m = 32 + u
            tt = p.op("tensor", lambda E, u=u, m=m: E.transpose(
                psE[:, 128 * u:128 * (u + 1)], xbc[:, 128 * m:128 * (m + 1)],
                idbf_sb[:, :]), waits=[pe_rel] if u == 0 else [],
                inc="m1xt" if u == 7 else None)
        bte = p.op("vector", lambda E: E.tensor_copy(out=b_tm[:, :], in_=psE[:, :]),
                   waits=[tt])
        prev["psE_rel"] = bte
        dxt = None
        for h in range(64):
            dxt = p.op("vector", lambda E, h=h: E.tensor_scalar_mul(
                dtx_tm[:, 64 * h:64 * (h + 1)], x_tm[:, 64 * h:64 * (h + 1)],
                dtp_tm[:, h:h + 1]), waits=[e3] if h == 0 else [])
        stw = [silt]
        if c > 0:
            stw.append(prev["psB_rel"])
        st_last = None
        for g in range(8):
            st_last = p.op("tensor", lambda E, g=g: E.matmul(
                psB[:, 128 * g:128 * (g + 1)],
                xbc[:, 128 * (32 + g):128 * (33 + g)],
                xbc[:, 128 * (40 + g):128 * (41 + g)], start=True, stop=True),
                waits=stw if g == 0 else [], inc="m1st" if g == 7 else None)
        yev = {}
        dpt2 = {}
        det_t = {}
        ptt = None
        crl = None
        for h in range(64):
            g = h // 8
            hp = h % 2
            if h % 8 == 0:
                crw = [cdst]
                if ptt is not None:
                    crw.append(ptt)
                crl = p.dma("sync", cumrow[0:1, :],
                            cum_d[g * 8:g * 8 + 8, :].rearrange("h s -> (h s)"),
                            "crl", waits=crw)
            bcw = [crl]
            if h >= 2:
                bcw.append(dpt2[h - 2])
            elif c > 0:
                bcw.append(prev["psD_rel"])
            bct = p.op("tensor", lambda E, h=h, hp=hp: E.matmul(
                psD[:, 256 + 128 * hp:256 + 128 * (hp + 1)], onesr_sb[:, :],
                cumrow[0:1, 128 * (h % 8):128 * (h % 8 + 1)],
                start=True, stop=True), waits=bcw, inc="m1bc")
            dpt2[h] = p.op("vector", lambda E, hp=hp, h=h: E.scalar_tensor_tensor(
                out=dpre[hp][:, :], in0=psD[:, 256 + 128 * hp:256 + 128 * (hp + 1)],
                scalar=ncum_tm[:, h:h + 1], in1=maskt_sb[:, :],
                op0=ALU.add, op1=ALU.add),
                waits=[bct] + ([det_t[h - 2]] if h >= 2 else []))
            det = p.op("scalar", lambda E, hp=hp: E.activation(
                out=dexp[hp][:, :], in_=dpre[hp][:, :], func=AF.Exp),
                waits=[dpt2[h]])
            det_t[h] = det
            ptw = [det, st_last]
            if h >= 1:
                ptw.append(("m1y", p.val("m1y")))
            ptt = p.op("vector", lambda E, g=g, hp=hp: E.tensor_mul(
                ptil[:, :], dexp[hp][:, :], psB[:, 128 * g:128 * (g + 1)]),
                waits=ptw)
            sbt = p.op("vector", lambda E, h=h: E.tensor_copy(
                out=st_bf[:, :], in_=state[:, 64 * h:64 * (h + 1)]),
                waits=[("m1y2", p.val("m1y2"))])
            y2w = [sbt]
            if h >= 2:
                y2w.append(yev[h - 2])
            y2t = p.op("tensor", lambda E, g=g, hp=hp: E.matmul(
                psA[:, 512 * hp + 0:512 * hp + 64],
                xbc[:, 128 * (40 + g):128 * (41 + g)], st_bf[:, :],
                start=True, stop=True), waits=y2w, inc="m1y2")
            y1t = p.op("tensor", lambda E, hp=hp, h=h: E.matmul(
                psA[:, 512 * hp + 64:512 * hp + 128], ptil[:, :],
                dtx_tm[:, 64 * h:64 * (h + 1)], start=True, stop=True),
                waits=[p.engine_done_tok("vector"), dxt])
            bwt = p.op("vector", lambda E, g=g, h=h: E.tensor_scalar_mul(
                bw_sb[:, :], b_tm[:, 128 * g:128 * (g + 1)], wdec[:, h:h + 1]),
                waits=[bte, wex, ("m1y", p.val("m1y"))])
            sut = p.op("tensor", lambda E, hp=hp, h=h: E.matmul(
                psA[:, 512 * hp + 128:512 * hp + 192], bw_sb[:, :],
                dtx_tm[:, 64 * h:64 * (h + 1)], start=True, stop=True),
                waits=[bwt], inc="m1y")
            p.op("vector", lambda E, hp=hp: E.tensor_copy(
                out=y1sb[:, :], in_=psA[:, 512 * hp + 64:512 * hp + 128]),
                waits=[sut])
            p.op("vector", lambda E, hp=hp, h=h: E.scalar_tensor_tensor(
                out=y_tm[:, 64 * h:64 * (h + 1)],
                in0=psA[:, 512 * hp + 0:512 * hp + 64],
                scalar=expl[:, h:h + 1], in1=y1sb[:, :],
                op0=ALU.mult, op1=ALU.add),
                waits=[prev.get("yst"), elt] if h == 0 else [])
            p.op("vector", lambda E, h=h: E.scalar_tensor_tensor(
                out=y_tm[:, 64 * h:64 * (h + 1)], in0=x_tm[:, 64 * h:64 * (h + 1)],
                scalar=ddcols[:, h:h + 1], in1=y_tm[:, 64 * h:64 * (h + 1)],
                op0=ALU.mult, op1=ALU.add),
                waits=[dde] if (c == 0 and h == 0) else [])
            yev[h] = p.op("vector", lambda E, hp=hp, h=h: E.scalar_tensor_tensor(
                out=state[:, 64 * h:64 * (h + 1)], in0=state[:, 64 * h:64 * (h + 1)],
                scalar=dcols[:, h:h + 1],
                in1=psA[:, 512 * hp + 128:512 * hp + 192],
                op0=ALU.mult, op1=ALU.add), waits=[dce] if h == 0 else [])
        prev["psD_rel"] = dpt2[63]
        prev["psB_rel"] = ptt
        prev["xbc_rel"] = ("m1st", p.val("m1st"))
        prev["yst"] = p.dma("sync", y_o[c][:, :], y_tm[:, :], f"yst{c % 2}",
                            waits=[yev[63]])
        prev["cst"] = p.dma("sync", c_o[c][:, :], xbc[:, 128 * 40:128 * 48],
                            f"cst{c % 2}", waits=[("m1st", p.val("m1st"))])
    p.dma("sync", u_o[:, :], state[:, :], "ust",
          waits=[("@vector", p.selfc["vector"])])
    p.dma("sync", ds_o[:, :], dsl[:, :], "dsst",
          waits=[("@vector", p.selfc["vector"])])
    p.barrier("sync", waits=[("yst0", p.val("yst0")), ("yst1", p.val("yst1")),
                             ("cst0", p.val("cst0")), ("cst1", p.val("cst1")),
                             ("elst", p.val("elst")), ("dcst", p.val("dcst")),
                             ("ust", p.val("ust")), ("dsst", p.val("dsst")),
                             ("gst0", p.val("gst0")), ("gst1", p.val("gst1"))])
    return p.build()




def build_m2():
    p = Prog("m2")
    nc = p.nc
    y_i = nc.dram_tensor("y_i", [8, 128, 4096], F32, kind="ExternalInput").ap()
    c_i = nc.dram_tensor("c_i", [8, 128, 1024], BF16, kind="ExternalInput").ap()
    el = nc.dram_tensor("el", [8, 128, 64], F32, kind="ExternalInput").ap()
    dch = nc.dram_tensor("dch", [8, 1, 64], F32, kind="ExternalInput").ap()
    gate = nc.dram_tensor("gate", [32, 128, 1024], BF16, kind="ExternalInput").ap()
    x2 = nc.dram_tensor("x2", [TC, D], F32, kind="ExternalInput").ap()
    uprev = nc.dram_tensor("uprev", [3, 128, 4096], F32, kind="ExternalInput").ap()
    dprev = nc.dram_tensor("dprev", [1, 128], F32, kind="ExternalInput").ap()
    wout = nc.dram_tensor("wout", [32, 128, 4096], BF16, kind="ExternalInput").ap()
    nwr = nc.dram_tensor("nwr", [1, 4096], F32, kind="ExternalInput").ap()
    onesr = nc.dram_tensor("onesr", [1, 128], F32, kind="ExternalInput").ap()
    idbf = nc.dram_tensor("idbf", [128, 128], BF16, kind="ExternalInput").ap()
    x3_o = nc.dram_tensor("x3_o", [TC, D], F32, kind="ExternalOutput").ap()
    ln_o = nc.dram_tensor("ln_o", [8, 128, 4096], BF16, kind="ExternalOutput").ap()

    ybuf = nc.alloc_sbuf_tensor("ybuf", [128, 4096], F32)
    srun = nc.alloc_sbuf_tensor("srun", [128, 4096], F32)
    csb = nc.alloc_sbuf_tensor("csb", [128, 1024], BF16)
    elsb = nc.alloc_sbuf_tensor("elsb", [128, 64], F32)
    dchr = nc.alloc_sbuf_tensor("dchr", [1, 64], F32)
    dcols = nc.alloc_sbuf_tensor("dcols", [128, 64], F32)
    dprow = nc.alloc_sbuf_tensor("dprow", [1, 128], F32)
    frow = nc.alloc_sbuf_tensor("frow", [1, 64], F32)
    fcols = nc.alloc_sbuf_tensor("fcols", [128, 64], F32)
    st_bf = nc.alloc_sbuf_tensor("st_bf", [128, 64], BF16)
    sc512 = nc.alloc_sbuf_tensor("sc512", [128, 512], F32)
    grp = nc.alloc_sbuf_tensor("grp", [128, 8], F32)
    nwb = nc.alloc_sbuf_tensor("nwb", [128, 4096], BF16)
    gfm = nc.alloc_sbuf_tensor("gfm", [128, 4096], BF16)
    gtm = nc.alloc_sbuf_tensor("gtm", [128, 4096], BF16)
    yntm = nc.alloc_sbuf_tensor("yntm", [128, 4096], BF16)
    ynimg = nc.alloc_sbuf_tensor("ynimg", [128, 32 * 512], BF16)
    wob = [nc.alloc_sbuf_tensor(f"wob{i}", [128, 4096], BF16) for i in range(2)]
    projsb = nc.alloc_sbuf_tensor("projsb", [128, 32 * 512], BF16)
    xres = [nc.alloc_sbuf_tensor(f"xres{i}", [128, D], F32) for i in range(2)]
    ident = nc.alloc_sbuf_tensor("ident", [128, 128], BF16)
    strip = nc.alloc_sbuf_tensor("strip", [128, 4096], BF16)
    onesr_sb = nc.alloc_sbuf_tensor("onesr_sb", [1, 128], F32)

    psY = nc.alloc_psum_tensor("m2psY", [128, 512], F32)
    psW = [nc.alloc_psum_tensor(f"m2psW{i}", [128, 512], F32) for i in range(2)]
    psM = nc.alloc_psum_tensor("m2psM", [128, 512], F32)
    psb = nc.alloc_psum_tensor("m2psb", [128, 1024], BF16)

    ynap = ynimg[:, :].rearrange("q (kc tt) -> q kc tt", tt=512)
    stripap = strip[:, :].rearrange("q (kc t) -> q kc t", t=128)

    p.dma("sync", onesr_sb[:, :], onesr[:, :], "cld")
    p.dma("sync", dprow[:, :], dprev[:, :], "cld")
    cld = p.dma("sync", ident[:, :], idbf[:, :], "cld")
    p.barrier("tensor", waits=[cld])
    p.barrier("vector", waits=[cld])
    p.barrier("scalar", waits=[cld])

    nwl = p.dma("sync", sc512[0:1, :], nwr[0:1, 0:512], "nwl")
    for j in range(8):
        nwl2 = p.dma("sync", sc512[0:1, :], nwr[0:1, 512 * j:512 * (j + 1)], "nwl",
                     waits=[("m2nw", p.val("m2nw"))]) if j > 0 else nwl
        nwm = p.op("tensor", lambda E: E.matmul(psM[:, :], onesr_sb[:, :],
                                                sc512[0:1, :], start=True,
                                                stop=True), waits=[nwl2], inc="m2nw")
        p.op("vector", lambda E, j=j: E.tensor_copy(
            out=nwb[:, 512 * j:512 * (j + 1)], in_=psM[:, :]), waits=[nwm])
    nwdone = p.engine_done_tok("vector")

    sl0 = p.dma("sync", srun[:, :], uprev[0][:, :], "ul0")
    p.op("vector", lambda E: E.tensor_copy(out=frow[:, :], in_=dprow[0:1, 0:64]),
         waits=[("cld", p.val("cld"))])
    for j in (1, 2):
        ul = p.dma("sync", ybuf[:, :], uprev[j][:, :], f"ul{j}",
                   waits=[("@vector", p.selfc["vector"])])
        fbm = p.op("tensor", lambda E: E.matmul(psM[:, 0:64], onesr_sb[:, :],
                                                frow[:, :], start=True, stop=True),
                   waits=[p.engine_done_tok("vector"), ("m2nw", 8)], inc="m2f")
        fce = p.op("vector", lambda E: E.tensor_copy(out=fcols[:, :],
                                                     in_=psM[:, 0:64]), waits=[fbm])
        la = None
        for h in range(64):
            la = p.op("vector", lambda E, h=h: E.scalar_tensor_tensor(
                out=srun[:, 64 * h:64 * (h + 1)], in0=ybuf[:, 64 * h:64 * (h + 1)],
                scalar=fcols[:, h:h + 1], in1=srun[:, 64 * h:64 * (h + 1)],
                op0=ALU.mult, op1=ALU.add),
                waits=[ul, sl0, fce] if h == 0 else [])
        if j == 1:
            p.op("vector", lambda E: E.tensor_mul(frow[:, :], frow[:, :],
                                                  dprow[0:1, 64:128]))
    pool = {"ap": psb, "rel": None}
    lne = LNEmitter(p, nc, ident, pool, "m2ln")
    lnrel, x3st = {}, {}
    strip_st = None
    tr_after = None
    last_wo = None
    for half in range(2):
        for t in range(4):
            c = 4 * half + t
            yw = [("@vector", p.selfc["vector"])]
            yl = p.dma("sync", ybuf[:, :], y_i[c][:, :], f"yld{c % 2}", waits=yw)
            cl_ = p.dma("sync", csb[:, :], c_i[c][:, :], f"cld{c % 2}",
                        waits=[("m2y", p.val("m2y"))])
            ell = p.dma("sync", elsb[:, :], el[c][:, :], f"eld{c % 2}",
                        waits=[("@vector", p.selfc["vector"])])
            dl = p.dma("sync", dchr[:, :], dch[c][:, :], f"dld{c % 2}",
                       waits=[("m2dc", p.val("m2dc"))])
            dbm = p.op("tensor", lambda E: E.matmul(psM[:, 64:128], onesr_sb[:, :],
                                                    dchr[:, :], start=True,
                                                    stop=True),
                       waits=[dl, ("@vector", p.selfc["vector"])], inc="m2dc")
            dce = p.op("vector", lambda E: E.tensor_copy(out=dcols[:, :],
                                                         in_=psM[:, 64:128]),
                       waits=[dbm])
            gl = p.dma("sync", gfm[:, :].rearrange("q (m t) -> q m t", t=128),
                       gate[:, :, 128 * c:128 * (c + 1)].rearrange("m q t -> q m t"),
                       f"gld{c % 2}", waits=[("m2gs", p.val("m2gs"))])
            sil = p.op("scalar", lambda E: E.activation(out=gfm[:, :], in_=gfm[:, :],
                                                        func=AF.Silu), waits=[gl])
            gtt = None
            for g4 in range(4):
                gw = [pool["rel"]] + ([sil] if g4 == 0 else [])
                tt = None
                for u in range(8):
                    m = 8 * g4 + u
                    tt = p.op("tensor", lambda E, u=u, m=m: E.transpose(
                        psb[:, 128 * u:128 * (u + 1)], gfm[:, 128 * m:128 * (m + 1)],
                        ident[:, :]), waits=gw if u == 0 else [],
                        inc="m2gs" if u == 7 else None)
                gtt = p.op("vector", lambda E, g4=g4: E.tensor_copy(
                    out=gtm[:, 1024 * g4:1024 * (g4 + 1)], in_=psb[:, :]),
                    waits=[tt])
                pool["rel"] = gtt
            yup = {}
            for h in range(64):
                g = h // 8
                sbt = p.op("vector", lambda E, h=h: E.tensor_copy(
                    out=st_bf[:, :], in_=srun[:, 64 * h:64 * (h + 1)]),
                    waits=[("m2y", p.val("m2y"))])
                y2w = [sbt]
                if h >= 1:
                    y2w.append(yup[h - 1])
                y2t = p.op("tensor", lambda E, g=g, h=h: E.matmul(
                    psY[:, 64 * (h % 8):64 * (h % 8 + 1)],
                    csb[:, 128 * g:128 * (g + 1)], st_bf[:, :],
                    start=True, stop=True), waits=y2w + [cl_], inc="m2y")
                yup[h] = p.op("vector", lambda E, h=h: E.scalar_tensor_tensor(
                    out=ybuf[:, 64 * h:64 * (h + 1)],
                    in0=psY[:, 64 * (h % 8):64 * (h % 8 + 1)],
                    scalar=elsb[:, h:h + 1], in1=ybuf[:, 64 * h:64 * (h + 1)],
                    op0=ALU.mult, op1=ALU.add),
                    waits=[("m2y", p.val("m2y")), yl, ell] if h == 0
                    else [("m2y", p.val("m2y"))])
            sdec = None
            for h in range(64):
                sdec = p.op("vector", lambda E, h=h: E.tensor_scalar_mul(
                    srun[:, 64 * h:64 * (h + 1)], srun[:, 64 * h:64 * (h + 1)],
                    dcols[:, h:h + 1]), waits=[dce] if h == 0 else [])
            rst = None
            for g in range(8):
                sq = p.op("scalar", lambda E, g=g: E.activation(
                    out=sc512[:, :], in_=ybuf[:, 512 * g:512 * (g + 1)],
                    func=AF.Square),
                    waits=[yup[63]] if g == 0 else [rst])
                rst = p.op("vector", lambda E, g=g: E.reduce_sum(
                    grp[:, g:g + 1], sc512[:, :], axis=AX.X), waits=[sq])
            rst = p.op("vector", lambda E: E.tensor_scalar(
                out=grp[:, :], in0=grp[:, :], scalar1=1.0 / 512.0, scalar2=EPS,
                op0=ALU.mult, op1=ALU.add))
            p.op("scalar", lambda E: E.activation(out=grp[:, :], in_=grp[:, :],
                                                  func=AF.Ln), waits=[rst])
            rex = p.op("scalar", lambda E: E.activation(out=grp[:, :], in_=grp[:, :],
                                                        func=AF.Exp, scale=-0.5))
            ynt = None
            for g in range(8):
                ynt = p.op("vector", lambda E, g=g: E.tensor_scalar_mul(
                    yntm[:, 512 * g:512 * (g + 1)], ybuf[:, 512 * g:512 * (g + 1)],
                    grp[:, g:g + 1]),
                    waits=[rex, ("m2yt", p.val("m2yt"))] if g == 0 else [])
            p.op("vector", lambda E: E.tensor_mul(yntm[:, :], yntm[:, :], nwb[:, :]),
                 waits=[nwdone] if c == 0 else [])
            ygt = p.op("vector", lambda E: E.tensor_mul(yntm[:, :], yntm[:, :],
                                                        gtm[:, :]), waits=[gtt])
            for g4 in range(4):
                tt = None
                for u in range(8):
                    m = 8 * g4 + u
                    tt = p.op("tensor", lambda E, u=u, m=m: E.transpose(
                        psb[:, 128 * u:128 * (u + 1)],
                        yntm[:, 128 * m:128 * (m + 1)], ident[:, :]),
                        waits=[ygt, pool["rel"]] if u == 0 else [],
                        inc="m2yt" if u == 7 else None)
                ev = p.op("vector", lambda E, g4=g4, t=t: E.tensor_copy(
                    out=ynap[:, g4 * 8:(g4 + 1) * 8, 128 * t:128 * (t + 1)],
                    in_=psb[:, :].rearrange("q (u tt) -> q u tt", u=8)),
                    waits=[tt] + ([("m2wo", p.val("m2wo"))]
                                  if (t == 0 and g4 == 0) else []))
                pool["rel"] = ev
        yndone = p.engine_done_tok("vector")
        evp = {}
        womm = {}
        for mc in range(32):
            ww = [womm[mc - 2]] if mc >= 2 else (
                [last_wo] if last_wo is not None else [])
            wt = p.dma("sync", wob[mc % 2][:, :], wout[mc][:, :], f"wld{mc % 2}",
                       waits=ww)
            mmt = None
            w0 = [wt, yndone]
            if mc >= 2:
                w0.append(evp[mc - 2])
            if mc <= 1 and tr_after is not None:
                w0.append(tr_after)
            for kc in range(32):
                mmt = p.op("tensor", lambda E, kc=kc, mc=mc: E.matmul(
                    psW[mc % 2][:, :], wob[mc % 2][:, 128 * kc:128 * (kc + 1)],
                    ynap[:, kc, :], start=(kc == 0), stop=(kc == 31)),
                    waits=w0 if kc == 0 else [],
                    inc="m2wo" if kc == 31 else None)
            womm[mc] = mmt
            evp[mc] = p.op("vector", lambda E, mc=mc: E.tensor_copy(
                out=projsb[:, 512 * mc:512 * (mc + 1)], in_=psW[mc % 2][:, :]),
                waits=[mmt])
        last_wo = womm[31]
        for t in range(4):
            i = 4 * half + t
            xw = []
            if i >= 2:
                xw = [lnrel[i - 2], x3st[i - 2]]
            xt = p.dma("sync", xres[i % 2][:, :], x2[128 * i:128 * (i + 1), :],
                       f"xld{i % 2}", waits=xw)
            add = None
            for g2 in range(4):
                w = [evp[31]] if g2 == 0 else []
                if pool["rel"] is not None:
                    w.append(pool["rel"])
                tt = None
                for u in range(8):
                    mcc = 8 * g2 + u
                    tt = p.op("tensor", lambda E, u=u, mcc=mcc, t=t: E.transpose(
                        psb[:, 128 * u:128 * (u + 1)],
                        projsb[:, 512 * mcc + 128 * t:512 * mcc + 128 * (t + 1)],
                        ident[:, :]),
                        waits=w if u == 0 else [], inc="m2tr" if u == 7 else None)
                add = p.op("vector", lambda E, g2=g2, i=i: E.tensor_add(
                    xres[i % 2][:, 1024 * g2:1024 * (g2 + 1)],
                    xres[i % 2][:, 1024 * g2:1024 * (g2 + 1)], psb[:, :]),
                    waits=[tt, xt])
                pool["rel"] = add
            x3st[i] = p.dma("sync", x3_o[128 * i:128 * (i + 1), :],
                            xres[i % 2][:, :], f"x3st{i % 2}", waits=[add])
            rel, ev = lne.chunk(xres[i % 2], add,
                                lambda g, st_=strip_st: (stripap[:, g * 8:(g + 1) * 8, :],
                                                         [st_]))
            lnrel[i] = rel
            strip_st = p.dma("sync", ln_o[i][:, :], strip[:, :], "lnst", waits=[ev])
        tr_after = ("m2tr", p.val("m2tr"))
    p.barrier("sync", waits=[("x3st0", p.val("x3st0")), ("x3st1", p.val("x3st1")),
                             ("lnst", p.val("lnst"))])
    return p.build()


def run_m1(f2res, inputs_f32, wbf):
    ipT = np.asarray(wbf["L1_attn_in_proj_w"]).T
    ipT = np.concatenate([ipT, np.zeros((4096, 64), BF)], axis=1)
    wip_img = pack_lhsT(np.ascontiguousarray(ipT))
    cw = np.asarray(inputs_f32["L1_attn_conv_weight"], np.float32)
    convw = np.ascontiguousarray(cw.reshape(48, 128, 4).transpose(1, 0, 2)
                                 .reshape(128, 192))
    convb = np.ascontiguousarray(
        np.asarray(inputs_f32["L1_attn_conv_bias"], np.float32).reshape(48, 128).T)
    dtb = np.asarray(inputs_f32["L1_attn_dt_bias"], np.float32).reshape(64, 1)
    alog = np.asarray(inputs_f32["L1_attn_A_log"], np.float32).reshape(64, 1)
    dd = np.asarray(inputs_f32["L1_attn_D"], np.float32).reshape(1, 64)
    r_ = np.arange(128)
    trisu = (r_[:, None] > r_[None, :]).astype(BF)
    maskt = np.where(r_[:, None] <= r_[None, :], 0.0, MNEG).astype(np.float32)
    onesc = np.ones((128, 1), BF)
    onesr = np.ones((1, 128), np.float32)
    idbf = np.eye(128, dtype=BF)
    idf = np.eye(128, dtype=np.float32)
    nc = build_m1()
    in_maps = []
    for c in range(NCORES):
        own = assemble_lnimg(f2res[c]["ln_o"]).reshape(128, 32, 1024)
        arr = np.zeros((128, 32, 1027), BF)
        if c % 4 > 0:
            pv = assemble_lnimg(f2res[c - 1]["ln_o"]).reshape(128, 32, 1024)
            arr[:, :, 0:3] = pv[:, :, 1021:1024]
        arr[:, :, 3:] = own
        in_maps.append({"lnin": np.ascontiguousarray(arr.reshape(128, 32 * 1027)),
                        "wip": wip_img, "convw": convw, "convb": convb,
                        "dtb": dtb, "alog": alog, "dd": dd, "trisu": trisu,
                        "maskt": maskt, "onesc": onesc, "onesr": onesr,
                        "idbf": idbf, "idf": idf})
    return run_bass_kernel_spmd(nc, in_maps, core_ids=list(range(NCORES))).results


def run_m2(m1res, f2res, inputs_f32, wbf):
    wout_img = pack_lhsT(to_bf(np.asarray(wbf["L1_attn_out_proj_w"]).T))
    nwr = np.asarray(inputs_f32["L1_attn_norm_weight"], np.float32).reshape(1, 4096)
    onesr = np.ones((1, 128), np.float32)
    idbf = np.eye(128, dtype=BF)
    nc = build_m2()
    in_maps = []
    for c in range(NCORES):
        j = c % 4
        upv = np.zeros((3, 128, 4096), np.float32)
        dpv = np.ones((1, 128), np.float32)
        for i_ in range(3):
            cc = c - 1 - i_
            if cc >= 4 * (c // 4):
                upv[i_] = np.asarray(m1res[cc]["u_o"])
        if j >= 2:
            dpv[0, 0:64] = np.asarray(m1res[c - 1]["ds_o"])[0]
        if j >= 3:
            dpv[0, 64:128] = np.asarray(m1res[c - 2]["ds_o"])[0]
        in_maps.append({"y_i": np.asarray(m1res[c]["y_o"]),
                        "c_i": np.asarray(m1res[c]["c_o"]),
                        "el": np.asarray(m1res[c]["el_o"]),
                        "dch": np.asarray(m1res[c]["dc_o"]),
                        "gate": np.asarray(m1res[c]["g_o"]),
                        "x2": np.asarray(f2res[c]["x2_o"]),
                        "uprev": upv, "dprev": dpv, "wout": wout_img,
                        "nwr": nwr, "onesr": onesr, "idbf": idbf})
    return run_bass_kernel_spmd(nc, in_maps, core_ids=list(range(NCORES))).results




def build_f3():
    p = Prog("f3")
    nc = p.nc
    lnin = nc.dram_tensor("lnin", [128, 32 * TC], BF16, kind="ExternalInput").ap()
    x3 = nc.dram_tensor("x3", [TC, D], F32, kind="ExternalInput").ap()
    wt = nc.dram_tensor("wt", [32, 128, 4096], BF16, kind="ExternalInput").ap()
    at = nc.dram_tensor("at", [2, 128, 4096], BF16, kind="ExternalInput").ap()
    bft = nc.dram_tensor("bft", [32, 128, 256], BF16, kind="ExternalInput").ap()
    bias = nc.dram_tensor("bias", [1, 4096], BF16, kind="ExternalInput").ap()
    coef = nc.dram_tensor("coef", [1, 16], F32, kind="ExternalInput").ap()
    scal = nc.dram_tensor("scal", [1, 16], F32, kind="ExternalInput").ap()
    sel2 = nc.dram_tensor("sel2", [16, 256], BF16, kind="ExternalInput").ap()
    ones1 = nc.dram_tensor("ones1", [1, 1], BF16, kind="ExternalInput").ap()
    ones5 = nc.dram_tensor("ones5", [1, 512], BF16, kind="ExternalInput").ap()
    idin = nc.dram_tensor("idin", [128, 128], BF16, kind="ExternalInput").ap()
    x4_o = nc.dram_tensor("x4_o", [TC, D], F32, kind="ExternalOutput").ap()

    img = nc.alloc_sbuf_tensor("img", [128, 32 * TC], BF16)
    wb = [nc.alloc_sbuf_tensor(f"wb{i}", [128, 4096], BF16) for i in range(2)]
    bb = [nc.alloc_sbuf_tensor(f"bb{i}", [128, 256], BF16) for i in range(2)]
    wrow = nc.alloc_sbuf_tensor("wrow", [1, 16], F32)
    wrow_bf = nc.alloc_sbuf_tensor("wrow_bf", [1, 16], BF16)
    wcol = nc.alloc_sbuf_tensor("wcol", [16, 1], BF16)
    wvec = nc.alloc_sbuf_tensor("wvec", [128, 2], F32)
    coef_sb = nc.alloc_sbuf_tensor("coef_sb", [1, 16], F32)
    scal_sb = nc.alloc_sbuf_tensor("scal_sb", [1, 16], F32)
    sel_sb = nc.alloc_sbuf_tensor("sel_sb", [16, 256], BF16)
    ones1_sb = nc.alloc_sbuf_tensor("ones1_sb", [1, 1], BF16)
    ones5_sb = nc.alloc_sbuf_tensor("ones5_sb", [1, 512], BF16)
    bias_sb = nc.alloc_sbuf_tensor("bias_sb", [1, 4096], BF16)
    xas = nc.alloc_sbuf_tensor("xas", [128, 2 * TC], BF16)
    projsb = nc.alloc_sbuf_tensor("projsb", [128, 32 * 512], BF16)
    xres = [nc.alloc_sbuf_tensor(f"xres{i}", [128, D], F32) for i in range(2)]
    ident = nc.alloc_sbuf_tensor("ident", [128, 128], BF16)

    ps = [nc.alloc_psum_tensor(f"f3ps{i}", [128, 1024], F32) for i in range(2)]
    ps1 = nc.alloc_psum_tensor("f3psq", [128, 512], F32)
    psb = nc.alloc_psum_tensor("f3psb", [128, 1024], BF16)

    imgap = img[:, :].rearrange("q (kc tt) -> q kc tt", tt=TC)

    ildt = p.dma("sync", img[:, :], lnin[:, :], "ild")
    p.dma("sync", coef_sb[:, :], coef[:, :], "cld")
    p.dma("sync", scal_sb[:, :], scal[:, :], "cld")
    p.dma("sync", sel_sb[:, :], sel2[:, :], "cld")
    p.dma("sync", ones1_sb[:, :], ones1[:, :], "cld")
    p.dma("sync", ones5_sb[:, :], ones5[:, :], "cld")
    p.dma("sync", bias_sb[:, :], bias[:, :], "cld")
    cld = p.dma("sync", ident[:, :], idin[:, :], "cld")
    p.barrier("vector", waits=[cld])
    p.barrier("tensor", waits=[cld, ildt])

    wmul = p.op("vector", lambda E: E.tensor_mul(wrow[:, :], coef_sb[:, :],
                                                 scal_sb[:, :]))
    wbt = p.op("vector", lambda E: E.tensor_copy(out=wrow_bf[:, :], in_=wrow[:, :]))
    ctm = p.op("tensor", lambda E: E.matmul(ps1[0:16, 0:1], wrow_bf[:, :],
                                            ones1_sb[:, :], start=True, stop=True),
               waits=[wbt], inc="f3c")
    cce = p.op("vector", lambda E: E.tensor_copy(out=wcol[:, :], in_=ps1[0:16, 0:1]),
               waits=[ctm])
    vmm = None
    for j in range(2):
        vmm = p.op("tensor", lambda E, j=j: E.matmul(
            ps1[:, 4 + j:5 + j], sel_sb[:, 128 * j:128 * (j + 1)], wcol[:, :],
            start=True, stop=True), waits=[cce] if j == 0 else [],
            inc="f3v" if j == 1 else None)
    cvt = p.op("vector", lambda E: E.tensor_copy(out=wvec[:, :], in_=ps1[:, 4:6]),
               waits=[vmm])

    xde = {}
    at_t = {}
    for j in range(2):
        at_t[j] = p.dma("sync", wb[j][:, :], at[j][:, :], f"wld{j}")
        mmt = None
        for n in range(2):
            for kc in range(32):
                mmt = p.op("tensor", lambda E, kc=kc, n=n, j=j: E.matmul(
                    ps[j][:, 512 * n:512 * (n + 1)],
                    wb[j][:, 128 * kc:128 * (kc + 1)],
                    imgap[:, kc, 512 * n:512 * (n + 1)],
                    start=(kc == 0), stop=(kc == 31)),
                    waits=[at_t[j]] if kc == 0 else [],
                    inc="f3xa" if kc == 31 else None)
            xde[(j, n)] = p.op("vector", lambda E, j=j, n=n: E.tensor_scalar(
                out=xas[:, TC * j + 512 * n:TC * j + 512 * (n + 1)],
                in0=ps[j][:, 512 * n:512 * (n + 1)],
                scalar1=wvec[:, j:j + 1], scalar2=None, op0=ALU.mult),
                waits=[mmt, cvt] if n == 0 else [mmt])

    x4st = {}
    tr_after = None
    last_wo = None
    add_prev = {}
    pool_rel = None
    for half in range(2):
        evp = {}
        womm = {}
        for mc in range(32):
            ww = []
            if mc >= 2:
                ww = [womm[mc - 2]]
            elif half == 0:
                ww = [xde[(1, 1)]] if mc < 2 else []
            else:
                ww = [last_wo]
            wt_ = p.dma("sync", wb[mc % 2][:, :], wt[mc][:, :], f"wld{mc % 2}",
                        waits=ww)
            ut = p.dma("sync", bb[mc % 2][:, :], bft[mc][:, :], f"uld{mc % 2}",
                       waits=ww)
            mmt = None
            w0 = [wt_, ut]
            if mc >= 2:
                w0.append(evp[mc - 2])
            if mc <= 1 and tr_after is not None:
                w0.append(tr_after)
            for kc in range(32):
                mmt = p.op("tensor", lambda E, kc=kc, mc=mc, half=half: E.matmul(
                    ps[mc % 2][:, 0:512], wb[mc % 2][:, 128 * kc:128 * (kc + 1)],
                    imgap[:, kc, 512 * half:512 * (half + 1)],
                    start=(kc == 0), stop=False),
                    waits=w0 if kc == 0 else [])
            for j in range(2):
                mmt = p.op("tensor", lambda E, j=j, mc=mc, half=half: E.matmul(
                    ps[mc % 2][:, 0:512], bb[mc % 2][:, 128 * j:128 * (j + 1)],
                    xas[:, TC * j + 512 * half:TC * j + 512 * (half + 1)],
                    start=False, stop=False), waits=[])
            mmt = p.op("tensor", lambda E, mc=mc: E.matmul(
                ps[mc % 2][:, 0:512], bias_sb[0:1, 128 * mc:128 * (mc + 1)],
                ones5_sb[:, :], start=False, stop=True), waits=[],
                inc="f3wo")
            womm[mc] = mmt
            evp[mc] = p.op("vector", lambda E, mc=mc: E.tensor_copy(
                out=projsb[:, 512 * mc:512 * (mc + 1)], in_=ps[mc % 2][:, 0:512]),
                waits=[mmt])
        last_wo = womm[31]
        for t in range(4):
            i = 4 * half + t
            xw = [x4st[i - 2]] if i >= 2 else []
            xt = p.dma("sync", xres[i % 2][:, :], x3[128 * i:128 * (i + 1), :],
                       f"xld{i % 2}", waits=xw)
            add = None
            for g2 in range(4):
                w = [evp[31]] if g2 == 0 else []
                if pool_rel is not None:
                    w.append(pool_rel)
                tt = None
                for u in range(8):
                    mcc = 8 * g2 + u
                    tt = p.op("tensor", lambda E, u=u, mcc=mcc, t=t: E.transpose(
                        psb[:, 128 * u:128 * (u + 1)],
                        projsb[:, 512 * mcc + 128 * t:512 * mcc + 128 * (t + 1)],
                        ident[:, :]),
                        waits=w if u == 0 else [], inc="f3tr" if u == 7 else None)
                add = p.op("vector", lambda E, g2=g2, i=i: E.tensor_add(
                    xres[i % 2][:, 1024 * g2:1024 * (g2 + 1)],
                    xres[i % 2][:, 1024 * g2:1024 * (g2 + 1)], psb[:, :]),
                    waits=[tt, xt])
                pool_rel = add
            x4st[i] = p.dma("sync", x4_o[128 * i:128 * (i + 1), :],
                            xres[i % 2][:, :], f"x4st{i % 2}", waits=[add])
        tr_after = ("f3tr", p.val("f3tr"))
    p.barrier("sync", waits=[("x4st0", p.val("x4st0")), ("x4st1", p.val("x4st1"))])
    return p.build()


def run_f3(m2res, inputs_f32, wbf):
    wt_img = pack_lhsT(to_bf(np.asarray(wbf["L1_ffn_W"]).T))
    at_img = pack_lhsT(np.ascontiguousarray(
        np.asarray(wbf["L1_ffn_A"]).transpose(2, 0, 1).reshape(4096, 256)))
    bft_img = pack_lhsT(np.ascontiguousarray(
        np.asarray(wbf["L1_ffn_Bf"]).transpose(0, 2, 1).reshape(256, 4096)))
    bias = to_bf(np.asarray(inputs_f32["L1_ffn_bias"], np.float32).reshape(1, 4096))
    coef = np.asarray(inputs_f32["L1_ffn_coeffs"], np.float32).reshape(1, 16)
    scal = np.asarray(inputs_f32["L1_ffn_scales"], np.float32).reshape(1, 16)
    sel = np.zeros((16, 256), BF)
    for j in range(2):
        for pp in range(128):
            sel[8 * j + pp // 16, 128 * j + pp] = BF(1.0)
    nc = build_f3()
    in_maps = []
    for c in range(NCORES):
        in_maps.append({"lnin": assemble_lnimg(m2res[c]["ln_o"]),
                        "x3": np.asarray(m2res[c]["x3_o"]), "wt": wt_img,
                        "at": at_img, "bft": bft_img, "bias": bias,
                        "coef": coef, "scal": scal, "sel2": sel,
                        "ones1": np.ones((1, 1), BF),
                        "ones5": np.ones((1, 512), BF),
                        "idin": np.eye(128, dtype=BF)})
    return run_bass_kernel_spmd(nc, in_maps, core_ids=list(range(NCORES))).results




def kernel(**inputs):
    X = np.ascontiguousarray(np.asarray(inputs["L0_attn_X"], np.float32))
    x_slices = [np.ascontiguousarray(X[c // 4, (c % 4) * TC:(c % 4 + 1) * TC])
                for c in range(NCORES)]
    wbf = cast_weights(inputs)
    a1res = run_a1(x_slices, wbf)
    a2res = run_a2(x_slices, a1res, wbf)
    f2res = run_f2(a2res, wbf)
    m1res = run_m1(f2res, inputs, wbf)
    m2res = run_m2(m1res, f2res, inputs, wbf)
    f3res = run_f3(m2res, inputs, wbf)
    out = np.empty((B, T, D), np.float32)
    for c in range(NCORES):
        out[c // 4, (c % 4) * TC:(c % 4 + 1) * TC] = np.asarray(f3res[c]["x4_o"])
    return out


# version 04
# speedup vs baseline: 1.1501x; 1.1501x over previous
import numpy as np
import ml_dtypes
import concourse.bass as bass
import concourse.mybir as mybir
from concourse.bass_utils import run_bass_kernel_spmd

F32 = mybir.dt.float32
BF16 = mybir.dt.bfloat16
AF = mybir.ActivationFunctionType
ALU = mybir.AluOpType
AX = mybir.AxisListType
BF = ml_dtypes.bfloat16

NCORES = 8
B = 2
T = 4096
D = 4096
TC = 1024
NH = 32
NKV = 8
DH = 128
EPS = 1e-5



class Prog:

    ENGS = ("tensor", "vector", "scalar", "gpsimd", "sync")
    SERIAL = ("vector", "scalar", "gpsimd")

    def __init__(self, name):
        self.nc = bass.Bass("TRN2", target_bir_lowering=False, debug=False)
        self.name = name
        self.plan = {e: [] for e in self.ENGS}
        self.sems = {}
        self.cnt = {}
        self.selfc = {e: 0 for e in self.SERIAL}
        for e in self.SERIAL:
            self.sem("@" + e)

    def sem(self, name):
        if name not in self.sems:
            self.sems[name] = self.nc.alloc_semaphore(name=name.replace("@", "s_"))
            self.cnt[name] = 0
        return name

    def val(self, name):
        self.sem(name)
        return self.cnt[name]

    def op(self, eng, fn, waits=(), inc=None, serial=True):
        waits = [t for t in waits if t is not None]
        waits = [(s, v) for (s, v) in waits if v > 0]
        for s, _ in waits:
            self.sem(s)
        incs = []
        tok = None
        if eng in self.selfc and serial:
            assert inc is None, "serialized engines use the self-sem only"
            sname = "@" + eng
            if self.selfc[eng] > 0:
                waits.append((sname, self.selfc[eng]))
            self.selfc[eng] += 1
            self.cnt[sname] = self.selfc[eng]
            incs.append((sname, 1))
            tok = (sname, self.selfc[eng])
        elif inc is not None:
            self.sem(inc)
            self.cnt[inc] += 1
            incs.append((inc, 1))
            tok = (inc, self.cnt[inc])
        self.plan[eng].append((waits, fn, incs))
        return tok

    def dma(self, eng, out, in_, inc, waits=(), **kw):
        self.sem(inc)
        waits = [t for t in waits if t is not None]
        waits = [(s, v) for (s, v) in waits if v > 0]
        for s, _ in waits:
            self.sem(s)
        if self.cnt[inc] > 0:
            waits.append((inc, self.cnt[inc]))
        if eng == "gpsimd" and self.selfc["gpsimd"] > 0:
            waits.append(("@gpsimd", self.selfc["gpsimd"]))
        self.cnt[inc] += 16
        self.plan[eng].append(
            (waits, lambda E, o=out, i=in_, k=kw: E.dma_start(out=o, in_=i, **k),
             [(inc, 16)]))
        return (inc, self.cnt[inc])

    def barrier(self, eng, waits=()):
        waits = [t for t in waits if t is not None]
        waits = [(s, v) for (s, v) in waits if v > 0]
        for s, _ in waits:
            self.sem(s)
        self.plan[eng].append((waits, None, []))

    def engine_done_tok(self, eng):
        return ("@" + eng, self.selfc[eng])

    def lint(self):
        errs = []
        for eng in self.ENGS:
            prior = {}
            for idx, (waits, fn, incs) in enumerate(self.plan[eng]):
                for s, v in waits:
                    if v > self.cnt.get(s, 0):
                        errs.append(f"{eng}[{idx}]: wait {s}>={v} > final "
                                    f"{self.cnt.get(s, 0)}")
                    if s == "@" + eng and v > prior.get(s, 0):
                        errs.append(f"{eng}[{idx}]: self-wait {s}>={v} > prior "
                                    f"{prior.get(s, 0)}")
                for s, n in incs:
                    prior[s] = prior.get(s, 0) + n
        if errs:
            raise RuntimeError(f"Prog({self.name}) lint: " + " ; ".join(errs[:8]))

    def build(self):
        self.lint()
        nc = self.nc
        with nc.Block() as block:
            for eng in self.ENGS:
                items = self.plan[eng]

                def mk(items_):
                    def body(E):
                        for waits, fn, incs in items_:
                            for s, v in waits:
                                E.wait_ge(self.sems[s], v)
                            if fn is None:
                                continue
                            inst = fn(E)
                            for s, n in incs:
                                inst = inst.then_inc(self.sems[s], n)
                    return body
                getattr(block, eng)(mk(items))
        return nc


def to_bf(x):
    return np.ascontiguousarray(np.asarray(x).astype(BF))


def pack_lhsT(wT):
    K, M = wT.shape
    Kc, Mc = K // 128, M // 128
    a = wT.reshape(Kc, 128, Mc, 128).transpose(2, 1, 0, 3).reshape(Mc, 128, Kc * 128)
    return np.ascontiguousarray(a)


def rope_tables(pos0, n):
    half = 64
    inv = 1.0 / (10000.0 ** (np.arange(half, dtype=np.float64) / half))
    ang = np.arange(pos0, pos0 + n, dtype=np.float64)[None, :] * inv[:, None]
    return (np.cos(ang).astype(np.float32), np.sin(ang).astype(np.float32))



WCH = 4096


def build_wcast(nch):
    p = Prog("wcast")
    nc = p.nc
    R = nch * 128
    src = nc.dram_tensor("wsrc", [R, WCH], F32, kind="ExternalInput").ap()
    dst = nc.dram_tensor("wdst", [R, WCH], BF16, kind="ExternalOutput").ap()
    bufs = [nc.alloc_sbuf_tensor(f"wi{i}", [128, WCH], F32) for i in range(3)]
    obufs = [nc.alloc_sbuf_tensor(f"wo{i}", [128, WCH], BF16) for i in range(3)]
    engs = ["vector", "scalar", "gpsimd"]
    st_toks = []
    for i in range(nch):
        b = i % 3
        w = [st_toks[i - 3]] if i >= 3 else []
        lt = p.dma("sync", bufs[b][:, :], src[128 * i:128 * (i + 1), :], f"ld{b}",
                   waits=w)
        e = engs[i % 3]
        if e == "scalar":
            ct = p.op(e, lambda E, b=b: E.copy(out=obufs[b][:, :], in_=bufs[b][:, :]),
                      waits=[lt])
        else:
            ct = p.op(e, lambda E, b=b: E.tensor_copy(out=obufs[b][:, :],
                                                      in_=bufs[b][:, :]), waits=[lt])
        st_toks.append(p.dma("sync", dst[128 * i:128 * (i + 1), :], obufs[b][:, :],
                             f"st{b}", waits=[ct]))
    p.barrier("sync", waits=[("st0", p.val("st0")), ("st1", p.val("st1")),
                             ("st2", p.val("st2"))])
    return p.build()


WEIGHT_KEYS = [
    "L0_attn_Wq", "L0_attn_Wk", "L0_attn_Wv", "L0_attn_Wo",
    "L0_ffn_W_org", "L0_ffn_lora_down", "L0_ffn_lora_up", "L0_ffn_lora_keys",
    "L0_ffn_Wq",
    "L1_attn_in_proj_w", "L1_attn_conv_weight", "L1_attn_conv_bias",
    "L1_attn_dt_bias", "L1_attn_A_log", "L1_attn_D", "L1_attn_norm_weight",
    "L1_attn_out_proj_w",
    "L1_ffn_W", "L1_ffn_bias", "L1_ffn_A", "L1_ffn_Bf", "L1_ffn_coeffs",
    "L1_ffn_scales",
]


def cast_weights(inputs):
    flats = [np.asarray(inputs[k], dtype=np.float32).reshape(-1) for k in WEIGHT_KEYS]
    sizes = [f.size for f in flats]
    cat = np.concatenate(flats)
    per = -(-cat.size // (NCORES * 128 * WCH)) * 128 * WCH
    cat = np.concatenate([cat, np.zeros(per * NCORES - cat.size, np.float32)])
    shards = cat.reshape(NCORES, per // WCH, WCH)
    nc = build_wcast(per // (WCH * 128))
    res = run_bass_kernel_spmd(nc, [{"wsrc": shards[c]} for c in range(NCORES)],
                               core_ids=list(range(NCORES)))
    out = np.concatenate([np.asarray(res.results[c]["wdst"]).reshape(-1)
                          for c in range(NCORES)])
    w = {}
    off = 0
    for k, s in zip(WEIGHT_KEYS, sizes):
        w[k] = out[off:off + s].reshape(np.asarray(inputs[k]).shape)
        off += s
    return w




def emit_ln(p, x_src_fn, nchunks, bufs, img, ident, ps_tr, tag, lnout_cb=None):
    xbuf, stats, mv, rstd, lntm = bufs
    TT = nchunks * 128
    imgap = img[:, :].rearrange("q (kc tt) -> q kc tt", tt=TT)
    nrm_toks, ev_toks, tr_toks = [], [], []
    for i in range(nchunks):
        b = i % 2
        w = [nrm_toks[i - 2]] if i >= 2 else []
        lt = x_src_fn(i, xbuf[b], w)
        for j in range(8):
            p.op("vector", lambda E, b=b, j=j: E.bn_stats(
                out=stats[:, 6 * j:6 * j + 6], in_=xbuf[b][:, 512 * j:512 * (j + 1)]),
                waits=[lt] if j == 0 else [])
        p.op("vector", lambda E: E.bn_aggr(out=mv[:, :], in_=stats[:, 0:48]))
        vt = p.op("vector", lambda E: E.tensor_scalar_add(rstd[:, :], mv[:, 1:2], EPS))
        p.op("scalar", lambda E: E.activation(out=rstd[:, :], in_=rstd[:, :],
                                              func=AF.Ln), waits=[vt])
        st = p.op("scalar", lambda E: E.activation(out=rstd[:, :], in_=rstd[:, :],
                                                   func=AF.Exp, scale=-0.5))
        nw = [st]
        if i >= 1:
            nw.append(tr_toks[-1])
        nt = p.op("vector", lambda E, b=b: E.tensor_scalar(
            out=lntm[:, :], in0=xbuf[b][:, :], scalar1=mv[:, 0:1], scalar2=rstd[:, :],
            op0=ALU.subtract, op1=ALU.mult), waits=nw)
        nrm_toks.append(nt)
        if lnout_cb is not None:
            nt = lnout_cb(i, xbuf[b], lntm) or nt
            nrm_toks[-1] = nt
        for g in range(4):
            pb = g % 2
            gi = i * 4 + g
            w = [nt] if g == 0 else []
            if gi >= 2:
                w.append(ev_toks[gi - 2])
            tt = None
            for u in range(8):
                kc = g * 8 + u
                tt = p.op("tensor", lambda E, pb=pb, u=u, kc=kc: E.transpose(
                    ps_tr[pb][:, 128 * u:128 * (u + 1)],
                    lntm[:, 128 * kc:128 * (kc + 1)], ident[:, :]),
                    waits=w if u == 0 else [],
                    inc=(tag + "_tr") if u == 7 else None)
            tr_toks.append(tt)
            ev_toks.append(p.op("vector", lambda E, pb=pb, g=g, i=i: E.tensor_copy(
                out=imgap[:, g * 8:(g + 1) * 8, 128 * i:128 * (i + 1)],
                in_=ps_tr[pb][:, :].rearrange("q (u t) -> q u t", u=8)),
                waits=[tt]))
    return ev_toks




def build_a1():
    p = Prog("a1")
    nc = p.nc
    x = nc.dram_tensor("x", [TC, D], F32, kind="ExternalInput").ap()
    wq = nc.dram_tensor("wq", [NH, 128, 4096], BF16, kind="ExternalInput").ap()
    wk = nc.dram_tensor("wk", [NKV, 128, 4096], BF16, kind="ExternalInput").ap()
    wv = nc.dram_tensor("wv", [NKV, 128, 4096], BF16, kind="ExternalInput").ap()
    ctab = nc.dram_tensor("ctab", [64, TC], F32, kind="ExternalInput").ap()
    stab = nc.dram_tensor("stab", [64, TC], F32, kind="ExternalInput").ap()
    idin = nc.dram_tensor("idin", [128, 128], BF16, kind="ExternalInput").ap()
    q_o = nc.dram_tensor("q_o", [NH, 128, TC], BF16, kind="ExternalOutput").ap()
    k_o = nc.dram_tensor("k_o", [NKV, 128, TC], BF16, kind="ExternalOutput").ap()
    v_o = nc.dram_tensor("v_o", [NKV, 128, TC], BF16, kind="ExternalOutput").ap()

    xbuf = [nc.alloc_sbuf_tensor(f"xb{i}", [128, D], F32) for i in range(2)]
    stats = nc.alloc_sbuf_tensor("stats", [128, 48], F32)
    mv = nc.alloc_sbuf_tensor("mv", [128, 2], F32)
    rstd = nc.alloc_sbuf_tensor("rstd", [128, 1], F32)
    lntm = nc.alloc_sbuf_tensor("lntm", [128, D], BF16)
    img = nc.alloc_sbuf_tensor("img", [128, 32 * TC], BF16)
    ident = nc.alloc_sbuf_tensor("ident", [128, 128], BF16)
    ps_tr = [nc.alloc_psum_tensor(f"pstr{i}", [128, 1024], BF16) for i in range(2)]
    ps_mm = [nc.alloc_psum_tensor(f"psmm{i}", [128, 512], F32) for i in range(4)]
    wbuf = [nc.alloc_sbuf_tensor(f"wb{i}", [128, 4096], BF16) for i in range(2)]
    cos = nc.alloc_sbuf_tensor("cos", [64, TC], F32)
    sin = nc.alloc_sbuf_tensor("sin", [64, TC], F32)
    t1 = nc.alloc_sbuf_tensor("t1", [64, TC], F32)
    t2 = nc.alloc_sbuf_tensor("t2", [64, TC], F32)
    ov = [nc.alloc_sbuf_tensor(f"ov{i}", [128, TC], BF16) for i in range(2)]

    p.dma("sync", ident[:, :], idin[:, :], "cld")
    p.dma("sync", cos[:, :], ctab[:, :], "cld")
    cld = p.dma("sync", sin[:, :], stab[:, :], "cld")
    p.barrier("tensor", waits=[cld])
    p.barrier("vector", waits=[cld])

    def xsrc(i, buf, w):
        return p.dma("sync", buf[:, :], x[128 * i:128 * (i + 1), :],
                     f"a_xld{i % 2}", waits=w)

    ev_toks = emit_ln(p, xsrc, 8, (xbuf, stats, mv, rstd, lntm), img, ident,
                      ps_tr, "a")
    imgdone = ev_toks[-1]
    imgap = img[:, :].rearrange("q (kc tt) -> q kc tt", tt=TC)

    jobs = ([("q", wq[h], q_o[h]) for h in range(NH)]
            + [("k", wk[h], k_o[h]) for h in range(NKV)]
            + [("v", wv[h], v_o[h]) for h in range(NKV)])

    mm_toks, ev2_toks, st_toks = [], [], []
    for ji, (kind, wsrc, osl) in enumerate(jobs):
        wb = ji % 2
        wlt = p.dma("sync", wbuf[wb][:, :], wsrc[:, :], f"wl{wb}",
                    waits=[mm_toks[ji - 2]] if ji >= 2 else [])
        mt = None
        for n in range(2):
            pb = 2 * (ji % 2) + n
            for kc in range(32):
                w = []
                if kc == 0 and n == 0:
                    w = [wlt, imgdone]
                    if ji >= 2:
                        w.append(ev2_toks[ji - 2])
                mt = p.op("tensor", lambda E, pb=pb, kc=kc, n=n, wb=wb: E.matmul(
                    ps_mm[pb][:, :], wbuf[wb][:, 128 * kc:128 * (kc + 1)],
                    imgap[:, kc, 512 * n:512 * (n + 1)],
                    start=(kc == 0), stop=(kc == 31)),
                    waits=w, inc="a_mme" if (kc == 31 and n == 1) else None)
        mm_toks.append(mt)
        ob = ov[ji % 2]
        sw = [st_toks[ji - 2]] if ji >= 2 else []
        if kind == "v":
            p.op("vector", lambda E, pb=2 * (ji % 2), ob=ob: E.tensor_copy(
                out=ob[:, 0:512], in_=ps_mm[pb][:, :]), waits=[mt] + sw)
            et = p.op("vector", lambda E, pb=2 * (ji % 2) + 1, ob=ob: E.tensor_copy(
                out=ob[:, 512:1024], in_=ps_mm[pb][:, :]))
        else:
            for n in range(2):
                ps = ps_mm[2 * (ji % 2) + n]
                sl = slice(512 * n, 512 * (n + 1))
                p.op("vector", lambda E, ps=ps, sl=sl: E.tensor_mul(
                    t1[:, sl], ps[0:64, :], cos[:, sl]),
                    waits=[mt] + sw if n == 0 else [])
                p.op("vector", lambda E, ps=ps, sl=sl: E.tensor_mul(
                    t2[:, sl], ps[64:128, :], sin[:, sl]))
                p.op("vector", lambda E, sl=sl, ob=ob: E.tensor_sub(
                    ob[0:64, sl], t1[:, sl], t2[:, sl]))
                p.op("vector", lambda E, ps=ps, sl=sl: E.tensor_mul(
                    t1[:, sl], ps[0:64, :], sin[:, sl]))
                p.op("vector", lambda E, ps=ps, sl=sl: E.tensor_mul(
                    t2[:, sl], ps[64:128, :], cos[:, sl]))
                et = p.op("vector", lambda E, sl=sl, ob=ob: E.tensor_add(
                    ob[64:128, sl], t1[:, sl], t2[:, sl]))
        ev2_toks.append(et)
        st_toks.append(p.dma("sync", osl[:, :], ob[:, :], f"a_sto{ji % 2}",
                             waits=[et]))
    p.barrier("sync", waits=[("a_sto0", p.val("a_sto0")),
                             ("a_sto1", p.val("a_sto1"))])
    return p.build()


def run_a1(x_slices, wbf):
    wq_img = pack_lhsT(to_bf(wbf["L0_attn_Wq"]))
    wk_img = pack_lhsT(to_bf(wbf["L0_attn_Wk"]))
    wv_img = pack_lhsT(to_bf(wbf["L0_attn_Wv"]))
    ident = np.eye(128, dtype=BF)
    nc = build_a1()
    in_maps = []
    for c in range(NCORES):
        cosb, sinb = rope_tables((c % 4) * TC, TC)
        in_maps.append({"x": x_slices[c], "wq": wq_img, "wk": wk_img,
                        "wv": wv_img, "ctab": cosb, "stab": sinb, "idin": ident})
    return run_bass_kernel_spmd(nc, in_maps, core_ids=list(range(NCORES))).results






class LNEmitter:

    def __init__(self, p, nc, ident, pool, tag):
        self.p, self.nc, self.ident, self.pool, self.tag = p, nc, ident, pool, tag
        self.stats = nc.alloc_sbuf_tensor(tag + "_stats", [128, 48], F32)
        self.mv = nc.alloc_sbuf_tensor(tag + "_mv", [128, 2], F32)
        self.rstd = nc.alloc_sbuf_tensor(tag + "_rstd", [128, 1], F32)
        self.lntm = nc.alloc_sbuf_tensor(tag + "_lntm", [128, D], BF16)
        self.last_tr = None

    def chunk(self, buf, ready, img_dst):
        p = self.p
        for j in range(8):
            p.op("vector", lambda E, j=j, buf=buf: E.bn_stats(
                out=self.stats[:, 6 * j:6 * j + 6],
                in_=buf[:, 512 * j:512 * (j + 1)]),
                waits=[ready] if j == 0 else [])
        p.op("vector", lambda E: E.bn_aggr(out=self.mv[:, :], in_=self.stats[:, 0:48]))
        vt = p.op("vector", lambda E: E.tensor_scalar_add(
            self.rstd[:, :], self.mv[:, 1:2], EPS))
        p.op("scalar", lambda E: E.activation(out=self.rstd[:, :], in_=self.rstd[:, :],
                                              func=AF.Ln), waits=[vt])
        st = p.op("scalar", lambda E: E.activation(out=self.rstd[:, :],
                                                   in_=self.rstd[:, :],
                                                   func=AF.Exp, scale=-0.5))
        nw = [st] + ([self.last_tr] if self.last_tr else [])
        rel = p.op("vector", lambda E, buf=buf: E.tensor_scalar(
            out=self.lntm[:, :], in0=buf[:, :], scalar1=self.mv[:, 0:1],
            scalar2=self.rstd[:, :], op0=ALU.subtract, op1=ALU.mult), waits=nw)
        ev = None
        for g in range(4):
            dst, extraw = img_dst(g)
            w = ([rel] if g == 0 else []) + ([self.pool["rel"]] if self.pool["rel"] else [])
            tt = None
            for u in range(8):
                kc = g * 8 + u
                tt = p.op("tensor", lambda E, u=u, kc=kc: E.transpose(
                    self.pool["ap"][:, 128 * u:128 * (u + 1)],
                    self.lntm[:, 128 * kc:128 * (kc + 1)], self.ident[:, :]),
                    waits=w if u == 0 else [],
                    inc=(self.tag + "_tr") if u == 7 else None)
            self.last_tr = tt
            ev = p.op("vector", lambda E, dst=dst: E.tensor_copy(
                out=dst, in_=self.pool["ap"][:, :].rearrange("q (u t) -> q u t", u=8)),
                waits=[tt] + [x for x in extraw if x])
            self.pool["rel"] = ev
        return rel, ev



SCL = 1.0 / float(np.sqrt(np.float32(128.0)))
EBIAS = -20.0


def build_a2(dbg=False):
    p = Prog("a2")
    nc = p.nc
    q_i = nc.dram_tensor("q_i", [NH, 128, TC], BF16, kind="ExternalInput").ap()
    kin = nc.dram_tensor("kin", [2, NKV, 128, T], BF16, kind="ExternalInput").ap()
    vin = nc.dram_tensor("vin", [2, NKV, 128, T], BF16, kind="ExternalInput").ap()
    vcol = nc.dram_tensor("vcol", [128, 64], BF16, kind="ExternalInput").ap()
    tri = nc.dram_tensor("tri", [128, 2048], BF16, kind="ExternalInput").ap()
    wo = nc.dram_tensor("wo", [32, 128, 4096], BF16, kind="ExternalInput").ap()
    x = nc.dram_tensor("x", [TC, D], F32, kind="ExternalInput").ap()
    onesr = nc.dram_tensor("onesr", [1, 128], F32, kind="ExternalInput").ap()
    idin = nc.dram_tensor("idin", [128, 128], BF16, kind="ExternalInput").ap()
    x1_o = nc.dram_tensor("x1_o", [TC, D], F32, kind="ExternalOutput").ap()
    if dbg:
        d_pt = nc.dram_tensor("d_pt", [128, 16384], BF16, kind="ExternalOutput").ap()
        d_l = nc.dram_tensor("d_l", [1, 512], F32, kind="ExternalOutput").ap()
        d_rl = nc.dram_tensor("d_rl", [128, 512], F32, kind="ExternalOutput").ap()
        d_att = nc.dram_tensor("d_att", [128, 512], BF16, kind="ExternalOutput").ap()
    ln_o = nc.dram_tensor("ln_o", [8, 128, 4096], BF16, kind="ExternalOutput").ap()
    pt_o = nc.dram_tensor("pt_o", [128, 32], F32, kind="ExternalOutput").ap()

    kb = [nc.alloc_sbuf_tensor(f"kb{i}", [128, T], BF16) for i in range(2)]
    vb = [nc.alloc_sbuf_tensor(f"vb{i}", [128, T], BF16) for i in range(2)]
    qb_ = [nc.alloc_sbuf_tensor(f"qbuf{i}", [128, 512], BF16) for i in range(2)]
    pt = nc.alloc_sbuf_tensor("pt", [128, 32 * 512], BF16)
    att = nc.alloc_sbuf_tensor("att", [128, 32 * 512], BF16)
    tri_sb = nc.alloc_sbuf_tensor("tri_sb", [128, 2048], BF16)
    vcol_sb = nc.alloc_sbuf_tensor("vcol_sb", [128, 64], BF16)
    lsb = nc.alloc_sbuf_tensor("lsb", [1, 512], F32)
    rlsb = nc.alloc_sbuf_tensor("rlsb", [128, 512], F32)
    ones_sb = nc.alloc_sbuf_tensor("ones_sb", [1, 128], F32)
    wob = [nc.alloc_sbuf_tensor(f"wob{i}", [128, 4096], BF16) for i in range(2)]
    projsb = nc.alloc_sbuf_tensor("projsb", [128, 32 * 512], BF16)
    xres = [nc.alloc_sbuf_tensor(f"xres{i}", [128, D], F32) for i in range(2)]
    ident = nc.alloc_sbuf_tensor("ident", [128, 128], BF16)
    strip = nc.alloc_sbuf_tensor("strip", [128, 4096], BF16)
    pacc = nc.alloc_sbuf_tensor("pacc", [128, 8 * 32], F32)
    ptls = nc.alloc_sbuf_tensor("ptls", [128, 32], F32)
    ebias = nc.alloc_sbuf_tensor("ebias", [128, 1], F32)

    pss = [nc.alloc_psum_tensor(f"pss{i}", [128, 1024], F32) for i in range(2)]
    psl = nc.alloc_psum_tensor("psl", [128, 512], F32)
    psav = nc.alloc_psum_tensor("psav", [128, 512], F32)
    psrl = nc.alloc_psum_tensor("psrl", [128, 512], F32)
    psb = nc.alloc_psum_tensor("psb", [128, 1024], BF16)

    p.dma("sync", tri_sb[:, :], tri[:, :], "cld")
    p.dma("sync", vcol_sb[:, :], vcol[:, :], "cld")
    p.dma("sync", ones_sb[:, :], onesr[:, :], "cld")
    cld = p.dma("sync", ident[:, :], idin[:, :], "cld")
    p.barrier("tensor", waits=[cld])
    p.barrier("vector", waits=[cld])
    p.barrier("scalar", waits=[cld])

    ebt = p.op("vector", lambda E: E.memset(ebias[:, :], EBIAS))
    p.barrier("scalar", waits=[ebt])
    pool = {"ap": psb, "rel": None}
    lne = LNEmitter(p, nc, ident, pool, "a2ln")
    stripap = strip[:, :].rearrange("q (kc t) -> q kc t", t=128)

    kbrel = [None, None]
    vbrel = [None, None]
    qrel = [None, None]
    exp_tok = [None, None]
    prev_av = None
    prev_evl = None
    prev_evrl = None
    prev_nrm = None
    wo_last_mm = None
    evp_last = None
    tr_after_qb = None
    lnrel, x1st = {}, {}
    strip_st = None
    kt_cur = [None, None]
    vt_cur = [None, None]

    for qb in range(2):
        nrm = None
        for h in range(NH):
            hk = h // 4
            hb = hk % 2
            if h % 4 == 0:
                kt_cur[hb] = p.dma("sync", kb[hb][:, :], kin[qb][hk][:, :],
                                   f"kld{hb}",
                                   waits=[kbrel[hb]] if kbrel[hb] else [])
                vt_cur[hb] = p.dma("sync", vb[hb][:, :], vin[qb][hk][:, :],
                                   f"vld{hb}",
                                   waits=[vbrel[hb]] if vbrel[hb] else [])
            qt = p.dma("sync", qb_[h % 2][:, :], q_i[h][:, 512 * qb:512 * (qb + 1)],
                       f"qld{h % 2}", waits=[qrel[h % 2]] if qrel[h % 2] else [])
            smm = None
            for g in range(16):
                pp = g % 2
                for u in range(2):
                    kc = 2 * g + u
                    w = []
                    if u == 0:
                        if exp_tok[pp] is not None:
                            w.append(exp_tok[pp])
                        if g == 0:
                            w += [kt_cur[hb], qt]
                            if evp_last is not None:
                                w.append(evp_last)
                                evp_last = None
                    smm = p.op("tensor", lambda E, pp=pp, u=u, kc=kc, hb=hb, hq=h % 2:
                               E.matmul(pss[pp][:, 512 * u:512 * (u + 1)],
                                        kb[hb][:, 128 * kc:128 * (kc + 1)],
                                        qb_[hq][:, :], start=True, stop=True),
                               waits=w, inc="a2s" if u == 1 else None)
                ew = [smm]
                if g <= 1 and prev_av is not None:
                    ew.append(prev_av)
                exp_tok[pp] = p.op("scalar", lambda E, pp=pp, g=g: E.activation(
                    out=pt[:, 1024 * g:1024 * (g + 1)], in_=pss[pp][:, :],
                    func=AF.Exp, scale=SCL, bias=ebias[:, :]), waits=ew)
            kbrel[hb] = smm
            qrel[h % 2] = smm
            mk = p.op("vector", lambda E: E.tensor_mul(
                pt[:, 28 * 512:32 * 512], pt[:, 28 * 512:32 * 512], tri_sb[:, :]),
                waits=[exp_tok[0], exp_tok[1]])
            lmm = None
            for kc in range(32):
                w = []
                if kc == 0:
                    w = [mk] + ([prev_evl] if prev_evl else [])
                lmm = p.op("tensor", lambda E, kc=kc, qb=qb: E.matmul(
                    psl[0:1, :], vcol_sb[:, 32 * qb + kc:32 * qb + kc + 1],
                    pt[:, 512 * kc:512 * (kc + 1)],
                    start=(kc == 0), stop=(kc == 31)),
                    waits=w, inc="a2l" if kc == 31 else None)
            avm = None
            for kc in range(32):
                w = []
                if kc == 0:
                    w = [vt_cur[hb]] + ([prev_nrm] if prev_nrm else [])
                avm = p.op("tensor", lambda E, kc=kc, hb=hb: E.matmul(
                    psav[:, :], vb[hb][:, 128 * kc:128 * (kc + 1)],
                    pt[:, 512 * kc:512 * (kc + 1)],
                    start=(kc == 0), stop=(kc == 31)),
                    waits=w, inc="a2av" if kc == 31 else None)
            vbrel[hb] = avm
            prev_av = avm
            evl = p.op("vector", lambda E: E.tensor_copy(out=lsb[:, :],
                                                         in_=psl[0:1, :]),
                       waits=[lmm])
            prev_evl = evl
            rcp = p.op("vector", lambda E: E.reciprocal(lsb[:, :], lsb[:, :]))
            bc = p.op("tensor", lambda E: E.matmul(psrl[:, :], ones_sb[:, :],
                                                   lsb[:, :], start=True, stop=True),
                      waits=[rcp] + ([prev_evrl] if prev_evrl else []), inc="a2rl")
            evrl = p.op("vector", lambda E: E.tensor_copy(out=rlsb[:, :],
                                                          in_=psrl[:, :]), waits=[bc])
            prev_evrl = evrl
            nw = [avm]
            if h == 0 and wo_last_mm is not None:
                nw.append(wo_last_mm)
            nrm = p.op("vector", lambda E, h=h: E.tensor_mul(
                att[:, 512 * h:512 * (h + 1)], psav[:, :], rlsb[:, :]), waits=nw)
            prev_nrm = nrm
            if dbg and qb == 0 and h == 0:
                p.dma("sync", d_pt[:, :], pt[:, :], "dbg", waits=[mk])
                p.dma("sync", d_l[:, :], lsb[:, :], "dbg", waits=[rcp])
                p.dma("sync", d_rl[:, :], rlsb[:, :], "dbg", waits=[evrl])
                p.dma("sync", d_att[:, :], att[:, 0:512], "dbg", waits=[nrm])
                prev_nrm = ("dbg", p.val("dbg"))
                prev_evl = prev_nrm
                prev_av = None
                p.barrier("tensor", waits=[("dbg", p.val("dbg"))])
        womm, evp = {}, {}
        for mc in range(32):
            ww = [womm[mc - 2]] if mc >= 2 else (
                [wo_last_mm] if wo_last_mm is not None else [])
            wt = p.dma("sync", wob[mc % 2][:, :], wo[mc][:, :], f"wld{mc % 2}",
                       waits=ww)
            mmt = None
            for kc in range(32):
                w = []
                if kc == 0:
                    w = [wt, nrm]
                    if mc >= 2:
                        w.append(evp[mc - 2])
                    if mc <= 1 and tr_after_qb is not None:
                        w.append(tr_after_qb)
                mmt = p.op("tensor", lambda E, kc=kc, mc=mc: E.matmul(
                    pss[mc % 2][:, 0:512], wob[mc % 2][:, 128 * kc:128 * (kc + 1)],
                    att[:, 512 * kc:512 * (kc + 1)],
                    start=(kc == 0), stop=(kc == 31)),
                    waits=w, inc="a2wo" if kc == 31 else None)
            womm[mc] = mmt
            evp[mc] = p.op("vector", lambda E, mc=mc: E.tensor_copy(
                out=projsb[:, 512 * mc:512 * (mc + 1)], in_=pss[mc % 2][:, 0:512]),
                waits=[mmt])
        wo_last_mm = womm[31]
        evp_last = evp[31]
        for t in range(4):
            i = 4 * qb + t
            xw = []
            if i >= 2:
                xw = [lnrel[i - 2], x1st[i - 2]]
            xt = p.dma("sync", xres[i % 2][:, :], x[128 * i:128 * (i + 1), :],
                       f"xld{i % 2}", waits=xw)
            add = None
            for g2 in range(4):
                w = [evp[31]] if g2 == 0 else []
                if pool["rel"] is not None:
                    w.append(pool["rel"])
                tt = None
                for u in range(8):
                    mcc = 8 * g2 + u
                    tt = p.op("tensor", lambda E, u=u, mcc=mcc, t=t: E.transpose(
                        psb[:, 128 * u:128 * (u + 1)],
                        projsb[:, 512 * mcc + 128 * t:512 * mcc + 128 * (t + 1)],
                        ident[:, :]),
                        waits=w if u == 0 else [], inc="a2tr" if u == 7 else None)
                add = p.op("vector", lambda E, g2=g2, i=i: E.tensor_add(
                    xres[i % 2][:, 1024 * g2:1024 * (g2 + 1)],
                    xres[i % 2][:, 1024 * g2:1024 * (g2 + 1)], psb[:, :]),
                    waits=[tt, xt])
                pool["rel"] = add
            x1st[i] = p.dma("sync", x1_o[128 * i:128 * (i + 1), :], xres[i % 2][:, :],
                            f"x1st{i % 2}", waits=[add])
            rel, ev = lne.chunk(xres[i % 2], add,
                                lambda g, st_=strip_st: (stripap[:, g * 8:(g + 1) * 8, :],
                                                         [st_]))
            lnrel[i] = rel
            prd = p.op("vector", lambda E, i=i: E.tensor_reduce(
                pacc[:, 32 * i:32 * (i + 1)], stripap[:, :, :], AX.X, ALU.add),
                waits=[ev])
            strip_st = p.dma("sync", ln_o[i][:, :], strip[:, :], "lnst", waits=[prd])
        tr_after_qb = ("a2tr", p.val("a2tr"))
    fr = p.op("vector", lambda E: E.tensor_reduce(
        ptls[:, :], pacc[:, :].rearrange("q (i k) -> q k i", i=8), AX.X, ALU.add))
    p.dma("sync", pt_o[:, :], ptls[:, :], "ptst", waits=[fr])
    p.barrier("sync", waits=[("x1st0", p.val("x1st0")), ("x1st1", p.val("x1st1")),
                             ("lnst", p.val("lnst")), ("ptst", p.val("ptst"))])
    return p.build()


def prep_a2_core(c, a1res, wo_img, x_slices, tri_m, onesr, ident):
    j = c % 4
    bc0 = 4 * (c // 4)
    Kf = [np.concatenate([np.asarray(a1res[cc]["k_o"])[hk] for cc in
                          range(bc0, bc0 + 4)], axis=1) for hk in range(NKV)]
    Vf = [np.concatenate([np.asarray(a1res[cc]["v_o"])[hk] for cc in
                          range(bc0, bc0 + 4)], axis=1) for hk in range(NKV)]
    kin = np.zeros((2, NKV, 128, T), BF)
    vin = np.zeros((2, NKV, 128, T), BF)
    vcol = np.zeros((128, 64), BF)
    for b in range(2):
        d0 = 8 * j + 4 * b
        for kc in range(32):
            a = (kc - 28 + d0) % 32
            valid = (kc >= 28) or (a < d0)
            if valid:
                vcol[:, 32 * b + kc] = BF(1.0)
            for hk in range(NKV):
                kin[b, hk, :, 128 * kc:128 * (kc + 1)] = Kf[hk][:, 128 * a:128 * (a + 1)]
                if valid:
                    vin[b, hk, :, 128 * kc:128 * (kc + 1)] = \
                        Kf[hk][:, 0:0] if False else Vf[hk][:, 128 * a:128 * (a + 1)].T
    return {"q_i": np.asarray(a1res[c]["q_o"]), "kin": kin, "vin": vin,
            "vcol": vcol, "tri": tri_m, "wo": wo_img, "x": x_slices[c],
            "onesr": onesr, "idin": ident}


def run_a2(x_slices, a1res, wbf):
    wo_img = pack_lhsT(to_bf(wbf["L0_attn_Wo"]))
    tri_m = np.zeros((128, 2048), BF)
    for i_ in range(4):
        for pp in range(128):
            tri_m[pp, 512 * i_:512 * (i_ + 1)] = (128 * i_ + pp <=
                                                  np.arange(512)).astype(BF)
    onesr = np.ones((1, 128), np.float32)
    ident = np.eye(128, dtype=BF)
    nc = build_a2()
    in_maps = [prep_a2_core(c, a1res, wo_img, x_slices, tri_m, onesr, ident)
               for c in range(NCORES)]
    return run_bass_kernel_spmd(nc, in_maps, core_ids=list(range(NCORES))).results




def build_f2(dbg=False):
    p = Prog("f2")
    nc = p.nc
    lnin = nc.dram_tensor("lnin", [128, 32 * TC], BF16, kind="ExternalInput").ap()
    x1 = nc.dram_tensor("x1", [TC, D], F32, kind="ExternalInput").ap()
    parts = nc.dram_tensor("parts", [4, 128, 32], F32, kind="ExternalInput").ap()
    worg = nc.dram_tensor("worg", [32, 128, 4096], BF16, kind="ExternalInput").ap()
    upw = nc.dram_tensor("upw", [32, 128, 512], BF16, kind="ExternalInput").ap()
    dnw = nc.dram_tensor("dnw", [4, 128, 4096], BF16, kind="ExternalInput").ap()
    wqt = nc.dram_tensor("wqt", [1, 128, 4096], BF16, kind="ExternalInput").ap()
    keyt = nc.dram_tensor("keyt", [128, 8], BF16, kind="ExternalInput").ap()
    sel = nc.dram_tensor("sel", [8, 512], BF16, kind="ExternalInput").ap()
    ones1 = nc.dram_tensor("ones1", [1, 1], BF16, kind="ExternalInput").ap()
    idin = nc.dram_tensor("idin", [128, 128], BF16, kind="ExternalInput").ap()
    x2_o = nc.dram_tensor("x2_o", [TC, D], F32, kind="ExternalOutput").ap()
    ln_o = nc.dram_tensor("ln_o", [8, 128, 4096], BF16, kind="ExternalOutput").ap()
    if dbg:
        d_mean = nc.dram_tensor("d_mean", [128, 32], F32, kind="ExternalOutput").ap()
        d_q = nc.dram_tensor("d_q", [128, 1], BF16, kind="ExternalOutput").ap()
        d_co = nc.dram_tensor("d_co", [1, 8], BF16, kind="ExternalOutput").ap()
        d_cv = nc.dram_tensor("d_cv", [128, 4], F32, kind="ExternalOutput").ap()
        d_xd = nc.dram_tensor("d_xd", [128, 4 * TC], BF16, kind="ExternalOutput").ap()

    img = nc.alloc_sbuf_tensor("img", [128, 32 * TC], BF16)
    wb = [nc.alloc_sbuf_tensor(f"wb{i}", [128, 4096], BF16) for i in range(2)]
    ub = [nc.alloc_sbuf_tensor(f"ub{i}", [128, 512], BF16) for i in range(2)]
    parts_sb = nc.alloc_sbuf_tensor("parts_sb", [128, 4 * 32], F32)
    mean_sb = nc.alloc_sbuf_tensor("mean_sb", [128, 32], F32)
    mean_bf = nc.alloc_sbuf_tensor("mean_bf", [128, 32], BF16)
    qsb = nc.alloc_sbuf_tensor("qsb", [128, 1], BF16)
    keyt_sb = nc.alloc_sbuf_tensor("keyt_sb", [128, 8], BF16)
    sel_sb = nc.alloc_sbuf_tensor("sel_sb", [8, 512], BF16)
    ones1_sb = nc.alloc_sbuf_tensor("ones1_sb", [1, 1], BF16)
    ssb = nc.alloc_sbuf_tensor("ssb", [1, 8], F32)
    nm = nc.alloc_sbuf_tensor("nm", [1, 1], F32)
    esum = nc.alloc_sbuf_tensor("esum", [1, 1], F32)
    co_bf = nc.alloc_sbuf_tensor("co_bf", [1, 8], BF16)
    cocol = nc.alloc_sbuf_tensor("cocol", [8, 1], BF16)
    cvec = nc.alloc_sbuf_tensor("cvec", [128, 4], F32)
    xds = nc.alloc_sbuf_tensor("xds", [128, 4 * TC], BF16)
    projsb = nc.alloc_sbuf_tensor("projsb", [128, 32 * 512], BF16)
    xres = [nc.alloc_sbuf_tensor(f"xres{i}", [128, D], F32) for i in range(2)]
    ident = nc.alloc_sbuf_tensor("ident", [128, 128], BF16)
    strip = nc.alloc_sbuf_tensor("strip", [128, 4096], BF16)

    ps = [nc.alloc_psum_tensor(f"f2ps{i}", [128, 1024], F32) for i in range(2)]
    ps1 = nc.alloc_psum_tensor("f2psq", [128, 512], F32)
    psb = nc.alloc_psum_tensor("f2psb", [128, 1024], BF16)

    imgap = img[:, :].rearrange("q (kc tt) -> q kc tt", tt=TC)
    stripap = strip[:, :].rearrange("q (kc t) -> q kc t", t=128)

    ildt = p.dma("sync", img[:, :], lnin[:, :], "ild")
    p.dma("sync", parts_sb[:, :].rearrange("q (j k) -> q j k", j=4),
          parts[:, :].rearrange("j q k -> q j k"), "cld")
    p.dma("sync", keyt_sb[:, :], keyt[:, :], "cld")
    p.dma("sync", sel_sb[:, :], sel[:, :], "cld")
    p.dma("sync", ones1_sb[:, :], ones1[:, :], "cld")
    cld = p.dma("sync", ident[:, :], idin[:, :], "cld")
    p.barrier("vector", waits=[cld])
    p.barrier("tensor", waits=[cld, ildt])

    p.op("vector", lambda E: E.tensor_add(mean_sb[:, :], parts_sb[:, 0:32],
                                          parts_sb[:, 32:64]))
    p.op("vector", lambda E: E.tensor_add(mean_sb[:, :], mean_sb[:, :],
                                          parts_sb[:, 64:96]))
    p.op("vector", lambda E: E.tensor_add(mean_sb[:, :], mean_sb[:, :],
                                          parts_sb[:, 96:128]))
    mt = p.op("vector", lambda E: E.tensor_scalar_mul(mean_bf[:, :], mean_sb[:, :],
                                                      1.0 / 4096.0))
    wqld = p.dma("sync", wb[0][:, :], wqt[0][:, :], "wld0")
    qmm = None
    for kc in range(32):
        qmm = p.op("tensor", lambda E, kc=kc: E.matmul(
            ps1[:, 0:1], wb[0][:, 128 * kc:128 * (kc + 1)],
            mean_bf[:, kc:kc + 1], start=(kc == 0), stop=(kc == 31)),
            waits=[wqld, mt] if kc == 0 else [], inc="qmm" if kc == 31 else None)
    qe = p.op("vector", lambda E: E.tensor_copy(out=qsb[:, :], in_=ps1[:, 0:1]),
              waits=[qmm])
    smm = p.op("tensor", lambda E: E.matmul(ps1[0:1, 4:12], qsb[:, :],
                                            keyt_sb[:, :], start=True, stop=True),
               waits=[qe], inc="smm")
    se = p.op("vector", lambda E: E.tensor_scalar_mul(ssb[:, :], ps1[0:1, 4:12],
                                                      SCL), waits=[smm])
    p.op("vector", lambda E: E.reduce_max(nm[:, :], ssb[:, :], axis=AX.X,
                                          negate=True))
    et = p.op("scalar", lambda E: E.activation(out=ssb[:, :], in_=ssb[:, :],
                                               func=AF.Exp, bias=nm[:, :],
                                               accum_out=esum[:, :]),
              waits=[p.engine_done_tok("vector")])
    p.op("vector", lambda E: E.reciprocal(esum[:, :], esum[:, :]), waits=[et])
    ct = p.op("vector", lambda E: E.tensor_scalar_mul(co_bf[:, :], ssb[:, :],
                                                      esum[:, :]))
    ctm = p.op("tensor", lambda E: E.matmul(ps1[0:8, 16:17], co_bf[:, :],
                                            ones1_sb[:, :], start=True, stop=True),
               waits=[ct], inc="ctm")
    cce = p.op("vector", lambda E: E.tensor_copy(out=cocol[:, :],
                                                 in_=ps1[0:8, 16:17]), waits=[ctm])
    vmm = None
    for j in range(4):
        vmm = p.op("tensor", lambda E, j=j: E.matmul(
            ps1[:, 20 + j:21 + j], sel_sb[:, 128 * j:128 * (j + 1)], cocol[:, :],
            start=True, stop=True), waits=[cce] if j == 0 else [],
            inc="vmm" if j == 3 else None)
    cvt = p.op("vector", lambda E: E.tensor_copy(out=cvec[:, :], in_=ps1[:, 20:24]),
               waits=[vmm])
    if dbg:
        p.dma("sync", d_mean[:, :], mean_sb[:, :], "dbg", waits=[mt])
        p.dma("sync", d_q[:, :], qsb[:, :], "dbg", waits=[qe])
        p.dma("sync", d_co[:, :], co_bf[:, :], "dbg", waits=[ct])
        p.dma("sync", d_cv[:, :], cvec[:, :], "dbg", waits=[cvt])

    dnt = {}
    xde = {}
    prev_dn_mm = None
    for j in range(4):
        dw = []
        if j == 1:
            dw = [qmm]
        elif j >= 2:
            dw = [prev_dn_mm]
        dnt[j] = p.dma("sync", wb[(j + 1) % 2][:, :], dnw[j][:, :],
                       f"wld{(j + 1) % 2}", waits=dw)
        mmt = None
        for n in range(2):
            w0 = [dnt[j]]
            if j >= 2:
                w0.append(xde[(j - 2, n)])
            for kc in range(32):
                mmt = p.op("tensor", lambda E, kc=kc, n=n, j=j: E.matmul(
                    ps[j % 2][:, 512 * n:512 * (n + 1)],
                    wb[(j + 1) % 2][:, 128 * kc:128 * (kc + 1)],
                    imgap[:, kc, 512 * n:512 * (n + 1)],
                    start=(kc == 0), stop=(kc == 31)),
                    waits=w0 if kc == 0 else [],
                    inc="f2xd" if kc == 31 else None)
            xde[(j, n)] = p.op("vector", lambda E, j=j, n=n: E.tensor_scalar(
                out=xds[:, TC * j + 512 * n:TC * j + 512 * (n + 1)],
                in0=ps[j % 2][:, 512 * n:512 * (n + 1)],
                scalar1=cvec[:, j:j + 1], scalar2=None, op0=ALU.mult),
                waits=[mmt, cvt] if n == 0 else [mmt])
        prev_dn_mm = mmt
    if dbg:
        p.dma("sync", d_xd[:, :], xds[:, :], "dbg", waits=[xde[(3, 1)]])

    pool = {"ap": psb, "rel": None}
    lne = LNEmitter(p, nc, ident, pool, "f2ln")
    lnrel, x2st = {}, {}
    strip_st = None
    tr_after = None
    last_wo_of_half = None
    for half in range(2):
        evp = {}
        womm = {}
        for mc in range(32):
            ww = []
            if mc >= 2:
                ww = [womm[mc - 2]]
            elif half == 0:
                ww = [xde[(3, 1)]]
            else:
                ww = [last_wo_of_half]
            wt = p.dma("sync", wb[mc % 2][:, :], worg[mc][:, :], f"wld{mc % 2}",
                       waits=ww)
            ut = p.dma("sync", ub[mc % 2][:, :], upw[mc][:, :], f"uld{mc % 2}",
                       waits=ww)
            mmt = None
            w0 = [wt, ut]
            if mc >= 2:
                w0.append(evp[mc - 2])
            if mc <= 1 and tr_after is not None:
                w0.append(tr_after)
            for kc in range(32):
                mmt = p.op("tensor", lambda E, kc=kc, mc=mc, half=half: E.matmul(
                    ps[mc % 2][:, 0:512], wb[mc % 2][:, 128 * kc:128 * (kc + 1)],
                    imgap[:, kc, 512 * half:512 * (half + 1)],
                    start=(kc == 0), stop=False),
                    waits=w0 if kc == 0 else [])
            for j in range(4):
                mmt = p.op("tensor", lambda E, j=j, mc=mc, half=half: E.matmul(
                    ps[mc % 2][:, 0:512], ub[mc % 2][:, 128 * j:128 * (j + 1)],
                    xds[:, TC * j + 512 * half:TC * j + 512 * (half + 1)],
                    start=False, stop=(j == 3)),
                    waits=[], inc="f2wo" if j == 3 else None)
            womm[mc] = mmt
            evp[mc] = p.op("vector", lambda E, mc=mc: E.tensor_copy(
                out=projsb[:, 512 * mc:512 * (mc + 1)], in_=ps[mc % 2][:, 0:512]),
                waits=[mmt])
        last_wo_of_half = womm[31]
        for t in range(4):
            i = 4 * half + t
            xw = []
            if i >= 2:
                xw = [lnrel[i - 2], x2st[i - 2]]
            xt = p.dma("sync", xres[i % 2][:, :], x1[128 * i:128 * (i + 1), :],
                       f"xld{i % 2}", waits=xw)
            add = None
            for g2 in range(4):
                w = [evp[31]] if g2 == 0 else []
                if pool["rel"] is not None:
                    w.append(pool["rel"])
                tt = None
                for u in range(8):
                    mcc = 8 * g2 + u
                    tt = p.op("tensor", lambda E, u=u, mcc=mcc, t=t: E.transpose(
                        psb[:, 128 * u:128 * (u + 1)],
                        projsb[:, 512 * mcc + 128 * t:512 * mcc + 128 * (t + 1)],
                        ident[:, :]),
                        waits=w if u == 0 else [], inc="f2tr" if u == 7 else None)
                add = p.op("vector", lambda E, g2=g2, i=i: E.tensor_add(
                    xres[i % 2][:, 1024 * g2:1024 * (g2 + 1)],
                    xres[i % 2][:, 1024 * g2:1024 * (g2 + 1)], psb[:, :]),
                    waits=[tt, xt])
                pool["rel"] = add
            x2st[i] = p.dma("sync", x2_o[128 * i:128 * (i + 1), :], xres[i % 2][:, :],
                            f"x2st{i % 2}", waits=[add])
            rel, ev = lne.chunk(xres[i % 2], add,
                                lambda g, st_=strip_st: (stripap[:, g * 8:(g + 1) * 8, :],
                                                         [st_]))
            lnrel[i] = rel
            strip_st = p.dma("sync", ln_o[i][:, :], strip[:, :], "lnst", waits=[ev])
        tr_after = ("f2tr", p.val("f2tr"))
    p.barrier("sync", waits=[("x2st0", p.val("x2st0")), ("x2st1", p.val("x2st1")),
                             ("lnst", p.val("lnst"))])
    return p.build()


def assemble_lnimg(blocks):
    a = np.asarray(blocks).reshape(8, 128, 32, 128)
    return np.ascontiguousarray(a.transpose(1, 2, 0, 3).reshape(128, 32 * 1024))


def run_f2(a2res, wbf):
    worg_img = pack_lhsT(to_bf(np.asarray(wbf["L0_ffn_W_org"]).T))
    up_img = pack_lhsT(np.ascontiguousarray(
        np.asarray(wbf["L0_ffn_lora_up"]).reshape(512, 4096)))
    dn_img = pack_lhsT(np.ascontiguousarray(
        np.asarray(wbf["L0_ffn_lora_down"]).transpose(1, 0, 2).reshape(4096, 512)))
    wq_img = pack_lhsT(to_bf(np.asarray(wbf["L0_ffn_Wq"]).T))
    keyt = np.ascontiguousarray(np.asarray(wbf["L0_ffn_lora_keys"]).T)
    sel = np.zeros((8, 512), BF)
    for j in range(4):
        for pp in range(128):
            sel[2 * j + pp // 64, 128 * j + pp] = BF(1.0)
    ones1 = np.ones((1, 1), BF)
    ident = np.eye(128, dtype=BF)
    nc = build_f2()
    in_maps = []
    for c in range(NCORES):
        bc0 = 4 * (c // 4)
        parts = np.stack([np.asarray(a2res[cc]["pt_o"]) for cc in
                          range(bc0, bc0 + 4)])
        in_maps.append({"lnin": assemble_lnimg(a2res[c]["ln_o"]),
                        "x1": np.asarray(a2res[c]["x1_o"]),
                        "parts": parts, "worg": worg_img, "upw": up_img,
                        "dnw": dn_img, "wqt": wq_img, "keyt": keyt, "sel": sel,
                        "ones1": ones1, "idin": ident})
    return run_bass_kernel_spmd(nc, in_maps, core_ids=list(range(NCORES))).results



MNEG = -30000.0


def build_m1(nchunks=8, no_gate=False, no_conv=False, no_free=False, lv=0):
    p = Prog("m1")
    nc = p.nc
    TOV = TC + 3
    lnin = nc.dram_tensor("lnin", [128, 32 * TOV], BF16, kind="ExternalInput").ap()
    wip = nc.dram_tensor("wip", [81, 128, 4096], BF16, kind="ExternalInput").ap()
    convw = nc.dram_tensor("convw", [128, 192], F32, kind="ExternalInput").ap()
    convb = nc.dram_tensor("convb", [128, 48], F32, kind="ExternalInput").ap()
    dtb = nc.dram_tensor("dtb", [64, 1], F32, kind="ExternalInput").ap()
    alog = nc.dram_tensor("alog", [64, 1], F32, kind="ExternalInput").ap()
    dd = nc.dram_tensor("dd", [1, 64], F32, kind="ExternalInput").ap()
    trisu = nc.dram_tensor("trisu", [128, 128], BF16, kind="ExternalInput").ap()
    maskt = nc.dram_tensor("maskt", [128, 128], F32, kind="ExternalInput").ap()
    onesc = nc.dram_tensor("onesc", [128, 1], BF16, kind="ExternalInput").ap()
    onesr = nc.dram_tensor("onesr", [1, 128], F32, kind="ExternalInput").ap()
    idbf = nc.dram_tensor("idbf", [128, 128], BF16, kind="ExternalInput").ap()
    idf = nc.dram_tensor("idf", [128, 128], F32, kind="ExternalInput").ap()
    y_o = nc.dram_tensor("y_o", [8, 128, 4096], F32, kind="ExternalOutput").ap()
    c_o = nc.dram_tensor("c_o", [8, 128, 1024], BF16, kind="ExternalOutput").ap()
    el_o = nc.dram_tensor("el_o", [8, 128, 64], F32, kind="ExternalOutput").ap()
    dc_o = nc.dram_tensor("dc_o", [8, 1, 64], F32, kind="ExternalOutput").ap()
    gt_o = nc.dram_tensor("gt_o", [8, 128, 4096], BF16, kind="ExternalOutput").ap()
    u_o = nc.dram_tensor("u_o", [128, 4096], F32, kind="ExternalOutput").ap()
    ds_o = nc.dram_tensor("ds_o", [1, 64], F32, kind="ExternalOutput").ap()

    cum_d = nc.dram_tensor("cum_d", [64, 128], F32).ap()

    xbc = nc.alloc_sbuf_tensor("xbc", [128, 48 * 1024], BF16)
    dt_all = nc.alloc_sbuf_tensor("dt_all", [64, 1024], BF16)
    convw_sb = nc.alloc_sbuf_tensor("convw_sb", [128, 192], F32)
    convb_sb = nc.alloc_sbuf_tensor("convb_sb", [128, 48], F32)
    dtb_sb = nc.alloc_sbuf_tensor("dtb_sb", [64, 1], F32)
    acol = nc.alloc_sbuf_tensor("acol", [64, 1], F32)
    dd_sb = nc.alloc_sbuf_tensor("dd_sb", [1, 64], F32)
    ddcols = nc.alloc_sbuf_tensor("ddcols", [128, 64], F32)
    trisu_sb = nc.alloc_sbuf_tensor("trisu_sb", [128, 128], BF16)
    maskt_sb = nc.alloc_sbuf_tensor("maskt_sb", [128, 128], F32)
    onesc_sb = nc.alloc_sbuf_tensor("onesc_sb", [128, 1], BF16)
    onesr_sb = nc.alloc_sbuf_tensor("onesr_sb", [1, 128], F32)
    idbf_sb = nc.alloc_sbuf_tensor("idbf_sb", [128, 128], BF16)
    idf_sb = nc.alloc_sbuf_tensor("idf_sb", [128, 128], F32)
    ones64 = nc.alloc_sbuf_tensor("ones64", [64, 128], F32)

    psA = nc.alloc_psum_tensor("m1psA", [128, 1024], F32)
    psB = nc.alloc_psum_tensor("m1psB", [128, 1024], F32)
    psC = nc.alloc_psum_tensor("m1psC", [128, 512], F32)
    psD = nc.alloc_psum_tensor("m1psD", [128, 512], F32)
    psE = nc.alloc_psum_tensor("m1psE", [128, 1024], BF16)
    psF = nc.alloc_psum_tensor("m1psF", [128, 512], F32)

    p.dma("sync", convw_sb[:, :], convw[:, :], "cld")
    p.dma("sync", convb_sb[:, :], convb[:, :], "cld")
    p.dma("sync", dtb_sb[:, :], dtb[:, :], "cld")
    p.dma("sync", acol[:, :], alog[:, :], "cld")
    p.dma("sync", dd_sb[:, :], dd[:, :], "cld")
    p.dma("sync", trisu_sb[:, :], trisu[:, :], "cld")
    p.dma("sync", maskt_sb[:, :], maskt[:, :], "cld")
    p.dma("sync", onesc_sb[:, :], onesc[:, :], "cld")
    p.dma("sync", onesr_sb[:, :], onesr[:, :], "cld")
    p.dma("sync", idbf_sb[:, :], idbf[:, :], "cld")
    cld = p.dma("sync", idf_sb[:, :], idf[:, :], "cld")

    if lv >= 2:
        pass
    at = None if lv >= 2 else p.op("scalar", lambda E: E.activation(
        out=acol[:, :], in_=acol[:, :], func=AF.Exp),
        waits=[("cld", p.val("cld"))])
    if lv < 2:
        p.op("vector", lambda E: E.tensor_scalar_mul(acol[:, :], acol[:, :], -1.0),
             waits=[at, ("cld", p.val("cld"))])
        p.op("vector", lambda E: E.memset(ones64[:, :], 1.0))
        ddm = p.op("tensor", lambda E: E.matmul(psD[:, 0:64], onesr_sb[:, :],
                                                dd_sb[:, :], start=True, stop=True),
                   waits=[("cld", p.val("cld"))], inc="m1dd")
        dde = p.op("vector", lambda E: E.tensor_copy(out=ddcols[:, :],
                                                     in_=psD[:, 0:64]),
                   waits=[ddm])
    else:
        dde = None

    import contextlib
    es = contextlib.ExitStack()
    if no_free:
        class _N:
            def enter_context(self, x):
                return x.__enter__()
            def close(self):
                pass
        es = _N()
    img = es.enter_context(nc.sbuf_tensor("img", [128, 32 * TOV], BF16))
    wb = [es.enter_context(nc.sbuf_tensor(f"wb{i}", [128, 4096], BF16))
          for i in range(2)]
    stg = [es.enter_context(nc.sbuf_tensor(f"stg{i}", [128, TOV], BF16))
           for i in range(2)]
    gbuf = es.enter_context(nc.sbuf_tensor("gbuf", [128, 1024], BF16))
    gtb = es.enter_context(nc.sbuf_tensor("gtb", [128, 1024], BF16))
    imgap = img[:, :].rearrange("q (kc tt) -> q kc tt", tt=TOV)
    ilt = p.dma("sync", img[:, :], lnin[:, :], "ild")
    p.barrier("tensor", waits=[("cld", p.val("cld")), ilt])

    stg_rel = {}
    evt_toks = {}
    gt_rel = None
    pe_rel = None
    for mc in range(81):
        pp = mc % 2
        big = psA if pp == 0 else psB
        tiny = psF[:, 0:3] if pp == 0 else psC[:, 0:3]
        wt = p.dma("sync", wb[mc % 2][:, :], wip[mc][:, :], f"wld{mc % 2}",
                   waits=[("m1ip", mc - 1)] if mc >= 2 else [])
        mmt = None
        for n in range(3):
            cl = [slice(0, 512), slice(512, 1024), slice(1024, 1027)][n]
            dsts = [big[:, 0:512], big[:, 512:1024], tiny]
            w0 = [wt]
            if mc >= 2:
                w0.append(evt_toks[mc - 2])
            for kc in range(32):
                mmt = p.op("tensor", lambda E, kc=kc, mc=mc, cl=cl, n=n,
                           dsts=dsts: E.matmul(
                    dsts[n], wb[mc % 2][:, 128 * kc:128 * (kc + 1)],
                    imgap[:, kc, cl], start=(kc == 0), stop=(kc == 31)),
                    waits=w0 if (kc == 0 and n == 0) else [],
                    inc="m1ip" if (kc == 31 and n == 2) else None)
        sb = stg[mc % 2]
        ew = [mmt]
        if mc >= 2:
            ew.append(stg_rel[mc - 2])
        p.op("vector", lambda E, sb=sb, big=big: E.tensor_copy(
            out=sb[:, 0:512], in_=big[:, 0:512]), waits=ew)
        p.op("vector", lambda E, sb=sb, big=big: E.tensor_copy(
            out=sb[:, 512:1024], in_=big[:, 512:1024]))
        evt = p.op("vector", lambda E, sb=sb, tiny=tiny: E.tensor_copy(
            out=sb[:, 1024:1027], in_=tiny))
        evt_toks[mc] = evt
        if mc < 32 and no_gate:
            stg_rel[mc] = evt
        elif mc < 32:
            sil = p.op("scalar", lambda E, sb=sb: E.activation(
                out=gbuf[:, :], in_=sb[:, 3:1027], func=AF.Silu),
                waits=[evt] + ([gt_rel] if gt_rel else []))
            tt = None
            for u in range(8):
                tt = p.op("tensor", lambda E, u=u: E.transpose(
                    psE[:, 128 * u:128 * (u + 1)], gbuf[:, 128 * u:128 * (u + 1)],
                    idbf_sb[:, :]),
                    waits=([sil] + ([pe_rel] if pe_rel else [])) if u == 0 else [],
                    inc="m1gt" if u == 7 else None)
            pe_rel = p.op("vector", lambda E: E.tensor_copy(out=gtb[:, :],
                                                            in_=psE[:, :]),
                          waits=[tt])
            gt_rel = p.dma("sync", gt_o[0:8, :, 128 * mc:128 * (mc + 1)]
                           .rearrange("cc q t -> q cc t"),
                           gtb[:, :].rearrange("q (cc t) -> q cc t", t=128),
                           f"gst{mc % 2}", waits=[pe_rel])
            stg_rel[mc] = sil
        elif mc < 80 and no_conv:
            stg_rel[mc] = evt
        elif mc < 80:
            m = mc - 32
            p.op("vector", lambda E, sb=sb, m=m: E.tensor_scalar(
                out=psA[:, 0:0] if False else xbc[0:0, 0:0], in0=sb[0:0, 0:0],
                scalar1=0.0, scalar2=None, op0=ALU.mult)) if False else None
            cvt = p.op("vector", lambda E, sb=sb, m=m: E.tensor_scalar(
                out=xbc[0:128, 1024 * m:1024 * (m + 1)] if False else
                xbc[:, 1024 * m:1024 * (m + 1)],
                in0=sb[:, 0:1024], scalar1=convw_sb[:, 4 * m:4 * m + 1],
                scalar2=convb_sb[:, m:m + 1], op0=ALU.mult, op1=ALU.add),
                waits=[evt])
            for k in range(1, 4):
                cvt = p.op("vector", lambda E, sb=sb, m=m, k=k:
                           E.scalar_tensor_tensor(
                    out=xbc[:, 1024 * m:1024 * (m + 1)], in0=sb[:, k:k + 1024],
                    scalar=convw_sb[:, 4 * m + k:4 * m + k + 1],
                    in1=xbc[:, 1024 * m:1024 * (m + 1)],
                    op0=ALU.mult, op1=ALU.add))
            sil = p.op("scalar", lambda E, m=m: E.activation(
                out=xbc[:, 1024 * m:1024 * (m + 1)],
                in_=xbc[:, 1024 * m:1024 * (m + 1)], func=AF.Silu), waits=[cvt])
            stg_rel[mc] = cvt
        elif lv >= 1:
            stg_rel[mc] = evt
        else:
            stg_rel[mc] = p.op("vector", lambda E, sb=sb: E.tensor_copy(
                out=dt_all[:, :], in_=sb[0:64, 3:1027]))
    silu_done = p.engine_done_tok("scalar")
    ipdone = ("m1ip", 81)
    if nchunks < 0:
        p.barrier("sync", waits=[("gst0", p.val("gst0")), ("gst1", p.val("gst1")),
                                 ipdone])
        p.barrier("tensor", waits=[ipdone])
        return p.build()
    es.close()

    x_tm = nc.alloc_sbuf_tensor("x_tm", [128, 4096], BF16)
    dtx_tm = nc.alloc_sbuf_tensor("dtx_tm", [128, 4096], BF16)
    b_tm = nc.alloc_sbuf_tensor("b_tm", [128, 8 * 128], BF16)
    bw_sb = [nc.alloc_sbuf_tensor(f"bw_sb{i}", [128, 128], BF16) for i in range(2)]
    dpre = [nc.alloc_sbuf_tensor(f"dpre{i}", [128, 128], F32) for i in range(2)]
    dexp = [nc.alloc_sbuf_tensor(f"dexp{i}", [128, 128], F32) for i in range(2)]
    ptil = [nc.alloc_sbuf_tensor(f"ptil{i}", [128, 128], BF16) for i in range(2)]
    state = nc.alloc_sbuf_tensor("state", [128, 4096], F32)
    state_bf = nc.alloc_sbuf_tensor("state_bf", [128, 4096], BF16)
    y_tm = nc.alloc_sbuf_tensor("y_tm", [128, 4096], F32)
    y1sb = [nc.alloc_sbuf_tensor(f"y1sb{i}", [128, 64], F32) for i in range(2)]
    dtp = nc.alloc_sbuf_tensor("dtp", [64, 128], F32)
    t64a = nc.alloc_sbuf_tensor("t64a", [64, 128], F32)
    t64b = nc.alloc_sbuf_tensor("t64b", [64, 128], F32)
    cum_hm = nc.alloc_sbuf_tensor("cum_hm", [64, 128], F32)
    cum_tm = nc.alloc_sbuf_tensor("cum_tm", [128, 64], F32)
    ncum_tm = nc.alloc_sbuf_tensor("ncum_tm", [128, 64], F32)
    expl = nc.alloc_sbuf_tensor("expl", [128, 64], F32)
    dta_bf = nc.alloc_sbuf_tensor("dta_bf", [128, 64], BF16)
    dtp_tm = nc.alloc_sbuf_tensor("dtp_tm", [128, 64], F32)
    wdec = nc.alloc_sbuf_tensor("wdec", [128, 64], F32)
    drow = nc.alloc_sbuf_tensor("drow", [1, 64], F32)
    dcols = nc.alloc_sbuf_tensor("dcols", [128, 64], F32)
    dsl = nc.alloc_sbuf_tensor("dsl", [1, 64], F32)
    cumrow = nc.alloc_sbuf_tensor("cumrow", [1, 1024], F32)

    msa = p.op("vector", lambda E: E.memset(state[:, :], 0.0), waits=[ipdone])
    p.op("vector", lambda E: E.memset(dsl[:, :], 1.0))
    p.barrier("vector", waits=[ipdone])
    p.barrier("scalar", waits=[ipdone])
    p.barrier("tensor", waits=[ipdone])

    def xv(m, c):
        return xbc[:, 1024 * m + 128 * c:1024 * m + 128 * (c + 1)]

    prev = {"psE_rel": pe_rel, "psB_rel": stg_rel[80]}
    for c in range(nchunks):
        p.op("scalar", lambda E, c=c: E.activation(out=t64a[:, :],
                                                   in_=dt_all[:, 128 * c:128 * (c + 1)],
                                                   func=AF.Abs, bias=dtb_sb[:, :]),
             waits=[prev["dpt"]] if c > 0 else [silu_done, stg_rel[80]])
        p.op("scalar", lambda E: E.activation(out=t64a[:, :], in_=t64a[:, :],
                                              func=AF.Exp, scale=-1.0))
        p.op("scalar", lambda E: E.activation(out=t64a[:, :], in_=t64a[:, :],
                                              func=AF.Ln, bias=ones64[:, 0:1]))
        rlu = p.op("scalar", lambda E, c=c: E.activation(
            out=t64b[:, :], in_=dt_all[:, 128 * c:128 * (c + 1)],
            func=AF.Relu, bias=dtb_sb[:, :]))
        dpt = p.op("vector", lambda E: E.tensor_add(dtp[:, :], t64a[:, :],
                                                    t64b[:, :]), waits=[rlu])
        prev["dpt"] = dpt
        p.op("vector", lambda E: E.tensor_scalar_mul(t64a[:, :], dtp[:, :],
                                                     acol[:, :]))
        cst = p.op("vector", lambda E: E.tensor_tensor_scan(
            out=cum_hm[:, :], data0=ones64[:, :], data1=t64a[:, :], initial=0.0,
            op0=ALU.mult, op1=ALU.add))
        cdst = p.dma("sync", cum_d[:, :], cum_hm[:, :], "cdst",
                     waits=[cst, ("crl", p.val("crl"))])
        w0 = [cst, ("m1dd", 1), dde]
        if c > 0:
            w0.append(prev["psD_rel"])
        p.op("tensor", lambda E: E.transpose(psD[:, 64:128], cum_hm[:, :],
                                             idf_sb[0:64, 0:64]), waits=w0)
        p.op("tensor", lambda E: E.transpose(psD[:, 128:192], t64a[:, :],
                                             idf_sb[0:64, 0:64]))
        tdp = p.op("tensor", lambda E: E.transpose(psD[:, 192:256], dtp[:, :],
                                                   idf_sb[0:64, 0:64]), waits=[dpt],
                   inc="m1tr")
        e1 = p.op("vector", lambda E: E.tensor_copy(out=cum_tm[:, :],
                                                    in_=psD[:, 64:128]), waits=[tdp])
        p.op("vector", lambda E: E.tensor_scalar_mul(ncum_tm[:, :], cum_tm[:, :],
                                                     -1.0))
        p.op("vector", lambda E: E.tensor_copy(out=dta_bf[:, :], in_=psD[:, 128:192]))
        e3 = p.op("vector", lambda E: E.tensor_copy(out=dtp_tm[:, :],
                                                    in_=psD[:, 192:256]))
        elt = p.op("scalar", lambda E: E.activation(out=expl[:, :], in_=cum_tm[:, :],
                                                    func=AF.Exp), waits=[e1])
        p.dma("sync", el_o[c][:, :], expl[:, :], "elst", waits=[elt])
        wmw = [e3]
        if c > 0:
            wmw.append(prev["psC_rel"])
        wmt = p.op("tensor", lambda E: E.matmul(psC[:, 0:64], trisu_sb[:, :],
                                                dta_bf[:, :], start=True, stop=True),
                   waits=wmw)
        dmt = p.op("tensor", lambda E: E.matmul(psC[0:1, 64:128], onesc_sb[:, :],
                                                dta_bf[:, :], start=True, stop=True),
                   inc="m1w")
        we = p.op("vector", lambda E: E.tensor_copy(out=wdec[:, :], in_=psC[:, 0:64]),
                  waits=[dmt])
        wex = p.op("scalar", lambda E: E.activation(out=wdec[:, :], in_=wdec[:, :],
                                                    func=AF.Exp), waits=[we])
        de = p.op("vector", lambda E: E.tensor_copy(out=drow[:, :],
                                                    in_=psC[0:1, 64:128]))
        dex = p.op("scalar", lambda E: E.activation(out=drow[:, :], in_=drow[:, :],
                                                    func=AF.Exp), waits=[de])
        p.dma("sync", dc_o[c][:, :], drow[:, :], "dcst", waits=[dex])
        dbm = p.op("tensor", lambda E: E.matmul(psC[:, 128:192], onesr_sb[:, :],
                                                drow[:, :], start=True, stop=True),
                   waits=[dex], inc="m1w")
        dce = p.op("vector", lambda E: E.tensor_copy(out=dcols[:, :],
                                                     in_=psC[:, 128:192]),
                   waits=[dbm])
        p.op("vector", lambda E: E.tensor_mul(dsl[:, :], dsl[:, :], drow[:, :]),
             waits=[dex])
        prev["psC_rel"] = dce
        pe_rel2 = prev.get("psE_rel")
        for g4 in range(4):
            w = ([prev["dtx_rel"]] if (c > 0 and g4 == 0) else [])
            if pe_rel2 is not None:
                w.append(pe_rel2)
            tt = None
            for u in range(8):
                m = 8 * g4 + u
                tt = p.op("tensor", lambda E, u=u, m=m, c=c: E.transpose(
                    psE[:, 128 * u:128 * (u + 1)], xv(m, c), idbf_sb[:, :]),
                    waits=w if u == 0 else [],
                    inc="m1xt" if u == 7 else None)
            pe_rel2 = p.op("vector", lambda E, g4=g4: E.tensor_copy(
                out=x_tm[:, 1024 * g4:1024 * (g4 + 1)], in_=psE[:, :]), waits=[tt])
        tt = None
        for u in range(8):
            tt = p.op("tensor", lambda E, u=u, c=c: E.transpose(
                psE[:, 128 * u:128 * (u + 1)], xv(32 + u, c), idbf_sb[:, :]),
                waits=[pe_rel2] if u == 0 else [], inc="m1xt" if u == 7 else None)
        bte = p.op("vector", lambda E: E.tensor_copy(out=b_tm[:, :], in_=psE[:, :]),
                   waits=[tt])
        prev["psE_rel"] = bte
        dxt = None
        for h in range(64):
            dxt = p.op("gpsimd", lambda E, h=h: E.tensor_scalar_mul(
                dtx_tm[:, 64 * h:64 * (h + 1)], x_tm[:, 64 * h:64 * (h + 1)],
                dtp_tm[:, h:h + 1]),
                waits=([e3, pe_rel2, bte] + ([prev["dtx_rel"]] if c > 0 else []))
                if h == 0 else [])
        snap = p.op("vector", lambda E: E.tensor_copy(out=state_bf[:, :],
                                                      in_=state[:, :]),
                    waits=[msa] if c == 0 else [])
        stw = [prev["psB_rel"]]
        st_last = None
        for g in range(8):
            st_last = p.op("tensor", lambda E, g=g, c=c: E.matmul(
                psB[:, 128 * g:128 * (g + 1)], xv(32 + g, c), xv(40 + g, c),
                start=True, stop=True),
                waits=stw if g == 0 else [], inc="m1st" if g == 7 else None)
        yev = {}
        dpt2 = {}
        det_t = {}
        ptt = None
        crl = None
        for h in range(64):
            g = h // 8
            hp = h % 2
            if h % 8 == 0:
                crw = [cdst]
                if ptt is not None:
                    crw.append(ptt)
                crl = p.dma("sync", cumrow[0:1, :],
                            cum_d[g * 8:g * 8 + 8, :].rearrange("h s -> (h s)"),
                            "crl", waits=crw)
            bcw = [crl]
            if h >= 2:
                bcw.append(dpt2[h - 2])
            elif c > 0:
                bcw.append(prev["psD_rel"])
            bct = p.op("tensor", lambda E, h=h, hp=hp: E.matmul(
                psD[:, 256 + 128 * hp:256 + 128 * (hp + 1)], onesr_sb[:, :],
                cumrow[0:1, 128 * (h % 8):128 * (h % 8 + 1)],
                start=True, stop=True), waits=bcw, inc="m1bc")
            dpt2[h] = p.op("vector", lambda E, hp=hp, h=h: E.scalar_tensor_tensor(
                out=dpre[hp][:, :], in0=psD[:, 256 + 128 * hp:256 + 128 * (hp + 1)],
                scalar=ncum_tm[:, h:h + 1], in1=maskt_sb[:, :],
                op0=ALU.add, op1=ALU.add),
                waits=[bct] + ([det_t[h - 2]] if h >= 2 else []))
            det = p.op("scalar", lambda E, hp=hp: E.activation(
                out=dexp[hp][:, :], in_=dpre[hp][:, :], func=AF.Exp),
                waits=[dpt2[h]])
            det_t[h] = det
            ptw = [det, st_last]
            if h >= 2:
                ptw.append(("m1y", p.val("m1y") - 1))
            ptt = p.op("vector", lambda E, g=g, hp=hp: E.tensor_mul(
                ptil[hp][:, :], dexp[hp][:, :], psB[:, 128 * g:128 * (g + 1)]),
                waits=ptw)
            y2w = [snap]
            if h >= 2:
                y2w.append(yev[h - 2])
            y2t = p.op("tensor", lambda E, g=g, hp=hp, h=h, c=c: E.matmul(
                psA[:, 512 * hp + 0:512 * hp + 64], xv(40 + g, c),
                state_bf[:, 64 * h:64 * (h + 1)],
                start=True, stop=True), waits=y2w, inc="m1y2")
            y1t = p.op("tensor", lambda E, hp=hp, h=h: E.matmul(
                psA[:, 512 * hp + 64:512 * hp + 128], ptil[hp][:, :],
                dtx_tm[:, 64 * h:64 * (h + 1)], start=True, stop=True),
                waits=[ptt, dxt])
            bwt = p.op("gpsimd", lambda E, g=g, h=h, hp=hp: E.tensor_scalar_mul(
                bw_sb[hp][:, :], b_tm[:, 128 * g:128 * (g + 1)], wdec[:, h:h + 1]),
                waits=([wex] if h == 0 else []) +
                ([("m1y", p.val("m1y") - 1)] if h >= 2 else []))
            sut = p.op("tensor", lambda E, hp=hp, h=h: E.matmul(
                psA[:, 512 * hp + 128:512 * hp + 192], bw_sb[hp][:, :],
                dtx_tm[:, 64 * h:64 * (h + 1)], start=True, stop=True),
                waits=[bwt], inc="m1y")
            p.op("vector", lambda E, hp=hp, h=h: E.scalar_tensor_tensor(
                out=y1sb[hp][:, :], in0=x_tm[:, 64 * h:64 * (h + 1)],
                scalar=ddcols[:, h:h + 1],
                in1=psA[:, 512 * hp + 64:512 * hp + 128],
                op0=ALU.mult, op1=ALU.add),
                waits=[sut] + ([dde, prev.get("yst"), elt] if h == 0 else []))
            p.op("vector", lambda E, hp=hp, h=h: E.scalar_tensor_tensor(
                out=y_tm[:, 64 * h:64 * (h + 1)],
                in0=psA[:, 512 * hp + 0:512 * hp + 64],
                scalar=expl[:, h:h + 1], in1=y1sb[hp][:, :],
                op0=ALU.mult, op1=ALU.add))
            yev[h] = p.op("vector", lambda E, hp=hp, h=h: E.scalar_tensor_tensor(
                out=state[:, 64 * h:64 * (h + 1)], in0=state[:, 64 * h:64 * (h + 1)],
                scalar=dcols[:, h:h + 1],
                in1=psA[:, 512 * hp + 128:512 * hp + 192],
                op0=ALU.mult, op1=ALU.add), waits=[dce] if h == 0 else [])
        prev["psD_rel"] = dpt2[63]
        prev["psB_rel"] = ptt
        prev["dtx_rel"] = ("m1y", p.val("m1y"))
        prev["yst"] = p.dma("sync", y_o[c][:, :], y_tm[:, :], f"yst{c % 2}",
                            waits=[yev[63]])
        p.dma("sync", c_o[c][:, :],
              xbc[:, :].rearrange("q (m t) -> q m t", t=1024)[:, 40:48,
                                                             128 * c:128 * (c + 1)],
              f"cst{c % 2}", waits=[("m1st", p.val("m1st"))])
    p.dma("sync", u_o[:, :], state[:, :], "ust",
          waits=[("@vector", p.selfc["vector"])])
    p.dma("sync", ds_o[:, :], dsl[:, :], "dsst",
          waits=[("@vector", p.selfc["vector"])])
    p.barrier("sync", waits=[("yst0", p.val("yst0")), ("yst1", p.val("yst1")),
                             ("cst0", p.val("cst0")), ("cst1", p.val("cst1")),
                             ("elst", p.val("elst")), ("dcst", p.val("dcst")),
                             ("ust", p.val("ust")), ("dsst", p.val("dsst")),
                             ("gst0", p.val("gst0")), ("gst1", p.val("gst1"))])
    return p.build()


def build_m2():
    p = Prog("m2")
    nc = p.nc
    y_i = nc.dram_tensor("y_i", [8, 128, 4096], F32, kind="ExternalInput").ap()
    c_i = nc.dram_tensor("c_i", [8, 128, 1024], BF16, kind="ExternalInput").ap()
    el = nc.dram_tensor("el", [8, 128, 64], F32, kind="ExternalInput").ap()
    dch = nc.dram_tensor("dch", [8, 1, 64], F32, kind="ExternalInput").ap()
    gate = nc.dram_tensor("gate", [8, 128, 4096], BF16, kind="ExternalInput").ap()
    x2 = nc.dram_tensor("x2", [TC, D], F32, kind="ExternalInput").ap()
    uprev = nc.dram_tensor("uprev", [3, 128, 4096], F32, kind="ExternalInput").ap()
    dprev = nc.dram_tensor("dprev", [1, 128], F32, kind="ExternalInput").ap()
    wout = nc.dram_tensor("wout", [32, 128, 4096], BF16, kind="ExternalInput").ap()
    nwr = nc.dram_tensor("nwr", [1, 4096], F32, kind="ExternalInput").ap()
    onesr = nc.dram_tensor("onesr", [1, 128], F32, kind="ExternalInput").ap()
    idbf = nc.dram_tensor("idbf", [128, 128], BF16, kind="ExternalInput").ap()
    x3_o = nc.dram_tensor("x3_o", [TC, D], F32, kind="ExternalOutput").ap()
    ln_o = nc.dram_tensor("ln_o", [8, 128, 4096], BF16, kind="ExternalOutput").ap()

    ybuf = nc.alloc_sbuf_tensor("ybuf", [128, 4096], F32)
    srun = nc.alloc_sbuf_tensor("srun", [128, 4096], F32)
    csb = nc.alloc_sbuf_tensor("csb", [128, 1024], BF16)
    elsb = nc.alloc_sbuf_tensor("elsb", [128, 64], F32)
    dchr = nc.alloc_sbuf_tensor("dchr", [1, 64], F32)
    dcols = nc.alloc_sbuf_tensor("dcols", [128, 64], F32)
    dprow = nc.alloc_sbuf_tensor("dprow", [1, 128], F32)
    frow = nc.alloc_sbuf_tensor("frow", [1, 64], F32)
    fcols = nc.alloc_sbuf_tensor("fcols", [128, 64], F32)
    st_bf = nc.alloc_sbuf_tensor("st_bf", [128, 4096], BF16)
    sc512 = nc.alloc_sbuf_tensor("sc512", [128, 512], F32)
    grp = nc.alloc_sbuf_tensor("grp", [128, 8], F32)
    nwb = nc.alloc_sbuf_tensor("nwb", [128, 4096], BF16)
    gtm = nc.alloc_sbuf_tensor("gtm", [128, 4096], BF16)
    yntm = nc.alloc_sbuf_tensor("yntm", [128, 4096], BF16)
    ynimg = nc.alloc_sbuf_tensor("ynimg", [128, 32 * 512], BF16)
    wob = [nc.alloc_sbuf_tensor(f"wob{i}", [128, 4096], BF16) for i in range(2)]
    projsb = nc.alloc_sbuf_tensor("projsb", [128, 32 * 512], BF16)
    xres = [nc.alloc_sbuf_tensor(f"xres{i}", [128, D], F32) for i in range(2)]
    ident = nc.alloc_sbuf_tensor("ident", [128, 128], BF16)
    strip = nc.alloc_sbuf_tensor("strip", [128, 4096], BF16)
    onesr_sb = nc.alloc_sbuf_tensor("onesr_sb", [1, 128], F32)

    psY = [nc.alloc_psum_tensor(f"m2psY{i}", [128, 512], F32) for i in range(2)]
    psW = [nc.alloc_psum_tensor(f"m2psW{i}", [128, 512], F32) for i in range(2)]
    psM = nc.alloc_psum_tensor("m2psM", [128, 512], F32)
    psb = nc.alloc_psum_tensor("m2psb", [128, 1024], BF16)

    ynap = ynimg[:, :].rearrange("q (kc tt) -> q kc tt", tt=512)
    stripap = strip[:, :].rearrange("q (kc t) -> q kc t", t=128)

    p.dma("sync", onesr_sb[:, :], onesr[:, :], "cld")
    p.dma("sync", dprow[:, :], dprev[:, :], "cld")
    cld = p.dma("sync", ident[:, :], idbf[:, :], "cld")
    p.barrier("tensor", waits=[cld])
    p.barrier("vector", waits=[cld])
    p.barrier("scalar", waits=[cld])

    nwl = p.dma("sync", sc512[0:1, :], nwr[0:1, 0:512], "nwl")
    nwev = None
    for j in range(8):
        nwl2 = p.dma("sync", sc512[0:1, :], nwr[0:1, 512 * j:512 * (j + 1)], "nwl",
                     waits=[nwev]) if j > 0 else nwl
        nwm = p.op("tensor", lambda E: E.matmul(psM[:, :], onesr_sb[:, :],
                                                sc512[0:1, :], start=True,
                                                stop=True),
                   waits=[nwl2] + ([nwev] if nwev else []), inc="m2nw")
        nwev = p.op("vector", lambda E, j=j: E.tensor_copy(
            out=nwb[:, 512 * j:512 * (j + 1)], in_=psM[:, :]), waits=[nwm])
    nwdone = p.engine_done_tok("vector")

    sl0 = p.dma("sync", srun[:, :], uprev[0][:, :], "ul0")
    p.op("vector", lambda E: E.tensor_copy(out=frow[:, :], in_=dprow[0:1, 0:64]),
         waits=[("cld", p.val("cld"))])
    for j in (1, 2):
        ul = p.dma("sync", ybuf[:, :], uprev[j][:, :], f"ul{j}",
                   waits=[("@vector", p.selfc["vector"])])
        fbm = p.op("tensor", lambda E: E.matmul(psM[:, 0:64], onesr_sb[:, :],
                                                frow[:, :], start=True, stop=True),
                   waits=[p.engine_done_tok("vector"), ("m2nw", 8)], inc="m2f")
        fce = p.op("vector", lambda E: E.tensor_copy(out=fcols[:, :],
                                                     in_=psM[:, 0:64]), waits=[fbm])
        la = None
        for h in range(64):
            la = p.op("vector", lambda E, h=h: E.scalar_tensor_tensor(
                out=srun[:, 64 * h:64 * (h + 1)], in0=ybuf[:, 64 * h:64 * (h + 1)],
                scalar=fcols[:, h:h + 1], in1=srun[:, 64 * h:64 * (h + 1)],
                op0=ALU.mult, op1=ALU.add),
                waits=[ul, sl0, fce] if h == 0 else [])
        if j == 1:
            p.op("vector", lambda E: E.tensor_mul(frow[:, :], frow[:, :],
                                                  dprow[0:1, 64:128]))
    pool = {"ap": psb, "rel": None}
    lne = LNEmitter(p, nc, ident, pool, "m2ln")
    lnrel, x3st = {}, {}
    strip_st = None
    tr_after = None
    last_wo = None
    prev_sdec = None
    prev_yup = None
    prev_ygt = None
    for half in range(2):
        for t in range(4):
            c = 4 * half + t
            yw = [("@vector", p.selfc["vector"]),
                  ("@scalar", p.selfc["scalar"])]
            yl = p.dma("sync", ybuf[:, :], y_i[c][:, :], f"yld{c % 2}", waits=yw)
            cl_ = p.dma("sync", csb[:, :], c_i[c][:, :], f"cld{c % 2}",
                        waits=[("m2y", p.val("m2y"))])
            ell = p.dma("sync", elsb[:, :], el[c][:, :], f"eld{c % 2}",
                        waits=[("@vector", p.selfc["vector"])])
            dl = p.dma("sync", dchr[:, :], dch[c][:, :], f"dld{c % 2}",
                       waits=[("m2dc", p.val("m2dc"))])
            dbm = p.op("tensor", lambda E: E.matmul(psM[:, 64:128], onesr_sb[:, :],
                                                    dchr[:, :], start=True,
                                                    stop=True),
                       waits=[dl, ("@vector", p.selfc["vector"])], inc="m2dc")
            dce = p.op("vector", lambda E: E.tensor_copy(out=dcols[:, :],
                                                         in_=psM[:, 64:128]),
                       waits=[dbm])
            gl = p.dma("sync", gtm[:, :], gate[c][:, :], f"gld{c % 2}",
                       waits=[prev_ygt] if prev_ygt else [])
            snap = p.op("vector", lambda E: E.tensor_copy(out=st_bf[:, :],
                                                          in_=srun[:, :]),
                        waits=[("m2y", p.val("m2y"))] +
                        ([prev_sdec] if prev_sdec else []) +
                        ([prev_yup] if prev_yup else []))
            yup = {}
            for h in range(64):
                g = h // 8
                y2w = [snap, cl_]
                if h >= 2:
                    y2w.append(yup[h - 2])
                y2t = p.op("tensor", lambda E, g=g, h=h: E.matmul(
                    psY[h % 2][:, 0:64],
                    csb[:, 128 * g:128 * (g + 1)],
                    st_bf[:, 64 * h:64 * (h + 1)],
                    start=True, stop=True), waits=y2w, inc="m2y")
                yup[h] = p.op("vector", lambda E, h=h: E.scalar_tensor_tensor(
                    out=ybuf[:, 64 * h:64 * (h + 1)],
                    in0=psY[h % 2][:, 0:64],
                    scalar=elsb[:, h:h + 1], in1=ybuf[:, 64 * h:64 * (h + 1)],
                    op0=ALU.mult, op1=ALU.add),
                    waits=[("m2y", p.val("m2y")), yl, ell] if h == 0
                    else [("m2y", p.val("m2y"))])
            sdec = None
            for h in range(64):
                sdec = p.op("gpsimd", lambda E, h=h: E.tensor_scalar_mul(
                    srun[:, 64 * h:64 * (h + 1)], srun[:, 64 * h:64 * (h + 1)],
                    dcols[:, h:h + 1]),
                    waits=[dce, snap] if h == 0 else [])
            prev_sdec = sdec
            prev_yup = yup[63]
            rst = None
            for g in range(8):
                sq = p.op("scalar", lambda E, g=g: E.activation(
                    out=sc512[:, :], in_=ybuf[:, 512 * g:512 * (g + 1)],
                    func=AF.Square),
                    waits=[yup[63]] if g == 0 else [rst])
                rst = p.op("vector", lambda E, g=g: E.reduce_sum(
                    grp[:, g:g + 1], sc512[:, :], axis=AX.X), waits=[sq])
            rst = p.op("vector", lambda E: E.tensor_scalar(
                out=grp[:, :], in0=grp[:, :], scalar1=1.0 / 512.0, scalar2=EPS,
                op0=ALU.mult, op1=ALU.add))
            p.op("scalar", lambda E: E.activation(out=grp[:, :], in_=grp[:, :],
                                                  func=AF.Ln), waits=[rst])
            rex = p.op("scalar", lambda E: E.activation(out=grp[:, :], in_=grp[:, :],
                                                        func=AF.Exp, scale=-0.5))
            ynt = None
            for g in range(8):
                ynt = p.op("vector", lambda E, g=g: E.tensor_scalar_mul(
                    yntm[:, 512 * g:512 * (g + 1)], ybuf[:, 512 * g:512 * (g + 1)],
                    grp[:, g:g + 1]),
                    waits=[rex, ("m2yt", p.val("m2yt"))] if g == 0 else [])
            p.op("vector", lambda E: E.tensor_mul(yntm[:, :], yntm[:, :], nwb[:, :]),
                 waits=[nwdone] if c == 0 else [])
            ygt = p.op("vector", lambda E: E.tensor_mul(yntm[:, :], yntm[:, :],
                                                        gtm[:, :]), waits=[gl])
            prev_ygt = ygt
            for g4 in range(4):
                tt = None
                for u in range(8):
                    m = 8 * g4 + u
                    tt = p.op("tensor", lambda E, u=u, m=m: E.transpose(
                        psb[:, 128 * u:128 * (u + 1)],
                        yntm[:, 128 * m:128 * (m + 1)], ident[:, :]),
                        waits=[ygt, pool["rel"]] if u == 0 else [],
                        inc="m2yt" if u == 7 else None)
                ev = p.op("vector", lambda E, g4=g4, t=t: E.tensor_copy(
                    out=ynap[:, g4 * 8:(g4 + 1) * 8, 128 * t:128 * (t + 1)],
                    in_=psb[:, :].rearrange("q (u tt) -> q u tt", u=8)),
                    waits=[tt] + ([("m2wo", p.val("m2wo"))]
                                  if (t == 0 and g4 == 0) else []))
                pool["rel"] = ev
        yndone = p.engine_done_tok("vector")
        evp = {}
        womm = {}
        for mc in range(32):
            ww = [womm[mc - 2]] if mc >= 2 else (
                [last_wo] if last_wo is not None else [])
            wt = p.dma("sync", wob[mc % 2][:, :], wout[mc][:, :], f"wld{mc % 2}",
                       waits=ww)
            mmt = None
            w0 = [wt, yndone]
            if mc >= 2:
                w0.append(evp[mc - 2])
            if mc <= 1 and tr_after is not None:
                w0.append(tr_after)
            for kc in range(32):
                mmt = p.op("tensor", lambda E, kc=kc, mc=mc: E.matmul(
                    psW[mc % 2][:, :], wob[mc % 2][:, 128 * kc:128 * (kc + 1)],
                    ynap[:, kc, :], start=(kc == 0), stop=(kc == 31)),
                    waits=w0 if kc == 0 else [],
                    inc="m2wo" if kc == 31 else None)
            womm[mc] = mmt
            evp[mc] = p.op("vector", lambda E, mc=mc: E.tensor_copy(
                out=projsb[:, 512 * mc:512 * (mc + 1)], in_=psW[mc % 2][:, :]),
                waits=[mmt])
        last_wo = womm[31]
        for t in range(4):
            i = 4 * half + t
            xw = []
            if i >= 2:
                xw = [lnrel[i - 2], x3st[i - 2]]
            xt = p.dma("sync", xres[i % 2][:, :], x2[128 * i:128 * (i + 1), :],
                       f"xld{i % 2}", waits=xw)
            add = None
            for g2 in range(4):
                w = [evp[31]] if g2 == 0 else []
                if pool["rel"] is not None:
                    w.append(pool["rel"])
                tt = None
                for u in range(8):
                    mcc = 8 * g2 + u
                    tt = p.op("tensor", lambda E, u=u, mcc=mcc, t=t: E.transpose(
                        psb[:, 128 * u:128 * (u + 1)],
                        projsb[:, 512 * mcc + 128 * t:512 * mcc + 128 * (t + 1)],
                        ident[:, :]),
                        waits=w if u == 0 else [], inc="m2tr" if u == 7 else None)
                add = p.op("vector", lambda E, g2=g2, i=i: E.tensor_add(
                    xres[i % 2][:, 1024 * g2:1024 * (g2 + 1)],
                    xres[i % 2][:, 1024 * g2:1024 * (g2 + 1)], psb[:, :]),
                    waits=[tt, xt])
                pool["rel"] = add
            x3st[i] = p.dma("sync", x3_o[128 * i:128 * (i + 1), :],
                            xres[i % 2][:, :], f"x3st{i % 2}", waits=[add])
            rel, ev = lne.chunk(xres[i % 2], add,
                                lambda g, st_=strip_st: (stripap[:, g * 8:(g + 1) * 8, :],
                                                         [st_]))
            lnrel[i] = rel
            strip_st = p.dma("sync", ln_o[i][:, :], strip[:, :], "lnst", waits=[ev])
        tr_after = ("m2tr", p.val("m2tr"))
    p.barrier("sync", waits=[("x3st0", p.val("x3st0")), ("x3st1", p.val("x3st1")),
                             ("lnst", p.val("lnst"))])
    return p.build()


def run_m1(f2res, inputs_f32, wbf):
    ipT = np.asarray(wbf["L1_attn_in_proj_w"]).T
    ipT = np.concatenate([ipT, np.zeros((4096, 64), BF)], axis=1)
    wip_img = pack_lhsT(np.ascontiguousarray(ipT))
    cw = np.asarray(inputs_f32["L1_attn_conv_weight"], np.float32)
    convw = np.ascontiguousarray(cw.reshape(48, 128, 4).transpose(1, 0, 2)
                                 .reshape(128, 192))
    convb = np.ascontiguousarray(
        np.asarray(inputs_f32["L1_attn_conv_bias"], np.float32).reshape(48, 128).T)
    dtb = np.asarray(inputs_f32["L1_attn_dt_bias"], np.float32).reshape(64, 1)
    alog = np.asarray(inputs_f32["L1_attn_A_log"], np.float32).reshape(64, 1)
    dd = np.asarray(inputs_f32["L1_attn_D"], np.float32).reshape(1, 64)
    r_ = np.arange(128)
    trisu = (r_[:, None] > r_[None, :]).astype(BF)
    maskt = np.where(r_[:, None] <= r_[None, :], 0.0, MNEG).astype(np.float32)
    onesc = np.ones((128, 1), BF)
    onesr = np.ones((1, 128), np.float32)
    idbf = np.eye(128, dtype=BF)
    idf = np.eye(128, dtype=np.float32)
    nc = build_m1()
    in_maps = []
    for c in range(NCORES):
        own = assemble_lnimg(f2res[c]["ln_o"]).reshape(128, 32, 1024)
        arr = np.zeros((128, 32, 1027), BF)
        if c % 4 > 0:
            pv = assemble_lnimg(f2res[c - 1]["ln_o"]).reshape(128, 32, 1024)
            arr[:, :, 0:3] = pv[:, :, 1021:1024]
        arr[:, :, 3:] = own
        in_maps.append({"lnin": np.ascontiguousarray(arr.reshape(128, 32 * 1027)),
                        "wip": wip_img, "convw": convw, "convb": convb,
                        "dtb": dtb, "alog": alog, "dd": dd, "trisu": trisu,
                        "maskt": maskt, "onesc": onesc, "onesr": onesr,
                        "idbf": idbf, "idf": idf})
    return run_bass_kernel_spmd(nc, in_maps, core_ids=list(range(NCORES))).results


def run_m2(m1res, f2res, inputs_f32, wbf):
    wout_img = pack_lhsT(to_bf(np.asarray(wbf["L1_attn_out_proj_w"]).T))
    nwr = np.asarray(inputs_f32["L1_attn_norm_weight"], np.float32).reshape(1, 4096)
    onesr = np.ones((1, 128), np.float32)
    idbf = np.eye(128, dtype=BF)
    nc = build_m2()
    in_maps = []
    for c in range(NCORES):
        j = c % 4
        upv = np.zeros((3, 128, 4096), np.float32)
        dpv = np.ones((1, 128), np.float32)
        for i_ in range(3):
            cc = c - 1 - i_
            if cc >= 4 * (c // 4):
                upv[i_] = np.asarray(m1res[cc]["u_o"])
        if j >= 2:
            dpv[0, 0:64] = np.asarray(m1res[c - 1]["ds_o"])[0]
        if j >= 3:
            dpv[0, 64:128] = np.asarray(m1res[c - 2]["ds_o"])[0]
        in_maps.append({"y_i": np.asarray(m1res[c]["y_o"]),
                        "c_i": np.asarray(m1res[c]["c_o"]),
                        "el": np.asarray(m1res[c]["el_o"]),
                        "dch": np.asarray(m1res[c]["dc_o"]),
                        "gate": np.asarray(m1res[c]["gt_o"]),
                        "x2": np.asarray(f2res[c]["x2_o"]),
                        "uprev": upv, "dprev": dpv, "wout": wout_img,
                        "nwr": nwr, "onesr": onesr, "idbf": idbf})
    return run_bass_kernel_spmd(nc, in_maps, core_ids=list(range(NCORES))).results




def build_f3():
    p = Prog("f3")
    nc = p.nc
    lnin = nc.dram_tensor("lnin", [128, 32 * TC], BF16, kind="ExternalInput").ap()
    x3 = nc.dram_tensor("x3", [TC, D], F32, kind="ExternalInput").ap()
    wt = nc.dram_tensor("wt", [32, 128, 4096], BF16, kind="ExternalInput").ap()
    at = nc.dram_tensor("at", [2, 128, 4096], BF16, kind="ExternalInput").ap()
    bft = nc.dram_tensor("bft", [32, 128, 256], BF16, kind="ExternalInput").ap()
    bias = nc.dram_tensor("bias", [1, 4096], BF16, kind="ExternalInput").ap()
    coef = nc.dram_tensor("coef", [1, 16], F32, kind="ExternalInput").ap()
    scal = nc.dram_tensor("scal", [1, 16], F32, kind="ExternalInput").ap()
    sel2 = nc.dram_tensor("sel2", [16, 256], BF16, kind="ExternalInput").ap()
    ones1 = nc.dram_tensor("ones1", [1, 1], BF16, kind="ExternalInput").ap()
    ones5 = nc.dram_tensor("ones5", [1, 512], BF16, kind="ExternalInput").ap()
    idin = nc.dram_tensor("idin", [128, 128], BF16, kind="ExternalInput").ap()
    x4_o = nc.dram_tensor("x4_o", [TC, D], F32, kind="ExternalOutput").ap()

    img = nc.alloc_sbuf_tensor("img", [128, 32 * TC], BF16)
    wb = [nc.alloc_sbuf_tensor(f"wb{i}", [128, 4096], BF16) for i in range(2)]
    bb = [nc.alloc_sbuf_tensor(f"bb{i}", [128, 256], BF16) for i in range(2)]
    wrow = nc.alloc_sbuf_tensor("wrow", [1, 16], F32)
    wrow_bf = nc.alloc_sbuf_tensor("wrow_bf", [1, 16], BF16)
    wcol = nc.alloc_sbuf_tensor("wcol", [16, 1], BF16)
    wvec = nc.alloc_sbuf_tensor("wvec", [128, 2], F32)
    coef_sb = nc.alloc_sbuf_tensor("coef_sb", [1, 16], F32)
    scal_sb = nc.alloc_sbuf_tensor("scal_sb", [1, 16], F32)
    sel_sb = nc.alloc_sbuf_tensor("sel_sb", [16, 256], BF16)
    ones1_sb = nc.alloc_sbuf_tensor("ones1_sb", [1, 1], BF16)
    ones5_sb = nc.alloc_sbuf_tensor("ones5_sb", [1, 512], BF16)
    bias_sb = nc.alloc_sbuf_tensor("bias_sb", [1, 4096], BF16)
    xas = nc.alloc_sbuf_tensor("xas", [128, 2 * TC], BF16)
    projsb = nc.alloc_sbuf_tensor("projsb", [128, 32 * 512], BF16)
    xres = [nc.alloc_sbuf_tensor(f"xres{i}", [128, D], F32) for i in range(2)]
    ident = nc.alloc_sbuf_tensor("ident", [128, 128], BF16)

    ps = [nc.alloc_psum_tensor(f"f3ps{i}", [128, 1024], F32) for i in range(2)]
    ps1 = nc.alloc_psum_tensor("f3psq", [128, 512], F32)
    psb = nc.alloc_psum_tensor("f3psb", [128, 1024], BF16)

    imgap = img[:, :].rearrange("q (kc tt) -> q kc tt", tt=TC)

    ildt = p.dma("sync", img[:, :], lnin[:, :], "ild")
    p.dma("sync", coef_sb[:, :], coef[:, :], "cld")
    p.dma("sync", scal_sb[:, :], scal[:, :], "cld")
    p.dma("sync", sel_sb[:, :], sel2[:, :], "cld")
    p.dma("sync", ones1_sb[:, :], ones1[:, :], "cld")
    p.dma("sync", ones5_sb[:, :], ones5[:, :], "cld")
    p.dma("sync", bias_sb[:, :], bias[:, :], "cld")
    cld = p.dma("sync", ident[:, :], idin[:, :], "cld")
    p.barrier("vector", waits=[cld])
    p.barrier("tensor", waits=[cld, ildt])

    wmul = p.op("vector", lambda E: E.tensor_mul(wrow[:, :], coef_sb[:, :],
                                                 scal_sb[:, :]))
    wbt = p.op("vector", lambda E: E.tensor_copy(out=wrow_bf[:, :], in_=wrow[:, :]))
    ctm = p.op("tensor", lambda E: E.matmul(ps1[0:16, 0:1], wrow_bf[:, :],
                                            ones1_sb[:, :], start=True, stop=True),
               waits=[wbt], inc="f3c")
    cce = p.op("vector", lambda E: E.tensor_copy(out=wcol[:, :], in_=ps1[0:16, 0:1]),
               waits=[ctm])
    vmm = None
    for j in range(2):
        vmm = p.op("tensor", lambda E, j=j: E.matmul(
            ps1[:, 4 + j:5 + j], sel_sb[:, 128 * j:128 * (j + 1)], wcol[:, :],
            start=True, stop=True), waits=[cce] if j == 0 else [],
            inc="f3v" if j == 1 else None)
    cvt = p.op("vector", lambda E: E.tensor_copy(out=wvec[:, :], in_=ps1[:, 4:6]),
               waits=[vmm])

    xde = {}
    at_t = {}
    for j in range(2):
        at_t[j] = p.dma("sync", wb[j][:, :], at[j][:, :], f"wld{j}")
        mmt = None
        for n in range(2):
            for kc in range(32):
                mmt = p.op("tensor", lambda E, kc=kc, n=n, j=j: E.matmul(
                    ps[j][:, 512 * n:512 * (n + 1)],
                    wb[j][:, 128 * kc:128 * (kc + 1)],
                    imgap[:, kc, 512 * n:512 * (n + 1)],
                    start=(kc == 0), stop=(kc == 31)),
                    waits=[at_t[j]] if kc == 0 else [],
                    inc="f3xa" if kc == 31 else None)
            xde[(j, n)] = p.op("vector", lambda E, j=j, n=n: E.tensor_scalar(
                out=xas[:, TC * j + 512 * n:TC * j + 512 * (n + 1)],
                in0=ps[j][:, 512 * n:512 * (n + 1)],
                scalar1=wvec[:, j:j + 1], scalar2=None, op0=ALU.mult),
                waits=[mmt, cvt] if n == 0 else [mmt])

    x4st = {}
    tr_after = None
    last_wo = None
    add_prev = {}
    pool_rel = None
    for half in range(2):
        evp = {}
        womm = {}
        for mc in range(32):
            ww = []
            if mc >= 2:
                ww = [womm[mc - 2]]
            elif half == 0:
                ww = [xde[(1, 1)]] if mc < 2 else []
            else:
                ww = [last_wo]
            wt_ = p.dma("sync", wb[mc % 2][:, :], wt[mc][:, :], f"wld{mc % 2}",
                        waits=ww)
            ut = p.dma("sync", bb[mc % 2][:, :], bft[mc][:, :], f"uld{mc % 2}",
                       waits=ww)
            mmt = None
            w0 = [wt_, ut]
            if mc >= 2:
                w0.append(evp[mc - 2])
            if mc <= 1 and tr_after is not None:
                w0.append(tr_after)
            for kc in range(32):
                mmt = p.op("tensor", lambda E, kc=kc, mc=mc, half=half: E.matmul(
                    ps[mc % 2][:, 0:512], wb[mc % 2][:, 128 * kc:128 * (kc + 1)],
                    imgap[:, kc, 512 * half:512 * (half + 1)],
                    start=(kc == 0), stop=False),
                    waits=w0 if kc == 0 else [])
            for j in range(2):
                mmt = p.op("tensor", lambda E, j=j, mc=mc, half=half: E.matmul(
                    ps[mc % 2][:, 0:512], bb[mc % 2][:, 128 * j:128 * (j + 1)],
                    xas[:, TC * j + 512 * half:TC * j + 512 * (half + 1)],
                    start=False, stop=False), waits=[])
            mmt = p.op("tensor", lambda E, mc=mc: E.matmul(
                ps[mc % 2][:, 0:512], bias_sb[0:1, 128 * mc:128 * (mc + 1)],
                ones5_sb[:, :], start=False, stop=True), waits=[],
                inc="f3wo")
            womm[mc] = mmt
            evp[mc] = p.op("vector", lambda E, mc=mc: E.tensor_copy(
                out=projsb[:, 512 * mc:512 * (mc + 1)], in_=ps[mc % 2][:, 0:512]),
                waits=[mmt])
        last_wo = womm[31]
        for t in range(4):
            i = 4 * half + t
            xw = [x4st[i - 2]] if i >= 2 else []
            xt = p.dma("sync", xres[i % 2][:, :], x3[128 * i:128 * (i + 1), :],
                       f"xld{i % 2}", waits=xw)
            add = None
            for g2 in range(4):
                w = [evp[31]] if g2 == 0 else []
                if pool_rel is not None:
                    w.append(pool_rel)
                tt = None
                for u in range(8):
                    mcc = 8 * g2 + u
                    tt = p.op("tensor", lambda E, u=u, mcc=mcc, t=t: E.transpose(
                        psb[:, 128 * u:128 * (u + 1)],
                        projsb[:, 512 * mcc + 128 * t:512 * mcc + 128 * (t + 1)],
                        ident[:, :]),
                        waits=w if u == 0 else [], inc="f3tr" if u == 7 else None)
                add = p.op("vector", lambda E, g2=g2, i=i: E.tensor_add(
                    xres[i % 2][:, 1024 * g2:1024 * (g2 + 1)],
                    xres[i % 2][:, 1024 * g2:1024 * (g2 + 1)], psb[:, :]),
                    waits=[tt, xt])
                pool_rel = add
            x4st[i] = p.dma("sync", x4_o[128 * i:128 * (i + 1), :],
                            xres[i % 2][:, :], f"x4st{i % 2}", waits=[add])
        tr_after = ("f3tr", p.val("f3tr"))
    p.barrier("sync", waits=[("x4st0", p.val("x4st0")), ("x4st1", p.val("x4st1"))])
    return p.build()


def run_f3(m2res, inputs_f32, wbf):
    wt_img = pack_lhsT(to_bf(np.asarray(wbf["L1_ffn_W"]).T))
    at_img = pack_lhsT(np.ascontiguousarray(
        np.asarray(wbf["L1_ffn_A"]).transpose(2, 0, 1).reshape(4096, 256)))
    bft_img = pack_lhsT(np.ascontiguousarray(
        np.asarray(wbf["L1_ffn_Bf"]).transpose(0, 2, 1).reshape(256, 4096)))
    bias = to_bf(np.asarray(inputs_f32["L1_ffn_bias"], np.float32).reshape(1, 4096))
    coef = np.asarray(inputs_f32["L1_ffn_coeffs"], np.float32).reshape(1, 16)
    scal = np.asarray(inputs_f32["L1_ffn_scales"], np.float32).reshape(1, 16)
    sel = np.zeros((16, 256), BF)
    for j in range(2):
        for pp in range(128):
            sel[8 * j + pp // 16, 128 * j + pp] = BF(1.0)
    nc = build_f3()
    in_maps = []
    for c in range(NCORES):
        in_maps.append({"lnin": assemble_lnimg(m2res[c]["ln_o"]),
                        "x3": np.asarray(m2res[c]["x3_o"]), "wt": wt_img,
                        "at": at_img, "bft": bft_img, "bias": bias,
                        "coef": coef, "scal": scal, "sel2": sel,
                        "ones1": np.ones((1, 1), BF),
                        "ones5": np.ones((1, 512), BF),
                        "idin": np.eye(128, dtype=BF)})
    return run_bass_kernel_spmd(nc, in_maps, core_ids=list(range(NCORES))).results




def kernel(**inputs):
    X = np.ascontiguousarray(np.asarray(inputs["L0_attn_X"], np.float32))
    x_slices = [np.ascontiguousarray(X[c // 4, (c % 4) * TC:(c % 4 + 1) * TC])
                for c in range(NCORES)]
    wbf = cast_weights(inputs)
    a1res = run_a1(x_slices, wbf)
    a2res = run_a2(x_slices, a1res, wbf)
    f2res = run_f2(a2res, wbf)
    m1res = run_m1(f2res, inputs, wbf)
    m2res = run_m2(m1res, f2res, inputs, wbf)
    f3res = run_f3(m2res, inputs, wbf)
    out = np.empty((B, T, D), np.float32)
    for c in range(NCORES):
        out[c // 4, (c % 4) * TC:(c % 4 + 1) * TC] = np.asarray(f3res[c]["x4_o"])
    return out


# version 05 (final)
# speedup vs baseline: 1.1896x; 1.0344x over previous
import numpy as np
import ml_dtypes
import concourse.bass as bass
import concourse.mybir as mybir
from concourse.bass_utils import run_bass_kernel_spmd

F32 = mybir.dt.float32
BF16 = mybir.dt.bfloat16
AF = mybir.ActivationFunctionType
ALU = mybir.AluOpType
AX = mybir.AxisListType
BF = ml_dtypes.bfloat16

NCORES = 8
B = 2
T = 4096
D = 4096
TC = 1024
NH = 32
NKV = 8
DH = 128
EPS = 1e-5



class Prog:

    ENGS = ("tensor", "vector", "scalar", "gpsimd", "sync")
    SERIAL = ("vector", "scalar", "gpsimd")

    def __init__(self, name):
        self.nc = bass.Bass("TRN2", target_bir_lowering=False, debug=False)
        self.name = name
        self.plan = {e: [] for e in self.ENGS}
        self.sems = {}
        self.cnt = {}
        self.selfc = {e: 0 for e in self.SERIAL}
        for e in self.SERIAL:
            self.sem("@" + e)

    def stage_barrier(self):
        waits = [(s, v) for s, v in self.cnt.items() if v > 0]
        for e in self.ENGS:
            self.barrier(e, waits=waits)

    def collective(self, kind, op, groups, ins, outs, inc, waits=()):
        self.sem(inc)
        waits = [t for t in waits if t is not None]
        waits = [(s, v) for (s, v) in waits if v > 0]
        if self.cnt[inc] > 0:
            waits.append((inc, self.cnt[inc]))
        if self.selfc["gpsimd"] > 0:
            waits.append(("@gpsimd", self.selfc["gpsimd"]))
        self.cnt[inc] += 16
        self.plan["gpsimd"].append(
            ([(s, v, self.sems[s]) for s, v in waits],
             lambda E, k=kind, o=op, g=groups, i=ins, u=outs:
             E.collective_compute(k, o, replica_groups=g, ins=i, outs=u),
             [(inc, 16, self.sems[inc])]))
        return (inc, self.cnt[inc])

    def sem(self, name):
        if name not in self.sems:
            pool = getattr(self, "_sem_pool", None)
            if pool:
                h, v = pool.pop()
                self.sems[name] = h
                self.cnt[name] = v
            else:
                self.sems[name] = self.nc.alloc_semaphore(
                    name=name.replace("@", "s_"))
                self.cnt[name] = 0
        return name

    def release_sems(self, prefix):
        if not hasattr(self, "_sem_pool"):
            self._sem_pool = []
        if not hasattr(self, "_final_cnt"):
            self._final_cnt = {}
        for name in [n for n in list(self.sems) if n.startswith(prefix)]:
            self._final_cnt[name] = self.cnt[name]
            self._sem_pool.append((self.sems.pop(name), self.cnt.pop(name)))

    def val(self, name):
        self.sem(name)
        return self.cnt[name]

    def op(self, eng, fn, waits=(), inc=None, serial=True):
        waits = [t for t in waits if t is not None]
        waits = [(s, v) for (s, v) in waits if v > 0]
        for s, _ in waits:
            self.sem(s)
        incs = []
        tok = None
        if eng in self.selfc and serial:
            assert inc is None, "serialized engines use the self-sem only"
            sname = "@" + eng
            if self.selfc[eng] > 0:
                waits.append((sname, self.selfc[eng]))
            self.selfc[eng] += 1
            self.cnt[sname] = self.selfc[eng]
            incs.append((sname, 1))
            tok = (sname, self.selfc[eng])
        elif inc is not None:
            self.sem(inc)
            self.cnt[inc] += 1
            incs.append((inc, 1))
            tok = (inc, self.cnt[inc])
        self.plan[eng].append(
            ([(s, v, self.sems[s]) for s, v in waits], fn,
             [(s, n, self.sems[s]) for s, n in incs]))
        return tok

    def dma(self, eng, out, in_, inc, waits=(), **kw):
        self.sem(inc)
        waits = [t for t in waits if t is not None]
        waits = [(s, v) for (s, v) in waits if v > 0]
        for s, _ in waits:
            self.sem(s)
        if self.cnt[inc] > 0:
            waits.append((inc, self.cnt[inc]))
        if eng == "gpsimd" and self.selfc["gpsimd"] > 0:
            waits.append(("@gpsimd", self.selfc["gpsimd"]))
        self.cnt[inc] += 16
        self.plan[eng].append(
            ([(s, v, self.sems[s]) for s, v in waits],
             lambda E, o=out, i=in_, k=kw: E.dma_start(out=o, in_=i, **k),
             [(inc, 16, self.sems[inc])]))
        return (inc, self.cnt[inc])

    def barrier(self, eng, waits=()):
        waits = [t for t in waits if t is not None]
        waits = [(s, v) for (s, v) in waits if v > 0]
        for s, _ in waits:
            self.sem(s)
        self.plan[eng].append(([(s, v, self.sems[s]) for s, v in waits], None, []))

    def engine_done_tok(self, eng):
        return ("@" + eng, self.selfc[eng])

    def lint(self):
        errs = []
        for eng in self.ENGS:
            prior = {}
            for idx, (waits, fn, incs) in enumerate(self.plan[eng]):
                for s, v, _h in waits:
                    fin = self.cnt.get(s, getattr(self, "_final_cnt", {}).get(s, 0))
                    if v > fin:
                        errs.append(f"{eng}[{idx}]: wait {s}>={v} > final {fin}")
                    if s == "@" + eng and v > prior.get(s, 0):
                        errs.append(f"{eng}[{idx}]: self-wait {s}>={v} > prior "
                                    f"{prior.get(s, 0)}")
                for s, n, _h in incs:
                    prior[s] = prior.get(s, 0) + n
        if errs:
            raise RuntimeError(f"Prog({self.name}) lint: " + " ; ".join(errs[:8]))

    def build(self):
        self.lint()
        nc = self.nc
        with nc.Block() as block:
            for eng in self.ENGS:
                items = self.plan[eng]

                def mk(items_):
                    def body(E):
                        for waits, fn, incs in items_:
                            for s, v, h in waits:
                                E.wait_ge(h, v)
                            if fn is None:
                                continue
                            inst = fn(E)
                            for s, n, h in incs:
                                inst = inst.then_inc(h, n)
                    return body
                getattr(block, eng)(mk(items))
        return nc


def to_bf(x):
    return np.ascontiguousarray(np.asarray(x).astype(BF))


def pack_lhsT(wT):
    K, M = wT.shape
    Kc, Mc = K // 128, M // 128
    a = wT.reshape(Kc, 128, Mc, 128).transpose(2, 1, 0, 3).reshape(Mc, 128, Kc * 128)
    return np.ascontiguousarray(a)


def rope_tables(pos0, n):
    half = 64
    inv = 1.0 / (10000.0 ** (np.arange(half, dtype=np.float64) / half))
    ang = np.arange(pos0, pos0 + n, dtype=np.float64)[None, :] * inv[:, None]
    return (np.cos(ang).astype(np.float32), np.sin(ang).astype(np.float32))



WCH = 4096


def build_wcast(nch):
    p = Prog("wcast")
    nc = p.nc
    R = nch * 128
    src = nc.dram_tensor("wsrc", [R, WCH], F32, kind="ExternalInput").ap()
    dst = nc.dram_tensor("wdst", [R, WCH], BF16, kind="ExternalOutput").ap()
    bufs = [nc.alloc_sbuf_tensor(f"wi{i}", [128, WCH], F32) for i in range(3)]
    obufs = [nc.alloc_sbuf_tensor(f"wo{i}", [128, WCH], BF16) for i in range(3)]
    engs = ["vector", "scalar", "gpsimd"]
    st_toks = []
    for i in range(nch):
        b = i % 3
        w = [st_toks[i - 3]] if i >= 3 else []
        lt = p.dma("sync", bufs[b][:, :], src[128 * i:128 * (i + 1), :], f"ld{b}",
                   waits=w)
        e = engs[i % 3]
        if e == "scalar":
            ct = p.op(e, lambda E, b=b: E.copy(out=obufs[b][:, :], in_=bufs[b][:, :]),
                      waits=[lt])
        else:
            ct = p.op(e, lambda E, b=b: E.tensor_copy(out=obufs[b][:, :],
                                                      in_=bufs[b][:, :]), waits=[lt])
        st_toks.append(p.dma("sync", dst[128 * i:128 * (i + 1), :], obufs[b][:, :],
                             f"st{b}", waits=[ct]))
    p.barrier("sync", waits=[("st0", p.val("st0")), ("st1", p.val("st1")),
                             ("st2", p.val("st2"))])
    return p.build()


WEIGHT_KEYS = [
    "L0_attn_Wq", "L0_attn_Wk", "L0_attn_Wv", "L0_attn_Wo",
    "L0_ffn_W_org", "L0_ffn_lora_down", "L0_ffn_lora_up", "L0_ffn_lora_keys",
    "L0_ffn_Wq",
    "L1_attn_in_proj_w", "L1_attn_conv_weight", "L1_attn_conv_bias",
    "L1_attn_dt_bias", "L1_attn_A_log", "L1_attn_D", "L1_attn_norm_weight",
    "L1_attn_out_proj_w",
    "L1_ffn_W", "L1_ffn_bias", "L1_ffn_A", "L1_ffn_Bf", "L1_ffn_coeffs",
    "L1_ffn_scales",
]


def cast_weights(inputs):
    flats = [np.asarray(inputs[k], dtype=np.float32).reshape(-1) for k in WEIGHT_KEYS]
    sizes = [f.size for f in flats]
    cat = np.concatenate(flats)
    per = -(-cat.size // (NCORES * 128 * WCH)) * 128 * WCH
    cat = np.concatenate([cat, np.zeros(per * NCORES - cat.size, np.float32)])
    shards = cat.reshape(NCORES, per // WCH, WCH)
    nc = build_wcast(per // (WCH * 128))
    res = run_bass_kernel_spmd(nc, [{"wsrc": shards[c]} for c in range(NCORES)],
                               core_ids=list(range(NCORES)))
    out = np.concatenate([np.asarray(res.results[c]["wdst"]).reshape(-1)
                          for c in range(NCORES)])
    w = {}
    off = 0
    for k, s in zip(WEIGHT_KEYS, sizes):
        w[k] = out[off:off + s].reshape(np.asarray(inputs[k]).shape)
        off += s
    return w




def emit_ln(p, x_src_fn, nchunks, bufs, img, ident, ps_tr, tag, lnout_cb=None):
    xbuf, stats, mv, rstd, lntm = bufs
    TT = nchunks * 128
    imgap = img[:, :].rearrange("q (kc tt) -> q kc tt", tt=TT)
    nrm_toks, ev_toks, tr_toks = [], [], []
    for i in range(nchunks):
        b = i % 2
        w = [nrm_toks[i - 2]] if i >= 2 else []
        lt = x_src_fn(i, xbuf[b], w)
        for j in range(8):
            p.op("vector", lambda E, b=b, j=j: E.bn_stats(
                out=stats[:, 6 * j:6 * j + 6], in_=xbuf[b][:, 512 * j:512 * (j + 1)]),
                waits=[lt] if j == 0 else [])
        p.op("vector", lambda E: E.bn_aggr(out=mv[:, :], in_=stats[:, 0:48]))
        vt = p.op("vector", lambda E: E.tensor_scalar_add(rstd[:, :], mv[:, 1:2], EPS))
        p.op("scalar", lambda E: E.activation(out=rstd[:, :], in_=rstd[:, :],
                                              func=AF.Ln), waits=[vt])
        st = p.op("scalar", lambda E: E.activation(out=rstd[:, :], in_=rstd[:, :],
                                                   func=AF.Exp, scale=-0.5))
        nw = [st]
        if i >= 1:
            nw.append(tr_toks[-1])
        nt = p.op("vector", lambda E, b=b: E.tensor_scalar(
            out=lntm[:, :], in0=xbuf[b][:, :], scalar1=mv[:, 0:1], scalar2=rstd[:, :],
            op0=ALU.subtract, op1=ALU.mult), waits=nw)
        nrm_toks.append(nt)
        if lnout_cb is not None:
            nt = lnout_cb(i, xbuf[b], lntm) or nt
            nrm_toks[-1] = nt
        for g in range(4):
            pb = g % 2
            gi = i * 4 + g
            w = [nt] if g == 0 else []
            if gi >= 2:
                w.append(ev_toks[gi - 2])
            tt = None
            for u in range(8):
                kc = g * 8 + u
                tt = p.op("tensor", lambda E, pb=pb, u=u, kc=kc: E.transpose(
                    ps_tr[pb][:, 128 * u:128 * (u + 1)],
                    lntm[:, 128 * kc:128 * (kc + 1)], ident[:, :]),
                    waits=w if u == 0 else [],
                    inc=(tag + "_tr") if u == 7 else None)
            tr_toks.append(tt)
            ev_toks.append(p.op("vector", lambda E, pb=pb, g=g, i=i: E.tensor_copy(
                out=imgap[:, g * 8:(g + 1) * 8, 128 * i:128 * (i + 1)],
                in_=ps_tr[pb][:, :].rearrange("q (u t) -> q u t", u=8)),
                waits=[tt]))
    return ev_toks




def build_a1():
    p = Prog("a1")
    nc = p.nc
    x = nc.dram_tensor("x", [TC, D], F32, kind="ExternalInput").ap()
    wq = nc.dram_tensor("wq", [NH, 128, 4096], BF16, kind="ExternalInput").ap()
    wk = nc.dram_tensor("wk", [NKV, 128, 4096], BF16, kind="ExternalInput").ap()
    wv = nc.dram_tensor("wv", [NKV, 128, 4096], BF16, kind="ExternalInput").ap()
    ctab = nc.dram_tensor("ctab", [64, TC], F32, kind="ExternalInput").ap()
    stab = nc.dram_tensor("stab", [64, TC], F32, kind="ExternalInput").ap()
    idin = nc.dram_tensor("idin", [128, 128], BF16, kind="ExternalInput").ap()
    q_o = nc.dram_tensor("q_o", [NH, 128, TC], BF16, kind="ExternalOutput").ap()
    k_o = nc.dram_tensor("k_o", [NKV, 128, TC], BF16, kind="ExternalOutput").ap()
    v_o = nc.dram_tensor("v_o", [NKV, 128, TC], BF16, kind="ExternalOutput").ap()

    xbuf = [nc.alloc_sbuf_tensor(f"xb{i}", [128, D], F32) for i in range(2)]
    stats = nc.alloc_sbuf_tensor("stats", [128, 48], F32)
    mv = nc.alloc_sbuf_tensor("mv", [128, 2], F32)
    rstd = nc.alloc_sbuf_tensor("rstd", [128, 1], F32)
    lntm = nc.alloc_sbuf_tensor("lntm", [128, D], BF16)
    img = nc.alloc_sbuf_tensor("img", [128, 32 * TC], BF16)
    ident = nc.alloc_sbuf_tensor("ident", [128, 128], BF16)
    ps_tr = [nc.alloc_psum_tensor(f"pstr{i}", [128, 1024], BF16) for i in range(2)]
    ps_mm = [nc.alloc_psum_tensor(f"psmm{i}", [128, 512], F32) for i in range(4)]
    wbuf = [nc.alloc_sbuf_tensor(f"wb{i}", [128, 4096], BF16) for i in range(2)]
    cos = nc.alloc_sbuf_tensor("cos", [64, TC], F32)
    sin = nc.alloc_sbuf_tensor("sin", [64, TC], F32)
    t1 = nc.alloc_sbuf_tensor("t1", [64, TC], F32)
    t2 = nc.alloc_sbuf_tensor("t2", [64, TC], F32)
    ov = [nc.alloc_sbuf_tensor(f"ov{i}", [128, TC], BF16) for i in range(2)]

    p.dma("sync", ident[:, :], idin[:, :], "cld")
    p.dma("sync", cos[:, :], ctab[:, :], "cld")
    cld = p.dma("sync", sin[:, :], stab[:, :], "cld")
    p.barrier("tensor", waits=[cld])
    p.barrier("vector", waits=[cld])

    def xsrc(i, buf, w):
        return p.dma("sync", buf[:, :], x[128 * i:128 * (i + 1), :],
                     f"a_xld{i % 2}", waits=w)

    ev_toks = emit_ln(p, xsrc, 8, (xbuf, stats, mv, rstd, lntm), img, ident,
                      ps_tr, "a")
    imgdone = ev_toks[-1]
    imgap = img[:, :].rearrange("q (kc tt) -> q kc tt", tt=TC)

    jobs = ([("q", wq[h], q_o[h]) for h in range(NH)]
            + [("k", wk[h], k_o[h]) for h in range(NKV)]
            + [("v", wv[h], v_o[h]) for h in range(NKV)])

    mm_toks, ev2_toks, st_toks = [], [], []
    for ji, (kind, wsrc, osl) in enumerate(jobs):
        wb = ji % 2
        wlt = p.dma("sync", wbuf[wb][:, :], wsrc[:, :], f"wl{wb}",
                    waits=[mm_toks[ji - 2]] if ji >= 2 else [])
        mt = None
        for n in range(2):
            pb = 2 * (ji % 2) + n
            for kc in range(32):
                w = []
                if kc == 0 and n == 0:
                    w = [wlt, imgdone]
                    if ji >= 2:
                        w.append(ev2_toks[ji - 2])
                mt = p.op("tensor", lambda E, pb=pb, kc=kc, n=n, wb=wb: E.matmul(
                    ps_mm[pb][:, :], wbuf[wb][:, 128 * kc:128 * (kc + 1)],
                    imgap[:, kc, 512 * n:512 * (n + 1)],
                    start=(kc == 0), stop=(kc == 31)),
                    waits=w, inc="a_mme" if (kc == 31 and n == 1) else None)
        mm_toks.append(mt)
        ob = ov[ji % 2]
        sw = [st_toks[ji - 2]] if ji >= 2 else []
        if kind == "v":
            p.op("vector", lambda E, pb=2 * (ji % 2), ob=ob: E.tensor_copy(
                out=ob[:, 0:512], in_=ps_mm[pb][:, :]), waits=[mt] + sw)
            et = p.op("vector", lambda E, pb=2 * (ji % 2) + 1, ob=ob: E.tensor_copy(
                out=ob[:, 512:1024], in_=ps_mm[pb][:, :]))
        else:
            for n in range(2):
                ps = ps_mm[2 * (ji % 2) + n]
                sl = slice(512 * n, 512 * (n + 1))
                p.op("vector", lambda E, ps=ps, sl=sl: E.tensor_mul(
                    t1[:, sl], ps[0:64, :], cos[:, sl]),
                    waits=[mt] + sw if n == 0 else [])
                p.op("vector", lambda E, ps=ps, sl=sl: E.tensor_mul(
                    t2[:, sl], ps[64:128, :], sin[:, sl]))
                p.op("vector", lambda E, sl=sl, ob=ob: E.tensor_sub(
                    ob[0:64, sl], t1[:, sl], t2[:, sl]))
                p.op("vector", lambda E, ps=ps, sl=sl: E.tensor_mul(
                    t1[:, sl], ps[0:64, :], sin[:, sl]))
                p.op("vector", lambda E, ps=ps, sl=sl: E.tensor_mul(
                    t2[:, sl], ps[64:128, :], cos[:, sl]))
                et = p.op("vector", lambda E, sl=sl, ob=ob: E.tensor_add(
                    ob[64:128, sl], t1[:, sl], t2[:, sl]))
        ev2_toks.append(et)
        st_toks.append(p.dma("sync", osl[:, :], ob[:, :], f"a_sto{ji % 2}",
                             waits=[et]))
    p.barrier("sync", waits=[("a_sto0", p.val("a_sto0")),
                             ("a_sto1", p.val("a_sto1"))])
    return p.build()


def run_a1(x_slices, wbf):
    wq_img = pack_lhsT(to_bf(wbf["L0_attn_Wq"]))
    wk_img = pack_lhsT(to_bf(wbf["L0_attn_Wk"]))
    wv_img = pack_lhsT(to_bf(wbf["L0_attn_Wv"]))
    ident = np.eye(128, dtype=BF)
    nc = build_a1()
    in_maps = []
    for c in range(NCORES):
        cosb, sinb = rope_tables((c % 4) * TC, TC)
        in_maps.append({"x": x_slices[c], "wq": wq_img, "wk": wk_img,
                        "wv": wv_img, "ctab": cosb, "stab": sinb, "idin": ident})
    return run_bass_kernel_spmd(nc, in_maps, core_ids=list(range(NCORES))).results






class LNEmitter:

    def __init__(self, p, nc, ident, pool, tag, alloc=None):
        self.p, self.nc, self.ident, self.pool, self.tag = p, nc, ident, pool, tag
        if alloc is None:
            def alloc(name, shape, dt_):
                return nc.alloc_sbuf_tensor(name, shape, dt_)
        self.stats = alloc(tag + "_stats", [128, 48], F32)
        self.mv = alloc(tag + "_mv", [128, 2], F32)
        self.rstd = alloc(tag + "_rstd", [128, 1], F32)
        self.lntm = alloc(tag + "_lntm", [128, D], BF16)
        self.last_tr = None

    def chunk(self, buf, ready, img_dst):
        p = self.p
        for j in range(8):
            p.op("vector", lambda E, j=j, buf=buf: E.bn_stats(
                out=self.stats[:, 6 * j:6 * j + 6],
                in_=buf[:, 512 * j:512 * (j + 1)]),
                waits=[ready] if j == 0 else [])
        p.op("vector", lambda E: E.bn_aggr(out=self.mv[:, :], in_=self.stats[:, 0:48]))
        vt = p.op("vector", lambda E: E.tensor_scalar_add(
            self.rstd[:, :], self.mv[:, 1:2], EPS))
        p.op("scalar", lambda E: E.activation(out=self.rstd[:, :], in_=self.rstd[:, :],
                                              func=AF.Ln), waits=[vt])
        st = p.op("scalar", lambda E: E.activation(out=self.rstd[:, :],
                                                   in_=self.rstd[:, :],
                                                   func=AF.Exp, scale=-0.5))
        nw = [st] + ([self.last_tr] if self.last_tr else [])
        rel = p.op("vector", lambda E, buf=buf: E.tensor_scalar(
            out=self.lntm[:, :], in0=buf[:, :], scalar1=self.mv[:, 0:1],
            scalar2=self.rstd[:, :], op0=ALU.subtract, op1=ALU.mult), waits=nw)
        ev = None
        for g in range(4):
            dst, extraw = img_dst(g)
            w = ([rel] if g == 0 else []) + ([self.pool["rel"]] if self.pool["rel"] else [])
            tt = None
            for u in range(8):
                kc = g * 8 + u
                tt = p.op("tensor", lambda E, u=u, kc=kc: E.transpose(
                    self.pool["ap"][:, 128 * u:128 * (u + 1)],
                    self.lntm[:, 128 * kc:128 * (kc + 1)], self.ident[:, :]),
                    waits=w if u == 0 else [],
                    inc=(self.tag + "_tr") if u == 7 else None)
            self.last_tr = tt
            ev = p.op("vector", lambda E, dst=dst: E.tensor_copy(
                out=dst, in_=self.pool["ap"][:, :].rearrange("q (u t) -> q u t", u=8)),
                waits=[tt] + [x for x in extraw if x])
            self.pool["rel"] = ev
        return rel, ev



SCL = 1.0 / float(np.sqrt(np.float32(128.0)))
EBIAS = -20.0


def build_a2(dbg=False):
    p = Prog("a2")
    nc = p.nc
    q_i = nc.dram_tensor("q_i", [NH, 128, TC], BF16, kind="ExternalInput").ap()
    kin = nc.dram_tensor("kin", [2, NKV, 128, T], BF16, kind="ExternalInput").ap()
    vin = nc.dram_tensor("vin", [2, NKV, 128, T], BF16, kind="ExternalInput").ap()
    vcol = nc.dram_tensor("vcol", [128, 64], BF16, kind="ExternalInput").ap()
    tri = nc.dram_tensor("tri", [128, 2048], BF16, kind="ExternalInput").ap()
    wo = nc.dram_tensor("wo", [32, 128, 4096], BF16, kind="ExternalInput").ap()
    x = nc.dram_tensor("x", [TC, D], F32, kind="ExternalInput").ap()
    onesr = nc.dram_tensor("onesr", [1, 128], F32, kind="ExternalInput").ap()
    idin = nc.dram_tensor("idin", [128, 128], BF16, kind="ExternalInput").ap()
    x1_o = nc.dram_tensor("x1_o", [TC, D], F32, kind="ExternalOutput").ap()
    if dbg:
        d_pt = nc.dram_tensor("d_pt", [128, 16384], BF16, kind="ExternalOutput").ap()
        d_l = nc.dram_tensor("d_l", [1, 512], F32, kind="ExternalOutput").ap()
        d_rl = nc.dram_tensor("d_rl", [128, 512], F32, kind="ExternalOutput").ap()
        d_att = nc.dram_tensor("d_att", [128, 512], BF16, kind="ExternalOutput").ap()
    ln_o = nc.dram_tensor("ln_o", [8, 128, 4096], BF16, kind="ExternalOutput").ap()
    pt_o = nc.dram_tensor("pt_o", [128, 32], F32, kind="ExternalOutput").ap()

    kb = [nc.alloc_sbuf_tensor(f"kb{i}", [128, T], BF16) for i in range(2)]
    vb = [nc.alloc_sbuf_tensor(f"vb{i}", [128, T], BF16) for i in range(2)]
    qb_ = [nc.alloc_sbuf_tensor(f"qbuf{i}", [128, 512], BF16) for i in range(2)]
    pt = nc.alloc_sbuf_tensor("pt", [128, 32 * 512], BF16)
    att = nc.alloc_sbuf_tensor("att", [128, 32 * 512], BF16)
    tri_sb = nc.alloc_sbuf_tensor("tri_sb", [128, 2048], BF16)
    vcol_sb = nc.alloc_sbuf_tensor("vcol_sb", [128, 64], BF16)
    lsb = nc.alloc_sbuf_tensor("lsb", [1, 512], F32)
    rlsb = nc.alloc_sbuf_tensor("rlsb", [128, 512], F32)
    ones_sb = nc.alloc_sbuf_tensor("ones_sb", [1, 128], F32)
    wob = [nc.alloc_sbuf_tensor(f"wob{i}", [128, 4096], BF16) for i in range(2)]
    projsb = nc.alloc_sbuf_tensor("projsb", [128, 32 * 512], BF16)
    xres = [nc.alloc_sbuf_tensor(f"xres{i}", [128, D], F32) for i in range(2)]
    ident = nc.alloc_sbuf_tensor("ident", [128, 128], BF16)
    strip = nc.alloc_sbuf_tensor("strip", [128, 4096], BF16)
    pacc = nc.alloc_sbuf_tensor("pacc", [128, 8 * 32], F32)
    ptls = nc.alloc_sbuf_tensor("ptls", [128, 32], F32)
    ebias = nc.alloc_sbuf_tensor("ebias", [128, 1], F32)

    pss = [nc.alloc_psum_tensor(f"pss{i}", [128, 1024], F32) for i in range(2)]
    psl = nc.alloc_psum_tensor("psl", [128, 512], F32)
    psav = nc.alloc_psum_tensor("psav", [128, 512], F32)
    psrl = nc.alloc_psum_tensor("psrl", [128, 512], F32)
    psb = nc.alloc_psum_tensor("psb", [128, 1024], BF16)

    p.dma("sync", tri_sb[:, :], tri[:, :], "cld")
    p.dma("sync", vcol_sb[:, :], vcol[:, :], "cld")
    p.dma("sync", ones_sb[:, :], onesr[:, :], "cld")
    cld = p.dma("sync", ident[:, :], idin[:, :], "cld")
    p.barrier("tensor", waits=[cld])
    p.barrier("vector", waits=[cld])
    p.barrier("scalar", waits=[cld])

    ebt = p.op("vector", lambda E: E.memset(ebias[:, :], EBIAS))
    p.barrier("scalar", waits=[ebt])
    pool = {"ap": psb, "rel": None}
    lne = LNEmitter(p, nc, ident, pool, "a2ln")
    stripap = strip[:, :].rearrange("q (kc t) -> q kc t", t=128)

    kbrel = [None, None]
    vbrel = [None, None]
    qrel = [None, None]
    exp_tok = [None, None]
    prev_av = None
    prev_evl = None
    prev_evrl = None
    prev_nrm = None
    wo_last_mm = None
    evp_last = None
    tr_after_qb = None
    lnrel, x1st = {}, {}
    strip_st = None
    kt_cur = [None, None]
    vt_cur = [None, None]

    for qb in range(2):
        nrm = None
        for h in range(NH):
            hk = h // 4
            hb = hk % 2
            if h % 4 == 0:
                kt_cur[hb] = p.dma("sync", kb[hb][:, :], kin[qb][hk][:, :],
                                   f"kld{hb}",
                                   waits=[kbrel[hb]] if kbrel[hb] else [])
                vt_cur[hb] = p.dma("sync", vb[hb][:, :], vin[qb][hk][:, :],
                                   f"vld{hb}",
                                   waits=[vbrel[hb]] if vbrel[hb] else [])
            qt = p.dma("sync", qb_[h % 2][:, :], q_i[h][:, 512 * qb:512 * (qb + 1)],
                       f"qld{h % 2}", waits=[qrel[h % 2]] if qrel[h % 2] else [])
            smm = None
            for g in range(16):
                pp = g % 2
                for u in range(2):
                    kc = 2 * g + u
                    w = []
                    if u == 0:
                        if exp_tok[pp] is not None:
                            w.append(exp_tok[pp])
                        if g == 0:
                            w += [kt_cur[hb], qt]
                            if evp_last is not None:
                                w.append(evp_last)
                                evp_last = None
                    smm = p.op("tensor", lambda E, pp=pp, u=u, kc=kc, hb=hb, hq=h % 2:
                               E.matmul(pss[pp][:, 512 * u:512 * (u + 1)],
                                        kb[hb][:, 128 * kc:128 * (kc + 1)],
                                        qb_[hq][:, :], start=True, stop=True),
                               waits=w, inc="a2s" if u == 1 else None)
                ew = [smm]
                if g <= 1 and prev_av is not None:
                    ew.append(prev_av)
                exp_tok[pp] = p.op("scalar", lambda E, pp=pp, g=g: E.activation(
                    out=pt[:, 1024 * g:1024 * (g + 1)], in_=pss[pp][:, :],
                    func=AF.Exp, scale=SCL, bias=ebias[:, :]), waits=ew)
            kbrel[hb] = smm
            qrel[h % 2] = smm
            mk = p.op("vector", lambda E: E.tensor_mul(
                pt[:, 28 * 512:32 * 512], pt[:, 28 * 512:32 * 512], tri_sb[:, :]),
                waits=[exp_tok[0], exp_tok[1]])
            lmm = None
            for kc in range(32):
                w = []
                if kc == 0:
                    w = [mk] + ([prev_evl] if prev_evl else [])
                lmm = p.op("tensor", lambda E, kc=kc, qb=qb: E.matmul(
                    psl[0:1, :], vcol_sb[:, 32 * qb + kc:32 * qb + kc + 1],
                    pt[:, 512 * kc:512 * (kc + 1)],
                    start=(kc == 0), stop=(kc == 31)),
                    waits=w, inc="a2l" if kc == 31 else None)
            avm = None
            for kc in range(32):
                w = []
                if kc == 0:
                    w = [vt_cur[hb]] + ([prev_nrm] if prev_nrm else [])
                avm = p.op("tensor", lambda E, kc=kc, hb=hb: E.matmul(
                    psav[:, :], vb[hb][:, 128 * kc:128 * (kc + 1)],
                    pt[:, 512 * kc:512 * (kc + 1)],
                    start=(kc == 0), stop=(kc == 31)),
                    waits=w, inc="a2av" if kc == 31 else None)
            vbrel[hb] = avm
            prev_av = avm
            evl = p.op("vector", lambda E: E.tensor_copy(out=lsb[:, :],
                                                         in_=psl[0:1, :]),
                       waits=[lmm])
            prev_evl = evl
            rcp = p.op("vector", lambda E: E.reciprocal(lsb[:, :], lsb[:, :]))
            bc = p.op("tensor", lambda E: E.matmul(psrl[:, :], ones_sb[:, :],
                                                   lsb[:, :], start=True, stop=True),
                      waits=[rcp] + ([prev_evrl] if prev_evrl else []), inc="a2rl")
            evrl = p.op("vector", lambda E: E.tensor_copy(out=rlsb[:, :],
                                                          in_=psrl[:, :]), waits=[bc])
            prev_evrl = evrl
            nw = [avm]
            if h == 0 and wo_last_mm is not None:
                nw.append(wo_last_mm)
            nrm = p.op("vector", lambda E, h=h: E.tensor_mul(
                att[:, 512 * h:512 * (h + 1)], psav[:, :], rlsb[:, :]), waits=nw)
            prev_nrm = nrm
            if dbg and qb == 0 and h == 0:
                p.dma("sync", d_pt[:, :], pt[:, :], "dbg", waits=[mk])
                p.dma("sync", d_l[:, :], lsb[:, :], "dbg", waits=[rcp])
                p.dma("sync", d_rl[:, :], rlsb[:, :], "dbg", waits=[evrl])
                p.dma("sync", d_att[:, :], att[:, 0:512], "dbg", waits=[nrm])
                prev_nrm = ("dbg", p.val("dbg"))
                prev_evl = prev_nrm
                prev_av = None
                p.barrier("tensor", waits=[("dbg", p.val("dbg"))])
        womm, evp = {}, {}
        for mc in range(32):
            ww = [womm[mc - 2]] if mc >= 2 else (
                [wo_last_mm] if wo_last_mm is not None else [])
            wt = p.dma("sync", wob[mc % 2][:, :], wo[mc][:, :], f"wld{mc % 2}",
                       waits=ww)
            mmt = None
            for kc in range(32):
                w = []
                if kc == 0:
                    w = [wt, nrm]
                    if mc >= 2:
                        w.append(evp[mc - 2])
                    if mc <= 1 and tr_after_qb is not None:
                        w.append(tr_after_qb)
                mmt = p.op("tensor", lambda E, kc=kc, mc=mc: E.matmul(
                    pss[mc % 2][:, 0:512], wob[mc % 2][:, 128 * kc:128 * (kc + 1)],
                    att[:, 512 * kc:512 * (kc + 1)],
                    start=(kc == 0), stop=(kc == 31)),
                    waits=w, inc="a2wo" if kc == 31 else None)
            womm[mc] = mmt
            evp[mc] = p.op("vector", lambda E, mc=mc: E.tensor_copy(
                out=projsb[:, 512 * mc:512 * (mc + 1)], in_=pss[mc % 2][:, 0:512]),
                waits=[mmt])
        wo_last_mm = womm[31]
        evp_last = evp[31]
        for t in range(4):
            i = 4 * qb + t
            xw = []
            if i >= 2:
                xw = [lnrel[i - 2], x1st[i - 2]]
            xt = p.dma("sync", xres[i % 2][:, :], x[128 * i:128 * (i + 1), :],
                       f"xld{i % 2}", waits=xw)
            add = None
            for g2 in range(4):
                w = [evp[31]] if g2 == 0 else []
                if pool["rel"] is not None:
                    w.append(pool["rel"])
                tt = None
                for u in range(8):
                    mcc = 8 * g2 + u
                    tt = p.op("tensor", lambda E, u=u, mcc=mcc, t=t: E.transpose(
                        psb[:, 128 * u:128 * (u + 1)],
                        projsb[:, 512 * mcc + 128 * t:512 * mcc + 128 * (t + 1)],
                        ident[:, :]),
                        waits=w if u == 0 else [], inc="a2tr" if u == 7 else None)
                add = p.op("vector", lambda E, g2=g2, i=i: E.tensor_add(
                    xres[i % 2][:, 1024 * g2:1024 * (g2 + 1)],
                    xres[i % 2][:, 1024 * g2:1024 * (g2 + 1)], psb[:, :]),
                    waits=[tt, xt])
                pool["rel"] = add
            x1st[i] = p.dma("sync", x1_o[128 * i:128 * (i + 1), :], xres[i % 2][:, :],
                            f"x1st{i % 2}", waits=[add])
            rel, ev = lne.chunk(xres[i % 2], add,
                                lambda g, st_=strip_st: (stripap[:, g * 8:(g + 1) * 8, :],
                                                         [st_]))
            lnrel[i] = rel
            prd = p.op("vector", lambda E, i=i: E.tensor_reduce(
                pacc[:, 32 * i:32 * (i + 1)], stripap[:, :, :], AX.X, ALU.add),
                waits=[ev])
            strip_st = p.dma("sync", ln_o[i][:, :], strip[:, :], "lnst", waits=[prd])
        tr_after_qb = ("a2tr", p.val("a2tr"))
    fr = p.op("vector", lambda E: E.tensor_reduce(
        ptls[:, :], pacc[:, :].rearrange("q (i k) -> q k i", i=8), AX.X, ALU.add))
    p.dma("sync", pt_o[:, :], ptls[:, :], "ptst", waits=[fr])
    p.barrier("sync", waits=[("x1st0", p.val("x1st0")), ("x1st1", p.val("x1st1")),
                             ("lnst", p.val("lnst")), ("ptst", p.val("ptst"))])
    return p.build()


def prep_a2_core(c, a1res, wo_img, x_slices, tri_m, onesr, ident):
    j = c % 4
    bc0 = 4 * (c // 4)
    Kf = [np.concatenate([np.asarray(a1res[cc]["k_o"])[hk] for cc in
                          range(bc0, bc0 + 4)], axis=1) for hk in range(NKV)]
    Vf = [np.concatenate([np.asarray(a1res[cc]["v_o"])[hk] for cc in
                          range(bc0, bc0 + 4)], axis=1) for hk in range(NKV)]
    kin = np.zeros((2, NKV, 128, T), BF)
    vin = np.zeros((2, NKV, 128, T), BF)
    vcol = np.zeros((128, 64), BF)
    for b in range(2):
        d0 = 8 * j + 4 * b
        for kc in range(32):
            a = (kc - 28 + d0) % 32
            valid = (kc >= 28) or (a < d0)
            if valid:
                vcol[:, 32 * b + kc] = BF(1.0)
            for hk in range(NKV):
                kin[b, hk, :, 128 * kc:128 * (kc + 1)] = Kf[hk][:, 128 * a:128 * (a + 1)]
                if valid:
                    vin[b, hk, :, 128 * kc:128 * (kc + 1)] = \
                        Kf[hk][:, 0:0] if False else Vf[hk][:, 128 * a:128 * (a + 1)].T
    return {"q_i": np.asarray(a1res[c]["q_o"]), "kin": kin, "vin": vin,
            "vcol": vcol, "tri": tri_m, "wo": wo_img, "x": x_slices[c],
            "onesr": onesr, "idin": ident}


def run_a2(x_slices, a1res, wbf):
    wo_img = pack_lhsT(to_bf(wbf["L0_attn_Wo"]))
    tri_m = np.zeros((128, 2048), BF)
    for i_ in range(4):
        for pp in range(128):
            tri_m[pp, 512 * i_:512 * (i_ + 1)] = (128 * i_ + pp <=
                                                  np.arange(512)).astype(BF)
    onesr = np.ones((1, 128), np.float32)
    ident = np.eye(128, dtype=BF)
    nc = build_a2()
    in_maps = [prep_a2_core(c, a1res, wo_img, x_slices, tri_m, onesr, ident)
               for c in range(NCORES)]
    return run_bass_kernel_spmd(nc, in_maps, core_ids=list(range(NCORES))).results




def build_f2(dbg=False):
    p = Prog("f2")
    nc = p.nc
    lnin = nc.dram_tensor("lnin", [128, 32 * TC], BF16, kind="ExternalInput").ap()
    x1 = nc.dram_tensor("x1", [TC, D], F32, kind="ExternalInput").ap()
    parts = nc.dram_tensor("parts", [4, 128, 32], F32, kind="ExternalInput").ap()
    worg = nc.dram_tensor("worg", [32, 128, 4096], BF16, kind="ExternalInput").ap()
    upw = nc.dram_tensor("upw", [32, 128, 512], BF16, kind="ExternalInput").ap()
    dnw = nc.dram_tensor("dnw", [4, 128, 4096], BF16, kind="ExternalInput").ap()
    wqt = nc.dram_tensor("wqt", [1, 128, 4096], BF16, kind="ExternalInput").ap()
    keyt = nc.dram_tensor("keyt", [128, 8], BF16, kind="ExternalInput").ap()
    sel = nc.dram_tensor("sel", [8, 512], BF16, kind="ExternalInput").ap()
    ones1 = nc.dram_tensor("ones1", [1, 1], BF16, kind="ExternalInput").ap()
    idin = nc.dram_tensor("idin", [128, 128], BF16, kind="ExternalInput").ap()
    x2_o = nc.dram_tensor("x2_o", [TC, D], F32, kind="ExternalOutput").ap()
    ln_o = nc.dram_tensor("ln_o", [8, 128, 4096], BF16, kind="ExternalOutput").ap()
    if dbg:
        d_mean = nc.dram_tensor("d_mean", [128, 32], F32, kind="ExternalOutput").ap()
        d_q = nc.dram_tensor("d_q", [128, 1], BF16, kind="ExternalOutput").ap()
        d_co = nc.dram_tensor("d_co", [1, 8], BF16, kind="ExternalOutput").ap()
        d_cv = nc.dram_tensor("d_cv", [128, 4], F32, kind="ExternalOutput").ap()
        d_xd = nc.dram_tensor("d_xd", [128, 4 * TC], BF16, kind="ExternalOutput").ap()

    img = nc.alloc_sbuf_tensor("img", [128, 32 * TC], BF16)
    wb = [nc.alloc_sbuf_tensor(f"wb{i}", [128, 4096], BF16) for i in range(2)]
    ub = [nc.alloc_sbuf_tensor(f"ub{i}", [128, 512], BF16) for i in range(2)]
    parts_sb = nc.alloc_sbuf_tensor("parts_sb", [128, 4 * 32], F32)
    mean_sb = nc.alloc_sbuf_tensor("mean_sb", [128, 32], F32)
    mean_bf = nc.alloc_sbuf_tensor("mean_bf", [128, 32], BF16)
    qsb = nc.alloc_sbuf_tensor("qsb", [128, 1], BF16)
    keyt_sb = nc.alloc_sbuf_tensor("keyt_sb", [128, 8], BF16)
    sel_sb = nc.alloc_sbuf_tensor("sel_sb", [8, 512], BF16)
    ones1_sb = nc.alloc_sbuf_tensor("ones1_sb", [1, 1], BF16)
    ssb = nc.alloc_sbuf_tensor("ssb", [1, 8], F32)
    nm = nc.alloc_sbuf_tensor("nm", [1, 1], F32)
    esum = nc.alloc_sbuf_tensor("esum", [1, 1], F32)
    co_bf = nc.alloc_sbuf_tensor("co_bf", [1, 8], BF16)
    cocol = nc.alloc_sbuf_tensor("cocol", [8, 1], BF16)
    cvec = nc.alloc_sbuf_tensor("cvec", [128, 4], F32)
    xds = nc.alloc_sbuf_tensor("xds", [128, 4 * TC], BF16)
    projsb = nc.alloc_sbuf_tensor("projsb", [128, 32 * 512], BF16)
    xres = [nc.alloc_sbuf_tensor(f"xres{i}", [128, D], F32) for i in range(2)]
    ident = nc.alloc_sbuf_tensor("ident", [128, 128], BF16)
    strip = nc.alloc_sbuf_tensor("strip", [128, 4096], BF16)

    ps = [nc.alloc_psum_tensor(f"f2ps{i}", [128, 1024], F32) for i in range(2)]
    ps1 = nc.alloc_psum_tensor("f2psq", [128, 512], F32)
    psb = nc.alloc_psum_tensor("f2psb", [128, 1024], BF16)

    imgap = img[:, :].rearrange("q (kc tt) -> q kc tt", tt=TC)
    stripap = strip[:, :].rearrange("q (kc t) -> q kc t", t=128)

    ildt = p.dma("sync", img[:, :], lnin[:, :], "ild")
    p.dma("sync", parts_sb[:, :].rearrange("q (j k) -> q j k", j=4),
          parts[:, :].rearrange("j q k -> q j k"), "cld")
    p.dma("sync", keyt_sb[:, :], keyt[:, :], "cld")
    p.dma("sync", sel_sb[:, :], sel[:, :], "cld")
    p.dma("sync", ones1_sb[:, :], ones1[:, :], "cld")
    cld = p.dma("sync", ident[:, :], idin[:, :], "cld")
    p.barrier("vector", waits=[cld])
    p.barrier("tensor", waits=[cld, ildt])

    p.op("vector", lambda E: E.tensor_add(mean_sb[:, :], parts_sb[:, 0:32],
                                          parts_sb[:, 32:64]))
    p.op("vector", lambda E: E.tensor_add(mean_sb[:, :], mean_sb[:, :],
                                          parts_sb[:, 64:96]))
    p.op("vector", lambda E: E.tensor_add(mean_sb[:, :], mean_sb[:, :],
                                          parts_sb[:, 96:128]))
    mt = p.op("vector", lambda E: E.tensor_scalar_mul(mean_bf[:, :], mean_sb[:, :],
                                                      1.0 / 4096.0))
    wqld = p.dma("sync", wb[0][:, :], wqt[0][:, :], "wld0")
    qmm = None
    for kc in range(32):
        qmm = p.op("tensor", lambda E, kc=kc: E.matmul(
            ps1[:, 0:1], wb[0][:, 128 * kc:128 * (kc + 1)],
            mean_bf[:, kc:kc + 1], start=(kc == 0), stop=(kc == 31)),
            waits=[wqld, mt] if kc == 0 else [], inc="qmm" if kc == 31 else None)
    qe = p.op("vector", lambda E: E.tensor_copy(out=qsb[:, :], in_=ps1[:, 0:1]),
              waits=[qmm])
    smm = p.op("tensor", lambda E: E.matmul(ps1[0:1, 4:12], qsb[:, :],
                                            keyt_sb[:, :], start=True, stop=True),
               waits=[qe], inc="smm")
    se = p.op("vector", lambda E: E.tensor_scalar_mul(ssb[:, :], ps1[0:1, 4:12],
                                                      SCL), waits=[smm])
    p.op("vector", lambda E: E.reduce_max(nm[:, :], ssb[:, :], axis=AX.X,
                                          negate=True))
    et = p.op("scalar", lambda E: E.activation(out=ssb[:, :], in_=ssb[:, :],
                                               func=AF.Exp, bias=nm[:, :],
                                               accum_out=esum[:, :]),
              waits=[p.engine_done_tok("vector")])
    p.op("vector", lambda E: E.reciprocal(esum[:, :], esum[:, :]), waits=[et])
    ct = p.op("vector", lambda E: E.tensor_scalar_mul(co_bf[:, :], ssb[:, :],
                                                      esum[:, :]))
    ctm = p.op("tensor", lambda E: E.matmul(ps1[0:8, 16:17], co_bf[:, :],
                                            ones1_sb[:, :], start=True, stop=True),
               waits=[ct], inc="ctm")
    cce = p.op("vector", lambda E: E.tensor_copy(out=cocol[:, :],
                                                 in_=ps1[0:8, 16:17]), waits=[ctm])
    vmm = None
    for j in range(4):
        vmm = p.op("tensor", lambda E, j=j: E.matmul(
            ps1[:, 20 + j:21 + j], sel_sb[:, 128 * j:128 * (j + 1)], cocol[:, :],
            start=True, stop=True), waits=[cce] if j == 0 else [],
            inc="vmm" if j == 3 else None)
    cvt = p.op("vector", lambda E: E.tensor_copy(out=cvec[:, :], in_=ps1[:, 20:24]),
               waits=[vmm])
    if dbg:
        p.dma("sync", d_mean[:, :], mean_sb[:, :], "dbg", waits=[mt])
        p.dma("sync", d_q[:, :], qsb[:, :], "dbg", waits=[qe])
        p.dma("sync", d_co[:, :], co_bf[:, :], "dbg", waits=[ct])
        p.dma("sync", d_cv[:, :], cvec[:, :], "dbg", waits=[cvt])

    dnt = {}
    xde = {}
    prev_dn_mm = None
    for j in range(4):
        dw = []
        if j == 1:
            dw = [qmm]
        elif j >= 2:
            dw = [prev_dn_mm]
        dnt[j] = p.dma("sync", wb[(j + 1) % 2][:, :], dnw[j][:, :],
                       f"wld{(j + 1) % 2}", waits=dw)
        mmt = None
        for n in range(2):
            w0 = [dnt[j]]
            if j >= 2:
                w0.append(xde[(j - 2, n)])
            for kc in range(32):
                mmt = p.op("tensor", lambda E, kc=kc, n=n, j=j: E.matmul(
                    ps[j % 2][:, 512 * n:512 * (n + 1)],
                    wb[(j + 1) % 2][:, 128 * kc:128 * (kc + 1)],
                    imgap[:, kc, 512 * n:512 * (n + 1)],
                    start=(kc == 0), stop=(kc == 31)),
                    waits=w0 if kc == 0 else [],
                    inc="f2xd" if kc == 31 else None)
            xde[(j, n)] = p.op("vector", lambda E, j=j, n=n: E.tensor_scalar(
                out=xds[:, TC * j + 512 * n:TC * j + 512 * (n + 1)],
                in0=ps[j % 2][:, 512 * n:512 * (n + 1)],
                scalar1=cvec[:, j:j + 1], scalar2=None, op0=ALU.mult),
                waits=[mmt, cvt] if n == 0 else [mmt])
        prev_dn_mm = mmt
    if dbg:
        p.dma("sync", d_xd[:, :], xds[:, :], "dbg", waits=[xde[(3, 1)]])

    pool = {"ap": psb, "rel": None}
    lne = LNEmitter(p, nc, ident, pool, "f2ln")
    lnrel, x2st = {}, {}
    strip_st = None
    tr_after = None
    last_wo_of_half = None
    for half in range(2):
        evp = {}
        womm = {}
        for mc in range(32):
            ww = []
            if mc >= 2:
                ww = [womm[mc - 2]]
            elif half == 0:
                ww = [xde[(3, 1)]]
            else:
                ww = [last_wo_of_half]
            wt = p.dma("sync", wb[mc % 2][:, :], worg[mc][:, :], f"wld{mc % 2}",
                       waits=ww)
            ut = p.dma("sync", ub[mc % 2][:, :], upw[mc][:, :], f"uld{mc % 2}",
                       waits=ww)
            mmt = None
            w0 = [wt, ut]
            if mc >= 2:
                w0.append(evp[mc - 2])
            if mc <= 1 and tr_after is not None:
                w0.append(tr_after)
            for kc in range(32):
                mmt = p.op("tensor", lambda E, kc=kc, mc=mc, half=half: E.matmul(
                    ps[mc % 2][:, 0:512], wb[mc % 2][:, 128 * kc:128 * (kc + 1)],
                    imgap[:, kc, 512 * half:512 * (half + 1)],
                    start=(kc == 0), stop=False),
                    waits=w0 if kc == 0 else [])
            for j in range(4):
                mmt = p.op("tensor", lambda E, j=j, mc=mc, half=half: E.matmul(
                    ps[mc % 2][:, 0:512], ub[mc % 2][:, 128 * j:128 * (j + 1)],
                    xds[:, TC * j + 512 * half:TC * j + 512 * (half + 1)],
                    start=False, stop=(j == 3)),
                    waits=[], inc="f2wo" if j == 3 else None)
            womm[mc] = mmt
            evp[mc] = p.op("vector", lambda E, mc=mc: E.tensor_copy(
                out=projsb[:, 512 * mc:512 * (mc + 1)], in_=ps[mc % 2][:, 0:512]),
                waits=[mmt])
        last_wo_of_half = womm[31]
        for t in range(4):
            i = 4 * half + t
            xw = []
            if i >= 2:
                xw = [lnrel[i - 2], x2st[i - 2]]
            xt = p.dma("sync", xres[i % 2][:, :], x1[128 * i:128 * (i + 1), :],
                       f"xld{i % 2}", waits=xw)
            add = None
            for g2 in range(4):
                w = [evp[31]] if g2 == 0 else []
                if pool["rel"] is not None:
                    w.append(pool["rel"])
                tt = None
                for u in range(8):
                    mcc = 8 * g2 + u
                    tt = p.op("tensor", lambda E, u=u, mcc=mcc, t=t: E.transpose(
                        psb[:, 128 * u:128 * (u + 1)],
                        projsb[:, 512 * mcc + 128 * t:512 * mcc + 128 * (t + 1)],
                        ident[:, :]),
                        waits=w if u == 0 else [], inc="f2tr" if u == 7 else None)
                add = p.op("vector", lambda E, g2=g2, i=i: E.tensor_add(
                    xres[i % 2][:, 1024 * g2:1024 * (g2 + 1)],
                    xres[i % 2][:, 1024 * g2:1024 * (g2 + 1)], psb[:, :]),
                    waits=[tt, xt])
                pool["rel"] = add
            x2st[i] = p.dma("sync", x2_o[128 * i:128 * (i + 1), :], xres[i % 2][:, :],
                            f"x2st{i % 2}", waits=[add])
            rel, ev = lne.chunk(xres[i % 2], add,
                                lambda g, st_=strip_st: (stripap[:, g * 8:(g + 1) * 8, :],
                                                         [st_]))
            lnrel[i] = rel
            strip_st = p.dma("sync", ln_o[i][:, :], strip[:, :], "lnst", waits=[ev])
        tr_after = ("f2tr", p.val("f2tr"))
    p.barrier("sync", waits=[("x2st0", p.val("x2st0")), ("x2st1", p.val("x2st1")),
                             ("lnst", p.val("lnst"))])
    return p.build()


def assemble_lnimg(blocks):
    a = np.asarray(blocks).reshape(8, 128, 32, 128)
    return np.ascontiguousarray(a.transpose(1, 2, 0, 3).reshape(128, 32 * 1024))


def run_f2(a2res, wbf):
    worg_img = pack_lhsT(to_bf(np.asarray(wbf["L0_ffn_W_org"]).T))
    up_img = pack_lhsT(np.ascontiguousarray(
        np.asarray(wbf["L0_ffn_lora_up"]).reshape(512, 4096)))
    dn_img = pack_lhsT(np.ascontiguousarray(
        np.asarray(wbf["L0_ffn_lora_down"]).transpose(1, 0, 2).reshape(4096, 512)))
    wq_img = pack_lhsT(to_bf(np.asarray(wbf["L0_ffn_Wq"]).T))
    keyt = np.ascontiguousarray(np.asarray(wbf["L0_ffn_lora_keys"]).T)
    sel = np.zeros((8, 512), BF)
    for j in range(4):
        for pp in range(128):
            sel[2 * j + pp // 64, 128 * j + pp] = BF(1.0)
    ones1 = np.ones((1, 1), BF)
    ident = np.eye(128, dtype=BF)
    nc = build_f2()
    in_maps = []
    for c in range(NCORES):
        bc0 = 4 * (c // 4)
        parts = np.stack([np.asarray(a2res[cc]["pt_o"]) for cc in
                          range(bc0, bc0 + 4)])
        in_maps.append({"lnin": assemble_lnimg(a2res[c]["ln_o"]),
                        "x1": np.asarray(a2res[c]["x1_o"]),
                        "parts": parts, "worg": worg_img, "upw": up_img,
                        "dnw": dn_img, "wqt": wq_img, "keyt": keyt, "sel": sel,
                        "ones1": ones1, "idin": ident})
    return run_bass_kernel_spmd(nc, in_maps, core_ids=list(range(NCORES))).results



MNEG = -30000.0


def build_m1(nchunks=8, no_gate=False, no_conv=False, no_free=False, lv=0):
    p = Prog("m1")
    nc = p.nc
    TOV = TC + 3
    lnin = nc.dram_tensor("lnin", [128, 32 * TOV], BF16, kind="ExternalInput").ap()
    wip = nc.dram_tensor("wip", [81, 128, 4096], BF16, kind="ExternalInput").ap()
    convw = nc.dram_tensor("convw", [128, 192], F32, kind="ExternalInput").ap()
    convb = nc.dram_tensor("convb", [128, 48], F32, kind="ExternalInput").ap()
    dtb = nc.dram_tensor("dtb", [64, 1], F32, kind="ExternalInput").ap()
    alog = nc.dram_tensor("alog", [64, 1], F32, kind="ExternalInput").ap()
    dd = nc.dram_tensor("dd", [1, 64], F32, kind="ExternalInput").ap()
    trisu = nc.dram_tensor("trisu", [128, 128], BF16, kind="ExternalInput").ap()
    maskt = nc.dram_tensor("maskt", [128, 128], F32, kind="ExternalInput").ap()
    onesc = nc.dram_tensor("onesc", [128, 1], BF16, kind="ExternalInput").ap()
    onesr = nc.dram_tensor("onesr", [1, 128], F32, kind="ExternalInput").ap()
    idbf = nc.dram_tensor("idbf", [128, 128], BF16, kind="ExternalInput").ap()
    idf = nc.dram_tensor("idf", [128, 128], F32, kind="ExternalInput").ap()
    selb = nc.dram_tensor("selb", [64, 4096], BF16, kind="ExternalInput").ap()
    y_o = nc.dram_tensor("y_o", [8, 128, 4096], F32, kind="ExternalOutput").ap()
    c_o = nc.dram_tensor("c_o", [8, 128, 1024], BF16, kind="ExternalOutput").ap()
    el_o = nc.dram_tensor("el_o", [8, 128, 64], F32, kind="ExternalOutput").ap()
    dc_o = nc.dram_tensor("dc_o", [8, 1, 64], F32, kind="ExternalOutput").ap()
    gt_o = nc.dram_tensor("gt_o", [8, 128, 4096], BF16, kind="ExternalOutput").ap()
    u_o = nc.dram_tensor("u_o", [128, 4096], F32, kind="ExternalOutput").ap()
    ds_o = nc.dram_tensor("ds_o", [1, 64], F32, kind="ExternalOutput").ap()

    cum_d = nc.dram_tensor("cum_d", [64, 128], F32).ap()

    xbc = nc.alloc_sbuf_tensor("xbc", [128, 48 * 1024], BF16)
    dt_all = nc.alloc_sbuf_tensor("dt_all", [64, 1024], BF16)
    convw_sb = nc.alloc_sbuf_tensor("convw_sb", [128, 192], F32)
    convb_sb = nc.alloc_sbuf_tensor("convb_sb", [128, 48], F32)
    dtb_sb = nc.alloc_sbuf_tensor("dtb_sb", [64, 1], F32)
    acol = nc.alloc_sbuf_tensor("acol", [64, 1], F32)
    dd_sb = nc.alloc_sbuf_tensor("dd_sb", [1, 64], F32)
    ddcols = nc.alloc_sbuf_tensor("ddcols", [128, 64], F32)
    trisu_sb = nc.alloc_sbuf_tensor("trisu_sb", [128, 128], BF16)
    maskt_sb = nc.alloc_sbuf_tensor("maskt_sb", [128, 128], F32)
    onesc_sb = nc.alloc_sbuf_tensor("onesc_sb", [128, 1], BF16)
    onesr_sb = nc.alloc_sbuf_tensor("onesr_sb", [1, 128], F32)
    idbf_sb = nc.alloc_sbuf_tensor("idbf_sb", [128, 128], BF16)
    idf_sb = nc.alloc_sbuf_tensor("idf_sb", [128, 128], F32)
    ones64 = nc.alloc_sbuf_tensor("ones64", [64, 128], F32)
    sel_sb = nc.alloc_sbuf_tensor("sel_sb", [64, 4096], BF16)

    psA = nc.alloc_psum_tensor("m1psA", [128, 1024], F32)
    psB = nc.alloc_psum_tensor("m1psB", [128, 1024], F32)
    psC = nc.alloc_psum_tensor("m1psC", [128, 512], F32)
    psD = nc.alloc_psum_tensor("m1psD", [128, 512], F32)
    psE = nc.alloc_psum_tensor("m1psE", [128, 1024], BF16)
    psF = nc.alloc_psum_tensor("m1psF", [128, 512], F32)

    p.dma("sync", convw_sb[:, :], convw[:, :], "cld")
    p.dma("sync", convb_sb[:, :], convb[:, :], "cld")
    p.dma("sync", dtb_sb[:, :], dtb[:, :], "cld")
    p.dma("sync", acol[:, :], alog[:, :], "cld")
    p.dma("sync", dd_sb[:, :], dd[:, :], "cld")
    p.dma("sync", trisu_sb[:, :], trisu[:, :], "cld")
    p.dma("sync", maskt_sb[:, :], maskt[:, :], "cld")
    p.dma("sync", onesc_sb[:, :], onesc[:, :], "cld")
    p.dma("sync", onesr_sb[:, :], onesr[:, :], "cld")
    p.dma("sync", idbf_sb[:, :], idbf[:, :], "cld")
    p.dma("sync", sel_sb[:, :], selb[:, :], "cld")
    cld = p.dma("sync", idf_sb[:, :], idf[:, :], "cld")

    if lv >= 2:
        pass
    at = None if lv >= 2 else p.op("scalar", lambda E: E.activation(
        out=acol[:, :], in_=acol[:, :], func=AF.Exp),
        waits=[("cld", p.val("cld"))])
    if lv < 2:
        p.op("vector", lambda E: E.tensor_scalar_mul(acol[:, :], acol[:, :], -1.0),
             waits=[at, ("cld", p.val("cld"))])
        p.op("vector", lambda E: E.memset(ones64[:, :], 1.0))
        ddm = p.op("tensor", lambda E: E.matmul(psD[:, 0:64], onesr_sb[:, :],
                                                dd_sb[:, :], start=True, stop=True),
                   waits=[("cld", p.val("cld"))], inc="m1dd")
        dde = p.op("vector", lambda E: E.tensor_copy(out=ddcols[:, :],
                                                     in_=psD[:, 0:64]),
                   waits=[ddm])
    else:
        dde = None

    import contextlib
    es = contextlib.ExitStack()
    if no_free:
        class _N:
            def enter_context(self, x):
                return x.__enter__()
            def close(self):
                pass
        es = _N()
    img = es.enter_context(nc.sbuf_tensor("img", [128, 32 * TOV], BF16))
    wb = [es.enter_context(nc.sbuf_tensor(f"wb{i}", [128, 4096], BF16))
          for i in range(3)]
    stg = [es.enter_context(nc.sbuf_tensor(f"stg{i}", [128, TOV], BF16))
           for i in range(2)]
    gbuf = es.enter_context(nc.sbuf_tensor("gbuf", [128, 1024], BF16))
    gtb = es.enter_context(nc.sbuf_tensor("gtb", [128, 1024], BF16))
    imgap = img[:, :].rearrange("q (kc tt) -> q kc tt", tt=TOV)
    ilt = p.dma("sync", img[:, :], lnin[:, :], "ild")
    p.barrier("tensor", waits=[("cld", p.val("cld")), ilt])

    stg_rel = {}
    evt_toks = {}
    gt_rel = None
    pe_rel = None
    for mc in range(81):
        pp = mc % 2
        big = psA if pp == 0 else psB
        tiny = psF[:, 0:3] if pp == 0 else psC[:, 0:3]
        wt = p.dma("sync" if mc % 2 == 0 else "scalar",
                   wb[mc % 3][:, :], wip[mc][:, :], f"wld{mc % 3}",
                   waits=[("m1ip", mc - 2)] if mc >= 3 else [])
        mmt = None
        for n in range(3):
            cl = [slice(0, 512), slice(512, 1024), slice(1024, 1027)][n]
            dsts = [big[:, 0:512], big[:, 512:1024], tiny]
            w0 = [wt]
            if mc >= 2:
                w0.append(evt_toks[mc - 2])
            for kc in range(32):
                mmt = p.op("tensor", lambda E, kc=kc, mc=mc, cl=cl, n=n,
                           dsts=dsts: E.matmul(
                    dsts[n], wb[mc % 3][:, 128 * kc:128 * (kc + 1)],
                    imgap[:, kc, cl], start=(kc == 0), stop=(kc == 31)),
                    waits=w0 if (kc == 0 and n == 0) else [],
                    inc="m1ip" if (kc == 31 and n == 2) else None)
        sb = stg[mc % 2]
        ew = [mmt]
        if mc >= 2:
            ew.append(stg_rel[mc - 2])
        p.op("vector", lambda E, sb=sb, big=big: E.tensor_copy(
            out=sb[:, 0:512], in_=big[:, 0:512]), waits=ew)
        p.op("vector", lambda E, sb=sb, big=big: E.tensor_copy(
            out=sb[:, 512:1024], in_=big[:, 512:1024]))
        evt = p.op("vector", lambda E, sb=sb, tiny=tiny: E.tensor_copy(
            out=sb[:, 1024:1027], in_=tiny))
        evt_toks[mc] = evt
        if mc < 32 and no_gate:
            stg_rel[mc] = evt
        elif mc < 32:
            sil = p.op("scalar", lambda E, sb=sb: E.activation(
                out=gbuf[:, :], in_=sb[:, 3:1027], func=AF.Silu),
                waits=[evt] + ([gt_rel] if gt_rel else []))
            tt = None
            for u in range(8):
                tt = p.op("tensor", lambda E, u=u: E.transpose(
                    psE[:, 128 * u:128 * (u + 1)], gbuf[:, 128 * u:128 * (u + 1)],
                    idbf_sb[:, :]),
                    waits=([sil] + ([pe_rel] if pe_rel else [])) if u == 0 else [],
                    inc="m1gt" if u == 7 else None)
            pe_rel = p.op("vector", lambda E: E.tensor_copy(out=gtb[:, :],
                                                            in_=psE[:, :]),
                          waits=[tt])
            gt_rel = p.dma("gpsimd", gt_o[0:8, :, 128 * mc:128 * (mc + 1)]
                           .rearrange("cc q t -> q cc t"),
                           gtb[:, :].rearrange("q (cc t) -> q cc t", t=128),
                           f"gst{mc % 2}", waits=[pe_rel])
            stg_rel[mc] = sil
        elif mc < 80 and no_conv:
            stg_rel[mc] = evt
        elif mc < 80:
            m = mc - 32
            p.op("vector", lambda E, sb=sb, m=m: E.tensor_scalar(
                out=psA[:, 0:0] if False else xbc[0:0, 0:0], in0=sb[0:0, 0:0],
                scalar1=0.0, scalar2=None, op0=ALU.mult)) if False else None
            cvt = p.op("vector", lambda E, sb=sb, m=m: E.tensor_scalar(
                out=xbc[0:128, 1024 * m:1024 * (m + 1)] if False else
                xbc[:, 1024 * m:1024 * (m + 1)],
                in0=sb[:, 0:1024], scalar1=convw_sb[:, 4 * m:4 * m + 1],
                scalar2=convb_sb[:, m:m + 1], op0=ALU.mult, op1=ALU.add),
                waits=[evt])
            for k in range(1, 4):
                cvt = p.op("vector", lambda E, sb=sb, m=m, k=k:
                           E.scalar_tensor_tensor(
                    out=xbc[:, 1024 * m:1024 * (m + 1)], in0=sb[:, k:k + 1024],
                    scalar=convw_sb[:, 4 * m + k:4 * m + k + 1],
                    in1=xbc[:, 1024 * m:1024 * (m + 1)],
                    op0=ALU.mult, op1=ALU.add))
            sil = p.op("scalar", lambda E, m=m: E.activation(
                out=xbc[:, 1024 * m:1024 * (m + 1)],
                in_=xbc[:, 1024 * m:1024 * (m + 1)], func=AF.Silu), waits=[cvt])
            stg_rel[mc] = cvt
        elif lv >= 1:
            stg_rel[mc] = evt
        else:
            stg_rel[mc] = p.op("vector", lambda E, sb=sb: E.tensor_copy(
                out=dt_all[:, :], in_=sb[0:64, 3:1027]))
    silu_done = p.engine_done_tok("scalar")
    ipdone = ("m1ip", 81)
    if nchunks < 0:
        p.barrier("sync", waits=[("gst0", p.val("gst0")), ("gst1", p.val("gst1")),
                                 ipdone])
        p.barrier("tensor", waits=[ipdone])
        return p.build()
    es.close()

    x_tm = nc.alloc_sbuf_tensor("x_tm", [128, 4096], BF16)
    dtx_tm = nc.alloc_sbuf_tensor("dtx_tm", [128, 4096], BF16)
    dtxw_tm = nc.alloc_sbuf_tensor("dtxw_tm", [128, 4096], BF16)
    b_tm = nc.alloc_sbuf_tensor("b_tm", [128, 8 * 128], BF16)
    dpre = [nc.alloc_sbuf_tensor(f"dpre{i}", [128, 128], F32) for i in range(2)]
    dexp = [nc.alloc_sbuf_tensor(f"dexp{i}", [128, 128], F32) for i in range(2)]
    ptil = [nc.alloc_sbuf_tensor(f"ptil{i}", [128, 128], BF16) for i in range(2)]
    state = nc.alloc_sbuf_tensor("state", [128, 4096], F32)
    state_bf = nc.alloc_sbuf_tensor("state_bf", [128, 4096], BF16)
    y_tm = nc.alloc_sbuf_tensor("y_tm", [128, 4096], F32)
    y1sb = [nc.alloc_sbuf_tensor(f"y1sb{i}", [128, 64], F32) for i in range(2)]
    dtp = nc.alloc_sbuf_tensor("dtp", [64, 128], F32)
    dtp_bf = nc.alloc_sbuf_tensor("dtp_bf", [64, 128], BF16)
    dpw_hm = nc.alloc_sbuf_tensor("dpw_hm", [64, 128], BF16)
    wdec_hm = nc.alloc_sbuf_tensor("wdec_hm", [64, 128], BF16)
    t64a = nc.alloc_sbuf_tensor("t64a", [64, 128], F32)
    t64b = nc.alloc_sbuf_tensor("t64b", [64, 128], F32)
    cum_hm = nc.alloc_sbuf_tensor("cum_hm", [64, 128], F32)
    cum_tm = nc.alloc_sbuf_tensor("cum_tm", [128, 64], F32)
    ncum_tm = nc.alloc_sbuf_tensor("ncum_tm", [128, 64], F32)
    expl = nc.alloc_sbuf_tensor("expl", [128, 64], F32)
    dta_bf = nc.alloc_sbuf_tensor("dta_bf", [128, 64], BF16)
    drow = nc.alloc_sbuf_tensor("drow", [1, 64], F32)
    dcols = nc.alloc_sbuf_tensor("dcols", [128, 64], F32)
    dsl = nc.alloc_sbuf_tensor("dsl", [1, 64], F32)
    cumrow = [nc.alloc_sbuf_tensor(f"cumrow{i}", [1, 1024], F32)
              for i in range(2)]

    msa = p.op("vector", lambda E: E.memset(state[:, :], 0.0), waits=[ipdone])
    p.op("vector", lambda E: E.memset(dsl[:, :], 1.0))
    p.barrier("vector", waits=[ipdone])
    p.barrier("scalar", waits=[ipdone])
    p.barrier("tensor", waits=[ipdone])

    def xv(m, c):
        return xbc[:, 1024 * m + 128 * c:1024 * m + 128 * (c + 1)]

    prev = {"psE_rel": pe_rel, "psB_rel": stg_rel[80]}
    for c in range(nchunks):
        p.op("scalar", lambda E, c=c: E.activation(out=t64a[:, :],
                                                   in_=dt_all[:, 128 * c:128 * (c + 1)],
                                                   func=AF.Abs, bias=dtb_sb[:, :]),
             waits=[prev["dpt"]] if c > 0 else [silu_done, stg_rel[80]])
        p.op("scalar", lambda E: E.activation(out=t64a[:, :], in_=t64a[:, :],
                                              func=AF.Exp, scale=-1.0))
        p.op("scalar", lambda E: E.activation(out=t64a[:, :], in_=t64a[:, :],
                                              func=AF.Ln, bias=ones64[:, 0:1]))
        rlu = p.op("scalar", lambda E, c=c: E.activation(
            out=t64b[:, :], in_=dt_all[:, 128 * c:128 * (c + 1)],
            func=AF.Relu, bias=dtb_sb[:, :]))
        dpt = p.op("vector", lambda E: E.tensor_add(dtp[:, :], t64a[:, :],
                                                    t64b[:, :]), waits=[rlu])
        prev["dpt"] = dpt
        dpbt = p.op("vector", lambda E: E.tensor_copy(out=dtp_bf[:, :],
                                                      in_=dtp[:, :]))
        p.op("vector", lambda E: E.tensor_scalar_mul(t64a[:, :], dtp[:, :],
                                                     acol[:, :]))
        cst = p.op("vector", lambda E: E.tensor_tensor_scan(
            out=cum_hm[:, :], data0=ones64[:, :], data1=t64a[:, :], initial=0.0,
            op0=ALU.mult, op1=ALU.add))
        cdst = p.dma("sync", cum_d[:, :], cum_hm[:, :], "cdst",
                     waits=[cst, ("crl", p.val("crl"))])
        crl_tok = {}
        crl_tok[0] = p.dma("sync", cumrow[0][0:1, :],
                           cum_d[0:8, :].rearrange("h s -> (h s)"), "crl",
                           waits=[cdst, ("m1bc", p.val("m1bc"))])
        crl_tok[1] = p.dma("sync", cumrow[1][0:1, :],
                           cum_d[8:16, :].rearrange("h s -> (h s)"), "crl")
        w0 = [cst, ("m1dd", 1), dde]
        if c > 0:
            w0.append(prev["psD_rel"])
        p.op("tensor", lambda E: E.transpose(psD[:, 64:128], cum_hm[:, :],
                                             idf_sb[0:64, 0:64]), waits=w0)
        t2 = p.op("tensor", lambda E: E.transpose(psD[:, 128:192], t64a[:, :],
                                                  idf_sb[0:64, 0:64]),
                  inc="m1tr")
        e1 = p.op("vector", lambda E: E.tensor_copy(out=cum_tm[:, :],
                                                    in_=psD[:, 64:128]), waits=[t2])
        p.op("vector", lambda E: E.tensor_scalar_mul(ncum_tm[:, :], cum_tm[:, :],
                                                     -1.0))
        e3 = p.op("vector", lambda E: E.tensor_copy(out=dta_bf[:, :],
                                                    in_=psD[:, 128:192]))
        elt = p.op("scalar", lambda E: E.activation(out=expl[:, :], in_=cum_tm[:, :],
                                                    func=AF.Exp),
                   waits=[e1, ("elst", p.val("elst"))])
        p.dma("scalar", el_o[c][:, :], expl[:, :], "elst", waits=[elt])
        wmw = [e3]
        if c > 0:
            wmw.append(prev["psC_rel"])
        wmt = p.op("tensor", lambda E: E.matmul(psC[0:64, 256:384], dta_bf[:, :],
                                                trisu_sb[:, :], start=True,
                                                stop=True),
                   waits=wmw)
        dmt = p.op("tensor", lambda E: E.matmul(psC[0:1, 64:128], onesc_sb[:, :],
                                                dta_bf[:, :], start=True, stop=True),
                   inc="m1w")
        we = p.op("vector", lambda E: E.tensor_copy(out=wdec_hm[:, :],
                                                    in_=psC[0:64, 256:384]),
                  waits=[dmt])
        wex = p.op("scalar", lambda E: E.activation(out=wdec_hm[:, :],
                                                    in_=wdec_hm[:, :],
                                                    func=AF.Exp), waits=[we])
        dpwt = p.op("vector", lambda E: E.tensor_mul(dpw_hm[:, :], dtp_bf[:, :],
                                                     wdec_hm[:, :]), waits=[wex])
        de = p.op("vector", lambda E: E.tensor_copy(out=drow[:, :],
                                                    in_=psC[0:1, 64:128]),
                  waits=[("dcst", p.val("dcst"))])
        dex = p.op("scalar", lambda E: E.activation(out=drow[:, :], in_=drow[:, :],
                                                    func=AF.Exp), waits=[de])
        p.dma("scalar", dc_o[c][:, :], drow[:, :], "dcst", waits=[dex])
        dbm = p.op("tensor", lambda E: E.matmul(psC[:, 128:192], onesr_sb[:, :],
                                                drow[:, :], start=True, stop=True),
                   waits=[dex], inc="m1w")
        dce = p.op("vector", lambda E: E.tensor_copy(out=dcols[:, :],
                                                     in_=psC[:, 128:192]),
                   waits=[dbm])
        p.op("vector", lambda E: E.tensor_mul(dsl[:, :], dsl[:, :], drow[:, :]),
             waits=[dex])
        prev["psC_rel"] = dce
        pe_rel2 = prev.get("psE_rel")
        for g4 in range(4):
            w = ([prev["dtx_rel"]] if (c > 0 and g4 == 0) else [])
            if pe_rel2 is not None:
                w.append(pe_rel2)
            tt = None
            for u in range(8):
                m = 8 * g4 + u
                tt = p.op("tensor", lambda E, u=u, m=m, c=c: E.transpose(
                    psE[:, 128 * u:128 * (u + 1)], xv(m, c), idbf_sb[:, :]),
                    waits=w if u == 0 else [],
                    inc="m1xt" if u == 7 else None)
            pe_rel2 = p.op("vector", lambda E, g4=g4: E.tensor_copy(
                out=x_tm[:, 1024 * g4:1024 * (g4 + 1)], in_=psE[:, :]), waits=[tt])
        tt = None
        for u in range(8):
            tt = p.op("tensor", lambda E, u=u, c=c: E.transpose(
                psE[:, 128 * u:128 * (u + 1)], xv(32 + u, c), idbf_sb[:, :]),
                waits=[pe_rel2] if u == 0 else [], inc="m1xt" if u == 7 else None)
        bte = p.op("vector", lambda E: E.tensor_copy(out=b_tm[:, :], in_=psE[:, :]),
                   waits=[tt])
        prev["psE_rel"] = bte
        snap = p.op("vector", lambda E: E.tensor_copy(out=state_bf[:, :],
                                                      in_=state[:, :]),
                    waits=[msa] if c == 0 else [])
        stw = [prev["psB_rel"]]
        st_last = None
        for g in range(8):
            st_last = p.op("tensor", lambda E, g=g, c=c: E.matmul(
                psB[:, 128 * g:128 * (g + 1)], xv(32 + g, c), xv(40 + g, c),
                start=True, stop=True),
                waits=stw if g == 0 else [], inc="m1st" if g == 7 else None)
        yev = {}
        dpt2 = {}
        det_t = {}
        TTd = {}
        TTw = {}
        ptt = None
        for h in range(64):
            g = h // 8
            hp = h % 2
            if h % 8 == 0:
                mmd = p.op("tensor", lambda E, g=g: E.matmul(
                    psF[:, 0:512], dtp_bf[:, :], sel_sb[:, 512 * g:512 * (g + 1)],
                    start=True, stop=True),
                    waits=([dpbt] if g == 0 else [TTw[g - 1]]) +
                    ([prev["psF_rel"]] if (g == 0 and c > 0) else []),
                    inc="m1px")
                TTd[g] = p.op("vector", lambda E, g=g: E.tensor_mul(
                    dtx_tm[:, 512 * g:512 * (g + 1)],
                    x_tm[:, 512 * g:512 * (g + 1)], psF[:, 0:512]),
                    waits=[mmd] + ([prev["dtx_rel"]]
                                   if (g == 0 and c > 0) else []))
                mmw = p.op("tensor", lambda E, g=g: E.matmul(
                    psF[:, 0:512], dpw_hm[:, :], sel_sb[:, 512 * g:512 * (g + 1)],
                    start=True, stop=True), waits=[TTd[g], dpwt], inc="m1px")
                TTw[g] = p.op("vector", lambda E, g=g: E.tensor_mul(
                    dtxw_tm[:, 512 * g:512 * (g + 1)],
                    x_tm[:, 512 * g:512 * (g + 1)], psF[:, 0:512]),
                    waits=[mmw])
                if g <= 6:
                    crl_tok[g + 1] = p.dma(
                        "sync", cumrow[(g + 1) % 2][0:1, :],
                        cum_d[(g + 1) * 8:(g + 1) * 8 + 8, :]
                        .rearrange("h s -> (h s)"),
                        "crl", waits=[("m1bc", p.val("m1bc"))])
            bcw = [crl_tok[g]]
            if h >= 2:
                bcw.append(dpt2[h - 2])
            elif c > 0:
                bcw.append(prev["psD_rel"])
            bct = p.op("tensor", lambda E, h=h, hp=hp, g=g: E.matmul(
                psD[:, 256 + 128 * hp:256 + 128 * (hp + 1)], onesr_sb[:, :],
                cumrow[g % 2][0:1, 128 * (h % 8):128 * (h % 8 + 1)],
                start=True, stop=True), waits=bcw, inc="m1bc")
            dpt2[h] = p.op("vector", lambda E, hp=hp, h=h: E.scalar_tensor_tensor(
                out=dpre[hp][:, :], in0=psD[:, 256 + 128 * hp:256 + 128 * (hp + 1)],
                scalar=ncum_tm[:, h:h + 1], in1=maskt_sb[:, :],
                op0=ALU.add, op1=ALU.add),
                waits=[bct] + ([det_t[h - 2]] if h >= 2 else []))
            det = p.op("scalar", lambda E, hp=hp: E.activation(
                out=dexp[hp][:, :], in_=dpre[hp][:, :], func=AF.Exp),
                waits=[dpt2[h]])
            det_t[h] = det
            ptw = [det, st_last]
            if h >= 2:
                ptw.append(("m1y", p.val("m1y") - 1))
            ptt = p.op("vector", lambda E, g=g, hp=hp: E.tensor_mul(
                ptil[hp][:, :], dexp[hp][:, :], psB[:, 128 * g:128 * (g + 1)]),
                waits=ptw)
            y2w = [snap]
            if h >= 2:
                y2w.append(yev[h - 2])
            y2t = p.op("tensor", lambda E, g=g, hp=hp, h=h, c=c: E.matmul(
                psA[:, 512 * hp + 0:512 * hp + 64], xv(40 + g, c),
                state_bf[:, 64 * h:64 * (h + 1)],
                start=True, stop=True), waits=y2w, inc="m1y2")
            y1t = p.op("tensor", lambda E, hp=hp, h=h: E.matmul(
                psA[:, 512 * hp + 64:512 * hp + 128], ptil[hp][:, :],
                dtx_tm[:, 64 * h:64 * (h + 1)], start=True, stop=True),
                waits=[ptt, TTd[g]])
            sut = p.op("tensor", lambda E, g=g, hp=hp, h=h: E.matmul(
                psA[:, 512 * hp + 128:512 * hp + 192],
                b_tm[:, 128 * g:128 * (g + 1)],
                dtxw_tm[:, 64 * h:64 * (h + 1)], start=True, stop=True),
                waits=[TTw[g]], inc="m1y")
            p.op("vector", lambda E, hp=hp, h=h: E.scalar_tensor_tensor(
                out=y1sb[hp][:, :], in0=x_tm[:, 64 * h:64 * (h + 1)],
                scalar=ddcols[:, h:h + 1],
                in1=psA[:, 512 * hp + 64:512 * hp + 128],
                op0=ALU.mult, op1=ALU.add),
                waits=[sut] + ([dde, prev.get("yst"), elt] if h == 0 else []))
            p.op("vector", lambda E, hp=hp, h=h: E.scalar_tensor_tensor(
                out=y_tm[:, 64 * h:64 * (h + 1)],
                in0=psA[:, 512 * hp + 0:512 * hp + 64],
                scalar=expl[:, h:h + 1], in1=y1sb[hp][:, :],
                op0=ALU.mult, op1=ALU.add))
            yev[h] = p.op("vector", lambda E, hp=hp, h=h: E.scalar_tensor_tensor(
                out=state[:, 64 * h:64 * (h + 1)], in0=state[:, 64 * h:64 * (h + 1)],
                scalar=dcols[:, h:h + 1],
                in1=psA[:, 512 * hp + 128:512 * hp + 192],
                op0=ALU.mult, op1=ALU.add), waits=[dce] if h == 0 else [])
        prev["psD_rel"] = dpt2[63]
        prev["psB_rel"] = ptt
        prev["psF_rel"] = TTw[7]
        prev["dtx_rel"] = ("m1y", p.val("m1y"))
        prev["yst"] = p.dma("gpsimd", y_o[c][:, :], y_tm[:, :], f"yst{c % 2}",
                            waits=[yev[63]])
        p.dma("gpsimd", c_o[c][:, :],
              xbc[:, :].rearrange("q (m t) -> q m t", t=1024)[:, 40:48,
                                                             128 * c:128 * (c + 1)],
              f"cst{c % 2}", waits=[("m1st", p.val("m1st"))])
    p.dma("gpsimd", u_o[:, :], state[:, :], "ust",
          waits=[("@vector", p.selfc["vector"])])
    p.dma("gpsimd", ds_o[:, :], dsl[:, :], "dsst",
          waits=[("@vector", p.selfc["vector"])])
    p.barrier("sync", waits=[("yst0", p.val("yst0")), ("yst1", p.val("yst1")),
                             ("cst0", p.val("cst0")), ("cst1", p.val("cst1")),
                             ("elst", p.val("elst")), ("dcst", p.val("dcst")),
                             ("ust", p.val("ust")), ("dsst", p.val("dsst")),
                             ("gst0", p.val("gst0")), ("gst1", p.val("gst1"))])
    return p.build()


def build_m2():
    p = Prog("m2")
    nc = p.nc
    y_i = nc.dram_tensor("y_i", [8, 128, 4096], F32, kind="ExternalInput").ap()
    c_i = nc.dram_tensor("c_i", [8, 128, 1024], BF16, kind="ExternalInput").ap()
    el = nc.dram_tensor("el", [8, 128, 64], F32, kind="ExternalInput").ap()
    dch = nc.dram_tensor("dch", [8, 1, 64], F32, kind="ExternalInput").ap()
    gate = nc.dram_tensor("gate", [8, 128, 4096], BF16, kind="ExternalInput").ap()
    x2 = nc.dram_tensor("x2", [TC, D], F32, kind="ExternalInput").ap()
    uprev = nc.dram_tensor("uprev", [3, 128, 4096], F32, kind="ExternalInput").ap()
    dprev = nc.dram_tensor("dprev", [1, 128], F32, kind="ExternalInput").ap()
    wout = nc.dram_tensor("wout", [32, 128, 4096], BF16, kind="ExternalInput").ap()
    nwr = nc.dram_tensor("nwr", [1, 4096], F32, kind="ExternalInput").ap()
    onesr = nc.dram_tensor("onesr", [1, 128], F32, kind="ExternalInput").ap()
    idbf = nc.dram_tensor("idbf", [128, 128], BF16, kind="ExternalInput").ap()
    x3_o = nc.dram_tensor("x3_o", [TC, D], F32, kind="ExternalOutput").ap()
    ln_o = nc.dram_tensor("ln_o", [8, 128, 4096], BF16, kind="ExternalOutput").ap()

    ybuf = nc.alloc_sbuf_tensor("ybuf", [128, 4096], F32)
    srun = nc.alloc_sbuf_tensor("srun", [128, 4096], F32)
    csb = nc.alloc_sbuf_tensor("csb", [128, 1024], BF16)
    elsb = nc.alloc_sbuf_tensor("elsb", [128, 64], F32)
    dchr = nc.alloc_sbuf_tensor("dchr", [1, 64], F32)
    dcols = nc.alloc_sbuf_tensor("dcols", [128, 64], F32)
    dprow = nc.alloc_sbuf_tensor("dprow", [1, 128], F32)
    frow = nc.alloc_sbuf_tensor("frow", [1, 64], F32)
    fcols = nc.alloc_sbuf_tensor("fcols", [128, 64], F32)
    st_bf = nc.alloc_sbuf_tensor("st_bf", [128, 4096], BF16)
    sc512 = nc.alloc_sbuf_tensor("sc512", [128, 512], F32)
    grp = nc.alloc_sbuf_tensor("grp", [128, 8], F32)
    nwb = nc.alloc_sbuf_tensor("nwb", [128, 4096], BF16)
    gtm = nc.alloc_sbuf_tensor("gtm", [128, 4096], BF16)
    yntm = nc.alloc_sbuf_tensor("yntm", [128, 4096], BF16)
    ynimg = nc.alloc_sbuf_tensor("ynimg", [128, 32 * 512], BF16)
    wob = [nc.alloc_sbuf_tensor(f"wob{i}", [128, 4096], BF16) for i in range(2)]
    projsb = nc.alloc_sbuf_tensor("projsb", [128, 32 * 512], BF16)
    xres = [nc.alloc_sbuf_tensor(f"xres{i}", [128, D], F32) for i in range(2)]
    ident = nc.alloc_sbuf_tensor("ident", [128, 128], BF16)
    strip = nc.alloc_sbuf_tensor("strip", [128, 4096], BF16)
    onesr_sb = nc.alloc_sbuf_tensor("onesr_sb", [1, 128], F32)

    psY = [nc.alloc_psum_tensor(f"m2psY{i}", [128, 512], F32) for i in range(2)]
    psW = [nc.alloc_psum_tensor(f"m2psW{i}", [128, 512], F32) for i in range(2)]
    psM = nc.alloc_psum_tensor("m2psM", [128, 512], F32)
    psb = nc.alloc_psum_tensor("m2psb", [128, 1024], BF16)

    ynap = ynimg[:, :].rearrange("q (kc tt) -> q kc tt", tt=512)
    stripap = strip[:, :].rearrange("q (kc t) -> q kc t", t=128)

    p.dma("sync", onesr_sb[:, :], onesr[:, :], "cld")
    p.dma("sync", dprow[:, :], dprev[:, :], "cld")
    cld = p.dma("sync", ident[:, :], idbf[:, :], "cld")
    p.barrier("tensor", waits=[cld])
    p.barrier("vector", waits=[cld])
    p.barrier("scalar", waits=[cld])

    nwl = p.dma("sync", sc512[0:1, :], nwr[0:1, 0:512], "nwl")
    nwev = None
    for j in range(8):
        nwl2 = p.dma("sync", sc512[0:1, :], nwr[0:1, 512 * j:512 * (j + 1)], "nwl",
                     waits=[nwev]) if j > 0 else nwl
        nwm = p.op("tensor", lambda E: E.matmul(psM[:, :], onesr_sb[:, :],
                                                sc512[0:1, :], start=True,
                                                stop=True),
                   waits=[nwl2] + ([nwev] if nwev else []), inc="m2nw")
        nwev = p.op("vector", lambda E, j=j: E.tensor_copy(
            out=nwb[:, 512 * j:512 * (j + 1)], in_=psM[:, :]), waits=[nwm])
    nwdone = p.engine_done_tok("vector")

    sl0 = p.dma("sync", srun[:, :], uprev[0][:, :], "ul0")
    p.op("vector", lambda E: E.tensor_copy(out=frow[:, :], in_=dprow[0:1, 0:64]),
         waits=[("cld", p.val("cld"))])
    for j in (1, 2):
        ul = p.dma("sync", ybuf[:, :], uprev[j][:, :], f"ul{j}",
                   waits=[("@vector", p.selfc["vector"])])
        fbm = p.op("tensor", lambda E: E.matmul(psM[:, 0:64], onesr_sb[:, :],
                                                frow[:, :], start=True, stop=True),
                   waits=[p.engine_done_tok("vector"), ("m2nw", 8)], inc="m2f")
        fce = p.op("vector", lambda E: E.tensor_copy(out=fcols[:, :],
                                                     in_=psM[:, 0:64]), waits=[fbm])
        la = None
        for h in range(64):
            la = p.op("vector", lambda E, h=h: E.scalar_tensor_tensor(
                out=srun[:, 64 * h:64 * (h + 1)], in0=ybuf[:, 64 * h:64 * (h + 1)],
                scalar=fcols[:, h:h + 1], in1=srun[:, 64 * h:64 * (h + 1)],
                op0=ALU.mult, op1=ALU.add),
                waits=[ul, sl0, fce] if h == 0 else [])
        if j == 1:
            p.op("vector", lambda E: E.tensor_mul(frow[:, :], frow[:, :],
                                                  dprow[0:1, 64:128]))
    pool = {"ap": psb, "rel": None}
    lne = LNEmitter(p, nc, ident, pool, "m2ln")
    lnrel, x3st = {}, {}
    strip_st = None
    tr_after = None
    last_wo = None
    prev_sdec = None
    prev_yup = None
    prev_ygt = None
    for half in range(2):
        for t in range(4):
            c = 4 * half + t
            yw = [("@vector", p.selfc["vector"]),
                  ("@scalar", p.selfc["scalar"])]
            yl = p.dma("sync", ybuf[:, :], y_i[c][:, :], f"yld{c % 2}", waits=yw)
            cl_ = p.dma("sync", csb[:, :], c_i[c][:, :], f"cld{c % 2}",
                        waits=[("m2y", p.val("m2y"))])
            ell = p.dma("sync", elsb[:, :], el[c][:, :], f"eld{c % 2}",
                        waits=[("@vector", p.selfc["vector"])])
            dl = p.dma("sync", dchr[:, :], dch[c][:, :], f"dld{c % 2}",
                       waits=[("m2dc", p.val("m2dc"))])
            dbm = p.op("tensor", lambda E: E.matmul(psM[:, 64:128], onesr_sb[:, :],
                                                    dchr[:, :], start=True,
                                                    stop=True),
                       waits=[dl, ("@vector", p.selfc["vector"])], inc="m2dc")
            dce = p.op("vector", lambda E: E.tensor_copy(out=dcols[:, :],
                                                         in_=psM[:, 64:128]),
                       waits=[dbm])
            gl = p.dma("sync", gtm[:, :], gate[c][:, :], f"gld{c % 2}",
                       waits=[prev_ygt] if prev_ygt else [])
            snap = p.op("vector", lambda E: E.tensor_copy(out=st_bf[:, :],
                                                          in_=srun[:, :]),
                        waits=[("m2y", p.val("m2y"))] +
                        ([prev_sdec] if prev_sdec else []) +
                        ([prev_yup] if prev_yup else []))
            yup = {}
            for h in range(64):
                g = h // 8
                y2w = [snap, cl_]
                if h >= 2:
                    y2w.append(yup[h - 2])
                y2t = p.op("tensor", lambda E, g=g, h=h: E.matmul(
                    psY[h % 2][:, 0:64],
                    csb[:, 128 * g:128 * (g + 1)],
                    st_bf[:, 64 * h:64 * (h + 1)],
                    start=True, stop=True), waits=y2w, inc="m2y")
                yup[h] = p.op("vector", lambda E, h=h: E.scalar_tensor_tensor(
                    out=ybuf[:, 64 * h:64 * (h + 1)],
                    in0=psY[h % 2][:, 0:64],
                    scalar=elsb[:, h:h + 1], in1=ybuf[:, 64 * h:64 * (h + 1)],
                    op0=ALU.mult, op1=ALU.add),
                    waits=[("m2y", p.val("m2y")), yl, ell] if h == 0
                    else [("m2y", p.val("m2y"))])
            sdec = None
            for h in range(64):
                sdec = p.op("gpsimd", lambda E, h=h: E.tensor_scalar_mul(
                    srun[:, 64 * h:64 * (h + 1)], srun[:, 64 * h:64 * (h + 1)],
                    dcols[:, h:h + 1]),
                    waits=[dce, snap] if h == 0 else [])
            prev_sdec = sdec
            prev_yup = yup[63]
            rst = None
            for g in range(8):
                sq = p.op("scalar", lambda E, g=g: E.activation(
                    out=sc512[:, :], in_=ybuf[:, 512 * g:512 * (g + 1)],
                    func=AF.Square),
                    waits=[yup[63]] if g == 0 else [rst])
                rst = p.op("vector", lambda E, g=g: E.reduce_sum(
                    grp[:, g:g + 1], sc512[:, :], axis=AX.X), waits=[sq])
            rst = p.op("vector", lambda E: E.tensor_scalar(
                out=grp[:, :], in0=grp[:, :], scalar1=1.0 / 512.0, scalar2=EPS,
                op0=ALU.mult, op1=ALU.add))
            p.op("scalar", lambda E: E.activation(out=grp[:, :], in_=grp[:, :],
                                                  func=AF.Ln), waits=[rst])
            rex = p.op("scalar", lambda E: E.activation(out=grp[:, :], in_=grp[:, :],
                                                        func=AF.Exp, scale=-0.5))
            ynt = None
            for g in range(8):
                ynt = p.op("vector", lambda E, g=g: E.tensor_scalar_mul(
                    yntm[:, 512 * g:512 * (g + 1)], ybuf[:, 512 * g:512 * (g + 1)],
                    grp[:, g:g + 1]),
                    waits=[rex, ("m2yt", p.val("m2yt"))] if g == 0 else [])
            p.op("vector", lambda E: E.tensor_mul(yntm[:, :], yntm[:, :], nwb[:, :]),
                 waits=[nwdone] if c == 0 else [])
            ygt = p.op("vector", lambda E: E.tensor_mul(yntm[:, :], yntm[:, :],
                                                        gtm[:, :]), waits=[gl])
            prev_ygt = ygt
            for g4 in range(4):
                tt = None
                for u in range(8):
                    m = 8 * g4 + u
                    tt = p.op("tensor", lambda E, u=u, m=m: E.transpose(
                        psb[:, 128 * u:128 * (u + 1)],
                        yntm[:, 128 * m:128 * (m + 1)], ident[:, :]),
                        waits=[ygt, pool["rel"]] if u == 0 else [],
                        inc="m2yt" if u == 7 else None)
                ev = p.op("vector", lambda E, g4=g4, t=t: E.tensor_copy(
                    out=ynap[:, g4 * 8:(g4 + 1) * 8, 128 * t:128 * (t + 1)],
                    in_=psb[:, :].rearrange("q (u tt) -> q u tt", u=8)),
                    waits=[tt] + ([("m2wo", p.val("m2wo"))]
                                  if (t == 0 and g4 == 0) else []))
                pool["rel"] = ev
        yndone = p.engine_done_tok("vector")
        evp = {}
        womm = {}
        for mc in range(32):
            ww = [womm[mc - 2]] if mc >= 2 else (
                [last_wo] if last_wo is not None else [])
            wt = p.dma("sync", wob[mc % 2][:, :], wout[mc][:, :], f"wld{mc % 2}",
                       waits=ww)
            mmt = None
            w0 = [wt, yndone]
            if mc >= 2:
                w0.append(evp[mc - 2])
            if mc <= 1 and tr_after is not None:
                w0.append(tr_after)
            for kc in range(32):
                mmt = p.op("tensor", lambda E, kc=kc, mc=mc: E.matmul(
                    psW[mc % 2][:, :], wob[mc % 2][:, 128 * kc:128 * (kc + 1)],
                    ynap[:, kc, :], start=(kc == 0), stop=(kc == 31)),
                    waits=w0 if kc == 0 else [],
                    inc="m2wo" if kc == 31 else None)
            womm[mc] = mmt
            evp[mc] = p.op("vector", lambda E, mc=mc: E.tensor_copy(
                out=projsb[:, 512 * mc:512 * (mc + 1)], in_=psW[mc % 2][:, :]),
                waits=[mmt])
        last_wo = womm[31]
        for t in range(4):
            i = 4 * half + t
            xw = []
            if i >= 2:
                xw = [lnrel[i - 2], x3st[i - 2]]
            xt = p.dma("sync", xres[i % 2][:, :], x2[128 * i:128 * (i + 1), :],
                       f"xld{i % 2}", waits=xw)
            add = None
            for g2 in range(4):
                w = [evp[31]] if g2 == 0 else []
                if pool["rel"] is not None:
                    w.append(pool["rel"])
                tt = None
                for u in range(8):
                    mcc = 8 * g2 + u
                    tt = p.op("tensor", lambda E, u=u, mcc=mcc, t=t: E.transpose(
                        psb[:, 128 * u:128 * (u + 1)],
                        projsb[:, 512 * mcc + 128 * t:512 * mcc + 128 * (t + 1)],
                        ident[:, :]),
                        waits=w if u == 0 else [], inc="m2tr" if u == 7 else None)
                add = p.op("vector", lambda E, g2=g2, i=i: E.tensor_add(
                    xres[i % 2][:, 1024 * g2:1024 * (g2 + 1)],
                    xres[i % 2][:, 1024 * g2:1024 * (g2 + 1)], psb[:, :]),
                    waits=[tt, xt])
                pool["rel"] = add
            x3st[i] = p.dma("sync", x3_o[128 * i:128 * (i + 1), :],
                            xres[i % 2][:, :], f"x3st{i % 2}", waits=[add])
            rel, ev = lne.chunk(xres[i % 2], add,
                                lambda g, st_=strip_st: (stripap[:, g * 8:(g + 1) * 8, :],
                                                         [st_]))
            lnrel[i] = rel
            strip_st = p.dma("sync", ln_o[i][:, :], strip[:, :], "lnst", waits=[ev])
        tr_after = ("m2tr", p.val("m2tr"))
    p.barrier("sync", waits=[("x3st0", p.val("x3st0")), ("x3st1", p.val("x3st1")),
                             ("lnst", p.val("lnst"))])
    return p.build()


def run_m1(f2res, inputs_f32, wbf):
    ipT = np.asarray(wbf["L1_attn_in_proj_w"]).T
    ipT = np.concatenate([ipT, np.zeros((4096, 64), BF)], axis=1)
    wip_img = pack_lhsT(np.ascontiguousarray(ipT))
    cw = np.asarray(inputs_f32["L1_attn_conv_weight"], np.float32)
    convw = np.ascontiguousarray(cw.reshape(48, 128, 4).transpose(1, 0, 2)
                                 .reshape(128, 192))
    convb = np.ascontiguousarray(
        np.asarray(inputs_f32["L1_attn_conv_bias"], np.float32).reshape(48, 128).T)
    dtb = np.asarray(inputs_f32["L1_attn_dt_bias"], np.float32).reshape(64, 1)
    alog = np.asarray(inputs_f32["L1_attn_A_log"], np.float32).reshape(64, 1)
    dd = np.asarray(inputs_f32["L1_attn_D"], np.float32).reshape(1, 64)
    r_ = np.arange(128)
    trisu = (r_[:, None] > r_[None, :]).astype(BF)
    maskt = np.where(r_[:, None] <= r_[None, :], 0.0, MNEG).astype(np.float32)
    onesc = np.ones((128, 1), BF)
    onesr = np.ones((1, 128), np.float32)
    idbf = np.eye(128, dtype=BF)
    idf = np.eye(128, dtype=np.float32)
    selb = np.kron(np.eye(64, dtype=np.float32),
                   np.ones((1, 64), np.float32)).astype(BF)
    nc = build_m1()
    in_maps = []
    for c in range(NCORES):
        own = assemble_lnimg(f2res[c]["ln_o"]).reshape(128, 32, 1024)
        arr = np.zeros((128, 32, 1027), BF)
        if c % 4 > 0:
            pv = assemble_lnimg(f2res[c - 1]["ln_o"]).reshape(128, 32, 1024)
            arr[:, :, 0:3] = pv[:, :, 1021:1024]
        arr[:, :, 3:] = own
        in_maps.append({"lnin": np.ascontiguousarray(arr.reshape(128, 32 * 1027)),
                        "wip": wip_img, "convw": convw, "convb": convb,
                        "dtb": dtb, "alog": alog, "dd": dd, "trisu": trisu,
                        "maskt": maskt, "onesc": onesc, "onesr": onesr,
                        "idbf": idbf, "idf": idf, "selb": selb})
    return run_bass_kernel_spmd(nc, in_maps, core_ids=list(range(NCORES))).results


def run_m2(m1res, f2res, inputs_f32, wbf):
    wout_img = pack_lhsT(to_bf(np.asarray(wbf["L1_attn_out_proj_w"]).T))
    nwr = np.asarray(inputs_f32["L1_attn_norm_weight"], np.float32).reshape(1, 4096)
    onesr = np.ones((1, 128), np.float32)
    idbf = np.eye(128, dtype=BF)
    nc = build_m2()
    in_maps = []
    for c in range(NCORES):
        j = c % 4
        upv = np.zeros((3, 128, 4096), np.float32)
        dpv = np.ones((1, 128), np.float32)
        for i_ in range(3):
            cc = c - 1 - i_
            if cc >= 4 * (c // 4):
                upv[i_] = np.asarray(m1res[cc]["u_o"])
        if j >= 2:
            dpv[0, 0:64] = np.asarray(m1res[c - 1]["ds_o"])[0]
        if j >= 3:
            dpv[0, 64:128] = np.asarray(m1res[c - 2]["ds_o"])[0]
        in_maps.append({"y_i": np.asarray(m1res[c]["y_o"]),
                        "c_i": np.asarray(m1res[c]["c_o"]),
                        "el": np.asarray(m1res[c]["el_o"]),
                        "dch": np.asarray(m1res[c]["dc_o"]),
                        "gate": np.asarray(m1res[c]["gt_o"]),
                        "x2": np.asarray(f2res[c]["x2_o"]),
                        "uprev": upv, "dprev": dpv, "wout": wout_img,
                        "nwr": nwr, "onesr": onesr, "idbf": idbf})
    return run_bass_kernel_spmd(nc, in_maps, core_ids=list(range(NCORES))).results




def build_f3():
    p = Prog("f3")
    nc = p.nc
    lnin = nc.dram_tensor("lnin", [128, 32 * TC], BF16, kind="ExternalInput").ap()
    x3 = nc.dram_tensor("x3", [TC, D], F32, kind="ExternalInput").ap()
    wt = nc.dram_tensor("wt", [32, 128, 4096], BF16, kind="ExternalInput").ap()
    at = nc.dram_tensor("at", [2, 128, 4096], BF16, kind="ExternalInput").ap()
    bft = nc.dram_tensor("bft", [32, 128, 256], BF16, kind="ExternalInput").ap()
    bias = nc.dram_tensor("bias", [1, 4096], BF16, kind="ExternalInput").ap()
    coef = nc.dram_tensor("coef", [1, 16], F32, kind="ExternalInput").ap()
    scal = nc.dram_tensor("scal", [1, 16], F32, kind="ExternalInput").ap()
    sel2 = nc.dram_tensor("sel2", [16, 256], BF16, kind="ExternalInput").ap()
    ones1 = nc.dram_tensor("ones1", [1, 1], BF16, kind="ExternalInput").ap()
    ones5 = nc.dram_tensor("ones5", [1, 512], BF16, kind="ExternalInput").ap()
    idin = nc.dram_tensor("idin", [128, 128], BF16, kind="ExternalInput").ap()
    x4_o = nc.dram_tensor("x4_o", [TC, D], F32, kind="ExternalOutput").ap()

    img = nc.alloc_sbuf_tensor("img", [128, 32 * TC], BF16)
    wb = [nc.alloc_sbuf_tensor(f"wb{i}", [128, 4096], BF16) for i in range(2)]
    bb = [nc.alloc_sbuf_tensor(f"bb{i}", [128, 256], BF16) for i in range(2)]
    wrow = nc.alloc_sbuf_tensor("wrow", [1, 16], F32)
    wrow_bf = nc.alloc_sbuf_tensor("wrow_bf", [1, 16], BF16)
    wcol = nc.alloc_sbuf_tensor("wcol", [16, 1], BF16)
    wvec = nc.alloc_sbuf_tensor("wvec", [128, 2], F32)
    coef_sb = nc.alloc_sbuf_tensor("coef_sb", [1, 16], F32)
    scal_sb = nc.alloc_sbuf_tensor("scal_sb", [1, 16], F32)
    sel_sb = nc.alloc_sbuf_tensor("sel_sb", [16, 256], BF16)
    ones1_sb = nc.alloc_sbuf_tensor("ones1_sb", [1, 1], BF16)
    ones5_sb = nc.alloc_sbuf_tensor("ones5_sb", [1, 512], BF16)
    bias_sb = nc.alloc_sbuf_tensor("bias_sb", [1, 4096], BF16)
    xas = nc.alloc_sbuf_tensor("xas", [128, 2 * TC], BF16)
    projsb = nc.alloc_sbuf_tensor("projsb", [128, 32 * 512], BF16)
    xres = [nc.alloc_sbuf_tensor(f"xres{i}", [128, D], F32) for i in range(2)]
    ident = nc.alloc_sbuf_tensor("ident", [128, 128], BF16)

    ps = [nc.alloc_psum_tensor(f"f3ps{i}", [128, 1024], F32) for i in range(2)]
    ps1 = nc.alloc_psum_tensor("f3psq", [128, 512], F32)
    psb = nc.alloc_psum_tensor("f3psb", [128, 1024], BF16)

    imgap = img[:, :].rearrange("q (kc tt) -> q kc tt", tt=TC)

    ildt = p.dma("sync", img[:, :], lnin[:, :], "ild")
    p.dma("sync", coef_sb[:, :], coef[:, :], "cld")
    p.dma("sync", scal_sb[:, :], scal[:, :], "cld")
    p.dma("sync", sel_sb[:, :], sel2[:, :], "cld")
    p.dma("sync", ones1_sb[:, :], ones1[:, :], "cld")
    p.dma("sync", ones5_sb[:, :], ones5[:, :], "cld")
    p.dma("sync", bias_sb[:, :], bias[:, :], "cld")
    cld = p.dma("sync", ident[:, :], idin[:, :], "cld")
    p.barrier("vector", waits=[cld])
    p.barrier("tensor", waits=[cld, ildt])

    wmul = p.op("vector", lambda E: E.tensor_mul(wrow[:, :], coef_sb[:, :],
                                                 scal_sb[:, :]))
    wbt = p.op("vector", lambda E: E.tensor_copy(out=wrow_bf[:, :], in_=wrow[:, :]))
    ctm = p.op("tensor", lambda E: E.matmul(ps1[0:16, 0:1], wrow_bf[:, :],
                                            ones1_sb[:, :], start=True, stop=True),
               waits=[wbt], inc="f3c")
    cce = p.op("vector", lambda E: E.tensor_copy(out=wcol[:, :], in_=ps1[0:16, 0:1]),
               waits=[ctm])
    vmm = None
    for j in range(2):
        vmm = p.op("tensor", lambda E, j=j: E.matmul(
            ps1[:, 4 + j:5 + j], sel_sb[:, 128 * j:128 * (j + 1)], wcol[:, :],
            start=True, stop=True), waits=[cce] if j == 0 else [],
            inc="f3v" if j == 1 else None)
    cvt = p.op("vector", lambda E: E.tensor_copy(out=wvec[:, :], in_=ps1[:, 4:6]),
               waits=[vmm])

    xde = {}
    at_t = {}
    for j in range(2):
        at_t[j] = p.dma("sync", wb[j][:, :], at[j][:, :], f"wld{j}")
        mmt = None
        for n in range(2):
            for kc in range(32):
                mmt = p.op("tensor", lambda E, kc=kc, n=n, j=j: E.matmul(
                    ps[j][:, 512 * n:512 * (n + 1)],
                    wb[j][:, 128 * kc:128 * (kc + 1)],
                    imgap[:, kc, 512 * n:512 * (n + 1)],
                    start=(kc == 0), stop=(kc == 31)),
                    waits=[at_t[j]] if kc == 0 else [],
                    inc="f3xa" if kc == 31 else None)
            xde[(j, n)] = p.op("vector", lambda E, j=j, n=n: E.tensor_scalar(
                out=xas[:, TC * j + 512 * n:TC * j + 512 * (n + 1)],
                in0=ps[j][:, 512 * n:512 * (n + 1)],
                scalar1=wvec[:, j:j + 1], scalar2=None, op0=ALU.mult),
                waits=[mmt, cvt] if n == 0 else [mmt])

    x4st = {}
    tr_after = None
    last_wo = None
    add_prev = {}
    pool_rel = None
    for half in range(2):
        evp = {}
        womm = {}
        for mc in range(32):
            ww = []
            if mc >= 2:
                ww = [womm[mc - 2]]
            elif half == 0:
                ww = [xde[(1, 1)]] if mc < 2 else []
            else:
                ww = [last_wo]
            wt_ = p.dma("sync", wb[mc % 2][:, :], wt[mc][:, :], f"wld{mc % 2}",
                        waits=ww)
            ut = p.dma("sync", bb[mc % 2][:, :], bft[mc][:, :], f"uld{mc % 2}",
                       waits=ww)
            mmt = None
            w0 = [wt_, ut]
            if mc >= 2:
                w0.append(evp[mc - 2])
            if mc <= 1 and tr_after is not None:
                w0.append(tr_after)
            for kc in range(32):
                mmt = p.op("tensor", lambda E, kc=kc, mc=mc, half=half: E.matmul(
                    ps[mc % 2][:, 0:512], wb[mc % 2][:, 128 * kc:128 * (kc + 1)],
                    imgap[:, kc, 512 * half:512 * (half + 1)],
                    start=(kc == 0), stop=False),
                    waits=w0 if kc == 0 else [])
            for j in range(2):
                mmt = p.op("tensor", lambda E, j=j, mc=mc, half=half: E.matmul(
                    ps[mc % 2][:, 0:512], bb[mc % 2][:, 128 * j:128 * (j + 1)],
                    xas[:, TC * j + 512 * half:TC * j + 512 * (half + 1)],
                    start=False, stop=False), waits=[])
            mmt = p.op("tensor", lambda E, mc=mc: E.matmul(
                ps[mc % 2][:, 0:512], bias_sb[0:1, 128 * mc:128 * (mc + 1)],
                ones5_sb[:, :], start=False, stop=True), waits=[],
                inc="f3wo")
            womm[mc] = mmt
            evp[mc] = p.op("vector", lambda E, mc=mc: E.tensor_copy(
                out=projsb[:, 512 * mc:512 * (mc + 1)], in_=ps[mc % 2][:, 0:512]),
                waits=[mmt])
        last_wo = womm[31]
        for t in range(4):
            i = 4 * half + t
            xw = [x4st[i - 2]] if i >= 2 else []
            xt = p.dma("sync", xres[i % 2][:, :], x3[128 * i:128 * (i + 1), :],
                       f"xld{i % 2}", waits=xw)
            add = None
            for g2 in range(4):
                w = [evp[31]] if g2 == 0 else []
                if pool_rel is not None:
                    w.append(pool_rel)
                tt = None
                for u in range(8):
                    mcc = 8 * g2 + u
                    tt = p.op("tensor", lambda E, u=u, mcc=mcc, t=t: E.transpose(
                        psb[:, 128 * u:128 * (u + 1)],
                        projsb[:, 512 * mcc + 128 * t:512 * mcc + 128 * (t + 1)],
                        ident[:, :]),
                        waits=w if u == 0 else [], inc="f3tr" if u == 7 else None)
                add = p.op("vector", lambda E, g2=g2, i=i: E.tensor_add(
                    xres[i % 2][:, 1024 * g2:1024 * (g2 + 1)],
                    xres[i % 2][:, 1024 * g2:1024 * (g2 + 1)], psb[:, :]),
                    waits=[tt, xt])
                pool_rel = add
            x4st[i] = p.dma("sync", x4_o[128 * i:128 * (i + 1), :],
                            xres[i % 2][:, :], f"x4st{i % 2}", waits=[add])
        tr_after = ("f3tr", p.val("f3tr"))
    p.barrier("sync", waits=[("x4st0", p.val("x4st0")), ("x4st1", p.val("x4st1"))])
    return p.build()


def run_f3(m2res, inputs_f32, wbf):
    wt_img = pack_lhsT(to_bf(np.asarray(wbf["L1_ffn_W"]).T))
    at_img = pack_lhsT(np.ascontiguousarray(
        np.asarray(wbf["L1_ffn_A"]).transpose(2, 0, 1).reshape(4096, 256)))
    bft_img = pack_lhsT(np.ascontiguousarray(
        np.asarray(wbf["L1_ffn_Bf"]).transpose(0, 2, 1).reshape(256, 4096)))
    bias = to_bf(np.asarray(inputs_f32["L1_ffn_bias"], np.float32).reshape(1, 4096))
    coef = np.asarray(inputs_f32["L1_ffn_coeffs"], np.float32).reshape(1, 16)
    scal = np.asarray(inputs_f32["L1_ffn_scales"], np.float32).reshape(1, 16)
    sel = np.zeros((16, 256), BF)
    for j in range(2):
        for pp in range(128):
            sel[8 * j + pp // 16, 128 * j + pp] = BF(1.0)
    nc = build_f3()
    in_maps = []
    for c in range(NCORES):
        in_maps.append({"lnin": assemble_lnimg(m2res[c]["ln_o"]),
                        "x3": np.asarray(m2res[c]["x3_o"]), "wt": wt_img,
                        "at": at_img, "bft": bft_img, "bias": bias,
                        "coef": coef, "scal": scal, "sel2": sel,
                        "ones1": np.ones((1, 1), BF),
                        "ones5": np.ones((1, 512), BF),
                        "idin": np.eye(128, dtype=BF)})
    return run_bass_kernel_spmd(nc, in_maps, core_ids=list(range(NCORES))).results




def kernel(**inputs):
    X = np.ascontiguousarray(np.asarray(inputs["L0_attn_X"], np.float32))
    x_slices = [np.ascontiguousarray(X[c // 4, (c % 4) * TC:(c % 4 + 1) * TC])
                for c in range(NCORES)]
    wbf = cast_weights(inputs)
    a1res = run_a1(x_slices, wbf)
    a2res = run_a2(x_slices, a1res, wbf)
    f2res = run_f2(a2res, wbf)
    m1res = run_m1(f2res, inputs, wbf)
    m2res = run_m2(m1res, f2res, inputs, wbf)
    f3res = run_f3(m2res, inputs, wbf)
    out = np.empty((B, T, D), np.float32)
    for c in range(NCORES):
        out[c // 4, (c % 4) * TC:(c % 4 + 1) * TC] = np.asarray(f3res[c]["x4_o"])
    return out



CORE_GROUPS = [[0, 1, 2, 3], [4, 5, 6, 7]]


def ld_img(p, imgsb, src_blocks, tmp, n_kc=32, tov=None, col0=0, deps=()):
    TOV_ = tov or TC
    im3 = imgsb[:, :].rearrange("q (kc tt) -> q kc tt", tt=TOV_)
    tok = None
    prev_cp = None
    for i in range(8):
        lt = p.dma("sync", tmp[:, :], src_blocks[i][:, :], f"ldim{i % 2}",
                   waits=list(deps) + ([prev_cp] if prev_cp else []))
        tok = p.op("vector", lambda E, i=i: E.tensor_copy(
            out=im3[:, :, col0 + 128 * i:col0 + 128 * (i + 1)],
            in_=tmp[:, :].rearrange("q (kc t) -> q kc t", t=128)), waits=[lt])
        prev_cp = tok
    return tok


def st_fused(pf, name, stage):
    return f"{stage}_{name}"


def emit_a1(p, nct, deps):
    nc = p.nc
    import contextlib
    es = contextlib.ExitStack()
    x = nct["x"]
    wq, wk, wv = nct["wq_p"], nct["wk_p"], nct["wv_p"]
    ctab, stab = nct["ctab"], nct["stab"]
    idbf_sb = nct["idbf_sb"]
    q_d, k_d, v_d = nct["q_d"], nct["kown_d"], nct["vown_d"]
    xbuf = [es.enter_context(nc.sbuf_tensor(f"a1xb{i}", [128, D], F32))
            for i in range(2)]
    lntm = es.enter_context(nc.sbuf_tensor("a1lntm", [128, D], BF16))
    img = es.enter_context(nc.sbuf_tensor("a1img", [128, 32 * TC], BF16))
    wbuf = [es.enter_context(nc.sbuf_tensor(f"a1wb{i}", [128, 4096], BF16))
            for i in range(2)]
    cos = es.enter_context(nc.sbuf_tensor("a1cos", [64, TC], F32))
    sin = es.enter_context(nc.sbuf_tensor("a1sin", [64, TC], F32))
    t1 = es.enter_context(nc.sbuf_tensor("a1t1", [64, TC], F32))
    t2 = es.enter_context(nc.sbuf_tensor("a1t2", [64, TC], F32))
    ov = [es.enter_context(nc.sbuf_tensor(f"a1ov{i}", [128, TC], BF16))
          for i in range(2)]
    vt_sb = es.enter_context(nc.sbuf_tensor("a1vt", [128, TC], BF16))
    stats = es.enter_context(nc.sbuf_tensor("a1st", [128, 48], F32))
    mv = es.enter_context(nc.sbuf_tensor("a1mv", [128, 2], F32))
    rstd = es.enter_context(nc.sbuf_tensor("a1rs", [128, 1], F32))
    pes = contextlib.ExitStack()
    ps_tr = [pes.enter_context(nc.psum_tensor(f"a1ptr{i}", [128, 1024], BF16))
             for i in range(2)]
    ps_mm = [pes.enter_context(nc.psum_tensor(f"a1pmm{i}", [128, 512], F32))
             for i in range(4)]

    p.dma("sync", cos[:, :], ctab[:, :], "a1cld", waits=deps)
    cld = p.dma("sync", sin[:, :], stab[:, :], "a1cld")
    p.barrier("tensor", waits=[cld] + list(deps))
    p.barrier("vector", waits=[cld])

    def xsrc(i, buf, w):
        return p.dma("sync", buf[:, :], x[128 * i:128 * (i + 1), :],
                     f"a1xld{i % 2}", waits=w)

    ev_toks = emit_ln(p, xsrc, 8, (xbuf, stats, mv, rstd, lntm), img, idbf_sb,
                      ps_tr, "a1")
    imgdone = ev_toks[-1]
    imgap = img[:, :].rearrange("q (kc tt) -> q kc tt", tt=TC)
    jobs = ([("q", wq[h], q_d[h]) for h in range(NH)]
            + [("k", wk[h], k_d[h]) for h in range(NKV)]
            + [("v", wv[h], v_d[h]) for h in range(NKV)])
    mm_toks, ev2_toks, st_toks = [], [], []
    vst = None
    for ji, (kind, wsrc, osl) in enumerate(jobs):
        wb = ji % 2
        wlt = p.dma("gpsimd", wbuf[wb][:, :], wsrc[:, :], f"a1wl{wb}",
                    waits=[mm_toks[ji - 2]] if ji >= 2 else [])
        mt = None
        for n in range(2):
            pb = 2 * (ji % 2) + n
            for kc in range(32):
                w = []
                if kc == 0 and n == 0:
                    w = [wlt, imgdone]
                    if ji >= 2:
                        w.append(ev2_toks[ji - 2])
                mt = p.op("tensor", lambda E, pb=pb, kc=kc, n=n, wb=wb: E.matmul(
                    ps_mm[pb][:, :], wbuf[wb][:, 128 * kc:128 * (kc + 1)],
                    imgap[:, kc, 512 * n:512 * (n + 1)],
                    start=(kc == 0), stop=(kc == 31)),
                    waits=w, inc="a1mme" if (kc == 31 and n == 1) else None)
        mm_toks.append(mt)
        ob = ov[ji % 2]
        sw = [st_toks[ji - 2]] if ji >= 2 else []
        if kind == "v":
            p.op("vector", lambda E, pb=2 * (ji % 2), ob=ob: E.tensor_copy(
                out=ob[:, 0:512], in_=ps_mm[pb][:, :]), waits=[mt] + sw)
            et = p.op("vector", lambda E, pb=2 * (ji % 2) + 1, ob=ob: E.tensor_copy(
                out=ob[:, 512:1024], in_=ps_mm[pb][:, :]))
            tt = None
            for u in range(8):
                tt = p.op("tensor", lambda E, u=u, ob=ob: E.transpose(
                    ps_tr[0][:, 128 * (u % 8):128 * (u % 8 + 1)],
                    ob[:, 128 * u:128 * (u + 1)], idbf_sb[:, :]),
                    waits=([et] + ([vst] if vst else [])) if u == 0 else [],
                    inc="a1vt" if u == 7 else None)
            et = p.op("vector", lambda E: E.tensor_copy(out=vt_sb[:, :],
                                                        in_=ps_tr[0][:, :]),
                      waits=[tt])
            ev2_toks.append(et)
            vst = p.dma("sync", osl[:, :], vt_sb[:, :], f"a1vst",
                        waits=[et])
            st_toks.append(vst)
        else:
            for n in range(2):
                ps = ps_mm[2 * (ji % 2) + n]
                sl = slice(512 * n, 512 * (n + 1))
                p.op("vector", lambda E, ps=ps, sl=sl: E.tensor_mul(
                    t1[:, sl], ps[0:64, :], cos[:, sl]),
                    waits=[mt, cld] + sw if n == 0 else [])
                p.op("vector", lambda E, ps=ps, sl=sl: E.tensor_mul(
                    t2[:, sl], ps[64:128, :], sin[:, sl]))
                p.op("vector", lambda E, sl=sl, ob=ob: E.tensor_sub(
                    ob[0:64, sl], t1[:, sl], t2[:, sl]))
                p.op("vector", lambda E, ps=ps, sl=sl: E.tensor_mul(
                    t1[:, sl], ps[0:64, :], sin[:, sl]))
                p.op("vector", lambda E, ps=ps, sl=sl: E.tensor_mul(
                    t2[:, sl], ps[64:128, :], cos[:, sl]))
                et = p.op("vector", lambda E, sl=sl, ob=ob: E.tensor_add(
                    ob[64:128, sl], t1[:, sl], t2[:, sl]))
            ev2_toks.append(et)
            st_toks.append(p.dma("sync", osl[:, :], ob[:, :], f"a1st{ji % 2}",
                                 waits=[et]))
    es.close()
    pes.close()
    p.stage_barrier()

def emit_a2(p, nct, deps):
    nc = p.nc
    import contextlib
    es = contextlib.ExitStack()
    q_i, kg, vg = nct["q_d"], nct["kg_d"], nct["vg_d"]
    amask, wo, x = nct["amask"], nct["wo_p"], nct["x"]
    onesr_sb, onesc_sb, idbf_sb = nct["onesr_sb"], nct["onesc_sb"], nct["idbf_sb"]
    x1_d, ln1_d, pt_d = nct["x1_d"], nct["ln1_d"], nct["pt_d"]
    kb = [es.enter_context(nc.sbuf_tensor(f"a2kb{i}", [128, T], BF16))
          for i in range(2)]
    vb = [es.enter_context(nc.sbuf_tensor(f"a2vb{i}", [128, T], BF16))
          for i in range(2)]
    qb_ = [es.enter_context(nc.sbuf_tensor(f"a2qb{i}", [128, 512], BF16))
           for i in range(2)]
    pt = es.enter_context(nc.sbuf_tensor("a2pt", [128, 32 * 512], BF16))
    att = es.enter_context(nc.sbuf_tensor("a2att", [128, 32 * 512], BF16))

    lsb = es.enter_context(nc.sbuf_tensor("a2lsb", [1, 512], F32))
    rlsb = es.enter_context(nc.sbuf_tensor("a2rl", [128, 512], F32))
    wob = [es.enter_context(nc.sbuf_tensor(f"a2wo{i}", [128, 4096], BF16))
           for i in range(2)]
    projsb = es.enter_context(nc.sbuf_tensor("a2pj", [128, 32 * 512], BF16))
    msk = [projsb]
    xres = [es.enter_context(nc.sbuf_tensor(f"a2xr{i}", [128, D], F32))
            for i in range(2)]
    strip = es.enter_context(nc.sbuf_tensor("a2strip", [128, 4096], BF16))
    pacc = es.enter_context(nc.sbuf_tensor("a2pacc", [128, 8 * 32], F32))
    ptls = es.enter_context(nc.sbuf_tensor("a2ptls", [128, 32], F32))
    ebias = es.enter_context(nc.sbuf_tensor("a2eb", [128, 1], F32))
    pes = contextlib.ExitStack()
    pss = [pes.enter_context(nc.psum_tensor(f"a2ps{i}", [128, 1024], F32))
           for i in range(2)]
    psl = pes.enter_context(nc.psum_tensor("a2psl", [128, 512], F32))
    psav = pes.enter_context(nc.psum_tensor("a2psav", [128, 512], F32))
    psrl = pes.enter_context(nc.psum_tensor("a2psrl", [128, 512], F32))
    psb = pes.enter_context(nc.psum_tensor("a2psb", [128, 1024], BF16))

    ebt = p.op("vector", lambda E: E.memset(ebias[:, :], EBIAS), waits=deps)
    p.barrier("scalar", waits=[ebt])
    p.barrier("tensor", waits=list(deps))
    pool = {"ap": psb, "rel": None}
    lne = LNEmitter(p, nc, idbf_sb, pool, "a2f",
                    alloc=lambda n_, s_, d_: es.enter_context(
                        nc.sbuf_tensor(n_, s_, d_)))
    stripap = strip[:, :].rearrange("q (kc t) -> q kc t", t=128)
    kgv = kg[:, :, :, :]
    vgv = vg[:, :, :, :]

    kbrel = [None, None]
    vbrel = [None, None]
    qrel = [None, None]
    exp_tok = [None, None]
    prev_av = None
    prev_evl = None
    prev_evrl = None
    prev_nrm = None
    wo_last_mm = None
    evp_last = None
    tr_after_qb = None
    lnrel, x1st = {}, {}
    strip_st = None
    kt_cur = [None, None]
    vt_cur = [None, None]
    mld = None

    for qb in range(2):
        mw = list(deps) if qb == 0 else [tr_after_qb, evp_last]
        mld = p.dma("sync", msk[0][:, :], amask[qb][:, :], "a2mld", waits=mw)
        nrm = None
        for h in range(NH):
            hk = h // 4
            hb = hk % 2
            if h % 4 == 0:
                kt_cur[hb] = p.dma("sync", kb[hb][:, :].rearrange(
                    "q (r t) -> q r t", r=4),
                    kgv[:, hk, :, :].rearrange("r q t -> q r t"),
                    f"a2kld{hb}",
                    waits=[kbrel[hb]] if kbrel[hb] else list(deps))
                vt_cur[hb] = p.dma("sync", vb[hb][:, :].rearrange(
                    "q (r t) -> q r t", r=4),
                    vgv[:, hk, :, :].rearrange("r q t -> q r t"),
                    f"a2vld{hb}",
                    waits=[vbrel[hb]] if vbrel[hb] else list(deps))
            qt = p.dma("sync", qb_[h % 2][:, :], q_i[h][:, 512 * qb:512 * (qb + 1)],
                       f"a2qld{h % 2}", waits=[qrel[h % 2]] if qrel[h % 2] else [])
            smm = None
            for g in range(16):
                pp = g % 2
                for u in range(2):
                    kc = 2 * g + u
                    w = []
                    if u == 0:
                        if exp_tok[pp] is not None:
                            w.append(exp_tok[pp])
                        if g == 0:
                            w += [kt_cur[hb], qt]
                            if evp_last is not None:
                                w.append(evp_last)
                                evp_last = None
                    smm = p.op("tensor", lambda E, pp=pp, u=u, kc=kc, hb=hb,
                               hq=h % 2: E.matmul(
                        pss[pp][:, 512 * u:512 * (u + 1)],
                        kb[hb][:, 128 * kc:128 * (kc + 1)],
                        qb_[hq][:, :], start=True, stop=True),
                        waits=w, inc="a2s" if u == 1 else None)
                ew = [smm]
                if g <= 1 and prev_av is not None:
                    ew.append(prev_av)
                exp_tok[pp] = p.op("scalar", lambda E, pp=pp, g=g: E.activation(
                    out=pt[:, 1024 * g:1024 * (g + 1)], in_=pss[pp][:, :],
                    func=AF.Exp, scale=SCL, bias=ebias[:, :]), waits=ew)
            kbrel[hb] = smm
            qrel[h % 2] = smm
            mk = p.op("vector", lambda E: E.tensor_mul(
                pt[:, :], pt[:, :], msk[0][:, :]),
                waits=[exp_tok[0], exp_tok[1], mld])
            lmm = None
            for kc in range(32):
                w = []
                if kc == 0:
                    w = [mk] + ([prev_evl] if prev_evl else [])
                lmm = p.op("tensor", lambda E, kc=kc: E.matmul(
                    psl[0:1, :], onesc_sb[:, :],
                    pt[:, 512 * kc:512 * (kc + 1)],
                    start=(kc == 0), stop=(kc == 31)),
                    waits=w, inc="a2l" if kc == 31 else None)
            avm = None
            for kc in range(32):
                w = []
                if kc == 0:
                    w = [vt_cur[hb], mk] + ([prev_nrm] if prev_nrm else [])
                avm = p.op("tensor", lambda E, kc=kc, hb=hb: E.matmul(
                    psav[:, :], vb[hb][:, 128 * kc:128 * (kc + 1)],
                    pt[:, 512 * kc:512 * (kc + 1)],
                    start=(kc == 0), stop=(kc == 31)),
                    waits=w, inc="a2av" if kc == 31 else None)
            vbrel[hb] = avm
            prev_av = avm
            evl = p.op("vector", lambda E: E.tensor_copy(out=lsb[:, :],
                                                         in_=psl[0:1, :]),
                       waits=[lmm])
            prev_evl = evl
            rcp = p.op("vector", lambda E: E.reciprocal(lsb[:, :], lsb[:, :]))
            bc = p.op("tensor", lambda E: E.matmul(psrl[:, :], onesr_sb[:, :],
                                                   lsb[:, :], start=True,
                                                   stop=True),
                      waits=[rcp] + ([prev_evrl] if prev_evrl else []), inc="a2rl")
            evrl = p.op("vector", lambda E: E.tensor_copy(out=rlsb[:, :],
                                                          in_=psrl[:, :]),
                        waits=[bc])
            prev_evrl = evrl
            nw = [avm]
            if h == 0 and wo_last_mm is not None:
                nw.append(wo_last_mm)
            nrm = p.op("vector", lambda E, h=h: E.tensor_mul(
                att[:, 512 * h:512 * (h + 1)], psav[:, :], rlsb[:, :]), waits=nw)
            prev_nrm = nrm
        womm, evp = {}, {}
        for mc in range(32):
            ww = [womm[mc - 2]] if mc >= 2 else (
                [wo_last_mm] if wo_last_mm is not None else [])
            wt = p.dma("gpsimd", wob[mc % 2][:, :], wo[mc][:, :], f"a2wld{mc % 2}",
                       waits=ww)
            mmt = None
            for kc in range(32):
                w = []
                if kc == 0:
                    w = [wt, nrm]
                    if mc >= 2:
                        w.append(evp[mc - 2])
                    if mc <= 1 and tr_after_qb is not None:
                        w.append(tr_after_qb)
                mmt = p.op("tensor", lambda E, kc=kc, mc=mc: E.matmul(
                    pss[mc % 2][:, 0:512], wob[mc % 2][:, 128 * kc:128 * (kc + 1)],
                    att[:, 512 * kc:512 * (kc + 1)],
                    start=(kc == 0), stop=(kc == 31)),
                    waits=w, inc="a2wo" if kc == 31 else None)
            womm[mc] = mmt
            evp[mc] = p.op("vector", lambda E, mc=mc: E.tensor_copy(
                out=projsb[:, 512 * mc:512 * (mc + 1)], in_=pss[mc % 2][:, 0:512]),
                waits=[mmt])
        wo_last_mm = womm[31]
        evp_last = evp[31]
        for t in range(4):
            i = 4 * qb + t
            xw = []
            if i >= 2:
                xw = [lnrel[i - 2], x1st[i - 2]]
            xt = p.dma("sync", xres[i % 2][:, :], x[128 * i:128 * (i + 1), :],
                       f"a2xld{i % 2}", waits=xw)
            add = None
            for g2 in range(4):
                w = [evp[31]] if g2 == 0 else []
                if pool["rel"] is not None:
                    w.append(pool["rel"])
                tt = None
                for u in range(8):
                    mcc = 8 * g2 + u
                    tt = p.op("tensor", lambda E, u=u, mcc=mcc, t=t: E.transpose(
                        psb[:, 128 * u:128 * (u + 1)],
                        projsb[:, 512 * mcc + 128 * t:512 * mcc + 128 * (t + 1)],
                        idbf_sb[:, :]),
                        waits=w if u == 0 else [], inc="a2tr" if u == 7 else None)
                add = p.op("vector", lambda E, g2=g2, i=i: E.tensor_add(
                    xres[i % 2][:, 1024 * g2:1024 * (g2 + 1)],
                    xres[i % 2][:, 1024 * g2:1024 * (g2 + 1)], psb[:, :]),
                    waits=[tt, xt])
                pool["rel"] = add
            x1st[i] = p.dma("sync", x1_d[128 * i:128 * (i + 1), :],
                            xres[i % 2][:, :], f"a2x1st{i % 2}", waits=[add])
            rel, ev = lne.chunk(xres[i % 2], add,
                                lambda g, st_=strip_st: (stripap[:, g * 8:(g + 1) * 8, :],
                                                         [st_]))
            lnrel[i] = rel
            prd = p.op("vector", lambda E, i=i: E.tensor_reduce(
                pacc[:, 32 * i:32 * (i + 1)], stripap[:, :, :], AX.X, ALU.add),
                waits=[ev])
            strip_st = p.dma("sync", ln1_d[i][:, :], strip[:, :], "a2lnst",
                             waits=[prd])
        tr_after_qb = ("a2tr", p.val("a2tr"))
    fr = p.op("vector", lambda E: E.tensor_reduce(
        ptls[:, :], pacc[:, :].rearrange("q (i k) -> q k i", i=8), AX.X, ALU.add))
    p.dma("sync", pt_d[:, :], ptls[:, :], "a2ptst", waits=[fr])
    es.close()
    pes.close()
    p.stage_barrier()

def emit_f2(p, nct, deps):
    nc = p.nc
    import contextlib
    es = contextlib.ExitStack()
    x1, ptg = nct["x1_d"], nct["ptg_d"]
    worg, upw, dnw, wqt = nct["worg_p"], nct["up_p"], nct["dn_p"], nct["wqt_p"]
    keyt, sel, ones1, idbf_sb = nct["keyt"], nct["sel"], nct["ones1_sb"], nct["idbf_sb"]
    x2_d, ln2_d, ov_d = nct["x2_d"], nct["ln2_d"], nct["ov_d"]
    img = es.enter_context(nc.sbuf_tensor("f2img", [128, 32 * TC], BF16))
    tmp4 = es.enter_context(nc.sbuf_tensor("f2tmp4", [128, 4096], BF16))
    wb = [es.enter_context(nc.sbuf_tensor(f"f2wb{i}", [128, 4096], BF16))
          for i in range(2)]
    ub = [es.enter_context(nc.sbuf_tensor(f"f2ub{i}", [128, 512], BF16))
          for i in range(2)]
    parts_sb = es.enter_context(nc.sbuf_tensor("f2pts", [128, 4 * 32], F32))
    mean_sb = es.enter_context(nc.sbuf_tensor("f2mn", [128, 32], F32))
    mean_bf = es.enter_context(nc.sbuf_tensor("f2mnb", [128, 32], BF16))
    qsb = es.enter_context(nc.sbuf_tensor("f2q", [128, 1], BF16))
    keyt_sb = es.enter_context(nc.sbuf_tensor("f2kt", [128, 8], BF16))
    sel_sb = es.enter_context(nc.sbuf_tensor("f2sel", [8, 512], BF16))
    ones1_s = ones1
    ssb = es.enter_context(nc.sbuf_tensor("f2ss", [1, 8], F32))
    nm = es.enter_context(nc.sbuf_tensor("f2nm", [1, 1], F32))
    esum = es.enter_context(nc.sbuf_tensor("f2es", [1, 1], F32))
    co_bf = es.enter_context(nc.sbuf_tensor("f2co", [1, 8], BF16))
    cocol = es.enter_context(nc.sbuf_tensor("f2cc", [8, 1], BF16))
    cvec = es.enter_context(nc.sbuf_tensor("f2cv", [128, 4], F32))
    xds = es.enter_context(nc.sbuf_tensor("f2xds", [128, 4 * TC], BF16))
    projsb = es.enter_context(nc.sbuf_tensor("f2pj", [128, 32 * 512], BF16))
    xres = [es.enter_context(nc.sbuf_tensor(f"f2xr{i}", [128, D], F32))
            for i in range(2)]
    strip = es.enter_context(nc.sbuf_tensor("f2strip", [128, 4096], BF16))
    ovc = es.enter_context(nc.sbuf_tensor("f2ovc", [128, 96], BF16))
    pes = contextlib.ExitStack()
    ps = [pes.enter_context(nc.psum_tensor(f"f2ps{i}", [128, 1024], F32))
          for i in range(2)]
    ps1 = pes.enter_context(nc.psum_tensor("f2psq", [128, 512], F32))
    psb = pes.enter_context(nc.psum_tensor("f2psb", [128, 1024], BF16))

    imgap = img[:, :].rearrange("q (kc tt) -> q kc tt", tt=TC)
    stripap = strip[:, :].rearrange("q (kc t) -> q kc t", t=128)

    imgdone = ld_img(p, img, nct["ln1_d"], tmp4, deps=deps)
    p.dma("sync", parts_sb[:, :].rearrange("q (j k) -> q j k", j=4),
          ptg[:, :, :].rearrange("j q k -> q j k"), "f2cld", waits=deps)
    p.dma("gpsimd", keyt_sb[:, :], keyt[:, :], "f2kld", waits=deps)
    p.dma("sync", sel_sb[:, :], sel[:, :], "f2cld")
    cld = ("f2cld", p.val("f2cld"))
    kld = ("f2kld", p.val("f2kld"))
    p.barrier("vector", waits=[cld, kld])
    p.barrier("tensor", waits=[cld, kld] + list(deps))

    p.op("vector", lambda E: E.tensor_add(mean_sb[:, :], parts_sb[:, 0:32],
                                          parts_sb[:, 32:64]))
    p.op("vector", lambda E: E.tensor_add(mean_sb[:, :], mean_sb[:, :],
                                          parts_sb[:, 64:96]))
    p.op("vector", lambda E: E.tensor_add(mean_sb[:, :], mean_sb[:, :],
                                          parts_sb[:, 96:128]))
    mt = p.op("vector", lambda E: E.tensor_scalar_mul(mean_bf[:, :], mean_sb[:, :],
                                                      1.0 / 4096.0))
    wqld = p.dma("gpsimd", wb[0][:, :], wqt[0][:, :], "f2wld0")
    qmm = None
    for kc in range(32):
        qmm = p.op("tensor", lambda E, kc=kc: E.matmul(
            ps1[:, 0:1], wb[0][:, 128 * kc:128 * (kc + 1)],
            mean_bf[:, kc:kc + 1], start=(kc == 0), stop=(kc == 31)),
            waits=[wqld, mt] if kc == 0 else [], inc="f2qmm" if kc == 31 else None)
    qe = p.op("vector", lambda E: E.tensor_copy(out=qsb[:, :], in_=ps1[:, 0:1]),
              waits=[qmm])
    smm = p.op("tensor", lambda E: E.matmul(ps1[0:1, 4:12], qsb[:, :],
                                            keyt_sb[:, :], start=True, stop=True),
               waits=[qe], inc="f2smm")
    se = p.op("vector", lambda E: E.tensor_scalar_mul(ssb[:, :], ps1[0:1, 4:12],
                                                      SCL), waits=[smm])
    p.op("vector", lambda E: E.reduce_max(nm[:, :], ssb[:, :], axis=AX.X,
                                          negate=True))
    et = p.op("scalar", lambda E: E.activation(out=ssb[:, :], in_=ssb[:, :],
                                               func=AF.Exp, bias=nm[:, :],
                                               accum_out=esum[:, :]),
              waits=[p.engine_done_tok("vector")])
    p.op("vector", lambda E: E.reciprocal(esum[:, :], esum[:, :]), waits=[et])
    ct = p.op("vector", lambda E: E.tensor_scalar_mul(co_bf[:, :], ssb[:, :],
                                                      esum[:, :]))
    ctm = p.op("tensor", lambda E: E.matmul(ps1[0:8, 16:17], co_bf[:, :],
                                            ones1_s[:, :], start=True, stop=True),
               waits=[ct], inc="f2ctm")
    cce = p.op("vector", lambda E: E.tensor_copy(out=cocol[:, :],
                                                 in_=ps1[0:8, 16:17]), waits=[ctm])
    vmm = None
    for j in range(4):
        vmm = p.op("tensor", lambda E, j=j: E.matmul(
            ps1[:, 20 + j:21 + j], sel_sb[:, 128 * j:128 * (j + 1)], cocol[:, :],
            start=True, stop=True), waits=[cce] if j == 0 else [],
            inc="f2vmm" if j == 3 else None)
    cvt = p.op("vector", lambda E: E.tensor_copy(out=cvec[:, :], in_=ps1[:, 20:24]),
               waits=[vmm])
    dnt = {}
    xde = {}
    prev_dn_mm = None
    for j in range(4):
        dw = []
        if j == 1:
            dw = [qmm]
        elif j >= 2:
            dw = [prev_dn_mm]
        dnt[j] = p.dma("gpsimd", wb[(j + 1) % 2][:, :], dnw[j][:, :],
                       f"f2wld{(j + 1) % 2}", waits=dw)
        mmt = None
        for n in range(2):
            w0 = [dnt[j], imgdone]
            if j >= 2:
                w0.append(xde[(j - 2, n)])
            for kc in range(32):
                mmt = p.op("tensor", lambda E, kc=kc, n=n, j=j: E.matmul(
                    ps[j % 2][:, 512 * n:512 * (n + 1)],
                    wb[(j + 1) % 2][:, 128 * kc:128 * (kc + 1)],
                    imgap[:, kc, 512 * n:512 * (n + 1)],
                    start=(kc == 0), stop=(kc == 31)),
                    waits=w0 if kc == 0 else [],
                    inc="f2xd" if kc == 31 else None)
            xde[(j, n)] = p.op("vector", lambda E, j=j, n=n: E.tensor_scalar(
                out=xds[:, TC * j + 512 * n:TC * j + 512 * (n + 1)],
                in0=ps[j % 2][:, 512 * n:512 * (n + 1)],
                scalar1=cvec[:, j:j + 1], scalar2=None, op0=ALU.mult),
                waits=[mmt, cvt] if n == 0 else [mmt])
        prev_dn_mm = mmt
    pool = {"ap": psb, "rel": None}
    lne = LNEmitter(p, nc, idbf_sb, pool, "f2f",
                    alloc=lambda n_, s_, d_: es.enter_context(
                        nc.sbuf_tensor(n_, s_, d_)))
    lnrel, x2st = {}, {}
    strip_st = None
    tr_after = None
    last_wo_of_half = None
    ovcp = None
    for half in range(2):
        evp = {}
        womm = {}
        for mc in range(32):
            ww = []
            if mc >= 2:
                ww = [womm[mc - 2]]
            elif half == 0:
                ww = [xde[(3, 1)]]
            else:
                ww = [last_wo_of_half]
            wt = p.dma("gpsimd", wb[mc % 2][:, :], worg[mc][:, :],
                       f"f2wld{mc % 2}", waits=ww)
            ut = p.dma("gpsimd", ub[mc % 2][:, :], upw[mc][:, :],
                       f"f2uld{mc % 2}", waits=ww)
            mmt = None
            w0 = [wt, ut]
            if mc >= 2:
                w0.append(evp[mc - 2])
            if mc <= 1 and tr_after is not None:
                w0.append(tr_after)
            for kc in range(32):
                mmt = p.op("tensor", lambda E, kc=kc, mc=mc, half=half: E.matmul(
                    ps[mc % 2][:, 0:512], wb[mc % 2][:, 128 * kc:128 * (kc + 1)],
                    imgap[:, kc, 512 * half:512 * (half + 1)],
                    start=(kc == 0), stop=False),
                    waits=w0 if kc == 0 else [])
            for j in range(4):
                mmt = p.op("tensor", lambda E, j=j, mc=mc, half=half: E.matmul(
                    ps[mc % 2][:, 0:512], ub[mc % 2][:, 128 * j:128 * (j + 1)],
                    xds[:, TC * j + 512 * half:TC * j + 512 * (half + 1)],
                    start=False, stop=(j == 3)),
                    waits=[], inc="f2wo" if j == 3 else None)
            womm[mc] = mmt
            evp[mc] = p.op("vector", lambda E, mc=mc: E.tensor_copy(
                out=projsb[:, 512 * mc:512 * (mc + 1)], in_=ps[mc % 2][:, 0:512]),
                waits=[mmt])
        last_wo_of_half = womm[31]
        for t in range(4):
            i = 4 * half + t
            xw = []
            if i >= 2:
                xw = [lnrel[i - 2], x2st[i - 2]]
            xt = p.dma("sync", xres[i % 2][:, :], x1[128 * i:128 * (i + 1), :],
                       f"f2xld{i % 2}", waits=xw)
            add = None
            for g2 in range(4):
                w = [evp[31]] if g2 == 0 else []
                if pool["rel"] is not None:
                    w.append(pool["rel"])
                tt = None
                for u in range(8):
                    mcc = 8 * g2 + u
                    tt = p.op("tensor", lambda E, u=u, mcc=mcc, t=t: E.transpose(
                        psb[:, 128 * u:128 * (u + 1)],
                        projsb[:, 512 * mcc + 128 * t:512 * mcc + 128 * (t + 1)],
                        idbf_sb[:, :]),
                        waits=w if u == 0 else [], inc="f2tr" if u == 7 else None)
                add = p.op("vector", lambda E, g2=g2, i=i: E.tensor_add(
                    xres[i % 2][:, 1024 * g2:1024 * (g2 + 1)],
                    xres[i % 2][:, 1024 * g2:1024 * (g2 + 1)], psb[:, :]),
                    waits=[tt, xt])
                pool["rel"] = add
            x2st[i] = p.dma("sync", x2_d[128 * i:128 * (i + 1), :],
                            xres[i % 2][:, :], f"f2x2st{i % 2}", waits=[add])
            rel, ev = lne.chunk(xres[i % 2], add,
                                lambda g, st_=strip_st: (stripap[:, g * 8:(g + 1) * 8, :],
                                                         [st_]))
            lnrel[i] = rel
            if i == 7:
                ovcp = p.op("vector", lambda E: E.tensor_copy(
                    out=ovc[:, :].rearrange("q (kc t) -> q kc t", t=3),
                    in_=stripap[:, :, 125:128]), waits=[ev])
                p.dma("sync", ov_d[:, :], ovc[:, :], "f2ovst", waits=[ovcp])
            strip_st = p.dma("sync", ln2_d[i][:, :], strip[:, :], "f2lnst",
                             waits=[ev] + ([ovcp] if i == 7 else []))
        tr_after = ("f2tr", p.val("f2tr"))
    es.close()
    pes.close()
    p.stage_barrier()

def emit_m1(p, nct, deps):
    nc = p.nc
    import contextlib
    TOV = TC + 3
    pers = contextlib.ExitStack()
    xbc = pers.enter_context(nc.sbuf_tensor("m1xbc", [128, 48 * 1024], BF16))
    dt_all = pers.enter_context(nc.sbuf_tensor("m1dta", [64, 1024], BF16))
    convw_sb = pers.enter_context(nc.sbuf_tensor("m1cw", [128, 192], F32))
    convb_sb = pers.enter_context(nc.sbuf_tensor("m1cb", [128, 48], F32))
    dtb_sb = pers.enter_context(nc.sbuf_tensor("m1dtb", [64, 1], F32))
    acol = pers.enter_context(nc.sbuf_tensor("m1ac", [64, 1], F32))
    dd_sb = pers.enter_context(nc.sbuf_tensor("m1dd", [1, 64], F32))
    ddcols = pers.enter_context(nc.sbuf_tensor("m1ddc", [128, 64], F32))
    trisu_sb = pers.enter_context(nc.sbuf_tensor("m1tri", [128, 128], BF16))
    maskt_sb = pers.enter_context(nc.sbuf_tensor("m1msk", [128, 128], F32))
    ones64 = pers.enter_context(nc.sbuf_tensor("m1o64", [64, 128], F32))
    pes = contextlib.ExitStack()
    psA = pes.enter_context(nc.psum_tensor("m1psA", [128, 1024], F32))
    psB = pes.enter_context(nc.psum_tensor("m1psB", [128, 1024], F32))
    psC = pes.enter_context(nc.psum_tensor("m1psC", [128, 512], F32))
    psD = pes.enter_context(nc.psum_tensor("m1psD", [128, 512], F32))
    psE = pes.enter_context(nc.psum_tensor("m1psE", [128, 1024], BF16))
    psF = pes.enter_context(nc.psum_tensor("m1psF", [128, 512], F32))
    onesc_sb, onesr_sb = nct["onesc_sb"], nct["onesr_sb"]
    idbf_sb, idf_sb = nct["idbf_sb"], nct["idf_sb"]
    wip = nct["wip_p"]
    y_o, c_o, el_o, dc_o = nct["y_d"], nct["c_d"], nct["el_d"], nct["dc_d"]
    gt_o, u_o, ds_o = nct["gt_d"], nct["u_d"], nct["ds_d"]
    cum_d = nct["cum_d"]

    p.dma("sync", convw_sb[:, :], nct["convw"][:, :], "m1cld", waits=deps)
    p.dma("sync", convb_sb[:, :], nct["convb"][:, :], "m1cld")
    p.dma("sync", dtb_sb[:, :], nct["dtb"][:, :], "m1cld")
    p.dma("sync", acol[:, :], nct["alog"][:, :], "m1cld")
    p.dma("sync", dd_sb[:, :], nct["dd"][:, :], "m1cld")
    p.dma("sync", trisu_sb[:, :], nct["trisu"][:, :], "m1cld")
    cld = p.dma("sync", maskt_sb[:, :], nct["maskt"][:, :], "m1cld")
    at = p.op("scalar", lambda E: E.activation(out=acol[:, :], in_=acol[:, :],
                                               func=AF.Exp), waits=[cld])
    p.op("vector", lambda E: E.tensor_scalar_mul(acol[:, :], acol[:, :], -1.0),
         waits=[at, cld])
    p.op("vector", lambda E: E.memset(ones64[:, :], 1.0))
    ddm = p.op("tensor", lambda E: E.matmul(psD[:, 0:64], onesr_sb[:, :],
                                            dd_sb[:, :], start=True, stop=True),
               waits=[cld] + list(deps), inc="m1ddm")
    dde = p.op("vector", lambda E: E.tensor_copy(out=ddcols[:, :], in_=psD[:, 0:64]),
               waits=[ddm])

    es = contextlib.ExitStack()
    img = es.enter_context(nc.sbuf_tensor("m1img", [128, 32 * TOV], BF16))
    tmp4 = es.enter_context(nc.sbuf_tensor("m1tmp4", [128, 4096], BF16))
    wb = [es.enter_context(nc.sbuf_tensor(f"m1wb{i}", [128, 4096], BF16))
          for i in range(2)]
    stg = [es.enter_context(nc.sbuf_tensor(f"m1stg{i}", [128, TOV], BF16))
           for i in range(2)]
    gbuf = es.enter_context(nc.sbuf_tensor("m1gbuf", [128, 1024], BF16))
    gtb = es.enter_context(nc.sbuf_tensor("m1gtb", [128, 1024], BF16))
    ovb = [es.enter_context(nc.sbuf_tensor(f"m1ovb{i}", [128, 96], BF16))
           for i in range(4)]
    ovs = es.enter_context(nc.sbuf_tensor("m1ovs", [128, 96], F32))
    imgap = img[:, :].rearrange("q (kc tt) -> q kc tt", tt=TOV)
    imgdone = ld_img(p, img, nct["ln2_d"], tmp4, tov=TOV, col0=3, deps=deps)
    ovt = []
    for r in range(4):
        ovt.append(p.dma("sync", ovb[r][:, :], nct["ovg_d"][r][:, :], "m1ovl",
                         waits=deps))
    o1 = p.op("vector", lambda E: E.tensor_scalar_mul(
        ovs[:, :], ovb[0][:, :], nct["ovm_sb"][:, 0:1]), waits=ovt + [cld])
    for r in range(1, 4):
        o1 = p.op("vector", lambda E, r=r: E.scalar_tensor_tensor(
            out=ovs[:, :], in0=ovb[r][:, :], scalar=nct["ovm_sb"][:, r:r + 1],
            in1=ovs[:, :], op0=ALU.mult, op1=ALU.add))
    imgdone = p.op("vector", lambda E: E.tensor_copy(
        out=imgap[:, :, 0:3],
        in_=ovs[:, :].rearrange("q (kc t) -> q kc t", t=3)), waits=[o1, imgdone])
    p.barrier("tensor", waits=[("m1cld", p.val("m1cld"))])
    p.barrier("vector", waits=[("m1cld", p.val("m1cld"))])

    stg_rel = {}
    evt_toks = {}
    gt_rel = None
    pe_rel = None
    for mc in range(81):
        pp = mc % 2
        big = psA if pp == 0 else psB
        tiny = psF[:, 0:3] if pp == 0 else psC[:, 0:3]
        wt = p.dma("gpsimd", wb[mc % 2][:, :], wip[mc][:, :], f"m1wld{mc % 2}",
                   waits=[("m1ip", mc - 1)] if mc >= 2 else [])
        mmt = None
        for n in range(3):
            cl = [slice(0, 512), slice(512, 1024), slice(1024, 1027)][n]
            dsts = [big[:, 0:512], big[:, 512:1024], tiny]
            w0 = [wt, imgdone]
            if mc >= 2:
                w0.append(evt_toks[mc - 2])
            for kc in range(32):
                mmt = p.op("tensor", lambda E, kc=kc, mc=mc, cl=cl, n=n,
                           dsts=dsts: E.matmul(
                    dsts[n], wb[mc % 2][:, 128 * kc:128 * (kc + 1)],
                    imgap[:, kc, cl], start=(kc == 0), stop=(kc == 31)),
                    waits=w0 if (kc == 0 and n == 0) else [],
                    inc="m1ip" if (kc == 31 and n == 2) else None)
        sb = stg[mc % 2]
        ew = [mmt]
        if mc >= 2:
            ew.append(stg_rel[mc - 2])
        p.op("vector", lambda E, sb=sb, big=big: E.tensor_copy(
            out=sb[:, 0:512], in_=big[:, 0:512]), waits=ew)
        p.op("vector", lambda E, sb=sb, big=big: E.tensor_copy(
            out=sb[:, 512:1024], in_=big[:, 512:1024]))
        evt = p.op("vector", lambda E, sb=sb, tiny=tiny: E.tensor_copy(
            out=sb[:, 1024:1027], in_=tiny))
        evt_toks[mc] = evt
        if mc < 32:
            sil = p.op("scalar", lambda E, sb=sb: E.activation(
                out=gbuf[:, :], in_=sb[:, 3:1027], func=AF.Silu),
                waits=[evt] + ([gt_rel] if gt_rel else []))
            tt = None
            for u in range(8):
                tt = p.op("tensor", lambda E, u=u: E.transpose(
                    psE[:, 128 * u:128 * (u + 1)], gbuf[:, 128 * u:128 * (u + 1)],
                    idbf_sb[:, :]),
                    waits=([sil] + ([pe_rel] if pe_rel else [])) if u == 0 else [],
                    inc="m1gt" if u == 7 else None)
            pe_rel = p.op("vector", lambda E: E.tensor_copy(out=gtb[:, :],
                                                            in_=psE[:, :]),
                          waits=[tt])
            gt_rel = p.dma("sync", gt_o[0:8, :, 128 * mc:128 * (mc + 1)]
                           .rearrange("cc q t -> q cc t"),
                           gtb[:, :].rearrange("q (cc t) -> q cc t", t=128),
                           f"m1gst{mc % 2}", waits=[pe_rel])
            stg_rel[mc] = sil
        elif mc < 80:
            m = mc - 32
            cvt = p.op("vector", lambda E, sb=sb, m=m: E.tensor_scalar(
                out=xbc[:, 1024 * m:1024 * (m + 1)],
                in0=sb[:, 0:1024], scalar1=convw_sb[:, 4 * m:4 * m + 1],
                scalar2=convb_sb[:, m:m + 1], op0=ALU.mult, op1=ALU.add),
                waits=[evt])
            for k in range(1, 4):
                cvt = p.op("vector", lambda E, sb=sb, m=m, k=k:
                           E.scalar_tensor_tensor(
                    out=xbc[:, 1024 * m:1024 * (m + 1)], in0=sb[:, k:k + 1024],
                    scalar=convw_sb[:, 4 * m + k:4 * m + k + 1],
                    in1=xbc[:, 1024 * m:1024 * (m + 1)],
                    op0=ALU.mult, op1=ALU.add))
            p.op("scalar", lambda E, m=m: E.activation(
                out=xbc[:, 1024 * m:1024 * (m + 1)],
                in_=xbc[:, 1024 * m:1024 * (m + 1)], func=AF.Silu), waits=[cvt])
            stg_rel[mc] = cvt
        else:
            stg_rel[mc] = p.op("vector", lambda E, sb=sb: E.tensor_copy(
                out=dt_all[:, :], in_=sb[0:64, 3:1027]))
    silu_done = p.engine_done_tok("scalar")
    dtcp = stg_rel[80]
    ipdone = ("m1ip", 81)
    es.close()

    es2 = contextlib.ExitStack()
    def A2(name, shape, dt_):
        return es2.enter_context(nc.sbuf_tensor(name, shape, dt_))
    x_tm = A2("m1xtm", [128, 4096], BF16)
    dtx_tm = A2("m1dtx", [128, 4096], BF16)
    b_tm = A2("m1btm", [128, 8 * 128], BF16)
    bw_sb = [A2(f"m1bw{i}", [128, 128], BF16) for i in range(2)]
    dpre = [A2(f"m1dp{i}", [128, 128], F32) for i in range(2)]
    dexp = [A2(f"m1de{i}", [128, 128], F32) for i in range(2)]
    ptil = [A2(f"m1pt{i}", [128, 128], BF16) for i in range(2)]
    state = A2("m1state", [128, 4096], F32)
    state_bf = A2("m1stbf", [128, 4096], BF16)
    y_tm = A2("m1ytm", [128, 4096], F32)
    y1sb = [A2(f"m1y1{i}", [128, 64], F32) for i in range(2)]
    dtp = A2("m1dtp", [64, 128], F32)
    t64a = A2("m1t64a", [64, 128], F32)
    t64b = A2("m1t64b", [64, 128], F32)
    cum_hm = A2("m1chm", [64, 128], F32)
    cum_tm = A2("m1ctm", [128, 64], F32)
    ncum_tm = A2("m1nctm", [128, 64], F32)
    expl = A2("m1expl", [128, 64], F32)
    dta_bf = A2("m1dtab", [128, 64], BF16)
    dtp_tm = A2("m1dtpt", [128, 64], F32)
    wdec = A2("m1wd", [128, 64], F32)
    drow = A2("m1dr", [1, 64], F32)
    dcols = A2("m1dc", [128, 64], F32)
    dsl = A2("m1dsl", [1, 64], F32)
    cumrow = A2("m1cr", [1, 1024], F32)

    msa = p.op("vector", lambda E: E.memset(state[:, :], 0.0), waits=[ipdone])
    p.op("vector", lambda E: E.memset(dsl[:, :], 1.0))
    p.barrier("vector", waits=[ipdone])
    p.barrier("scalar", waits=[ipdone])
    p.barrier("tensor", waits=[ipdone])

    def xv(m, c):
        return xbc[:, 1024 * m + 128 * c:1024 * m + 128 * (c + 1)]

    prev = {"psE_rel": pe_rel, "psB_rel": stg_rel[80]}
    for c in range(8):
        p.op("scalar", lambda E, c=c: E.activation(
            out=t64a[:, :], in_=dt_all[:, 128 * c:128 * (c + 1)],
            func=AF.Abs, bias=dtb_sb[:, :]),
            waits=[prev["dpt"]] if c > 0 else [silu_done, dtcp])
        p.op("scalar", lambda E: E.activation(out=t64a[:, :], in_=t64a[:, :],
                                              func=AF.Exp, scale=-1.0))
        p.op("scalar", lambda E: E.activation(out=t64a[:, :], in_=t64a[:, :],
                                              func=AF.Ln, bias=ones64[:, 0:1]))
        rlu = p.op("scalar", lambda E, c=c: E.activation(
            out=t64b[:, :], in_=dt_all[:, 128 * c:128 * (c + 1)],
            func=AF.Relu, bias=dtb_sb[:, :]))
        dpt = p.op("vector", lambda E: E.tensor_add(dtp[:, :], t64a[:, :],
                                                    t64b[:, :]), waits=[rlu])
        prev["dpt"] = dpt
        p.op("vector", lambda E: E.tensor_scalar_mul(t64a[:, :], dtp[:, :],
                                                     acol[:, :]))
        cst = p.op("vector", lambda E: E.tensor_tensor_scan(
            out=cum_hm[:, :], data0=ones64[:, :], data1=t64a[:, :], initial=0.0,
            op0=ALU.mult, op1=ALU.add))
        cdst = p.dma("sync", cum_d[:, :], cum_hm[:, :], "m1cdst",
                     waits=[cst, ("m1crl", p.val("m1crl"))])
        w0 = [cst, ("m1ddm", 1), dde]
        if c > 0:
            w0.append(prev["psD_rel"])
        p.op("tensor", lambda E: E.transpose(psD[:, 64:128], cum_hm[:, :],
                                             idf_sb[0:64, 0:64]), waits=w0)
        p.op("tensor", lambda E: E.transpose(psD[:, 128:192], t64a[:, :],
                                             idf_sb[0:64, 0:64]))
        tdp = p.op("tensor", lambda E: E.transpose(psD[:, 192:256], dtp[:, :],
                                                   idf_sb[0:64, 0:64]), waits=[dpt],
                   inc="m1tr")
        e1 = p.op("vector", lambda E: E.tensor_copy(out=cum_tm[:, :],
                                                    in_=psD[:, 64:128]), waits=[tdp])
        p.op("vector", lambda E: E.tensor_scalar_mul(ncum_tm[:, :], cum_tm[:, :],
                                                     -1.0))
        p.op("vector", lambda E: E.tensor_copy(out=dta_bf[:, :], in_=psD[:, 128:192]))
        e3 = p.op("vector", lambda E: E.tensor_copy(out=dtp_tm[:, :],
                                                    in_=psD[:, 192:256]))
        elt = p.op("scalar", lambda E: E.activation(out=expl[:, :], in_=cum_tm[:, :],
                                                    func=AF.Exp), waits=[e1])
        p.dma("sync", el_o[c][:, :], expl[:, :], "m1elst", waits=[elt])
        wmw = [e3]
        if c > 0:
            wmw.append(prev["psC_rel"])
        wmt = p.op("tensor", lambda E: E.matmul(psC[:, 64:128], trisu_sb[:, :],
                                                dta_bf[:, :], start=True, stop=True),
                   waits=wmw)
        dmt = p.op("tensor", lambda E: E.matmul(psC[0:1, 128:192], onesc_sb[:, :],
                                                dta_bf[:, :], start=True, stop=True),
                   inc="m1w")
        we = p.op("vector", lambda E: E.tensor_copy(out=wdec[:, :],
                                                    in_=psC[:, 64:128]),
                  waits=[dmt])
        wex = p.op("scalar", lambda E: E.activation(out=wdec[:, :], in_=wdec[:, :],
                                                    func=AF.Exp), waits=[we])
        de = p.op("vector", lambda E: E.tensor_copy(out=drow[:, :],
                                                    in_=psC[0:1, 128:192]))
        dex = p.op("scalar", lambda E: E.activation(out=drow[:, :], in_=drow[:, :],
                                                    func=AF.Exp), waits=[de])
        p.dma("sync", dc_o[c][:, :], drow[:, :], "m1dcst", waits=[dex])
        dbm = p.op("tensor", lambda E: E.matmul(psC[:, 192:256], onesr_sb[:, :],
                                                drow[:, :], start=True, stop=True),
                   waits=[dex], inc="m1w")
        dce = p.op("vector", lambda E: E.tensor_copy(out=dcols[:, :],
                                                     in_=psC[:, 192:256]),
                   waits=[dbm])
        p.op("vector", lambda E: E.tensor_mul(dsl[:, :], dsl[:, :], drow[:, :]),
             waits=[dex])
        prev["psC_rel"] = dce
        pe_rel2 = prev.get("psE_rel")
        for g4 in range(4):
            w = ([prev["dtx_rel"]] if (c > 0 and g4 == 0) else [])
            if pe_rel2 is not None:
                w.append(pe_rel2)
            tt = None
            for u in range(8):
                m = 8 * g4 + u
                tt = p.op("tensor", lambda E, u=u, m=m, c=c: E.transpose(
                    psE[:, 128 * u:128 * (u + 1)], xv(m, c), idbf_sb[:, :]),
                    waits=w if u == 0 else [],
                    inc="m1xt" if u == 7 else None)
            pe_rel2 = p.op("vector", lambda E, g4=g4: E.tensor_copy(
                out=x_tm[:, 1024 * g4:1024 * (g4 + 1)], in_=psE[:, :]), waits=[tt])
        tt = None
        for u in range(8):
            tt = p.op("tensor", lambda E, u=u, c=c: E.transpose(
                psE[:, 128 * u:128 * (u + 1)], xv(32 + u, c), idbf_sb[:, :]),
                waits=[pe_rel2] if u == 0 else [], inc="m1xt" if u == 7 else None)
        bte = p.op("vector", lambda E: E.tensor_copy(out=b_tm[:, :], in_=psE[:, :]),
                   waits=[tt])
        prev["psE_rel"] = bte
        dxt = None
        for h in range(64):
            dxt = p.op("gpsimd", lambda E, h=h: E.tensor_scalar_mul(
                dtx_tm[:, 64 * h:64 * (h + 1)], x_tm[:, 64 * h:64 * (h + 1)],
                dtp_tm[:, h:h + 1]),
                waits=([e3, pe_rel2, bte] + ([prev["dtx_rel"]] if c > 0 else []))
                if h == 0 else [])
        snap = p.op("vector", lambda E: E.tensor_copy(out=state_bf[:, :],
                                                      in_=state[:, :]),
                    waits=[msa] if c == 0 else [])
        stw = [prev["psB_rel"]]
        st_last = None
        for g in range(8):
            st_last = p.op("tensor", lambda E, g=g, c=c: E.matmul(
                psB[:, 128 * g:128 * (g + 1)], xv(32 + g, c), xv(40 + g, c),
                start=True, stop=True),
                waits=stw if g == 0 else [], inc="m1st" if g == 7 else None)
        yev = {}
        dpt2 = {}
        det_t = {}
        ptt = None
        crl = None
        for h in range(64):
            g = h // 8
            hp = h % 2
            if h % 8 == 0:
                crw = [cdst]
                if ptt is not None:
                    crw.append(ptt)
                crl = p.dma("sync", cumrow[0:1, :],
                            cum_d[g * 8:g * 8 + 8, :].rearrange("h s -> (h s)"),
                            "m1crl", waits=crw)
            bcw = [crl]
            if h >= 2:
                bcw.append(dpt2[h - 2])
            elif c > 0:
                bcw.append(prev["psD_rel"])
            bct = p.op("tensor", lambda E, h=h, hp=hp: E.matmul(
                psD[:, 256 + 128 * hp:256 + 128 * (hp + 1)], onesr_sb[:, :],
                cumrow[0:1, 128 * (h % 8):128 * (h % 8 + 1)],
                start=True, stop=True), waits=bcw, inc="m1bc")
            dpt2[h] = p.op("vector", lambda E, hp=hp, h=h: E.scalar_tensor_tensor(
                out=dpre[hp][:, :], in0=psD[:, 256 + 128 * hp:256 + 128 * (hp + 1)],
                scalar=ncum_tm[:, h:h + 1], in1=maskt_sb[:, :],
                op0=ALU.add, op1=ALU.add),
                waits=[bct] + ([det_t[h - 2]] if h >= 2 else []))
            det = p.op("scalar", lambda E, hp=hp: E.activation(
                out=dexp[hp][:, :], in_=dpre[hp][:, :], func=AF.Exp),
                waits=[dpt2[h]])
            det_t[h] = det
            ptw = [det, st_last]
            if h >= 2:
                ptw.append(("m1y", p.val("m1y") - 1))
            ptt = p.op("vector", lambda E, g=g, hp=hp: E.tensor_mul(
                ptil[hp][:, :], dexp[hp][:, :], psB[:, 128 * g:128 * (g + 1)]),
                waits=ptw)
            y2w = [snap]
            if h >= 2:
                y2w.append(yev[h - 2])
            y2t = p.op("tensor", lambda E, g=g, hp=hp, h=h, c=c: E.matmul(
                psA[:, 512 * hp + 0:512 * hp + 64], xv(40 + g, c),
                state_bf[:, 64 * h:64 * (h + 1)],
                start=True, stop=True), waits=y2w, inc="m1y2")
            y1t = p.op("tensor", lambda E, hp=hp, h=h: E.matmul(
                psA[:, 512 * hp + 64:512 * hp + 128], ptil[hp][:, :],
                dtx_tm[:, 64 * h:64 * (h + 1)], start=True, stop=True),
                waits=[ptt, dxt])
            bwt = p.op("gpsimd", lambda E, g=g, h=h, hp=hp: E.tensor_scalar_mul(
                bw_sb[hp][:, :], b_tm[:, 128 * g:128 * (g + 1)], wdec[:, h:h + 1]),
                waits=([wex] if h == 0 else []) +
                ([("m1y", p.val("m1y") - 1)] if h >= 2 else []))
            sut = p.op("tensor", lambda E, hp=hp, h=h: E.matmul(
                psA[:, 512 * hp + 128:512 * hp + 192], bw_sb[hp][:, :],
                dtx_tm[:, 64 * h:64 * (h + 1)], start=True, stop=True),
                waits=[bwt], inc="m1y")
            p.op("vector", lambda E, hp=hp, h=h: E.scalar_tensor_tensor(
                out=y1sb[hp][:, :], in0=x_tm[:, 64 * h:64 * (h + 1)],
                scalar=ddcols[:, h:h + 1],
                in1=psA[:, 512 * hp + 64:512 * hp + 128],
                op0=ALU.mult, op1=ALU.add),
                waits=[sut] + ([dde, prev.get("yst"), elt] if h == 0 else []))
            p.op("vector", lambda E, hp=hp, h=h: E.scalar_tensor_tensor(
                out=y_tm[:, 64 * h:64 * (h + 1)],
                in0=psA[:, 512 * hp + 0:512 * hp + 64],
                scalar=expl[:, h:h + 1], in1=y1sb[hp][:, :],
                op0=ALU.mult, op1=ALU.add))
            yev[h] = p.op("vector", lambda E, hp=hp, h=h: E.scalar_tensor_tensor(
                out=state[:, 64 * h:64 * (h + 1)], in0=state[:, 64 * h:64 * (h + 1)],
                scalar=dcols[:, h:h + 1],
                in1=psA[:, 512 * hp + 128:512 * hp + 192],
                op0=ALU.mult, op1=ALU.add), waits=[dce] if h == 0 else [])
        prev["psD_rel"] = dpt2[63]
        prev["psB_rel"] = ptt
        prev["dtx_rel"] = ("m1y", p.val("m1y"))
        prev["yst"] = p.dma("sync", y_o[c][:, :], y_tm[:, :], f"m1yst{c % 2}",
                            waits=[yev[63]])
        p.dma("sync", c_o[c][:, :],
              xbc[:, :].rearrange("q (m t) -> q m t", t=1024)[:, 40:48,
                                                             128 * c:128 * (c + 1)],
              f"m1cst{c % 2}", waits=[("m1st", p.val("m1st"))])
    p.dma("sync", u_o[:, :], state[:, :], "m1ust",
          waits=[("@vector", p.selfc["vector"])])
    p.dma("sync", ds_o[:, :], dsl[:, :], "m1dsst",
          waits=[("@vector", p.selfc["vector"])])
    es2.close()
    pers.close()
    pes.close()
    p.stage_barrier()

def emit_m2(p, nct, deps):
    nc = p.nc
    import contextlib
    es = contextlib.ExitStack()
    def A(name, shape, dt_):
        return es.enter_context(nc.sbuf_tensor(name, shape, dt_))
    y_i, c_i, el, dch = nct["y_d"], nct["c_d"], nct["el_d"], nct["dc_d"]
    gate, x2 = nct["gt_d"], nct["x2_d"]
    ug, dsg = nct["ug_d"], nct["dsg_d"]
    wout = nct["wout_p"]
    onesr_sb, idbf_sb = nct["onesr_sb"], nct["idbf_sb"]
    x3_d, ln3_d = nct["x3_d"], nct["ln3_d"]
    ybuf = A("m2yb", [128, 4096], F32)
    srun = A("m2sr", [128, 4096], F32)
    csb = A("m2cs", [128, 1024], BF16)
    elsb = A("m2el", [128, 64], F32)
    dchr = A("m2dch", [1, 64], F32)
    dcols = A("m2dc", [128, 64], F32)
    dsr = A("m2dsr", [1, 256], F32)
    frow = A("m2fr", [1, 64], F32)
    fcols = A("m2fc", [128, 64], F32)
    st_bf = A("m2stbf", [128, 4096], BF16)
    sc512 = A("m2sc", [128, 512], F32)
    grp = A("m2grp", [128, 8], F32)
    nwb = A("m2nwb", [128, 4096], BF16)
    gtm = A("m2gtm", [128, 4096], BF16)
    yntm = A("m2yn", [128, 4096], BF16)
    ynimg = A("m2yni", [128, 32 * 512], BF16)
    wob = [A(f"m2wo{i}", [128, 4096], BF16) for i in range(2)]
    projsb = A("m2pj", [128, 32 * 512], BF16)
    xres = [A(f"m2xr{i}", [128, D], F32) for i in range(2)]
    strip = A("m2strip", [128, 4096], BF16)
    tmpf = A("m2tf", [1, 64], F32)
    pes = contextlib.ExitStack()
    psY = [pes.enter_context(nc.psum_tensor(f"m2psY{i}", [128, 512], F32))
           for i in range(2)]
    psW = [pes.enter_context(nc.psum_tensor(f"m2psW{i}", [128, 512], F32))
           for i in range(2)]
    psM = pes.enter_context(nc.psum_tensor("m2psM", [128, 512], F32))
    psb = pes.enter_context(nc.psum_tensor("m2psb", [128, 1024], BF16))

    ynap = ynimg[:, :].rearrange("q (kc tt) -> q kc tt", tt=512)
    stripap = strip[:, :].rearrange("q (kc t) -> q kc t", t=128)
    cld = p.dma("sync", dsr[0:1, :], dsg[:, :, :].rearrange("j o k -> o (j k)"),
                "m2cld", waits=deps)
    p.barrier("tensor", waits=[cld] + list(deps))
    p.barrier("vector", waits=[cld])
    p.barrier("scalar", waits=[cld])
    nwl = p.dma("sync", sc512[0:1, :], nct["nwr"][0:1, 0:512], "m2nwl")
    nwev = None
    for j in range(8):
        nwl2 = p.dma("sync", sc512[0:1, :], nct["nwr"][0:1, 512 * j:512 * (j + 1)],
                     "m2nwl", waits=[nwev]) if j > 0 else nwl
        nwm = p.op("tensor", lambda E: E.matmul(psM[:, :], onesr_sb[:, :],
                                                sc512[0:1, :], start=True,
                                                stop=True),
                   waits=[nwl2] + ([nwev] if nwev else []), inc="m2nw")
        nwev = p.op("vector", lambda E, j=j: E.tensor_copy(
            out=nwb[:, 512 * j:512 * (j + 1)], in_=psM[:, :]), waits=[nwm])
    nwdone = p.engine_done_tok("vector")
    msr = p.op("vector", lambda E: E.memset(srun[:, :], 0.0))
    la = msr
    for r in range(3):
        ul = p.dma("sync", ybuf[:, :], ug[r][:, :], f"m2ul{r}",
                   waits=[("@vector", p.selfc["vector"])] + list(deps))
        ft = p.op("vector", lambda E, r=r: E.tensor_scalar(
            out=frow[:, :], in0=dsr[0:1, 0:64],
            scalar1=nct["efac_sb"][0:1, 4 * r:4 * r + 1],
            scalar2=nct["enfac_sb"][0:1, 4 * r:4 * r + 1],
            op0=ALU.mult, op1=ALU.add), waits=[cld])
        for m_ in range(1, 4):
            p.op("vector", lambda E, r=r, m_=m_: E.tensor_scalar(
                out=tmpf[:, :], in0=dsr[0:1, 64 * m_:64 * (m_ + 1)],
                scalar1=nct["efac_sb"][0:1, 4 * r + m_:4 * r + m_ + 1],
                scalar2=nct["enfac_sb"][0:1, 4 * r + m_:4 * r + m_ + 1],
                op0=ALU.mult, op1=ALU.add))
            p.op("vector", lambda E: E.tensor_mul(frow[:, :], frow[:, :],
                                                  tmpf[:, :]))
        ft = p.op("vector", lambda E, r=r: E.tensor_scalar(
            out=frow[:, :], in0=frow[:, :],
            scalar1=nct["evalid_sb"][0:1, r:r + 1], scalar2=None, op0=ALU.mult))
        fbm = p.op("tensor", lambda E: E.matmul(psM[:, 0:64], onesr_sb[:, :],
                                                frow[:, :], start=True, stop=True),
                   waits=[ft, ("m2nw", 8)], inc="m2f")
        fce = p.op("vector", lambda E: E.tensor_copy(out=fcols[:, :],
                                                     in_=psM[:, 0:64]), waits=[fbm])
        for h in range(64):
            la = p.op("vector", lambda E, h=h: E.scalar_tensor_tensor(
                out=srun[:, 64 * h:64 * (h + 1)], in0=ybuf[:, 64 * h:64 * (h + 1)],
                scalar=fcols[:, h:h + 1], in1=srun[:, 64 * h:64 * (h + 1)],
                op0=ALU.mult, op1=ALU.add),
                waits=[ul, fce] if h == 0 else [])
    pool = {"ap": psb, "rel": None}
    lne = LNEmitter(p, nc, idbf_sb, pool, "m2f",
                    alloc=lambda n_, s_, d_: es.enter_context(
                        nc.sbuf_tensor(n_, s_, d_)))
    lnrel, x3st = {}, {}
    strip_st = None
    tr_after = None
    last_wo = None
    prev_sdec = None
    prev_yup = None
    prev_ygt = None
    for half in range(2):
        for t in range(4):
            c = 4 * half + t
            yw = [("@vector", p.selfc["vector"]),
                  ("@scalar", p.selfc["scalar"])]
            yl = p.dma("sync", ybuf[:, :], y_i[c][:, :], f"m2yld{c % 2}", waits=yw)
            cl_ = p.dma("sync", csb[:, :], c_i[c][:, :], f"m2cl{c % 2}",
                        waits=[("m2y", p.val("m2y"))])
            ell = p.dma("sync", elsb[:, :], el[c][:, :], f"m2eld{c % 2}",
                        waits=[("@vector", p.selfc["vector"])])
            dl = p.dma("sync", dchr[:, :], dch[c][:, :], f"m2dld{c % 2}",
                       waits=[("m2dc", p.val("m2dc"))])
            dbm = p.op("tensor", lambda E: E.matmul(psM[:, 64:128], onesr_sb[:, :],
                                                    dchr[:, :], start=True,
                                                    stop=True),
                       waits=[dl, ("@vector", p.selfc["vector"])], inc="m2dc")
            dce = p.op("vector", lambda E: E.tensor_copy(out=dcols[:, :],
                                                         in_=psM[:, 64:128]),
                       waits=[dbm])
            gl = p.dma("sync", gtm[:, :], gate[c][:, :], f"m2gld{c % 2}",
                       waits=[prev_ygt] if prev_ygt else [])
            snap = p.op("vector", lambda E: E.tensor_copy(out=st_bf[:, :],
                                                          in_=srun[:, :]),
                        waits=[("m2y", p.val("m2y"))] +
                        ([prev_sdec] if prev_sdec else []) +
                        ([prev_yup] if prev_yup else []))
            yup = {}
            for h in range(64):
                g = h // 8
                y2w = [snap, cl_]
                if h >= 2:
                    y2w.append(yup[h - 2])
                y2t = p.op("tensor", lambda E, g=g, h=h: E.matmul(
                    psY[h % 2][:, 0:64], csb[:, 128 * g:128 * (g + 1)],
                    st_bf[:, 64 * h:64 * (h + 1)],
                    start=True, stop=True), waits=y2w, inc="m2y")
                yup[h] = p.op("vector", lambda E, h=h: E.scalar_tensor_tensor(
                    out=ybuf[:, 64 * h:64 * (h + 1)], in0=psY[h % 2][:, 0:64],
                    scalar=elsb[:, h:h + 1], in1=ybuf[:, 64 * h:64 * (h + 1)],
                    op0=ALU.mult, op1=ALU.add),
                    waits=[("m2y", p.val("m2y")), yl, ell] if h == 0
                    else [("m2y", p.val("m2y"))])
            sdec = None
            for h in range(64):
                sdec = p.op("gpsimd", lambda E, h=h: E.tensor_scalar_mul(
                    srun[:, 64 * h:64 * (h + 1)], srun[:, 64 * h:64 * (h + 1)],
                    dcols[:, h:h + 1]),
                    waits=[dce, snap] if h == 0 else [])
            prev_sdec = sdec
            prev_yup = yup[63]
            rst = None
            for g in range(8):
                sq = p.op("scalar", lambda E, g=g: E.activation(
                    out=sc512[:, :], in_=ybuf[:, 512 * g:512 * (g + 1)],
                    func=AF.Square),
                    waits=[yup[63]] if g == 0 else [rst])
                rst = p.op("vector", lambda E, g=g: E.reduce_sum(
                    grp[:, g:g + 1], sc512[:, :], axis=AX.X), waits=[sq])
            rst = p.op("vector", lambda E: E.tensor_scalar(
                out=grp[:, :], in0=grp[:, :], scalar1=1.0 / 512.0, scalar2=EPS,
                op0=ALU.mult, op1=ALU.add))
            p.op("scalar", lambda E: E.activation(out=grp[:, :], in_=grp[:, :],
                                                  func=AF.Ln), waits=[rst])
            rex = p.op("scalar", lambda E: E.activation(out=grp[:, :], in_=grp[:, :],
                                                        func=AF.Exp, scale=-0.5))
            ynt = None
            for g in range(8):
                ynt = p.op("vector", lambda E, g=g: E.tensor_scalar_mul(
                    yntm[:, 512 * g:512 * (g + 1)], ybuf[:, 512 * g:512 * (g + 1)],
                    grp[:, g:g + 1]),
                    waits=[rex, ("m2yt", p.val("m2yt"))] if g == 0 else [])
            p.op("vector", lambda E: E.tensor_mul(yntm[:, :], yntm[:, :], nwb[:, :]),
                 waits=[nwdone] if c == 0 else [])
            ygt = p.op("vector", lambda E: E.tensor_mul(yntm[:, :], yntm[:, :],
                                                        gtm[:, :]), waits=[gl])
            prev_ygt = ygt
            for g4 in range(4):
                tt = None
                for u in range(8):
                    m = 8 * g4 + u
                    tt = p.op("tensor", lambda E, u=u, m=m: E.transpose(
                        psb[:, 128 * u:128 * (u + 1)],
                        yntm[:, 128 * m:128 * (m + 1)], idbf_sb[:, :]),
                        waits=[ygt, pool["rel"]] if u == 0 else [],
                        inc="m2yt" if u == 7 else None)
                ev = p.op("vector", lambda E, g4=g4, t=t: E.tensor_copy(
                    out=ynap[:, g4 * 8:(g4 + 1) * 8, 128 * t:128 * (t + 1)],
                    in_=psb[:, :].rearrange("q (u tt) -> q u tt", u=8)),
                    waits=[tt] + ([("m2wo", p.val("m2wo"))]
                                  if (t == 0 and g4 == 0) else []))
                pool["rel"] = ev
        yndone = p.engine_done_tok("vector")
        evp = {}
        womm = {}
        for mc in range(32):
            ww = [womm[mc - 2]] if mc >= 2 else (
                [last_wo] if last_wo is not None else [])
            wt = p.dma("gpsimd", wob[mc % 2][:, :], wout[mc][:, :],
                       f"m2wld{mc % 2}", waits=ww)
            mmt = None
            w0 = [wt, yndone]
            if mc >= 2:
                w0.append(evp[mc - 2])
            if mc <= 1 and tr_after is not None:
                w0.append(tr_after)
            for kc in range(32):
                mmt = p.op("tensor", lambda E, kc=kc, mc=mc: E.matmul(
                    psW[mc % 2][:, :], wob[mc % 2][:, 128 * kc:128 * (kc + 1)],
                    ynap[:, kc, :], start=(kc == 0), stop=(kc == 31)),
                    waits=w0 if kc == 0 else [],
                    inc="m2wo" if kc == 31 else None)
            womm[mc] = mmt
            evp[mc] = p.op("vector", lambda E, mc=mc: E.tensor_copy(
                out=projsb[:, 512 * mc:512 * (mc + 1)], in_=psW[mc % 2][:, :]),
                waits=[mmt])
        last_wo = womm[31]
        for t in range(4):
            i = 4 * half + t
            xw = []
            if i >= 2:
                xw = [lnrel[i - 2], x3st[i - 2]]
            xt = p.dma("sync", xres[i % 2][:, :], x2[128 * i:128 * (i + 1), :],
                       f"m2xld{i % 2}", waits=xw)
            add = None
            for g2 in range(4):
                w = [evp[31]] if g2 == 0 else []
                if pool["rel"] is not None:
                    w.append(pool["rel"])
                tt = None
                for u in range(8):
                    mcc = 8 * g2 + u
                    tt = p.op("tensor", lambda E, u=u, mcc=mcc, t=t: E.transpose(
                        psb[:, 128 * u:128 * (u + 1)],
                        projsb[:, 512 * mcc + 128 * t:512 * mcc + 128 * (t + 1)],
                        idbf_sb[:, :]),
                        waits=w if u == 0 else [], inc="m2tr" if u == 7 else None)
                add = p.op("vector", lambda E, g2=g2, i=i: E.tensor_add(
                    xres[i % 2][:, 1024 * g2:1024 * (g2 + 1)],
                    xres[i % 2][:, 1024 * g2:1024 * (g2 + 1)], psb[:, :]),
                    waits=[tt, xt])
                pool["rel"] = add
            x3st[i] = p.dma("sync", x3_d[128 * i:128 * (i + 1), :],
                            xres[i % 2][:, :], f"m2x3st{i % 2}", waits=[add])
            rel, ev = lne.chunk(xres[i % 2], add,
                                lambda g, st_=strip_st: (stripap[:, g * 8:(g + 1) * 8, :],
                                                         [st_]))
            lnrel[i] = rel
            strip_st = p.dma("sync", ln3_d[i][:, :], strip[:, :], "m2lnst",
                             waits=[ev])
        tr_after = ("m2tr", p.val("m2tr"))
    es.close()
    pes.close()
    p.stage_barrier()


def emit_f3(p, nct, deps):
    nc = p.nc
    import contextlib
    es = contextlib.ExitStack()
    def A(name, shape, dt_):
        return es.enter_context(nc.sbuf_tensor(name, shape, dt_))
    x3 = nct["x3_d"]
    wt3, at3, bft = nct["wt3_p"], nct["at_p"], nct["bft_p"]
    idbf_sb = nct["idbf_sb"]
    x4_o = nct["x4_o"]
    img = A("f3img", [128, 32 * TC], BF16)
    tmp4 = A("f3tmp4", [128, 4096], BF16)
    wb = [A(f"f3wb{i}", [128, 4096], BF16) for i in range(2)]
    bb = [A(f"f3bb{i}", [128, 256], BF16) for i in range(2)]
    wrow = A("f3wr", [1, 16], F32)
    wrow_bf = A("f3wrb", [1, 16], BF16)
    wcol = A("f3wc", [16, 1], BF16)
    wvec = A("f3wv", [128, 2], F32)
    coef_sb = A("f3co", [1, 16], F32)
    scal_sb = A("f3sc", [1, 16], F32)
    sel_sb = A("f3sel", [16, 256], BF16)
    bias_sb = A("f3bias", [1, 4096], BF16)
    xas = A("f3xas", [128, 2 * TC], BF16)
    projsb = A("f3pj", [128, 32 * 512], BF16)
    xres = [A(f"f3xr{i}", [128, D], F32) for i in range(2)]
    pes = contextlib.ExitStack()
    ps = [pes.enter_context(nc.psum_tensor(f"f3ps{i}", [128, 1024], F32))
          for i in range(2)]
    ps1 = pes.enter_context(nc.psum_tensor("f3psq", [128, 512], F32))
    psb = pes.enter_context(nc.psum_tensor("f3psb", [128, 1024], BF16))
    ones1_sb, ones5_sb = nct["ones1_sb"], nct["ones5_sb"]
    imgap = img[:, :].rearrange("q (kc tt) -> q kc tt", tt=TC)
    imgdone = ld_img(p, img, nct["ln3_d"], tmp4, deps=deps)
    p.dma("sync", coef_sb[:, :], nct["coef"][:, :], "f3cld", waits=deps)
    p.dma("sync", scal_sb[:, :], nct["scal"][:, :], "f3cld")
    p.dma("sync", sel_sb[:, :], nct["sel2"][:, :], "f3cld")
    cld = p.dma("gpsimd", bias_sb[:, :], nct["bias3"][:, :], "f3bld", waits=deps)
    p.barrier("vector", waits=[("f3cld", p.val("f3cld")), cld])
    p.barrier("tensor", waits=[("f3cld", p.val("f3cld")), cld] + list(deps))
    wmul = p.op("vector", lambda E: E.tensor_mul(wrow[:, :], coef_sb[:, :],
                                                 scal_sb[:, :]))
    wbt = p.op("vector", lambda E: E.tensor_copy(out=wrow_bf[:, :], in_=wrow[:, :]))
    ctm = p.op("tensor", lambda E: E.matmul(ps1[0:16, 0:1], wrow_bf[:, :],
                                            ones1_sb[:, :], start=True, stop=True),
               waits=[wbt], inc="f3c")
    cce = p.op("vector", lambda E: E.tensor_copy(out=wcol[:, :], in_=ps1[0:16, 0:1]),
               waits=[ctm])
    vmm = None
    for j in range(2):
        vmm = p.op("tensor", lambda E, j=j: E.matmul(
            ps1[:, 4 + j:5 + j], sel_sb[:, 128 * j:128 * (j + 1)], wcol[:, :],
            start=True, stop=True), waits=[cce] if j == 0 else [],
            inc="f3v" if j == 1 else None)
    cvt = p.op("vector", lambda E: E.tensor_copy(out=wvec[:, :], in_=ps1[:, 4:6]),
               waits=[vmm])
    xde = {}
    at_t = {}
    for j in range(2):
        at_t[j] = p.dma("gpsimd", wb[j][:, :], at3[j][:, :], f"f3wld{j}")
        mmt = None
        for n in range(2):
            for kc in range(32):
                mmt = p.op("tensor", lambda E, kc=kc, n=n, j=j: E.matmul(
                    ps[j][:, 512 * n:512 * (n + 1)],
                    wb[j][:, 128 * kc:128 * (kc + 1)],
                    imgap[:, kc, 512 * n:512 * (n + 1)],
                    start=(kc == 0), stop=(kc == 31)),
                    waits=[at_t[j], imgdone] if kc == 0 else [],
                    inc="f3xa" if kc == 31 else None)
            xde[(j, n)] = p.op("vector", lambda E, j=j, n=n: E.tensor_scalar(
                out=xas[:, TC * j + 512 * n:TC * j + 512 * (n + 1)],
                in0=ps[j][:, 512 * n:512 * (n + 1)],
                scalar1=wvec[:, j:j + 1], scalar2=None, op0=ALU.mult),
                waits=[mmt, cvt] if n == 0 else [mmt])
    x4st = {}
    tr_after = None
    last_wo = None
    pool_rel = None
    for half in range(2):
        evp = {}
        womm = {}
        for mc in range(32):
            ww = []
            if mc >= 2:
                ww = [womm[mc - 2]]
            elif half == 0:
                ww = [xde[(1, 1)]]
            else:
                ww = [last_wo]
            wt_ = p.dma("gpsimd", wb[mc % 2][:, :], wt3[mc][:, :],
                        f"f3wld{mc % 2}", waits=ww)
            ut = p.dma("gpsimd", bb[mc % 2][:, :], bft[mc][:, :],
                       f"f3uld{mc % 2}", waits=ww)
            mmt = None
            w0 = [wt_, ut]
            if mc >= 2:
                w0.append(evp[mc - 2])
            if mc <= 1 and tr_after is not None:
                w0.append(tr_after)
            for kc in range(32):
                mmt = p.op("tensor", lambda E, kc=kc, mc=mc, half=half: E.matmul(
                    ps[mc % 2][:, 0:512], wb[mc % 2][:, 128 * kc:128 * (kc + 1)],
                    imgap[:, kc, 512 * half:512 * (half + 1)],
                    start=(kc == 0), stop=False),
                    waits=w0 if kc == 0 else [])
            for j in range(2):
                mmt = p.op("tensor", lambda E, j=j, mc=mc, half=half: E.matmul(
                    ps[mc % 2][:, 0:512], bb[mc % 2][:, 128 * j:128 * (j + 1)],
                    xas[:, TC * j + 512 * half:TC * j + 512 * (half + 1)],
                    start=False, stop=False), waits=[])
            mmt = p.op("tensor", lambda E, mc=mc: E.matmul(
                ps[mc % 2][:, 0:512], bias_sb[0:1, 128 * mc:128 * (mc + 1)],
                ones5_sb[:, :], start=False, stop=True), waits=[], inc="f3wo")
            womm[mc] = mmt
            evp[mc] = p.op("vector", lambda E, mc=mc: E.tensor_copy(
                out=projsb[:, 512 * mc:512 * (mc + 1)], in_=ps[mc % 2][:, 0:512]),
                waits=[mmt])
        last_wo = womm[31]
        for t in range(4):
            i = 4 * half + t
            xw = [x4st[i - 2]] if i >= 2 else []
            xt = p.dma("sync", xres[i % 2][:, :], x3[128 * i:128 * (i + 1), :],
                       f"f3xld{i % 2}", waits=xw)
            add = None
            for g2 in range(4):
                w = [evp[31]] if g2 == 0 else []
                if pool_rel is not None:
                    w.append(pool_rel)
                tt = None
                for u in range(8):
                    mcc = 8 * g2 + u
                    tt = p.op("tensor", lambda E, u=u, mcc=mcc, t=t: E.transpose(
                        psb[:, 128 * u:128 * (u + 1)],
                        projsb[:, 512 * mcc + 128 * t:512 * mcc + 128 * (t + 1)],
                        idbf_sb[:, :]),
                        waits=w if u == 0 else [], inc="f3tr" if u == 7 else None)
                add = p.op("vector", lambda E, g2=g2, i=i: E.tensor_add(
                    xres[i % 2][:, 1024 * g2:1024 * (g2 + 1)],
                    xres[i % 2][:, 1024 * g2:1024 * (g2 + 1)], psb[:, :]),
                    waits=[tt, xt])
                pool_rel = add
            x4st[i] = p.dma("sync", x4_o[128 * i:128 * (i + 1), :],
                            xres[i % 2][:, :], f"f3x4st{i % 2}", waits=[add])
        tr_after = ("f3tr", p.val("f3tr"))
    es.close()
    pes.close()
    p.barrier("sync", waits=[("f3x4st0", p.val("f3x4st0")),
                             ("f3x4st1", p.val("f3x4st1"))])

def build_fused():
    p = Prog("fused")
    nc = p.nc
    N = {}
    EI = "ExternalInput"

    def di(name, shape, dt_, kind=None, shared=False):
        N[name] = nc.dram_tensor(name, shape, dt_, kind=kind or "Internal",
                                 addr_space="Shared" if shared else "Local").ap()
        return N[name]

    di("x", [TC, D], F32, EI)
    di("wq_p", [32, 128, 4096], F32, EI)
    di("wk_p", [8, 128, 4096], F32, EI)
    di("wv_p", [8, 128, 4096], F32, EI)
    di("wo_p", [32, 128, 4096], F32, EI)
    di("worg_p", [32, 128, 4096], F32, EI)
    di("up_p", [32, 128, 512], F32, EI)
    di("dn_p", [4, 128, 4096], F32, EI)
    di("wqt_p", [1, 128, 4096], F32, EI)
    di("wip_p", [81, 128, 4096], F32, EI)
    di("wout_p", [32, 128, 4096], F32, EI)
    di("wt3_p", [32, 128, 4096], F32, EI)
    di("at_p", [2, 128, 4096], F32, EI)
    di("bft_p", [32, 128, 256], F32, EI)
    di("keyt", [128, 8], F32, EI)
    di("sel", [8, 512], BF16, EI)
    di("sel2", [16, 256], BF16, EI)
    di("amask", [2, 128, 32 * 512], BF16, EI)
    di("ctab", [64, TC], F32, EI)
    di("stab", [64, TC], F32, EI)
    di("convw", [128, 192], F32, EI)
    di("convb", [128, 48], F32, EI)
    di("dtb", [64, 1], F32, EI)
    di("alog", [64, 1], F32, EI)
    di("dd", [1, 64], F32, EI)
    di("nwr", [1, 4096], F32, EI)
    di("bias3", [1, 4096], F32, EI)
    di("coef", [1, 16], F32, EI)
    di("scal", [1, 16], F32, EI)
    di("trisu", [128, 128], BF16, EI)
    di("maskt", [128, 128], F32, EI)
    di("idin", [128, 128], BF16, EI)
    di("idf", [128, 128], F32, EI)
    di("onesr", [1, 128], F32, EI)
    di("onesc", [128, 1], BF16, EI)
    di("ones1", [1, 1], BF16, EI)
    di("ones5", [1, 512], BF16, EI)
    di("ovm", [128, 4], F32, EI)
    di("efac", [1, 16], F32, EI)
    di("enfac", [1, 16], F32, EI)
    di("evalid", [1, 4], F32, EI)
    di("x4_o", [TC, D], F32, "ExternalOutput")

    di("q_d", [NH, 128, TC], BF16)
    di("kown_d", [NKV, 128, TC], BF16)
    di("vown_d", [NKV, 128, TC], BF16)
    di("kg_d", [4, NKV, 128, TC], BF16)
    di("vg_d", [4, NKV, 128, TC], BF16)
    di("x1_d", [TC, D], F32)
    di("ln1_d", [8, 128, 4096], BF16)
    di("pt_d", [128, 32], F32)
    di("ptg_d", [4, 128, 32], F32)
    di("x2_d", [TC, D], F32)
    di("ln2_d", [8, 128, 4096], BF16)
    di("ov_d", [128, 96], BF16)
    di("ovg_d", [4, 128, 96], BF16)
    di("y_d", [8, 128, 4096], F32)
    di("c_d", [8, 128, 1024], BF16)
    di("el_d", [8, 128, 64], F32)
    di("dc_d", [8, 1, 64], F32)
    di("gt_d", [8, 128, 4096], BF16)
    di("u_d", [128, 4096], F32)
    di("ds_d", [1, 64], F32)
    di("ug_d", [4, 128, 4096], F32)
    di("dsg_d", [4, 1, 64], F32)
    di("x3_d", [TC, D], F32)
    di("ln3_d", [8, 128, 4096], BF16)
    di("cum_d", [64, 128], F32)

    N["onesr_sb"] = nc.alloc_sbuf_tensor("g_onesr", [1, 128], F32)
    N["onesc_sb"] = nc.alloc_sbuf_tensor("g_onesc", [128, 1], BF16)
    N["idbf_sb"] = nc.alloc_sbuf_tensor("g_idbf", [128, 128], BF16)
    N["idf_sb"] = nc.alloc_sbuf_tensor("g_idf", [128, 128], F32)
    N["ones1_sb"] = nc.alloc_sbuf_tensor("g_ones1", [1, 1], BF16)
    N["ones5_sb"] = nc.alloc_sbuf_tensor("g_ones5", [1, 512], BF16)
    N["ovm_sb"] = nc.alloc_sbuf_tensor("g_ovm", [128, 4], F32)
    N["efac_sb"] = nc.alloc_sbuf_tensor("g_efac", [1, 16], F32)
    N["enfac_sb"] = nc.alloc_sbuf_tensor("g_enfac", [1, 16], F32)
    N["evalid_sb"] = nc.alloc_sbuf_tensor("g_eval", [1, 4], F32)
    p.dma("sync", N["ovm_sb"][:, :], N["ovm"][:, :], "gcld")
    p.dma("sync", N["efac_sb"][:, :], N["efac"][:, :], "gcld")
    p.dma("sync", N["enfac_sb"][:, :], N["enfac"][:, :], "gcld")
    p.dma("sync", N["evalid_sb"][:, :], N["evalid"][:, :], "gcld")
    p.dma("sync", N["onesr_sb"][:, :], N["onesr"][:, :], "gcld")
    p.dma("sync", N["onesc_sb"][:, :], N["onesc"][:, :], "gcld")
    p.dma("sync", N["idbf_sb"][:, :], N["idin"][:, :], "gcld")
    p.dma("sync", N["idf_sb"][:, :], N["idf"][:, :], "gcld")
    p.dma("sync", N["ones1_sb"][:, :], N["ones1"][:, :], "gcld")
    gc = p.dma("sync", N["ones5_sb"][:, :], N["ones5"][:, :], "gcld")
    deps0 = [gc]

    emit_a1(p, N, deps0)
    p.release_sems("a1")
    ck = p.collective("AllGather", ALU.bypass, CORE_GROUPS,
                      [N["kown_d"][:, :, :].rearrange("h q t -> (h q) t")],
                      [N["kg_d"][:, :, :, :].rearrange("r h q t -> (r h q) t")],
                      "ccK")
    cv = p.collective("AllGather", ALU.bypass, CORE_GROUPS,
                      [N["vown_d"][:, :, :].rearrange("h q t -> (h q) t")],
                      [N["vg_d"][:, :, :, :].rearrange("r h q t -> (r h q) t")],
                      "ccV")
    emit_a2(p, N, [ck, cv, gc])
    p.release_sems("a2")
    cpt = p.collective("AllGather", ALU.bypass, CORE_GROUPS,
                       [N["pt_d"][:, :]],
                       [N["ptg_d"][:, :, :].rearrange("r q k -> (r q) k")], "ccP")
    emit_f2(p, N, [cpt])
    p.release_sems("f2")
    cov = p.collective("AllGather", ALU.bypass, CORE_GROUPS,
                       [N["ov_d"][:, :]],
                       [N["ovg_d"][:, :, :].rearrange("r q k -> (r q) k")], "ccO")
    emit_m1(p, N, [cov])
    p.release_sems("m1")
    cu = p.collective("AllGather", ALU.bypass, CORE_GROUPS,
                      [N["u_d"][:, :]],
                      [N["ug_d"][:, :, :].rearrange("r q k -> (r q) k")], "ccU")
    cds = p.collective("AllGather", ALU.bypass, CORE_GROUPS,
                       [N["ds_d"][:, :]],
                       [N["dsg_d"][:, :, :].rearrange("r o k -> (r o) k")], "ccD")
    emit_m2(p, N, [cu, cds])
    p.release_sems("m2")
    emit_f3(p, N, [])
    return p.build()


def kernel_fused(inputs):
    X = np.ascontiguousarray(np.asarray(inputs["L0_attn_X"], np.float32))

    def f32(k):
        return np.asarray(inputs[k], np.float32)

    pk = {}
    pk["wq_p"] = pack_lhsT(f32("L0_attn_Wq"))
    pk["wk_p"] = pack_lhsT(f32("L0_attn_Wk"))
    pk["wv_p"] = pack_lhsT(f32("L0_attn_Wv"))
    pk["wo_p"] = pack_lhsT(f32("L0_attn_Wo"))
    pk["worg_p"] = pack_lhsT(np.ascontiguousarray(f32("L0_ffn_W_org").T))
    pk["up_p"] = pack_lhsT(np.ascontiguousarray(
        f32("L0_ffn_lora_up").reshape(512, 4096)))
    pk["dn_p"] = pack_lhsT(np.ascontiguousarray(
        f32("L0_ffn_lora_down").transpose(1, 0, 2).reshape(4096, 512)))
    pk["wqt_p"] = pack_lhsT(np.ascontiguousarray(f32("L0_ffn_Wq").T))
    ipT = np.ascontiguousarray(f32("L1_attn_in_proj_w").T)
    ipT = np.concatenate([ipT, np.zeros((4096, 64), np.float32)], axis=1)
    pk["wip_p"] = pack_lhsT(np.ascontiguousarray(ipT))
    pk["wout_p"] = pack_lhsT(np.ascontiguousarray(f32("L1_attn_out_proj_w").T))
    pk["wt3_p"] = pack_lhsT(np.ascontiguousarray(f32("L1_ffn_W").T))
    pk["at_p"] = pack_lhsT(np.ascontiguousarray(
        f32("L1_ffn_A").transpose(2, 0, 1).reshape(4096, 256)))
    pk["bft_p"] = pack_lhsT(np.ascontiguousarray(
        f32("L1_ffn_Bf").transpose(0, 2, 1).reshape(256, 4096)))
    pk["keyt"] = np.ascontiguousarray(f32("L0_ffn_lora_keys").T)
    sel = np.zeros((8, 512), BF)
    for j in range(4):
        for pp in range(128):
            sel[2 * j + pp // 64, 128 * j + pp] = BF(1.0)
    pk["sel"] = sel
    sel2 = np.zeros((16, 256), BF)
    for j in range(2):
        for pp in range(128):
            sel2[8 * j + pp // 16, 128 * j + pp] = BF(1.0)
    pk["sel2"] = sel2
    cw = f32("L1_attn_conv_weight")
    pk["convw"] = np.ascontiguousarray(cw.reshape(48, 128, 4).transpose(1, 0, 2)
                                       .reshape(128, 192))
    pk["convb"] = np.ascontiguousarray(
        f32("L1_attn_conv_bias").reshape(48, 128).T)
    pk["dtb"] = f32("L1_attn_dt_bias").reshape(64, 1)
    pk["alog"] = f32("L1_attn_A_log").reshape(64, 1)
    pk["dd"] = f32("L1_attn_D").reshape(1, 64)
    pk["nwr"] = f32("L1_attn_norm_weight").reshape(1, 4096)
    pk["bias3"] = f32("L1_ffn_bias").reshape(1, 4096)
    pk["coef"] = f32("L1_ffn_coeffs").reshape(1, 16)
    pk["scal"] = f32("L1_ffn_scales").reshape(1, 16)
    r_ = np.arange(128)
    pk["trisu"] = (r_[:, None] > r_[None, :]).astype(BF)
    pk["maskt"] = np.where(r_[:, None] <= r_[None, :], 0.0, MNEG).astype(np.float32)
    pk["idin"] = np.eye(128, dtype=BF)
    pk["idf"] = np.eye(128, dtype=np.float32)
    pk["onesr"] = np.ones((1, 128), np.float32)
    pk["onesc"] = np.ones((128, 1), BF)
    pk["ones1"] = np.ones((1, 1), BF)
    pk["ones5"] = np.ones((1, 512), BF)

    nc = build_fused()
    in_maps = []
    for c in range(NCORES):
        j = c % 4
        m = dict(pk)
        m["x"] = np.ascontiguousarray(X[c // 4, j * TC:(j + 1) * TC])
        cosb, sinb = rope_tables(j * TC, TC)
        m["ctab"], m["stab"] = cosb, sinb
        am = np.zeros((2, 128, 32 * 512), BF)
        for qb in range(2):
            q0 = j * TC + qb * 512
            kglob = (np.arange(32)[:, None, None] * 128 + r_[None, :, None])
            qglob = q0 + np.arange(512)[None, None, :]
            am[qb] = (kglob <= qglob).transpose(1, 0, 2).reshape(
                128, 32 * 512).astype(BF)
        m["amask"] = am
        ovm = np.zeros((128, 4), np.float32)
        if j > 0:
            ovm[:, j - 1] = 1.0
        m["ovm"] = ovm
        e = np.zeros((4, 4), np.float32)
        val = np.zeros((1, 4), np.float32)
        for r in range(4):
            if r < j:
                val[0, r] = 1.0
            for mm in range(4):
                if r < mm < j:
                    e[r, mm] = 1.0
        m["efac"] = e.reshape(1, 16)
        m["enfac"] = (1.0 - e).reshape(1, 16)
        m["evalid"] = val
        in_maps.append(m)
    res = run_bass_kernel_spmd(nc, in_maps, core_ids=list(range(NCORES)))
    out = np.empty((B, T, D), np.float32)
    for c in range(NCORES):
        out[c // 4, (c % 4) * TC:(c % 4 + 1) * TC] = np.asarray(
            res.results[c]["x4_o"])
    return out
